# version 11
# baseline (speedup 1.0000x reference)
import os
import numpy as np
import concourse.bass as bass
import concourse.mybir as mybir
from concourse.bass_utils import run_bass_kernel_spmd

F32 = mybir.dt.float32
BF16 = mybir.dt.bfloat16
I32 = mybir.dt.int32
AF = mybir.ActivationFunctionType
ALU = mybir.AluOpType
AX = mybir.AxisListType

D = 1024
T = 2048
NCH = 8
PBYTES = 229000

ENGINES = ("T", "V", "S", "G", "Y")
DMA_DEPTH = 4
DMA_SEMS = {q: [f"d{q}{j}" for j in range(DMA_DEPTH)] for q in ("Y", "S", "G")}
ALL_DMA_SEMS = [s for v in DMA_SEMS.values() for s in v]


class Buf:
    __slots__ = ("name", "w", "rs", "pend", "serial")

    def __init__(self, name="", serial=False):
        self.name = name
        self.w = None
        self.rs = []
        self.pend = []
        self.serial = serial


class Tile:
    __slots__ = ("h", "buf")

    def __init__(self, h, buf):
        self.h = h
        self.buf = buf

    def __getitem__(self, idx):
        return self.h[idx]

    @property
    def ap(self):
        return self.h[:]


class Ker:
    def __init__(self, nc):
        self.nc = nc
        self.recs = {e: [] for e in ENGINES}
        self.sems = {}
        for s in list(ENGINES) + ALL_DMA_SEMS:
            self.sems[s] = nc.alloc_semaphore(name="sem_" + s)
        self.cnt = {s: 0 for s in self.sems}
        self.dmacnt = {q: 0 for q in DMA_SEMS}
        self.waited = {e: {} for e in ENGINES}
        self.nops = 0
        self.const_vals = {}
        self.cst = nc.alloc_sbuf_tensor("cst", [128, 64], F32)
        self.base = int(nc.sbuf_base) + 4 * 1024
        self.const(0.0)
        self.const(1.0)
        self.residue = []
        self.pb = [Tile(nc.alloc_psum_tensor(f"pb{i}", [128, 512], F32), Buf(f"pb{i}", serial=True))
                   for i in range(6)]
        self.pbh = [Tile(nc.alloc_psum_tensor(f"pbh{i}", [128, 1024], BF16), Buf(f"pbh{i}", serial=True))
                    for i in range(2)]
        self.init_tok = ("V", 1)
        self.cnt["V"] = 1

    def const(self, v):
        v = float(np.float32(v))
        if v in self.const_vals:
            return v
        i = len(self.const_vals)
        assert i < 64
        self.const_vals[v] = i
        self.nc.const_aps.register(F32, v, self.cst[:, i:i + 1])
        return v

    def cap(self, v):
        v = self.const(v)
        i = self.const_vals[v]
        return self.cst[:, i:i + 1]

    def alloc(self, name, shape, dtype, align=64):
        esz = 2 if dtype == BF16 else 4
        nbytes = int(np.prod(shape[1:])) * esz
        start = (self.base + align - 1) // align * align
        end = start + nbytes
        assert end <= PBYTES, f"SBUF overflow: {name} at {start}+{nbytes}"
        self.base = end
        self.nops2 = getattr(self, "nops2", 0) + 1
        h = self.nc.alloc_sbuf_tensor_at(f"{name}_{start}_{self.nops2}", shape, dtype,
                                         offset=start)
        b = Buf(name)
        for (lo, hi, toks) in self.residue:
            if lo < end and start < hi:
                b.pend.extend(toks)
        return Tile(h, b)

    def mark(self):
        return self.base

    def release(self, m, tiles):
        toks = []
        for t in tiles:
            b = t.buf
            if b.w:
                toks.append(b.w)
            toks.extend(b.rs)
        if self.base > m:
            self.residue.append((m, self.base, toks))
        self.base = m

    def op(self, eng, emit, reads=(), writes=(), waits=(), dma=False, n_inc=1):
        toks = [self.init_tok]
        for b in reads:
            b = b.buf if isinstance(b, Tile) else b
            if b.w:
                toks.append(b.w)
            if b.serial:
                toks.extend(b.rs)
            toks.extend(b.pend)
        for b in writes:
            b = b.buf if isinstance(b, Tile) else b
            if b.w:
                toks.append(b.w)
            toks.extend(b.rs)
            toks.extend(b.pend)
        toks.extend(waits)
        need = {}
        for (s, v) in toks:
            if v > need.get(s, -1):
                need[s] = v
        wl = []
        for s, v in sorted(need.items()):
            if v > self.waited[eng].get(s, -1) and v > 0:
                wl.append((s, v))
                self.waited[eng][s] = v
        if dma:
            i = self.dmacnt[eng]
            self.dmacnt[eng] += 1
            sem = DMA_SEMS[eng][i % DMA_DEPTH]
            gate = 16 * (i // DMA_DEPTH)
            if gate > self.waited[eng].get(sem, -1):
                wl.append((sem, gate))
                self.waited[eng][sem] = gate
            self.cnt[sem] += 16
            tok = (sem, self.cnt[sem])
            amt = 16
        else:
            sem = eng
            self.cnt[sem] += 1
            tok = (sem, self.cnt[sem])
            amt = 1
        self.recs[eng].append((wl, emit, self.sems[sem], amt))
        self.nops += 1
        for b in reads:
            b = b.buf if isinstance(b, Tile) else b
            b.rs.append(tok)
        for b in writes:
            b = b.buf if isinstance(b, Tile) else b
            b.w = tok
            b.rs = []
            b.pend = []
        return tok

    def dma(self, eng, dst, src, reads=(), writes=()):
        return self.op(eng, (lambda e, d=dst, s=src: e.dma_start(d, s)),
                       reads=reads, writes=writes, dma=True)

    def wload(self, nm, dst, src, writes=()):
        rd = [self.WB[nm]] if nm in getattr(self, "WB", {}) else []
        return self.dma("Y", dst, src, reads=rd, writes=writes)

    def tt(self, eng, out, a, b, alu, reads=(), writes=()):
        return self.op(eng, (lambda e, o=out, x=a, y=b, u=alu: e.tensor_tensor(o, x, y, u)),
                       reads=reads, writes=writes)

    def ts(self, eng, out, a, s1, op0, s2=None, op1=ALU.bypass, reads=(), writes=()):
        if isinstance(s1, float):
            s1 = float(np.float32(s1))
        if isinstance(s2, float):
            s2 = float(np.float32(s2))
        return self.op(eng, (lambda e, o=out, x=a, a1=s1, a2=s2, p0=op0, p1=op1:
                             e.tensor_scalar(o, x, a1, a2, p0, p1)),
                       reads=reads, writes=writes)

    def stt(self, eng, out, in0, scalar, in1, op0, op1, reads=(), writes=()):
        return self.op(eng, (lambda e, o=out, i0=in0, sc=scalar, i1=in1, p0=op0, p1=op1:
                             e.scalar_tensor_tensor(o, i0, sc, i1, p0, p1)),
                       reads=reads, writes=writes)

    def act(self, out, in_, fn, scale=1.0, bias=0.0, reads=(), writes=(), accum_out=None):
        if isinstance(scale, float):
            scale = self.const(scale)
        if isinstance(bias, float):
            bias = self.const(bias)
        return self.op("S", (lambda e, o=out, i=in_, f=fn, sc=scale, bi=bias, ac=accum_out:
                             e.activation(o, i, f, bias=bi, scale=sc, accum_out=ac)),
                       reads=reads, writes=writes)

    def copy(self, eng, out, in_, reads=(), writes=()):
        if eng == "S":
            return self.op("S", (lambda e, o=out, i=in_: e.copy(o, i)),
                           reads=reads, writes=writes)
        return self.op(eng, (lambda e, o=out, i=in_: e.tensor_copy(o, i)),
                       reads=reads, writes=writes)

    def memset(self, eng, ap, v, writes=()):
        return self.op(eng, (lambda e, a=ap, x=v: e.memset(a, x)), writes=writes)

    def red(self, eng, out, in_, alu, axis=AX.X, reads=(), writes=()):
        return self.op(eng, (lambda e, o=out, i=in_, u=alu, x=axis:
                             e.tensor_reduce(o, i, x, u)),
                       reads=reads, writes=writes)

    def recip(self, out, in_, reads=(), writes=()):
        return self.op("V", (lambda e, o=out, i=in_: e.reciprocal(o, i)),
                       reads=reads, writes=writes)

    def scan(self, eng, out, d0, d1, init, op0, op1, reads=(), writes=()):
        return self.op(eng, (lambda e, o=out, a=d0, b=d1, i=init, p0=op0, p1=op1:
                             e.tensor_tensor_scan(o, a, b, i, p0, p1)),
                       reads=reads, writes=writes)

    def mmop(self, bank, pairs, reads=(), accum=False, out_slice=None, nparts=128,
             close=True):
        ap = bank.h[0:nparts, :] if out_slice is None else bank.h[0:nparts, out_slice]

        def em(e, ap=ap, pairs=tuple(pairs), accum=accum, close=close):
            n = len(pairs)
            ins = None
            for i, (l, r) in enumerate(pairs):
                ins = e.matmul(ap, l, r, start=(i == 0 and not accum),
                               stop=(i == n - 1 and close))
            return ins
        rd = list(reads)
        if accum:
            rd.append(bank)
        return self.op("T", em, reads=rd, writes=[bank])

    def transp(self, bank, src_ap, ident_ap, out_slice, reads=(), nparts=128):
        return self.op("T", (lambda e, b=bank, s=src_ap, i=ident_ap, o=out_slice, np_=nparts:
                             e.transpose(b.h[0:np_, o], s, i)),
                       reads=list(reads), writes=[bank])

    def affsel(self, out, in_, pattern, cmp, fill, cm, base=0, reads=(), writes=()):
        return self.op("G", (lambda e, o=out, i=in_, p=pattern, c=cmp, f=fill, m=cm, b=base:
                             e.affine_select(o, i, p, c, f, base=b, channel_multiplier=m)),
                       reads=reads, writes=writes)

    def iota(self, out, pattern, cm=0, base=0, writes=()):
        return self.op("G", (lambda e, o=out, p=pattern, m=cm, b=base:
                             e.iota(o, p, base=b, channel_multiplier=m)),
                       writes=writes)

    def finalize(self):
        nc = self.nc
        recs = self.recs
        sems = self.sems
        cvals = sorted(self.const_vals.items(), key=lambda kv: kv[1])
        final = [(s, self.cnt[s]) for s in ALL_DMA_SEMS if self.cnt[s] > 0]

        with nc.Block() as block:
            @block.vector
            def _(v):
                ins = None
                for val, i in cvals:
                    ins = v.memset(self.cst[:, i:i + 1], val)
                if ins is None:
                    ins = v.memset(self.cst[:, 0:1], 0.0)
                ins.then_inc(sems["V"], 1)
                for (wl, emit, sem, amt) in recs["V"]:
                    for (s, val) in wl:
                        v.wait_ge(sems[s], val)
                    emit(v).then_inc(sem, amt)

            @block.scalar
            def _(s):
                for (wl, emit, sem, amt) in recs["S"]:
                    for (sn, val) in wl:
                        s.wait_ge(sems[sn], val)
                    emit(s).then_inc(sem, amt)
                for (sn, val) in final:
                    if sn.startswith("dS"):
                        s.wait_ge(sems[sn], val)

            @block.tensor
            def _(t):
                for (wl, emit, sem, amt) in recs["T"]:
                    for (sn, val) in wl:
                        t.wait_ge(sems[sn], val)
                    emit(t).then_inc(sem, amt)

            @block.gpsimd
            def _(g):
                for (wl, emit, sem, amt) in recs["G"]:
                    for (sn, val) in wl:
                        g.wait_ge(sems[sn], val)
                    emit(g).then_inc(sem, amt)
                for (sn, val) in final:
                    if sn.startswith("dG"):
                        g.wait_ge(sems[sn], val)

            @block.sync
            def _(y):
                for (wl, emit, sem, amt) in recs["Y"]:
                    for (sn, val) in wl:
                        y.wait_ge(sems[sn], val)
                    emit(y).then_inc(sem, amt)
                for (sn, val) in final:
                    if sn.startswith("dY"):
                        y.wait_ge(sems[sn], val)



def setup_consts(k):
    k.ident = k.alloc("ident", [128, 128], BF16)
    k.identf = k.alloc("identf", [128, 128], F32)
    k.ones_bf = k.alloc("ones_bf", [128, 128], BF16)
    k.ones_f = k.alloc("ones_f", [128, 128], F32)
    k.ones1 = k.alloc("ones1", [1, 128], F32)
    k.memset("V", k.ident.ap, 1.0, writes=[k.ident])
    k.memset("V", k.identf.ap, 1.0, writes=[k.identf])
    k.memset("V", k.ones_bf.ap, 1.0, writes=[k.ones_bf])
    k.memset("V", k.ones_f.ap, 1.0, writes=[k.ones_f])
    k.memset("V", k.ones1.ap, 1.0, writes=[k.ones1])
    k.affsel(k.ident.ap, k.ident.ap, [[1, 128]], ALU.is_equal, 0.0, -1,
             reads=[k.ident], writes=[k.ident])
    k.affsel(k.identf.ap, k.identf.ap, [[1, 128]], ALU.is_equal, 0.0, -1,
             reads=[k.identf], writes=[k.identf])
    k.e8sel = k.alloc("e8sel", [8, 8 * 128], F32)
    k.memset("V", k.e8sel.ap, 1.0, writes=[k.e8sel])
    k.affsel(k.e8sel.ap, k.e8sel.ap, [[1, 8 * 128]], ALU.is_ge, 0.0, -128,
             reads=[k.e8sel], writes=[k.e8sel])
    k.affsel(k.e8sel.ap, k.e8sel.ap, [[-1, 8 * 128]], ALU.is_ge, 0.0, 128, base=127,
             reads=[k.e8sel], writes=[k.e8sel])


def emit_layernorm(k, xtiles, out_bf, scr, eps=1e-5):
    x2s, mb, rb = scr["x2s"], scr["mb"], scr["rb"]
    t0, t1 = scr["t0"], scr["t1"]
    for tc in range(T // 512):
        sl = slice(tc * 512, (tc + 1) * 512)
        xs = x2s[tc % 2]
        for c in range(NCH):
            k.act(xs.h[:, c * 512:(c + 1) * 512], xtiles[c].h[:, sl], AF.Square,
                  reads=[xtiles[c]], writes=[xs])
        k.mmop(k.pb[0], [(k.ones_f.ap, xtiles[c].h[:, sl]) for c in range(NCH)],
               reads=[k.ones_f] + list(xtiles))
        k.ts("V", mb.h[:, sl], k.pb[0].ap, 1.0 / D, ALU.mult, reads=[k.pb[0]], writes=[mb])
        k.mmop(k.pb[1], [(k.ones_bf.ap, xs.h[:, c * 512:(c + 1) * 512])
                         for c in range(NCH)], reads=[k.ones_bf, xs])
        k.ts("V", rb.h[:, sl], k.pb[1].ap, 1.0 / D, ALU.mult, reads=[k.pb[1]], writes=[rb])
    k.tt("G", t0.ap, mb.ap, mb.ap, ALU.mult, reads=[mb], writes=[t0])
    k.tt("V", rb.ap, rb.ap, t0.ap, ALU.subtract, reads=[rb, t0], writes=[rb])
    k.act(t0.ap, rb.ap, AF.Sqrt, bias=k.const(eps), reads=[rb], writes=[t0])
    k.recip(rb.ap, t0.ap, reads=[t0], writes=[rb])
    for c in range(NCH):
        csl = slice(c * T, (c + 1) * T)
        tm = t0 if c % 2 == 0 else t1
        k.tt("G", tm.ap, xtiles[c].ap, mb.ap, ALU.subtract,
             reads=[xtiles[c], mb], writes=[tm])
        k.tt("V", out_bf.h[:, csl], tm.ap, rb.ap, ALU.mult,
             reads=[tm, rb], writes=[out_bf])
    return out_bf


def emit_ln_stats(k, xtiles, scr, eps=1e-5):
    x2s, mb, rb = scr["x2s"], scr["mb"], scr["rb"]
    t0 = scr["t0"]
    for tc in range(T // 512):
        sl = slice(tc * 512, (tc + 1) * 512)
        xs = x2s[tc % 2]
        for c in range(NCH):
            k.act(xs.h[:, c * 512:(c + 1) * 512], xtiles[c].h[:, sl], AF.Square,
                  reads=[xtiles[c]], writes=[xs])
        k.mmop(k.pb[0], [(k.ones_f.ap, xtiles[c].h[:, sl]) for c in range(NCH)],
               reads=[k.ones_f] + list(xtiles))
        k.ts("V", mb.h[:, sl], k.pb[0].ap, 1.0 / D, ALU.mult, reads=[k.pb[0]], writes=[mb])
        k.mmop(k.pb[1], [(k.ones_bf.ap, xs.h[:, c * 512:(c + 1) * 512])
                         for c in range(NCH)], reads=[k.ones_bf, xs])
        k.ts("V", rb.h[:, sl], k.pb[1].ap, 1.0 / D, ALU.mult, reads=[k.pb[1]], writes=[rb])
    k.tt("G", t0.ap, mb.ap, mb.ap, ALU.mult, reads=[mb], writes=[t0])
    k.tt("V", rb.ap, rb.ap, t0.ap, ALU.subtract, reads=[rb, t0], writes=[rb])
    k.act(t0.ap, rb.ap, AF.Sqrt, bias=k.const(eps), reads=[rb], writes=[t0])
    k.recip(rb.ap, t0.ap, reads=[t0], writes=[rb])


def ln_scratch(k, A):
    x2 = A("lnx2a", [128, NCH * 512], BF16)
    t0 = A("lnt0", [128, T], F32)
    return dict(x2s=[x2, x2],
                mb=A("lnmb", [128, T], F32), rb=A("lnrb", [128, T], F32),
                t0=t0, t1=t0)



def step_moe(k, W):
    m0 = k.mark()
    tiles = []

    def A(name, shape, dtype):
        t = k.alloc(name, shape, dtype)
        tiles.append(t)
        return t

    xn = A("xn", [128, NCH * T], BF16)
    combf = A("combf", [8, T], F32)
    m1_scr = k.mark()
    stiles = []

    def A2(name, shape, dtype):
        t = k.alloc(name, shape, dtype)
        stiles.append(t)
        return t

    scr = ln_scratch(k, A2)
    emit_layernorm(k, k.X, xn, scr)

    A_outer = A
    A = A2
    gw = A("gw", [128, NCH * 8], BF16)
    k.wload("moe_gate_wT", gw.h[:].rearrange("p (c e) -> p c e", e=8),
          W["moe_gate_wT"].rearrange("(c p) e -> p c e", p=128), writes=[gw])
    logit = A("logit", [128, 16 * 8], F32)
    comb = A("comb", [128, 16 * 8], F32)
    for tt_ in range(16):
        k.mmop(k.pb[0], [(xn.h[:, c * T + tt_ * 128: c * T + tt_ * 128 + 128],
                          gw.h[:, c * 8:(c + 1) * 8]) for c in range(NCH)],
               reads=[xn, gw], out_slice=slice(0, 8))
        k.copy("V", logit.h[:, tt_ * 8:(tt_ + 1) * 8], k.pb[0].h[:, 0:8],
               reads=[k.pb[0]], writes=[logit])
    lg3 = logit.h[:].rearrange("p (t e) -> p t e", e=8)
    mx = A("mx", [128, 16], F32)
    sm = A("sm", [128, 16], F32)
    k.red("V", mx.ap, lg3, ALU.max, reads=[logit], writes=[mx])
    bc8 = bass.AP(mx.h, 0, [[16, 128], [1, 16], [0, 8]])
    k.tt("V", lg3, lg3, bc8, ALU.subtract, reads=[logit, mx], writes=[logit])
    k.act(logit.ap, logit.ap, AF.Exp, reads=[logit], writes=[logit])
    k.red("V", sm.ap, lg3, ALU.add, reads=[logit], writes=[sm])
    k.recip(sm.ap, sm.ap, reads=[sm], writes=[sm])
    bcs = bass.AP(sm.h, 0, [[16, 128], [1, 16], [0, 8]])
    k.tt("V", lg3, lg3, bcs, ALU.mult, reads=[logit, sm], writes=[logit])
    g4 = A("g4", [128, 16 * 4], F32)
    g43 = g4.h[:].rearrange("p (t g) -> p t g", g=4)
    k.red("V", g43, logit.h[:].rearrange("p (t g e) -> p t g e", g=4, e=2),
          ALU.max, reads=[logit], writes=[g4])
    m1 = A("m1", [128, 16], F32)
    m2 = A("m2", [128, 16], F32)
    msk = A("msk", [128, 16 * 4], F32)
    tmp4 = A("tmp4", [128, 16 * 4], F32)
    k.red("V", m1.ap, g43, ALU.max, reads=[g4], writes=[m1])
    bc4 = bass.AP(m1.h, 0, [[16, 128], [1, 16], [0, 4]])
    k.tt("V", msk.h[:].rearrange("p (t g) -> p t g", g=4), g43, bc4, ALU.is_lt,
         reads=[g4, m1], writes=[msk])
    k.tt("V", tmp4.ap, g4.ap, msk.ap, ALU.mult, reads=[g4, msk], writes=[tmp4])
    k.red("V", m2.ap, tmp4.h[:].rearrange("p (t g) -> p t g", g=4), ALU.max,
          reads=[tmp4], writes=[m2])
    bcm2 = bass.AP(m2.h, 0, [[16, 128], [1, 16], [0, 4]])
    k.tt("V", msk.h[:].rearrange("p (t g) -> p t g", g=4), g43, bcm2, ALU.is_ge,
         reads=[g4, m2], writes=[msk])
    smask = bass.AP(msk.h, 0, [[64, 128], [4, 16], [1, 4], [0, 2]])
    k.tt("V", comb.h[:].rearrange("p (t g e) -> p t g e", g=4, e=2),
         logit.h[:].rearrange("p (t g e) -> p t g e", g=4, e=2), smask, ALU.mult,
         reads=[logit, msk], writes=[comb])
    c8 = comb.h[:].rearrange("p (t e) -> p t e", e=8)
    e8 = A("e8", [128, 16 * 8], F32)
    k.red("V", m1.ap, c8, ALU.max, reads=[comb], writes=[m1])
    bce = bass.AP(m1.h, 0, [[16, 128], [1, 16], [0, 8]])
    k.tt("V", e8.h[:].rearrange("p (t e) -> p t e", e=8), c8, bce, ALU.is_lt,
         reads=[comb, m1], writes=[e8])
    k.tt("V", e8.ap, e8.ap, comb.ap, ALU.mult, reads=[e8, comb], writes=[e8])
    k.red("V", m2.ap, e8.h[:].rearrange("p (t e) -> p t e", e=8), ALU.max,
          reads=[e8], writes=[m2])
    bcm2e = bass.AP(m2.h, 0, [[16, 128], [1, 16], [0, 8]])
    k.tt("V", e8.h[:].rearrange("p (t e) -> p t e", e=8), c8, bcm2e, ALU.is_ge,
         reads=[comb, m2], writes=[e8])
    k.tt("V", m1.ap, m1.ap, m2.ap, ALU.add, reads=[m1, m2], writes=[m1])
    k.recip(m1.ap, m1.ap, reads=[m1], writes=[m1])
    k.tt("V", comb.ap, comb.ap, e8.ap, ALU.mult, reads=[comb, e8], writes=[comb])
    bcn = bass.AP(m1.h, 0, [[16, 128], [1, 16], [0, 8]])
    k.tt("V", c8, c8, bcn, ALU.mult, reads=[comb, m1], writes=[comb])
    for tt_ in range(16):
        k.transp(k.pb[2], comb.h[:, tt_ * 8:(tt_ + 1) * 8], k.identf.ap,
                 slice(0, 128), reads=[comb, k.identf], nparts=8)
        k.copy("V", combf.h[0:8, tt_ * 128:(tt_ + 1) * 128], k.pb[2].h[0:8, 0:128],
               reads=[k.pb[2]], writes=[combf])
    combf_b = combf

    k.release(m1_scr, stiles)
    A = A_outer

    wg_b = [A(f"wgb{i}", [128, NCH * 128], BF16) for i in range(2)]
    wu_b = [A(f"wub{i}", [128, NCH * 128], BF16) for i in range(2)]
    w2e = A("w2e", [128, 16 * D], BF16)
    actt = A("actt", [128, 8 * T], BF16)
    cbf = A("cbf", [128, 4 * 512], BF16)
    sil2 = [A(f"sil{i}", [128, 512], F32) for i in range(4)]
    for e in range(8):
        k.wload("moe_w2T", w2e.h[:].rearrange("p (kc n) -> p kc n", n=D),
              W["moe_w2T"][e].rearrange("(kc p) n -> p kc n", p=128), writes=[w2e])
        for tc4 in range(4):
            k.mmop(k.pb[2 + tc4 % 2], [(k.e8sel.h[0:8, e * 128:(e + 1) * 128],
                                        combf.h[0:8, tc4 * 512:(tc4 + 1) * 512])],
                   reads=[k.e8sel, combf])
            k.copy("V", cbf.h[:, tc4 * 512:(tc4 + 1) * 512], k.pb[2 + tc4 % 2].ap,
                   reads=[k.pb[2 + tc4 % 2]], writes=[cbf])
        for half in range(2):
            for ib8 in range(8):
                ib = half * 8 + ib8
                wg = wg_b[ib % 2]
                wu = wu_b[ib % 2]
                k.wload("moe_w13T", wg.h[:].rearrange("p (c i) -> p c i", i=128),
                      W["moe_w13T"][e, :, ib * 128:(ib + 1) * 128].rearrange(
                          "(c p) i -> p c i", p=128), writes=[wg])
                k.wload("moe_w13T", wu.h[:].rearrange("p (c i) -> p c i", i=128),
                      W["moe_w13T"][e, :, 2048 + ib * 128: 2048 + (ib + 1) * 128].rearrange(
                          "(c p) i -> p c i", p=128), writes=[wu])
                for tcc in range(4):
                    tsl = slice(tcc * 512, (tcc + 1) * 512)
                    bg = k.pb[(tcc % 2) * 2]
                    bu = k.pb[(tcc % 2) * 2 + 1]
                    sil = sil2[tcc % 4]
                    k.mmop(bg, [(wg.h[:, c * 128:(c + 1) * 128],
                                 xn.h[:, c * T + tcc * 512: c * T + tcc * 512 + 512])
                                for c in range(NCH)], reads=[wg, xn])
                    k.mmop(bu, [(wu.h[:, c * 128:(c + 1) * 128],
                                 xn.h[:, c * T + tcc * 512: c * T + tcc * 512 + 512])
                                for c in range(NCH)], reads=[wu, xn])
                    k.act(sil.ap, bg.ap, AF.Silu, reads=[bg], writes=[sil])
                    k.tt("V", sil.ap, sil.ap, bu.ap, ALU.mult, reads=[sil, bu], writes=[sil])
                    k.tt("G", actt.h[:, ib8 * T + tcc * 512: ib8 * T + tcc * 512 + 512],
                         sil.ap, cbf.h[:, tsl], ALU.mult, reads=[sil, cbf], writes=[actt])
            for tcc in range(4):
                for dp in range(NCH):
                    bank = k.pb[4 + dp % 2]
                    k.mmop(bank, [(w2e.h[:, (half * 8 + kc) * D + dp * 128:
                                          (half * 8 + kc) * D + dp * 128 + 128],
                                   actt.h[:, kc * T + tcc * 512: kc * T + tcc * 512 + 512])
                                  for kc in range(8)], reads=[w2e, actt])
                    xt = k.X[dp]
                    k.tt("V", xt.h[:, tcc * 512:(tcc + 1) * 512],
                         xt.h[:, tcc * 512:(tcc + 1) * 512], bank.ap, ALU.add,
                         reads=[xt, bank], writes=[xt])
    k.release(m0, tiles)



STEP_FNS = {}
STEP_FNS["moe"] = step_moe

STEP_WEIGHTS = {
    "moe": {"moe_gate_wT": (D, 8), "moe_w13T": (8, D, 4096), "moe_w2T": (8, 2048, D)},
}


F32_WEIGHTS = {"l0a_lns", "l0a_cms_b", "mam_conv", "mam_db", "mam_alog",
               "gd_norms", "gd_conv", "fp4_wg", "fp4_wu", "fp4_wd"}


def build_program(steps):
    nc = bass.Bass("TRN2", target_bir_lowering=False)
    xin = nc.dram_tensor("xT", [D, T], F32, kind="ExternalInput").ap()
    yout = nc.dram_tensor("y", [D, T], F32, kind="ExternalOutput").ap()
    W = {}
    seen = set()
    order = []
    for s in steps:
        for nm, shp in STEP_WEIGHTS[s].items():
            if nm not in seen:
                seen.add(nm)
                W[nm] = nc.dram_tensor(nm, list(shp), F32, kind="ExternalInput").ap()
                order.append((nm, shp))
    k = Ker(nc)
    setup_consts(k)
    k.X = [k.alloc(f"x{c}", [128, T], F32) for c in range(NCH)]
    for c in range(NCH):
        k.dma("Y", k.X[c].ap, xin[c * 128:(c + 1) * 128, :], writes=[k.X[c]])
    W2 = dict(W)
    k.WB = {}
    for nm, shp in order:
        if nm in F32_WEIGHTS:
            continue
        flat = int(np.prod(shp))
        assert flat % 128 == 0
        cols = flat // 128
        tw = nc.dram_tensor(nm + "_bf", list(shp), BF16, kind="Internal").ap()
        twb = Buf(nm + "_bf")
        W2[nm] = tw
        k.WB[nm] = twb
        sflat = bass.AP(W[nm].tensor, 0, [[cols, 128], [1, cols]])
        dflat = bass.AP(tw.tensor, 0, [[cols, 128], [1, cols]])
        k.dma("G", dflat, sflat, writes=[twb])
    for s in steps:
        STEP_FNS[s](k, W2)
    for c in range(NCH):
        k.dma("Y", yout[c * 128:(c + 1) * 128, :], k.X[c].ap, reads=[k.X[c]])
    k.finalize()
    return nc


def host_weights(inputs, steps):
    out = {}
    for s in steps:
        if s == "l1_attn_b":
            ii = inputs
            nw = np.stack([np.asarray(ii["L1_attn_b_norm1_w"]), np.asarray(ii["L1_attn_b_norm2_w"])])
            out["gd_norms"] = np.ascontiguousarray(nw.reshape(2, 8, 128).transpose(2, 0, 1).reshape(128, 16))
            cw = np.asarray(ii["L1_attn_b_conv_w"])
            cb = np.asarray(ii["L1_attn_b_conv_b"])[:, None]
            cc_ = np.concatenate([cw, cb], axis=1)
            out["gd_conv"] = np.ascontiguousarray(cc_.reshape(8, 128, 5).transpose(2, 1, 0).reshape(5 * 128, 8).T) if False else np.ascontiguousarray(cc_.reshape(8, 128, 5).transpose(1, 2, 0).reshape(128, 40))
            for nm, inm in (("gd_wq", "L1_attn_b_wq"), ("gd_wk", "L1_attn_b_wk"),
                            ("gd_wv", "L1_attn_b_wv"), ("gd_wb", "L1_attn_b_wb"),
                            ("gd_wa", "L1_attn_b_wa"), ("gd_wo", "L1_attn_b_wo"),
                            ("gd_wg", "L1_attn_b_wg"), ("gd_wu", "L1_attn_b_wu"),
                            ("gd_wd", "L1_attn_b_wd")):
                out[nm] = np.ascontiguousarray(np.asarray(ii[inm]))
        if s == "l0_attn_b":
            ii = inputs
            out["mam_win"] = np.ascontiguousarray(np.asarray(ii["L0_attn_b_W_in"]))
            cw = np.asarray(ii["L0_attn_b_conv_w"])[:, 0, :]
            out["mam_conv"] = np.ascontiguousarray(cw.reshape(8, 128, 3).transpose(1, 0, 2).reshape(128, 24))
            out["mam_wbc"] = np.ascontiguousarray(np.concatenate(
                [np.asarray(ii["L0_attn_b_W_B"]), np.asarray(ii["L0_attn_b_W_C"])], axis=1))
            out["mam_wd"] = np.ascontiguousarray(np.asarray(ii["L0_attn_b_W_delta"]))
            out["mam_db"] = np.ascontiguousarray(
                np.asarray(ii["L0_attn_b_delta_base"]).reshape(8, 128).T)
            out["mam_wt"] = np.ascontiguousarray(np.asarray(ii["L0_attn_b_W_tau"]))
            out["mam_alog"] = np.ascontiguousarray(
                np.asarray(ii["L0_attn_b_A_log"]).reshape(8, 128, 16).transpose(1, 0, 2).reshape(128, 128))
            out["mam_wout"] = np.ascontiguousarray(np.asarray(ii["L0_attn_b_W_out"]))
        if s == "l0_attn_a":
            ii = inputs
            lns = np.stack([np.asarray(ii["L0_attn_a_ln1_g"]), np.asarray(ii["L0_attn_a_ln1_b"]),
                            np.asarray(ii["L0_attn_a_ln2_g"]), np.asarray(ii["L0_attn_a_ln2_b"])])
            out["l0a_lns"] = np.ascontiguousarray(
                lns.reshape(4, 8, 128).transpose(2, 0, 1).reshape(128, 32))
            out["l0a_wq"] = np.ascontiguousarray(np.asarray(ii["L0_attn_a_wq"]))
            out["l0a_wk"] = np.ascontiguousarray(np.asarray(ii["L0_attn_a_wk"]))
            out["l0a_wv"] = np.ascontiguousarray(np.asarray(ii["L0_attn_a_wv"]))
            out["l0a_wo"] = np.ascontiguousarray(np.asarray(ii["L0_attn_a_wo"]))
            out["l0a_cms_w1"] = np.ascontiguousarray(np.asarray(ii["L0_attn_a_cms_w1"]))
            out["l0a_cms_w2"] = np.ascontiguousarray(np.asarray(ii["L0_attn_a_cms_w2"]))
            b1 = np.asarray(ii["L0_attn_a_cms_b1"]).reshape(3, 32, 128).transpose(0, 2, 1)
            b2 = np.asarray(ii["L0_attn_a_cms_b2"]).reshape(3, 8, 128).transpose(0, 2, 1)
            out["l0a_cms_b"] = np.ascontiguousarray(np.concatenate([b1, b2], axis=2))
        if s == "l1_attn_a":
            out["att_wq"] = np.ascontiguousarray(np.asarray(inputs["L1_attn_a_wq"]).reshape(1024, 1024))
            out["att_wk"] = np.ascontiguousarray(np.asarray(inputs["L1_attn_a_wk"]).reshape(1024, 1024))
            out["att_wv"] = np.ascontiguousarray(np.asarray(inputs["L1_attn_a_wv"]).reshape(1024, 1024))
            out["att_wo"] = np.ascontiguousarray(np.asarray(inputs["L1_attn_a_wo"]).reshape(1024, 1024))
        if s == "l1_ffn":
            out["fp4_wg"] = np.ascontiguousarray(np.asarray(inputs["L1_ffn_w_gate"]))
            out["fp4_wu"] = np.ascontiguousarray(np.asarray(inputs["L1_ffn_w_up"]))
            out["fp4_wd"] = np.ascontiguousarray(np.asarray(inputs["L1_ffn_w_down"]))
        if s == "moe":
            out["moe_gate_wT"] = np.ascontiguousarray(np.asarray(inputs["L0_ffn_gate_w"]).T)
            out["moe_w13T"] = np.ascontiguousarray(np.asarray(inputs["L0_ffn_w13_weight"]).transpose(0, 2, 1))
            out["moe_w2T"] = np.ascontiguousarray(np.asarray(inputs["L0_ffn_w2_weight"]).transpose(0, 2, 1))
    return out


MODULE_STEPS = ["l0_attn_a", "moe", "l0_attn_b", "moe",
                "l1_attn_a", "l1_ffn", "l1_attn_b", "l1_ffn"]
FUSE = True


def kernel(**inputs):
    x = np.asarray(inputs["L0_attn_a_x"], np.float32)
    B = x.shape[0]
    if FUSE:
        groups = [MODULE_STEPS]
    else:
        groups = [[s] for s in MODULE_STEPS]
    xTs = [np.ascontiguousarray(x[b].T) for b in range(B)]
    for steps in groups:
        nc = build_program(steps)
        wmap = host_weights(inputs, steps)
        in_maps = [dict(xT=xTs[b], **wmap) for b in range(B)]
        res = run_bass_kernel_spmd(nc, in_maps, core_ids=list(range(B)))
        xTs = [np.asarray(res.results[b]["y"], np.float32) for b in range(B)]
    return np.ascontiguousarray(np.stack([xt.T for xt in xTs], 0))



QMAX = 1408


def emit_fp4_quant(k, out_ap_fn, in_ap_fn, scrs, ncols, in_tiles, out_tiles):
    for lo in range(0, ncols, QMAX):
        n = min(QMAX, ncols - lo)
        nblk = n // 16
        k.qpar = getattr(k, "qpar", 0) ^ 1
        scr = scrs[k.qpar]
        absx, acc, tmp, mx = scr["absx"], scr["acc"], scr["tmp"], scr["mx"]
        in_ap = in_ap_fn(lo, n)
        out_ap = out_ap_fn(lo, n)
        aab = absx.h[:, 0:n]
        acb = acc.h[:, 0:n]
        tmb = tmp.h[:, 0:n]
        mxb = mx.h[:, 0:nblk]
        k.act(aab, in_ap, AF.Abs, reads=in_tiles, writes=[absx])
        k.red("V", mxb, absx.h[:, 0:n].rearrange("p (b s) -> p b s", s=16),
              ALU.max, reads=[absx], writes=[mx])
        k.ts("V", mxb, mxb, k.const(1.0 / 6.0), ALU.mult, s2=k.const(1e-12),
             op1=ALU.max, reads=[mx], writes=[mx])
        k.recip(mxb, mxb, reads=[mx], writes=[mx])
        rbc = bass.AP(mx.h, 0, [[mx.h.shape[1], 128], [1, nblk], [0, 16]])
        k.tt("G", aab, aab, rbc, ALU.mult, reads=[absx, mx], writes=[absx])
        k.ts("V", acb, aab, 0.25, ALU.is_gt, reads=[absx], writes=[acc])
        for thr in (0.75, 1.25, 1.75):
            k.ts("V", tmb, aab, k.const(thr), ALU.is_gt, reads=[absx], writes=[tmp])
            k.tt("G", acb, acb, tmb, ALU.add, reads=[acc, tmp], writes=[acc])
        k.ts("V", tmb, aab, k.const(2.5), ALU.is_gt, reads=[absx], writes=[tmp])
        k.stt("V", acb, acb, 0.5, tmb, ALU.mult, ALU.add, reads=[acc, tmp], writes=[acc])
        k.ts("V", tmb, aab, k.const(3.5), ALU.is_gt, reads=[absx], writes=[tmp])
        k.tt("G", acb, acb, tmb, ALU.add, reads=[acc, tmp], writes=[acc])
        k.ts("V", tmb, aab, k.const(5.0), ALU.is_gt, reads=[absx], writes=[tmp])
        k.stt("V", acb, tmb, 2.0, acb, ALU.mult, ALU.add, reads=[acc, tmp], writes=[acc])
        k.recip(mxb, mxb, reads=[mx], writes=[mx])
        k.tt("V", acb, acb, rbc, ALU.mult, reads=[acc, mx], writes=[acc])
        k.op("S", (lambda e, o=tmb, i=in_ap: e.sign(o, i)), reads=in_tiles, writes=[tmp])
        k.tt("G", out_ap, acb, tmb, ALU.mult, reads=[acc, tmp], writes=out_tiles)


def quant_weight_to_scratch(k, W, qscr, nat_name, scratch_ap, scratch_buf,
                            nrows, ncols, A):
    nrc = nrows // 128
    ncc = ncols // 128
    fch = [A(f"fq{i}", [128, ncols], F32) for i in range(2)]
    qch = A("qn", [128, ncols], BF16)
    tev = A("te", [128, 8 * 128], BF16)
    for rc in range(nrc):
        f = fch[rc % 2]
        k.dma("Y", f.ap, W[nat_name][rc * 128:(rc + 1) * 128, :], writes=[f])
        emit_fp4_quant(k, lambda lo, n, q=qch: q.h[:, lo:lo + n],
                       lambda lo, n, f=f: f.h[:, lo:lo + n],
                       qscr, ncols, [f], [qch])
        for g0 in range(0, ncc, 8):
            gn = min(8, ncc - g0)
            for j in range(gn):
                k.transp(k.pbh[0], qch.h[:, (g0 + j) * 128:(g0 + j + 1) * 128],
                         k.ident.ap, slice(j * 128, (j + 1) * 128), reads=[qch, k.ident])
            k.copy("V", tev.h[:, 0:gn * 128], k.pbh[0].h[:, 0:gn * 128],
                   reads=[k.pbh[0]], writes=[tev])
            k.dma("S", scratch_ap[g0 * 128:(g0 + gn) * 128,
                                  rc * 128:(rc + 1) * 128].rearrange(
                "(cc p) r -> p cc r", p=128),
                tev.h[:, 0:gn * 128].rearrange("p (cc r) -> p cc r", r=128),
                reads=[tev], writes=[scratch_buf])


def step_l1_ffn(k, W):
    IN = 2816
    ICC = IN // 128
    m0 = k.mark()
    tiles = []

    def A(name, shape, dtype):
        t = k.alloc(name, shape, dtype)
        tiles.append(t)
        return t

    xq = A("xq", [128, NCH * T], BF16)
    qscr = [dict(absx=A(f"qabs{i}", [128, QMAX], F32),
                 acc=A(f"qacc{i}", [128, QMAX], BF16),
                 tmp=A(f"qtmp{i}", [128, QMAX], BF16),
                 mx=A(f"qmx{i}", [128, QMAX // 16], F32)) for i in range(2)]

    m1 = k.mark()
    s0 = []

    def A0(name, shape, dtype):
        t = k.alloc(name, shape, dtype)
        s0.append(t)
        return t
    xn = A0("xn", [128, NCH * T], BF16)
    scr = ln_scratch(k, A0)
    emit_layernorm(k, k.X, xn, scr)
    xtm = A0("xtm", [128, D], BF16)
    qtm = A0("qtm", [128, D], BF16)
    for it in range(16):
        for j in range(NCH):
            k.transp(k.pbh[0], xn.h[:, j * T + it * 128: j * T + it * 128 + 128],
                     k.ident.ap, slice(j * 128, (j + 1) * 128), reads=[xn, k.ident])
        k.copy("V", xtm.ap, k.pbh[0].ap, reads=[k.pbh[0]], writes=[xtm])
        emit_fp4_quant(k, lambda lo, n, q=qtm: q.h[:, lo:lo + n],
                       lambda lo, n, x=xtm: x.h[:, lo:lo + n],
                       qscr, D, [xtm], [qtm])
        for j in range(NCH):
            k.transp(k.pbh[1], qtm.h[:, j * 128:(j + 1) * 128],
                     k.ident.ap, slice(j * 128, (j + 1) * 128), reads=[qtm, k.ident])
        k.op("V", (lambda e, o=xq, it=it: e.tensor_copy(
            bass.AP(o.h, it * 128, [[NCH * T, 128], [T, NCH], [1, 128]]),
            k.pbh[1].h[:].rearrange("p (c r) -> p c r", r=128))),
            reads=[k.pbh[1]], writes=[xq])
    k.release(m1, s0)

    if not hasattr(k, "fp4_done"):
        k.fp4_done = True
        nc = k.nc
        k.qwg_s = nc.dram_tensor("qwg_s", [D, IN], BF16, kind="Internal").ap()
        k.qwu_s = nc.dram_tensor("qwu_s", [D, IN], BF16, kind="Internal").ap()
        k.qwd_s = nc.dram_tensor("qwd_s", [IN, D], BF16, kind="Internal").ap()
        k.qwg_b, k.qwu_b, k.qwd_b = Buf("qwg_s"), Buf("qwu_s"), Buf("qwd_s")
        m2 = k.mark()
        wt = []

        def A1(name, shape, dtype):
            t = k.alloc(name, shape, dtype)
            wt.append(t)
            return t
        quant_weight_to_scratch(k, W, qscr, "fp4_wg", k.qwg_s, k.qwg_b, IN, D, A1)
        k.release(m2, wt)
        wt = []
        quant_weight_to_scratch(k, W, qscr, "fp4_wu", k.qwu_s, k.qwu_b, IN, D, A1)
        k.release(m2, wt)
        wt = []
        quant_weight_to_scratch(k, W, qscr, "fp4_wd", k.qwd_s, k.qwd_b, D, IN, A1)
        k.release(m2, wt)

    wb = [A(f"fw{i}", [128, NCH * 128], BF16) for i in range(4)]
    htm = [A(f"htm{i}", [128, IN], BF16) for i in range(4)]
    qh = A("qh", [128, ICC * 512], BF16)
    hst2 = [A(f"hst{i}", [128, 512], BF16) for i in range(2)]
    hfm2 = [A(f"hfm{i}", [128, 512], F32) for i in range(2)]
    dwb = [A(f"dw{i}", [128, ICC * 128], BF16) for i in range(2)]
    for tc in range(4):
        tb = slice(tc * 512, (tc + 1) * 512)
        for ic in range(ICC):
            wg = wb[(ic % 2) * 2]
            wu = wb[(ic % 2) * 2 + 1]
            k.dma("Y", wg.h[:].rearrange("p (c i) -> p c i", i=128),
                  k.qwg_s[:, ic * 128:(ic + 1) * 128].rearrange("(c p) i -> p c i", p=128),
                  reads=[k.qwg_b], writes=[wg])
            k.dma("Y", wu.h[:].rearrange("p (c i) -> p c i", i=128),
                  k.qwu_s[:, ic * 128:(ic + 1) * 128].rearrange("(c p) i -> p c i", p=128),
                  reads=[k.qwu_b], writes=[wu])
            bg = k.pb[(ic % 2) * 2]
            bu = k.pb[(ic % 2) * 2 + 1]
            hfm = hfm2[ic % 2]
            hst = hst2[ic % 2]
            k.mmop(bg, [(wg.h[:, c * 128:(c + 1) * 128],
                         xq.h[:, c * T + tc * 512: c * T + tc * 512 + 512])
                        for c in range(NCH)], reads=[wg, xq])
            k.mmop(bu, [(wu.h[:, c * 128:(c + 1) * 128],
                         xq.h[:, c * T + tc * 512: c * T + tc * 512 + 512])
                        for c in range(NCH)], reads=[wu, xq])
            k.act(hfm.ap, bg.ap, AF.Silu, reads=[bg], writes=[hfm])
            k.tt("V", hfm.ap, hfm.ap, bu.ap, ALU.mult, reads=[hfm, bu], writes=[hfm])
            k.copy("G", hst.ap, hfm.ap, reads=[hfm], writes=[hst])
            for j in range(4):
                k.transp(k.pbh[ic % 2], hst.h[:, j * 128:(j + 1) * 128], k.ident.ap,
                         slice(j * 128, (j + 1) * 128), reads=[hst, k.ident])
            for j in range(4):
                k.copy("V", htm[j].h[:, ic * 128:(ic + 1) * 128],
                       k.pbh[ic % 2].h[:, j * 128:(j + 1) * 128],
                       reads=[k.pbh[ic % 2]], writes=[htm[j]])
        for j in range(4):
            emit_fp4_quant(k, lambda lo, n, h=htm[j]: h.h[:, lo:lo + n],
                           lambda lo, n, h=htm[j]: h.h[:, lo:lo + n],
                           qscr, IN, [htm[j]], [htm[j]])
            for g0 in range(0, ICC, 8):
                gn = min(8, ICC - g0)
                for u in range(gn):
                    k.transp(k.pbh[1], htm[j].h[:, (g0 + u) * 128:(g0 + u + 1) * 128],
                             k.ident.ap, slice(u * 128, (u + 1) * 128),
                             reads=[htm[j], k.ident])
                k.op("V", (lambda e, o=qh, g0=g0, gn=gn, j=j: e.tensor_copy(
                    bass.AP(o.h, g0 * 512 + j * 128,
                            [[ICC * 512, 128], [512, gn], [1, 128]]),
                    k.pbh[1].h[:, 0:gn * 128].rearrange("p (c r) -> p c r", r=128))),
                    reads=[k.pbh[1]], writes=[qh])
        for dp in range(NCH):
            dw = dwb[dp % 2]
            k.dma("Y", dw.h[:].rearrange("p (c i) -> p c i", i=128),
                  k.qwd_s[:, dp * 128:(dp + 1) * 128].rearrange("(c p) i -> p c i", p=128),
                  reads=[k.qwd_b], writes=[dw])
            k.mmop(k.pb[4 + dp % 2], [(dw.h[:, c * 128:(c + 1) * 128],
                                       qh.h[:, c * 512:(c + 1) * 512])
                                      for c in range(ICC)], reads=[dw, qh])
            xt = k.X[dp]
            k.tt("V", xt.h[:, tb], xt.h[:, tb], k.pb[4 + dp % 2].ap, ALU.add,
                 reads=[xt, k.pb[4 + dp % 2]], writes=[xt])
    k.release(m0, tiles)


STEP_FNS["l1_ffn"] = step_l1_ffn
STEP_WEIGHTS["l1_ffn"] = {"fp4_wg": (2816, D), "fp4_wu": (2816, D), "fp4_wd": (D, 2816)}



def step_l1_attn_a(k, W):
    H, HD = 8, 128
    m0 = k.mark()
    tiles = []

    def A(name, shape, dtype):
        t = k.alloc(name, shape, dtype)
        tiles.append(t)
        return t

    cosm = A("cosm", [64, T], BF16)
    sinm = A("sinm", [64, T], BF16)
    y = A("attny", [128, NCH * T], BF16)
    triu = A("triu", [128, 128], BF16)
    k.memset("V", triu.ap, 1.0, writes=[triu])
    k.affsel(triu.ap, triu.ap, [[1, 128]], ALU.is_ge, 0.0, -1, reads=[triu], writes=[triu])

    m1 = k.mark()
    s0 = []

    def A0(name, shape, dtype):
        t = k.alloc(name, shape, dtype)
        s0.append(t)
        return t
    xn = A0("xn", [128, NCH * T], BF16)
    mln = k.mark()
    sln = []

    def A1(name, shape, dtype):
        t = k.alloc(name, shape, dtype)
        sln.append(t)
        return t
    scr = ln_scratch(k, A1)
    emit_layernorm(k, k.X, xn, scr)
    k.release(mln, sln)

    sln = []
    posi = A1("posi", [64, T], I32)
    jci = A1("jci", [64, 1], I32)
    jcf = A1("jcf", [64, 1], F32)
    posf = A1("posf", [64, T], F32)
    rad = A1("rad", [64, T], F32)
    jcf2 = A1("jcf2", [64, T], F32)
    k.iota(posi.ap, [[1, T]], cm=0, writes=[posi])
    k.iota(jci.ap, [[0, 1]], cm=1, writes=[jci])
    k.copy("V", jcf.ap, jci.ap, reads=[jci], writes=[jcf])
    k.copy("V", posf.ap, posi.ap, reads=[posi], writes=[posf])
    LB = float(np.log(10000.0) / 64.0)
    k.act(jcf.ap, jcf.ap, AF.Exp, scale=-LB, reads=[jcf], writes=[jcf])
    k.ts("V", rad.ap, posf.ap, jcf[0:64, 0:1], ALU.mult, reads=[posf, jcf], writes=[rad])
    PI = float(np.pi)
    MULT = float((1.0 / 128.0) ** 0.5)
    k.ts("V", rad.ap, rad.ap, float(1.0 / (2 * PI)), ALU.mult, reads=[rad], writes=[rad])
    k.copy("V", posi.ap, rad.ap, reads=[rad], writes=[posi])
    k.copy("V", posf.ap, posi.ap, reads=[posi], writes=[posf])
    k.tt("V", posf.ap, rad.ap, posf.ap, ALU.subtract, reads=[rad, posf], writes=[posf])
    k.ts("V", jcf2.ap, posf.ap, 0.5, ALU.is_gt, reads=[posf], writes=[jcf2])
    k.tt("V", posf.ap, posf.ap, jcf2.ap, ALU.subtract, reads=[posf, jcf2], writes=[posf])
    k.act(sinm.ap, posf.ap, AF.Sin, scale=float(2 * PI), reads=[posf], writes=[sinm])
    k.ts("V", rad.ap, rad.ap, 0.25, ALU.add, reads=[rad], writes=[rad])
    k.copy("V", posi.ap, rad.ap, reads=[rad], writes=[posi])
    k.copy("V", posf.ap, posi.ap, reads=[posi], writes=[posf])
    k.tt("V", posf.ap, rad.ap, posf.ap, ALU.subtract, reads=[rad, posf], writes=[posf])
    k.ts("V", jcf2.ap, posf.ap, 0.5, ALU.is_gt, reads=[posf], writes=[jcf2])
    k.tt("V", posf.ap, posf.ap, jcf2.ap, ALU.subtract, reads=[posf, jcf2], writes=[posf])
    k.act(cosm.ap, posf.ap, AF.Sin, scale=float(2 * PI), reads=[posf], writes=[cosm])
    k.ts("V", sinm.ap, sinm.ap, MULT, ALU.mult, reads=[sinm], writes=[sinm])
    k.ts("V", cosm.ap, cosm.ap, MULT, ALU.mult, reads=[cosm], writes=[cosm])
    k.release(mln, sln)

    qrop = [A0(f"qro{i}", [128, T], BF16) for i in range(2)]
    krop = [A0(f"kro{i}", [128, T], BF16) for i in range(2)]
    vtmp = [A0(f"vtm{i}", [128, 16 * 129], BF16) for i in range(2)]
    for i in range(2):
        vv = vtmp[i]
        k.memset("V", vv.h[:].rearrange("p (c j) -> p c j", j=129)[:, :, 128:129],
                 1.0, writes=[vv])
    wqp = [A0(f"w_q{i}", [128, NCH * 128], BF16) for i in range(2)]
    wkp = [A0(f"w_k{i}", [128, NCH * 128], BF16) for i in range(2)]
    wvp = [A0(f"w_v{i}", [128, NCH * 128], BF16) for i in range(2)]
    t1p = [A0(f"t1_{i}", [64, 1024], BF16) for i in range(2)]
    rsbp = [A0(f"rsb{i}", [128, 512], F32) for i in range(2)]
    q2bp = [A0(f"q2b{i}", [128, 512], BF16) for i in range(2)]
    eap = [A0(f"ea{i}", [64, 512], BF16) for i in range(2)]
    obp = [A0(f"ob{i}", [64, 512], BF16) for i in range(2)]
    ebp = [A0(f"eb{i}", [64, 512], BF16) for i in range(2)]
    oap = [A0(f"oa{i}", [64, 512], BF16) for i in range(2)]
    vsgp = [A0(f"vsg{i}", [128, 512], BF16) for i in range(2)]
    ptbp = [A0(f"ptb{i}", [128, 512], BF16) for i in range(2)]
    oqp = [A0(f"oq{i}", [128, 128], BF16) for i in range(2)]
    rcp = [A0(f"rc{i}", [128, 1], F32) for i in range(2)]

    for h in range(H):
        hp = h & 1
        hsl = slice(h * 128, (h + 1) * 128)
        qro, kro, vtm = qrop[hp], krop[hp], vtmp[hp]
        for (wnm, wpair, dst) in (("att_wq", wqp, qro), ("att_wk", wkp, kro)):
            wv_ = W[wnm]
            wtile = wpair[hp]
            k.wload(wnm, wtile.h[:].rearrange("p (c i) -> p c i", i=128),
                    wv_[:, hsl].rearrange("(c p) i -> p c i", p=128), writes=[wtile])
            for tc in range(4):
                tp = tc & 1
                tsl = slice(tc * 512, (tc + 1) * 512)
                t1, rsb, q2b = t1p[tp], rsbp[tp], q2bp[tp]
                ea, ob, eb, oa = eap[tp], obp[tp], ebp[tp], oap[tp]
                k.mmop(k.pb[0], [(wtile.h[:, c * 128:(c + 1) * 128],
                                  xn.h[:, c * T + tc * 512: c * T + tc * 512 + 512])
                                 for c in range(NCH)], reads=[wtile, xn])
                k.act(q2b.ap, k.pb[0].ap, AF.Square, reads=[k.pb[0]], writes=[q2b])
                k.mmop(k.pb[1], [(k.ones_bf.ap, q2b.ap)], reads=[k.ones_bf, q2b])
                k.act(rsb.ap, k.pb[1].ap, AF.Ln, scale=float(1.0 / 128.0),
                      bias=1e-6, reads=[k.pb[1]], writes=[rsb])
                k.act(rsb.ap, rsb.ap, AF.Exp, scale=-0.5, reads=[rsb], writes=[rsb])
                k.tt("V", t1.h[0:64, 0:512], k.pb[0].h[0:64, 0:512],
                     rsb.h[0:64, 0:512], ALU.mult, reads=[k.pb[0], rsb], writes=[t1])
                k.tt("V", t1.h[0:64, 512:1024], k.pb[0].h[64:128, 0:512],
                     rsb.h[64:128, 0:512], ALU.mult, reads=[k.pb[0], rsb], writes=[t1])
                co = cosm.h[0:64, tsl]
                si = sinm.h[0:64, tsl]
                t1e = t1.h[0:64, 0:512]
                t1o = t1.h[0:64, 512:1024]
                k.tt("V", ea.ap, t1e, co, ALU.mult, reads=[t1, cosm], writes=[ea])
                k.tt("G", ob.ap, t1o, si, ALU.mult, reads=[t1, sinm], writes=[ob])
                k.tt("G", eb.ap, t1e, si, ALU.mult, reads=[t1, sinm], writes=[eb])
                k.tt("V", oa.ap, t1o, co, ALU.mult, reads=[t1, cosm], writes=[oa])
                k.tt("V", dst.h[0:64, tsl], ea.ap, ob.ap, ALU.subtract,
                     reads=[ea, ob], writes=[dst])
                k.tt("G", dst.h[64:128, tsl], eb.ap, oa.ap, ALU.add,
                     reads=[eb, oa], writes=[dst])
        wvt = wvp[hp]
        k.wload("att_wv", wvt.h[:].rearrange("p (c i) -> p c i", i=128),
                W["att_wv"][:, hsl].rearrange("(c p) i -> p c i", p=128), writes=[wvt])
        for tc in range(4):
            tp = tc & 1
            vsg = vsgp[tp]
            k.mmop(k.pb[0], [(wvt.h[:, c * 128:(c + 1) * 128],
                              xn.h[:, c * T + tc * 512: c * T + tc * 512 + 512])
                             for c in range(NCH)], reads=[wvt, xn])
            k.copy("V", vsg.ap, k.pb[0].ap, reads=[k.pb[0]], writes=[vsg])
            for j in range(4):
                k.transp(k.pbh[0], vsg.h[:, j * 128:(j + 1) * 128], k.ident.ap,
                         slice(j * 128, (j + 1) * 128), reads=[vsg, k.ident])
            for j in range(4):
                ch = tc * 4 + j
                k.copy("V", vtm.h[:, ch * 129: ch * 129 + 128],
                       k.pbh[0].h[:, j * 128:(j + 1) * 128],
                       reads=[k.pbh[0]], writes=[vtm])
        for qt in range(16):
            qp = qt & 1
            qsl = slice(qt * 128, (qt + 1) * 128)
            opb = k.pb[3 + 2 * qp]
            nkt = qt + 1
            first = True
            for g0 in range(0, nkt, 4):
                gp = (g0 // 4) & 1
                spb = k.pb[2 + 2 * gp]
                ptb = ptbp[gp]
                gn = min(4, nkt - g0)
                for j in range(gn):
                    k.mmop(spb, [(kro.h[:, (g0 + j) * 128:(g0 + j + 1) * 128],
                                  qro.h[:, qsl])], reads=[kro, qro],
                           out_slice=slice(j * 128, (j + 1) * 128))
                k.act(ptb.h[:, 0:gn * 128], spb.h[:, 0:gn * 128], AF.Exp,
                      reads=[spb], writes=[ptb])
                if g0 + gn == nkt:
                    k.tt("V", ptb.h[:, (gn - 1) * 128:gn * 128],
                         ptb.h[:, (gn - 1) * 128:gn * 128], triu.ap, ALU.mult,
                         reads=[ptb, triu], writes=[ptb])
                for j in range(gn):
                    last = (g0 + j == nkt - 1)
                    k.mmop(opb, [(ptb.h[:, j * 128:(j + 1) * 128],
                                  vtm.h[:, (g0 + j) * 129: (g0 + j) * 129 + 129])],
                           reads=[ptb, vtm], accum=not first,
                           out_slice=slice(0, 129), close=last)
                    first = False
            rc, oq = rcp[qp], oqp[qp]
            k.recip(rc.ap, opb.h[:, 128:129], reads=[opb], writes=[rc])
            k.ts("V", oq.ap, opb.h[:, 0:128], rc.h[:, 0:1], ALU.mult,
                 reads=[opb, rc], writes=[oq])
            k.transp(k.pbh[1], oq.ap, k.ident.ap, slice(qp * 128, qp * 128 + 128),
                     reads=[oq, k.ident])
            k.copy("V", y.h[:, h * T + qt * 128: h * T + qt * 128 + 128],
                   k.pbh[1].h[:, qp * 128: qp * 128 + 128],
                   reads=[k.pbh[1]], writes=[y])
    k.release(m1, s0)

    wo = A("w_att_wo", [128, NCH * 128], BF16)
    for dp in range(NCH):
        k.wload("att_wo", wo.h[:].rearrange("p (c i) -> p c i", i=128),
              W["att_wo"][:, dp * 128:(dp + 1) * 128].rearrange("(c p) i -> p c i", p=128),
              writes=[wo])
        for tc in range(4):
            tsl = slice(tc * 512, (tc + 1) * 512)
            k.mmop(k.pb[2 + tc % 2], [(wo.h[:, c * 128:(c + 1) * 128],
                                       y.h[:, c * T + tc * 512: c * T + tc * 512 + 512])
                                      for c in range(NCH)], reads=[wo, y])
            xt = k.X[dp]
            k.tt("V", xt.h[:, tsl], xt.h[:, tsl], k.pb[2 + tc % 2].ap, ALU.add,
                 reads=[xt, k.pb[2 + tc % 2]], writes=[xt])
    k.release(m0, tiles)


STEP_FNS["l1_attn_a"] = step_l1_attn_a
STEP_WEIGHTS["l1_attn_a"] = {"att_wq": (D, D), "att_wk": (D, D),
                             "att_wv": (D, D), "att_wo": (D, D)}



def emit_layernorm_bf(k, src, out_bf, scr, gb, goff, bb, boff, eps=1e-5):
    x2s, mb, rb = scr["x2s"], scr["mb"], scr["rb"]
    t0, t1 = scr["t0"], scr["t1"]
    for tc in range(T // 512):
        sl = slice(tc * 512, (tc + 1) * 512)
        xs = x2s[tc % 2]
        for c in range(NCH):
            k.act(xs.h[:, c * 512:(c + 1) * 512], src.h[:, c * T + tc * 512: c * T + tc * 512 + 512],
                  AF.Square, reads=[src], writes=[xs])
        k.mmop(k.pb[0], [(k.ones_bf.ap, src.h[:, c * T + tc * 512: c * T + tc * 512 + 512])
                         for c in range(NCH)], reads=[k.ones_bf, src])
        k.ts("V", mb.h[:, sl], k.pb[0].ap, 1.0 / D, ALU.mult, reads=[k.pb[0]], writes=[mb])
        k.mmop(k.pb[1], [(k.ones_bf.ap, xs.h[:, c * 512:(c + 1) * 512])
                         for c in range(NCH)], reads=[k.ones_bf, xs])
        k.ts("V", rb.h[:, sl], k.pb[1].ap, 1.0 / D, ALU.mult, reads=[k.pb[1]], writes=[rb])
    k.tt("G", t0.ap, mb.ap, mb.ap, ALU.mult, reads=[mb], writes=[t0])
    k.tt("V", rb.ap, rb.ap, t0.ap, ALU.subtract, reads=[rb, t0], writes=[rb])
    k.act(t0.ap, rb.ap, AF.Sqrt, bias=k.const(eps), reads=[rb], writes=[t0])
    k.recip(rb.ap, t0.ap, reads=[t0], writes=[rb])
    for c in range(NCH):
        csl = slice(c * T, (c + 1) * T)
        k.tt("G", t0.ap, src.h[:, csl], mb.ap, ALU.subtract, reads=[src, mb], writes=[t0])
        k.tt("V", t0.ap, t0.ap, rb.ap, ALU.mult, reads=[t0, rb], writes=[t0])
        k.ts("V", out_bf.h[:, csl], t0.ap, gb.h[:, goff + c:goff + c + 1], ALU.mult,
             s2=bb.h[:, boff + c:boff + c + 1], op1=ALU.add,
             reads=[t0, gb, bb], writes=[out_bf])
    return out_bf


def step_l0_attn_a(k, W):
    m0 = k.mark()
    tiles = []

    def A(name, shape, dtype):
        t = k.alloc(name, shape, dtype)
        tiles.append(t)
        return t

    triu = A("triu0", [128, 128], BF16)
    k.memset("V", triu.ap, 1.0, writes=[triu])
    k.affsel(triu.ap, triu.ap, [[1, 128]], ALU.is_ge, 0.0, -1, reads=[triu], writes=[triu])
    lng = A("lng", [128, NCH * 4], F32)
    k.dma("Y", lng.ap, W["l0a_lns"], writes=[lng])
    lnbb = A("lnbb", [128, NCH], BF16)
    k.copy("V", lnbb.ap, lng.h[:, NCH:2 * NCH], reads=[lng], writes=[lnbb])
    x1 = A("x1c", [128, NCH * T], BF16)

    mA = k.mark()
    sA = []

    def AA(name, shape, dtype):
        t = k.alloc(name, shape, dtype)
        sA.append(t)
        return t
    h0 = AA("h0", [128, NCH * T], BF16)
    y = AA("lay", [128, NCH * T], BF16)

    mL = k.mark()
    sL = []

    def AL(name, shape, dtype):
        t = k.alloc(name, shape, dtype)
        sL.append(t)
        return t
    scr = ln_scratch(k, AL)
    emit_layernorm(k, k.X, h0, scr)
    k.release(mL, sL)
    if int(os.environ.get("L0A_PHASE", "9")) <= 1:
        k.release(mA, sA)
        k.release(m0, tiles)
        return

    sL = []
    qh = AL("qh", [128, T], BF16)
    kh = AL("kh", [128, T], BF16)
    vh = AL("vh", [128, T], BF16)
    msb = AL("msb", [128, 64], BF16)
    msf = AL("msf", [64, 64], F32)
    ptb = AL("pt0", [128, 128], BF16)
    vtc = AL("vtc", [128, 128], BF16)
    ktc = AL("ktc", [128, 128], BF16)
    wqb = AL("wqb", [128, NCH * 64], BF16)
    wkb = AL("wkb", [128, NCH * 64], BF16)
    wvb = AL("wvb", [128, NCH * 64], BF16)
    bv = AL("bv", [64, 3], F32)
    k.memset("V", qh.h[64:128, :], 0.0, writes=[qh])
    k.memset("V", kh.h[64:128, :], 0.0, writes=[kh])
    k.memset("V", vh.h[64:128, :], 0.0, writes=[vh])
    k.memset("V", msb.h[64:128, :], 0.0, writes=[msb])
    for h in range(16):
        hsl = slice(h * 64, (h + 1) * 64)
        for wi, (wnm, wtile, dst) in enumerate((("l0a_wq", wqb, qh),
                                                ("l0a_wk", wkb, kh),
                                                ("l0a_wv", wvb, vh))):
            k.wload("l0a_wo", wtile.h[:].rearrange("p (c i) -> p c i", i=64),
                  W[wnm][:, hsl].rearrange("(c p) i -> p c i", p=128), writes=[wtile])
            k.mmop(k.pb[5], [(wtile.h[:, c * 64:(c + 1) * 64], lnbb.h[:, c:c + 1])
                             for c in range(NCH)], reads=[wtile, lnbb],
                   out_slice=slice(0, 1), nparts=64)
            k.copy("V", bv.h[:, wi:wi + 1], k.pb[5].h[0:64, 0:1],
                   reads=[k.pb[5]], writes=[bv])
            for c in range(NCH):
                k.ts("V", wtile.h[:, c * 64:(c + 1) * 64],
                     wtile.h[:, c * 64:(c + 1) * 64], lng.h[:, c:c + 1], ALU.mult,
                     reads=[wtile, lng], writes=[wtile])
            for tc in range(4):
                k.mmop(k.pb[0], [(wtile.h[:, c * 64:(c + 1) * 64],
                                  h0.h[:, c * T + tc * 512: c * T + tc * 512 + 512])
                                 for c in range(NCH)], reads=[wtile, h0], nparts=64)
                k.ts("V", dst.h[0:64, tc * 512:(tc + 1) * 512],
                     k.pb[0].h[0:64, :], bv.h[:, wi:wi + 1], ALU.add,
                     reads=[k.pb[0], bv], writes=[dst])
        k.memset("V", msb.h[0:64, :], 0.0, writes=[msb])
        k.memset("V", msf.ap, 0.0, writes=[msf])
        SUB = int(os.environ.get("L0A_SUB", "9"))
        for c in range(16):
            csl = slice(c * 128, (c + 1) * 128)
            if SUB == 0:
                continue
            k.mmop(k.pb[1], [(kh.h[:, csl], qh.h[:, csl])], reads=[kh, qh],
                   out_slice=slice(0, 128))
            if SUB == 1:
                continue
            k.tt("V", ptb.ap, k.pb[1].h[:, 0:128], triu.ap, ALU.mult,
                 reads=[k.pb[1], triu], writes=[ptb])
            if SUB == 2:
                continue
            k.transp(k.pbh[0], vh.h[:, csl], k.ident.ap, slice(0, 128),
                     reads=[vh, k.ident])
            k.transp(k.pbh[0], kh.h[:, csl], k.ident.ap, slice(128, 256),
                     reads=[kh, k.ident])
            k.copy("V", vtc.ap, k.pbh[0].h[:, 0:128], reads=[k.pbh[0]], writes=[vtc])
            k.copy("V", ktc.ap, k.pbh[0].h[:, 128:256], reads=[k.pbh[0]], writes=[ktc])
            if SUB == 3:
                continue
            k.mmop(k.pb[2], [(msb.ap, qh.h[:, csl]),
                             (vtc.h[:, 0:64], ptb.ap)], reads=[msb, qh, vtc, ptb],
                   out_slice=slice(0, 128), nparts=64)
            if SUB == 4:
                continue
            k.copy("V",
                   y.h[(h % 2) * 64:(h % 2) * 64 + 64,
                       (h // 2) * T + c * 128:(h // 2) * T + c * 128 + 128],
                   k.pb[2].h[0:64, 0:128], reads=[k.pb[2]], writes=[y])
            if c < 15:
                k.mmop(k.pb[3], [(ktc.h[:, 0:64], vtc.h[:, 0:64])], reads=[ktc, vtc],
                       out_slice=slice(0, 64), nparts=64)
                k.tt("V", msf.ap, msf.ap, k.pb[3].h[0:64, 0:64], ALU.add,
                     reads=[msf, k.pb[3]], writes=[msf])
                k.copy("G", msb.h[0:64, :], msf.ap, reads=[msf], writes=[msb])
    k.release(mL, sL)

    sL = []
    wob = AL("wob", [128, NCH * 128], BF16)
    for dp in range(NCH):
        k.wload("l0a_wo", wob.h[:].rearrange("p (c i) -> p c i", i=128),
                W["l0a_wo"][:, dp * 128:(dp + 1) * 128].rearrange("(c p) i -> p c i", p=128),
                writes=[wob])
        for tc in range(4):
            tsl = slice(tc * 512, (tc + 1) * 512)
            bank = k.pb[2 + tc % 2]
            k.mmop(bank, [(wob.h[:, c * 128:(c + 1) * 128],
                           y.h[:, c * T + tc * 512: c * T + tc * 512 + 512])
                          for c in range(NCH)], reads=[wob, y])
            xt = k.X[dp]
            k.tt("V", x1.h[:, dp * T + tc * 512: dp * T + tc * 512 + 512],
                 bank.ap, h0.h[:, dp * T + tc * 512: dp * T + tc * 512 + 512],
                 ALU.add, reads=[bank, h0], writes=[x1])
            k.tt("V", xt.h[:, tsl], xt.h[:, tsl], bank.ap, ALU.add,
                 reads=[xt, bank], writes=[xt])
            k.tt("G", xt.h[:, tsl], xt.h[:, tsl],
                 h0.h[:, dp * T + tc * 512: dp * T + tc * 512 + 512], ALU.add,
                 reads=[xt, h0], writes=[xt])
    k.release(mL, sL)
    k.release(mA, sA)

    sA = []
    scr = ln_scratch(k, AA)
    emit_layernorm_bf(k, x1, x1, scr, lng, 2 * NCH, lng, 3 * NCH)
    k.release(mA, sA)
    ping = x1
    if int(os.environ.get("L0A_PHASE", "9")) <= 4:
        k.release(m0, tiles)
        return

    pong = A("pong", [128, NCH * T], BF16)
    ub = A("ub", [128, 32 * 512], BF16)
    w1b = [A(f"w1b{i}", [128, NCH * 128], BF16) for i in range(2)]
    w2b = [A(f"w2b{i}", [128, 32 * 128], BF16) for i in range(2)]
    bcol = A("bcol", [128, 32 + NCH], F32)
    zt2 = [A(f"zt{i}", [128, 512], F32) for i in range(2)]
    et2 = [A(f"et{i}", [128, 512], F32) for i in range(2)]
    GC1 = 0.7978845608028654
    GC2 = 0.044715 * 0.7978845608028654
    GC1f = float(np.float32(GC1))
    GC2f = float(np.float32(GC2))
    for lvl in range(3):
        src_t = ping if lvl % 2 == 0 else pong
        dst_t = pong if lvl % 2 == 0 else ping
        k.dma("Y", bcol.ap, W["l0a_cms_b"][lvl], writes=[bcol])
        for tc in range(4):
            for ib in range(32):
                wb_ = w1b[ib % 2]
                k.wload("l0a_cms_w1", wb_.h[:].rearrange("p (c i) -> p c i", i=128),
                      W["l0a_cms_w1"][lvl][:, ib * 128:(ib + 1) * 128].rearrange(
                          "(c p) i -> p c i", p=128), writes=[wb_])
                bank = k.pb[ib % 4]
                zt = zt2[ib % 2]
                et = et2[ib % 2]
                k.mmop(bank, [(wb_.h[:, c * 128:(c + 1) * 128],
                               src_t.h[:, c * T + tc * 512: c * T + tc * 512 + 512])
                              for c in range(NCH)], reads=[wb_, src_t])
                k.ts("V", zt.ap, bank.ap, bcol.h[:, ib:ib + 1], ALU.add,
                     reads=[bank, bcol], writes=[zt])
                k.act(et.ap, zt.ap, AF.Square, reads=[zt], writes=[et])
                k.ts("V", et.ap, et.ap, GC2f, ALU.mult, s2=GC1f, op1=ALU.add,
                     reads=[et], writes=[et])
                k.tt("G", et.ap, et.ap, zt.ap, ALU.mult, reads=[et, zt], writes=[et])
                k.act(et.ap, et.ap, AF.Tanh, reads=[et], writes=[et])
                k.ts("V", et.ap, et.ap, 0.5, ALU.mult, s2=0.5, op1=ALU.add,
                     reads=[et], writes=[et])
                k.tt("G", ub.h[:, ib * 512:(ib + 1) * 512], et.ap, zt.ap, ALU.mult,
                     reads=[et, zt], writes=[ub])
            for dp in range(NCH):
                wb2 = w2b[dp % 2]
                k.wload("l0a_cms_w2", wb2.h[:].rearrange("p (c i) -> p c i", i=128),
                      W["l0a_cms_w2"][lvl][:, dp * 128:(dp + 1) * 128].rearrange(
                          "(c p) i -> p c i", p=128), writes=[wb2])
                bank = k.pb[4 + dp % 2]
                zt = zt2[dp % 2]
                k.mmop(bank, [(wb2.h[:, c * 128:(c + 1) * 128],
                               ub.h[:, c * 512:(c + 1) * 512])
                              for c in range(32)], reads=[wb2, ub])
                if lvl < 2:
                    k.ts("V", dst_t.h[:, dp * T + tc * 512: dp * T + tc * 512 + 512],
                         bank.ap, bcol.h[:, 32 + dp:33 + dp], ALU.add,
                         reads=[bank, bcol], writes=[dst_t])
                else:
                    xt = k.X[dp]
                    tsl = slice(tc * 512, (tc + 1) * 512)
                    k.ts("V", zt.ap, bank.ap, bcol.h[:, 32 + dp:33 + dp], ALU.add,
                         reads=[bank, bcol], writes=[zt])
                    k.tt("G", xt.h[:, tsl], xt.h[:, tsl], zt.ap, ALU.add,
                         reads=[xt, zt], writes=[xt])
    k.release(m0, tiles)


STEP_FNS["l0_attn_a"] = step_l0_attn_a
STEP_WEIGHTS["l0_attn_a"] = {
    "l0a_lns": (128, NCH * 4), "l0a_wq": (D, D), "l0a_wk": (D, D),
    "l0a_wv": (D, D), "l0a_wo": (D, D), "l0a_cms_w1": (3, D, 4096),
    "l0a_cms_w2": (3, 4096, D), "l0a_cms_b": (3, 128, 32 + NCH),
}



def step_l0_attn_b(k, W):
    TH = T // 2
    m0 = k.mark()
    tiles = []

    def A(name, shape, dtype):
        t = k.alloc(name, shape, dtype)
        tiles.append(t)
        return t

    sel32 = A("sel32", [32, 32 * 128], BF16)
    k.memset("V", sel32.ap, 1.0, writes=[sel32])
    k.affsel(sel32.ap, sel32.ap, [[1, 32 * 128]], ALU.is_ge, 0.0, -128,
             reads=[sel32], writes=[sel32])
    k.affsel(sel32.ap, sel32.ap, [[-1, 32 * 128]], ALU.is_ge, 0.0, 128, base=127,
             reads=[sel32], writes=[sel32])
    Atile = A("Atile", [128, NCH * 16], F32)
    k.dma("Y", Atile.ap, W["mam_alog"], writes=[Atile])
    k.act(Atile.ap, Atile.ap, AF.Exp, reads=[Atile], writes=[Atile])
    k.ts("V", Atile.ap, Atile.ap, -1.0, ALU.mult, reads=[Atile], writes=[Atile])
    convw = A("convw", [128, NCH * 3], F32)
    k.dma("Y", convw.ap, W["mam_conv"], writes=[convw])
    dbc = A("dbc", [128, NCH], F32)
    k.dma("Y", dbc.ap, W["mam_db"], writes=[dbc])
    valc = A("valc", [128, NCH * T], BF16)
    d1t = A("d1t", [128, TH], F32)
    wib = [A(f"wib{i}", [128, NCH * 128], BF16) for i in range(2)]
    mbk = A("mbk", [128, T], BF16)
    rbk = A("rbk", [128, T], BF16)
    stt_ = A("stt", [128, NCH * 16], F32)
    wbc = A("wbc", [128, NCH * 32], BF16)
    k.wload("mam_wbc", wbc.h[:].rearrange("p (c i) -> p c i", i=32),
          W["mam_wbc"].rearrange("(c p) i -> p c i", p=128), writes=[wbc])

    mV = k.mark()
    sV = []

    def AV(name, shape, dtype):
        t = k.alloc(name, shape, dtype)
        sV.append(t)
        return t
    val = AV("val", [128, NCH * (T + 2)], BF16)
    mX = k.mark()
    sX = []

    def AX(name, shape, dtype):
        t = k.alloc(name, shape, dtype)
        sX.append(t)
        return t
    scr = ln_scratch(k, AX)
    emit_ln_stats(k, k.X, scr)
    k.copy("V", mbk.ap, scr["mb"].ap, reads=[scr["mb"]], writes=[mbk])
    k.copy("V", rbk.ap, scr["rb"].ap, reads=[scr["rb"]], writes=[rbk])
    k.release(mX, sX)
    xtc = AV("xtc", [128, NCH * 512], BF16)
    for oc in range(NCH):
        k.memset("V", val.h[:, oc * (T + 2):oc * (T + 2) + 1], 0.0, writes=[val])
        k.memset("V", val.h[:, oc * (T + 2) + T + 1:(oc + 1) * (T + 2)], 0.0, writes=[val])
    for tc in range(4):
        xsl = slice(tc * 512, (tc + 1) * 512)
        for c in range(NCH):
            k.tt("G", d1t.h[:, 0:512], k.X[c].h[:, xsl], mbk.h[:, xsl],
                 ALU.subtract, reads=[k.X[c], mbk], writes=[d1t])
            k.tt("V", xtc.h[:, c * 512:(c + 1) * 512], d1t.h[:, 0:512],
                 rbk.h[:, xsl], ALU.mult, reads=[d1t, rbk], writes=[xtc])
        for oc in range(NCH):
            wb = wib[oc % 2]
            k.wload("mam_win", wb.h[:].rearrange("p (c i) -> p c i", i=128),
                  W["mam_win"][:, oc * 128:(oc + 1) * 128].rearrange(
                      "(c p) i -> p c i", p=128), writes=[wb])
            bank = k.pb[oc % 2]
            k.mmop(bank, [(wb.h[:, c * 128:(c + 1) * 128],
                           xtc.h[:, c * 512:(c + 1) * 512])
                          for c in range(NCH)], reads=[wb, xtc])
            k.copy("V", val.h[:, oc * (T + 2) + 1 + tc * 512: oc * (T + 2) + 1 + tc * 512 + 512],
                   bank.ap, reads=[bank], writes=[val])
    for oc in range(NCH):
        base = oc * (T + 2)
        for hc in range(2):
            t0 = hc * TH
            sm1 = val.h[:, base + t0: base + t0 + TH]
            s0 = val.h[:, base + 1 + t0: base + 1 + t0 + TH]
            sp1 = val.h[:, base + 2 + t0: base + 2 + t0 + TH]
            k.ts("V", d1t.ap, sm1, convw.h[:, oc * 3:oc * 3 + 1], ALU.mult,
                 reads=[val, convw], writes=[d1t])
            k.stt("V", d1t.ap, s0, convw.h[:, oc * 3 + 1:oc * 3 + 2], d1t.ap,
                  ALU.mult, ALU.add, reads=[val, convw, d1t], writes=[d1t])
            k.stt("V", d1t.ap, sp1, convw.h[:, oc * 3 + 2:oc * 3 + 3], d1t.ap,
                  ALU.mult, ALU.add, reads=[val, convw, d1t], writes=[d1t])
            k.act(valc.h[:, oc * T + t0: oc * T + t0 + TH], d1t.ap, AF.Silu,
                  reads=[d1t], writes=[valc])
    k.release(mV, sV)

    for h2 in range(2):
        t0 = h2 * TH
        mH = k.mark()
        sH = []

        def AH(name, shape, dtype):
            t = k.alloc(name, shape, dtype)
            sH.append(t)
            return t
        bcf = AH("bcf", [32, TH], BF16)
        dlt = AH("dlt", [128, NCH * TH], BF16)
        gsil = AH("gsil", [128, NCH * TH], BF16)
        mXN = k.mark()
        xnh = k.alloc("xnh", [128, NCH * TH], BF16)
        for c in range(NCH):
            for tc2 in range(2):
                xsl = slice(t0 + tc2 * 512, t0 + tc2 * 512 + 512)
                k.tt("G", d1t.h[:, 0:512], k.X[c].h[:, xsl], mbk.h[:, xsl],
                     ALU.subtract, reads=[k.X[c], mbk], writes=[d1t])
                k.tt("V", xnh.h[:, c * TH + tc2 * 512: c * TH + tc2 * 512 + 512],
                     d1t.h[:, 0:512], rbk.h[:, xsl], ALU.mult,
                     reads=[d1t, rbk], writes=[xnh])
        for tc2 in range(2):
            k.mmop(k.pb[2], [(wbc.h[:, c * 32:(c + 1) * 32],
                              valc.h[:, c * T + t0 + tc2 * 512: c * T + t0 + tc2 * 512 + 512])
                             for c in range(NCH)], reads=[wbc, valc], nparts=32)
            k.copy("V", bcf.h[:, tc2 * 512:(tc2 + 1) * 512], k.pb[2].h[0:32, :],
                   reads=[k.pb[2]], writes=[bcf])
        mC = k.mark()
        sC = []
        cdh = k.alloc("cdh", [128, NCH * TH], BF16)
        sC.append(cdh)
        wdb = [k.alloc(f"wdb{i}", [128, NCH * 128], BF16) for i in range(2)]
        sC.extend(wdb)
        for dp in range(NCH):
            wb = wdb[dp % 2]
            k.wload("mam_wd", wb.h[:].rearrange("p (c i) -> p c i", i=128),
                  W["mam_wd"][:, dp * 128:(dp + 1) * 128].rearrange("(c p) i -> p c i", p=128),
                  writes=[wb])
            for tc2 in range(2):
                bank = k.pb[tc2 % 2]
                k.mmop(bank, [(wb.h[:, c * 128:(c + 1) * 128],
                               valc.h[:, c * T + t0 + tc2 * 512: c * T + t0 + tc2 * 512 + 512])
                              for c in range(NCH)], reads=[wb, valc])
                k.ts("V", cdh.h[:, dp * TH + tc2 * 512: dp * TH + tc2 * 512 + 512],
                     bank.ap, dbc.h[:, dp:dp + 1], ALU.add, reads=[bank, dbc], writes=[cdh])
        for dp in range(NCH):
            wb = wdb[dp % 2]
            k.wload("mam_wt", wb.h[:].rearrange("p (c i) -> p c i", i=128),
                  W["mam_wt"][:, dp * 128:(dp + 1) * 128].rearrange("(c p) i -> p c i", p=128),
                  writes=[wb])
            for tc2 in range(2):
                bank = k.pb[tc2 % 2]
                k.mmop(bank, [(wb.h[:, c * 128:(c + 1) * 128],
                               cdh.h[:, c * TH + tc2 * 512: c * TH + tc2 * 512 + 512])
                              for c in range(NCH)], reads=[wb, cdh])
                dsl = slice(dp * TH + tc2 * 512, dp * TH + tc2 * 512 + 512)
                k.act(d1t.h[:, 512:1024], bank.ap, AF.Abs, reads=[bank], writes=[d1t])
                k.act(d1t.h[:, 512:1024], d1t.h[:, 512:1024], AF.Exp, scale=-1.0,
                     reads=[d1t], writes=[d1t])
                k.act(d1t.h[:, 512:1024], d1t.h[:, 512:1024], AF.Ln, bias=1.0,
                     reads=[d1t], writes=[d1t])
                k.ts("V", d1t.h[:, 0:512], bank.ap, 0.0, ALU.max, reads=[bank], writes=[d1t])
                k.tt("V", dlt.h[:, dsl], d1t.h[:, 0:512], d1t.h[:, 512:1024], ALU.add,
                     reads=[d1t], writes=[dlt])
        k.release(mC, sC)

        for gc in range(NCH):
            wb = wib[gc % 2]
            k.wload("mam_win", wb.h[:].rearrange("p (c i) -> p c i", i=128),
                  W["mam_win"][:, D + gc * 128: D + (gc + 1) * 128].rearrange(
                      "(c p) i -> p c i", p=128), writes=[wb])
            for tc2 in range(2):
                bank = k.pb[tc2 % 2]
                k.mmop(bank, [(wb.h[:, c * 128:(c + 1) * 128],
                               xnh.h[:, c * TH + tc2 * 512: c * TH + tc2 * 512 + 512])
                              for c in range(NCH)], reads=[wb, xnh])
                k.act(gsil.h[:, gc * TH + tc2 * 512: gc * TH + tc2 * 512 + 512],
                      bank.ap, AF.Silu, reads=[bank], writes=[gsil])
        for dp in range(NCH):
            for tc2 in range(2):
                xsl = slice(t0 + tc2 * 512, t0 + tc2 * 512 + 512)
                k.tt("G", k.X[dp].h[:, xsl], k.X[dp].h[:, xsl],
                     xnh.h[:, dp * TH + tc2 * 512: dp * TH + tc2 * 512 + 512],
                     ALU.add, reads=[k.X[dp], xnh], writes=[k.X[dp]])
        k.release(mXN, [xnh])
        ashp = [AH(f"ash{i}", [128, TH], BF16) for i in range(2)]
        sgtp = [AH(f"sgt{i}", [128, TH], BF16) for i in range(2)]
        rmgp = [d1t, AH("rmg1", [128, TH], F32)]
        wshp = [AH(f"wsh{i}", [128, TH], BF16) for i in range(2)]
        sshp = [AH(f"ssh{i}", [128, TH], BF16) for i in range(2)]
        prdp = [AH(f"prd{i}", [128, TH], BF16) for i in range(2)]
        bbtp = [AH(f"bbt{i}", [128, TH], BF16) for i in range(2)]
        cbtp = [AH(f"cbt{i}", [128, TH], BF16) for i in range(2)]
        dvt = AH("dvt", [128, TH], BF16)
        ycb = AH("ycb", [128, TH], F32)
        for cb in range(NCH):
            dsl = slice(cb * TH, (cb + 1) * TH)
            k.tt("V", dvt.ap, dlt.h[:, dsl],
                 valc.h[:, cb * T + t0: cb * T + t0 + TH], ALU.mult,
                 reads=[dlt, valc], writes=[dvt])
            for sg in range(16):
                p = sg & 1
                ash, sgt, rmg = ashp[p], sgtp[p], rmgp[p]
                wsh, ssh, prd = wshp[p], sshp[p], prdp[p]
                bbt, cbt = bbtp[p], cbtp[p]
                for (row, dst) in ((sg, bbt), (16 + sg, cbt)):
                    for tc2 in range(2):
                        k.mmop(k.pb[2 + tc2], [(sel32.h[0:32, row * 128:(row + 1) * 128],
                                                bcf.h[0:32, tc2 * 512:(tc2 + 1) * 512])],
                               reads=[sel32, bcf])
                        k.act(dst.h[:, tc2 * 512:(tc2 + 1) * 512],
                              k.pb[2 + tc2].ap, AF.Copy,
                              reads=[k.pb[2 + tc2]], writes=[dst])
                acol = Atile.h[:, cb * 16 + sg: cb * 16 + sg + 1]
                k.ts("V", ash.ap, dlt.h[:, dsl], acol, ALU.mult,
                     reads=[dlt, Atile], writes=[ash])
                k.act(sgt.ap, ash.ap, AF.Sign, bias=1e-6, reads=[ash], writes=[sgt])
                k.act(rmg.ap, ash.ap, AF.Abs, bias=1e-6, reads=[ash], writes=[rmg])
                k.act(rmg.ap, rmg.ap, AF.Ln, reads=[rmg], writes=[rmg])
                k.act(rmg.ap, rmg.ap, AF.Exp, scale=-1.0, reads=[rmg], writes=[rmg])
                k.act(ash.ap, ash.ap, AF.Exp, reads=[ash], writes=[ash])
                k.tt("V", rmg.ap, rmg.ap, sgt.ap, ALU.mult,
                     reads=[rmg, sgt], writes=[rmg])
                k.stt("V", wsh.ap, ash.ap, -1.0, rmg.ap, ALU.add, ALU.mult,
                      reads=[ash, rmg], writes=[wsh])
                k.tt("G", wsh.ap, wsh.ap, bbt.ap, ALU.mult, reads=[wsh, bbt], writes=[wsh])
                k.tt("G", wsh.ap, wsh.ap, dvt.ap, ALU.mult,
                     reads=[wsh, dvt], writes=[wsh])
                init = 0.0 if h2 == 0 else stt_.h[:, cb * 16 + sg: cb * 16 + sg + 1]
                rd = [ash, wsh] + ([stt_] if h2 == 1 else [])
                k.scan("V", ssh.ap, ash.ap, wsh.ap, init, ALU.mult, ALU.add,
                       reads=rd, writes=[ssh])
                k.copy("V", stt_.h[:, cb * 16 + sg: cb * 16 + sg + 1],
                       ssh.h[:, TH - 1:TH], reads=[ssh], writes=[stt_])
                if sg == 0:
                    k.tt("G", ycb.ap, ssh.ap, cbt.ap, ALU.mult,
                         reads=[ssh, cbt], writes=[ycb])
                else:
                    k.tt("G", prd.ap, ssh.ap, cbt.ap, ALU.mult,
                         reads=[ssh, cbt], writes=[prd])
                    k.tt("V", ycb.ap, ycb.ap, prd.ap, ALU.add,
                         reads=[ycb, prd], writes=[ycb])
            k.tt("V", gsil.h[:, dsl], ycb.ap, gsil.h[:, dsl], ALU.mult,
                 reads=[ycb, gsil], writes=[gsil])
        for dp in range(NCH):
            wb = wib[dp % 2]
            k.wload("mam_wout", wb.h[:].rearrange("p (c i) -> p c i", i=128),
                  W["mam_wout"][:, dp * 128:(dp + 1) * 128].rearrange(
                      "(c p) i -> p c i", p=128), writes=[wb])
            for tc2 in range(2):
                bank = k.pb[2 + tc2 % 2]
                k.mmop(bank, [(wb.h[:, c * 128:(c + 1) * 128],
                               gsil.h[:, c * TH + tc2 * 512: c * TH + tc2 * 512 + 512])
                              for c in range(NCH)], reads=[wb, gsil])
                xt = k.X[dp]
                xsl = slice(t0 + tc2 * 512, t0 + tc2 * 512 + 512)
                k.tt("V", xt.h[:, xsl], xt.h[:, xsl], bank.ap, ALU.add,
                     reads=[xt, bank], writes=[xt])
        k.release(mH, sH)
    k.release(m0, tiles)


STEP_FNS["l0_attn_b"] = step_l0_attn_b
STEP_WEIGHTS["l0_attn_b"] = {
    "mam_win": (D, 2 * D), "mam_conv": (128, NCH * 3), "mam_wbc": (D, 32),
    "mam_wd": (D, D), "mam_db": (128, NCH), "mam_wt": (D, D),
    "mam_alog": (128, NCH * 16), "mam_wout": (D, D),
}



def step_l1_attn_b(k, W):
    H, HD, C = 16, 64, 128
    NCHK = T // C
    NI = 2764
    ICW = [128] * 21 + [76]
    m0 = k.mark()
    tiles = []

    def A(name, shape, dtype):
        t = k.alloc(name, shape, dtype)
        tiles.append(t)
        return t

    mbk = A("mbk", [128, T], BF16)
    rbk = A("rbk", [128, T], BF16)
    n12 = A("n12", [128, 2 * NCH], F32)
    k.dma("Y", n12.ap, W["gd_norms"], writes=[n12])
    cwb = A("cwb", [128, 5 * NCH], F32)
    k.dma("Y", cwb.ap, W["gd_conv"], writes=[cwb])
    trilS = A("trilS", [128, 128], BF16)
    k.memset("V", trilS.ap, 1.0, writes=[trilS])
    k.affsel(trilS.ap, trilS.ap, [[-1, 128]], ALU.is_ge, 0.0, 1, base=-1,
             reads=[trilS], writes=[trilS])
    triuI = A("triuI", [128, 128], BF16)
    k.memset("V", triuI.ap, 1.0, writes=[triuI])
    k.affsel(triuI.ap, triuI.ap, [[1, 128]], ALU.is_ge, 0.0, -1,
             reads=[triuI], writes=[triuI])
    sel16 = A("sel16", [16, 16 * 128], F32)
    k.memset("V", sel16.ap, 1.0, writes=[sel16])
    k.affsel(sel16.ap, sel16.ap, [[1, 16 * 128]], ALU.is_ge, 0.0, -128,
             reads=[sel16], writes=[sel16])
    k.affsel(sel16.ap, sel16.ap, [[-1, 16 * 128]], ALU.is_ge, 0.0, 128, base=127,
             reads=[sel16], writes=[sel16])
    sel2 = A("sel2", [2, 2 * 128], BF16)
    k.memset("V", sel2.ap, 1.0, writes=[sel2])
    k.affsel(sel2.ap, sel2.ap, [[1, 2 * 128]], ALU.is_ge, 0.0, -64,
             reads=[sel2], writes=[sel2])
    k.affsel(sel2.ap, sel2.ap, [[-1, 2 * 128]], ALU.is_ge, 0.0, 64, base=63,
             reads=[sel2], writes=[sel2])

    mL = k.mark()
    sL = []

    def AL(name, shape, dtype):
        t = k.alloc(name, shape, dtype)
        sL.append(t)
        return t
    scr = ln_scratch(k, AL)
    emit_ln_stats(k, k.X, scr)
    k.copy("V", mbk.ap, scr["mb"].ap, reads=[scr["mb"]], writes=[mbk])
    k.copy("V", rbk.ap, scr["rb"].ap, reads=[scr["rb"]], writes=[rbk])
    k.release(mL, sL)

    mMid = k.mark()
    sMid = []

    def AM(name, shape, dtype):
        t = k.alloc(name, shape, dtype)
        sMid.append(t)
        return t
    o_s = k.nc.dram_tensor("gd_o_scratch", [D, T], BF16, kind="Internal").ap()
    o_sb = Buf("gd_o_scratch")
    o_buf = AM("obuf", [128, NCH * C], BF16)
    xc = AM("xc", [128, NCH * T], BF16)
    mC = k.mark()
    sC = []

    def AC(name, shape, dtype):
        t = k.alloc(name, shape, dtype)
        sC.append(t)
        return t
    xn1 = AC("xn1", [128, T + 3], BF16)
    rsb = AC("rsb", [128, T], F32)
    h2t = AC("h2t", [128, 512], BF16)
    cvt = AC("cvt", [128, 512], F32)
    for tc in range(4):
        sl = slice(tc * 512, (tc + 1) * 512)
        for c in range(NCH):
            k.tt("G", cvt.ap, k.X[c].h[:, sl], mbk.h[:, sl], ALU.subtract,
                 reads=[k.X[c], mbk], writes=[cvt])
            k.tt("V", h2t.ap, cvt.ap, rbk.h[:, sl], ALU.mult,
                 reads=[cvt, rbk], writes=[h2t])
            k.act(h2t.ap, h2t.ap, AF.Square, reads=[h2t], writes=[h2t])
            k.mmop(k.pb[0], [(k.ones_bf.ap, h2t.ap)], reads=[k.ones_bf, h2t],
                   accum=(c > 0), close=(c == NCH - 1))
        k.ts("V", rsb.h[:, sl], k.pb[0].ap, 1.0 / D, ALU.mult, s2=1e-6, op1=ALU.add,
             reads=[k.pb[0]], writes=[rsb])
        k.act(rsb.h[:, sl], rsb.h[:, sl], AF.Sqrt, reads=[rsb], writes=[rsb])
    k.recip(rsb.ap, rsb.ap, reads=[rsb], writes=[rsb])
    for c in range(NCH):
        k.memset("V", xn1.h[:, 0:3], 0.0, writes=[xn1])
        for tc in range(4):
            sl = slice(tc * 512, (tc + 1) * 512)
            k.tt("G", cvt.ap, k.X[c].h[:, sl], mbk.h[:, sl], ALU.subtract,
                 reads=[k.X[c], mbk], writes=[cvt])
            k.tt("V", cvt.ap, cvt.ap, rbk.h[:, sl], ALU.mult,
                 reads=[cvt, rbk], writes=[cvt])
            k.tt("G", cvt.ap, cvt.ap, rsb.h[:, sl], ALU.mult, reads=[cvt, rsb], writes=[cvt])
            k.ts("V", xn1.h[:, 3 + tc * 512: 3 + tc * 512 + 512],
                 cvt.ap, n12.h[:, c:c + 1], ALU.mult, reads=[cvt, n12], writes=[xn1])
        for hc in range(4):
            t0 = hc * 512
            k.ts("V", cvt.ap, xn1.h[:, t0: t0 + 512],
                 cwb.h[:, c:c + 1], ALU.mult, reads=[xn1, cwb], writes=[cvt])
            for j in range(1, 4):
                k.stt("V", cvt.ap, xn1.h[:, j + t0: j + t0 + 512],
                      cwb.h[:, j * NCH + c: j * NCH + c + 1], cvt.ap,
                      ALU.mult, ALU.add, reads=[xn1, cwb, cvt], writes=[cvt])
            k.act(xc.h[:, c * T + t0: c * T + t0 + 512], cvt.ap, AF.Silu,
                  bias=cwb.h[:, 4 * NCH + c: 4 * NCH + c + 1],
                  reads=[cvt, cwb], writes=[xc])
    k.release(mC, sC)
    if int(os.environ.get("GD_PH", "9")) <= 1:
        k.release(mMid, sMid)
        k.release(m0, tiles)
        return

    mD = k.mark()
    sD = []

    def AD(name, shape, dtype):
        t = k.alloc(name, shape, dtype)
        sD.append(t)
        return t
    betaf = AD("betaf", [16, T], BF16)
    lgf = AD("lgf", [16, T], F32)
    Sst = AD("Sst", [64, H * 64], F32)
    Sbf = AD("Sbf", [64, H * 64], BF16)
    k.memset("V", Sst.ap, 0.0, writes=[Sst])
    k.memset("V", Sbf.ap, 0.0, writes=[Sbf])
    mB = k.mark()
    sB = []
    wba = k.alloc("wba", [128, NCH * 16], BF16)
    t16a = k.alloc("t16a", [16, 512], F32)
    t16b = k.alloc("t16b", [16, 512], F32)
    sB.extend([wba, t16a, t16b])
    for (wnm, coff) in (("gd_wb", 0), ("gd_wa", 1)):
        k.wload("gd_wo", wba.h[:].rearrange("p (c i) -> p c i", i=16),
              W[wnm].rearrange("(c p) i -> p c i", p=128), writes=[wba])
        for tc in range(4):
            sl = slice(tc * 512, (tc + 1) * 512)
            k.mmop(k.pb[0], [(wba.h[:, c * 16:(c + 1) * 16],
                              xc.h[:, c * T + tc * 512: c * T + tc * 512 + 512])
                             for c in range(NCH)], reads=[wba, xc], nparts=16)
            if coff == 0:
                k.act(betaf.h[:, sl], k.pb[0].h[0:16, :], AF.Sigmoid,
                      reads=[k.pb[0]], writes=[betaf])
            else:
                k.ts("V", t16a.ap, k.pb[0].h[0:16, :], 4.0, ALU.add,
                     reads=[k.pb[0]], writes=[t16a])
                k.act(t16b.ap, t16a.ap, AF.Abs, reads=[t16a], writes=[t16b])
                k.act(t16b.ap, t16b.ap, AF.Exp, scale=-1.0, reads=[t16b], writes=[t16b])
                k.act(t16b.ap, t16b.ap, AF.Ln, bias=1.0, reads=[t16b], writes=[t16b])
                k.ts("V", t16a.ap, t16a.ap, 0.0, ALU.min, reads=[t16a], writes=[t16a])
                k.tt("V", lgf.h[:, sl], t16a.ap, t16b.ap, ALU.subtract,
                     reads=[t16a, t16b], writes=[lgf])
    zz16 = t16a
    k.memset("V", zz16.ap, 0.0, writes=[zz16])
    for cc in range(NCHK):
        k.scan("V", lgf.h[0:16, cc * C:(cc + 1) * C], lgf.h[0:16, cc * C:(cc + 1) * C],
               zz16.h[0:16, 0:C], 0.0, ALU.add, ALU.add, reads=[lgf, zz16], writes=[lgf])
    k.release(mB, sB)
    if int(os.environ.get("GD_PH", "9")) <= 2:
        k.release(mMid, sMid)
        k.release(m0, tiles)
        return

    qcf = AD("qcf", [128, NCH * C], BF16)
    kcf = AD("kcf", [128, NCH * C], BF16)
    vcf = AD("vcf", [128, NCH * C], BF16)
    wpj = [AD(f"wpj{i}", [128, NCH * 1024], BF16) for i in range(2)]
    wvs = [AD(f"wvs{i}", [128, NCH * 128], BF16) for i in range(2)]
    lgtm = AD("lgtm", [128, 16], F32)
    bttm = AD("bttm", [128, 16], F32)
    sq2 = AD("sq2", [128, C], BF16)
    rs2 = AD("rs2", [2, C], F32)
    rs2b = AD("rs2b", [2, C], BF16)
    nrm = AD("nrm", [128, C], BF16)
    US = []
    for ui in range(2):
        u = {}
        for nm, sh, dt_ in (("gC2", [64, 1], F32), ("gcol", [128, 1], F32),
                            ("m2f", [128, C], F32), ("t1f", [128, C], F32),
                            ("qh_t", [64, C], BF16), ("kh_t", [64, C], BF16),
                            ("kb_t", [64, C], BF16), ("Ebf", [128, C], BF16),
                            ("ETbf", [128, C], BF16), ("grow", [128, C], BF16),
                            ("Lp", [128, C], F32), ("Lt", [128, C], F32),
                            ("L2p", [128, C], F32), ("L2t", [128, C], F32),
                            ("MT", [128, C], BF16), ("Vtm", [128, 64], BF16),
                            ("Rtm", [128, 64], F32), ("Xtm", [128, 64], F32),
                            ("Xbf", [128, 64], BF16), ("obf", [128, 64], BF16),
                            ("ktm", [128, 64], BF16)):
            u[nm] = AD(f"{nm}{ui}", sh, dt_)
        US.append(u)
    trilSf = AD("trilSf", [128, 128], F32)
    hsum2 = AD("hsum2", [128, 2], BF16)
    k.memset("V", trilSf.ap, 1.0, writes=[trilSf])
    k.affsel(trilSf.ap, trilSf.ap, [[-1, 128]], ALU.is_ge, 0.0, 1, base=-1,
             reads=[trilSf], writes=[trilSf])
    k.transp(k.pbh[0], sel2.h[0:2, 0:128], k.ident.h[0:2, 0:2], slice(0, 2),
             reads=[sel2, k.ident], nparts=128)
    k.copy("V", hsum2.ap, k.pbh[0].h[:, 0:2], reads=[k.pbh[0]], writes=[hsum2])
    for cc in range(NCHK):
        csl = slice(cc * C, (cc + 1) * C)
        for wi, (wnm, dst) in enumerate((("gd_wq", qcf), ("gd_wk", kcf))):
            wt = wpj[wi]
            if cc == 0:
                k.wload(wnm, wt.h[:].rearrange("p (c i) -> p c i", i=1024),
                        W[wnm].rearrange("(c p) i -> p c i", p=128), writes=[wt])
            for oc in range(NCH):
                bank = k.pb[oc % 2]
                k.mmop(bank, [(wt.h[:, c * 1024 + oc * 128: c * 1024 + oc * 128 + 128],
                               xc.h[:, c * T + cc * C: c * T + cc * C + C])
                              for c in range(NCH)], reads=[wt, xc],
                       out_slice=slice(0, C))
                k.copy("V", dst.h[:, oc * C:(oc + 1) * C], bank.h[:, 0:C],
                       reads=[bank], writes=[dst])
        for oc in range(NCH):
            wv_ = wvs[oc % 2]
            k.wload("gd_wv", wv_.h[:].rearrange("p (c i) -> p c i", i=128),
                    W["gd_wv"][:, oc * 128:(oc + 1) * 128].rearrange(
                        "(c p) i -> p c i", p=128), writes=[wv_])
            bank = k.pb[2 + oc % 2]
            k.mmop(bank, [(wv_.h[:, c * 128:(c + 1) * 128],
                           xc.h[:, c * T + cc * C: c * T + cc * C + C])
                          for c in range(NCH)], reads=[wv_, xc],
                   out_slice=slice(0, C))
            k.copy("V", vcf.h[:, oc * C:(oc + 1) * C], bank.h[:, 0:C],
                   reads=[bank], writes=[vcf])
        for dst in (qcf, kcf):
            for oc in range(NCH):
                osl = slice(oc * C, (oc + 1) * C)
                k.act(sq2.ap, dst.h[:, osl], AF.Square, reads=[dst], writes=[sq2])
                k.mmop(k.pb[2], [(hsum2.ap, sq2.ap)], reads=[hsum2, sq2], nparts=2,
                       out_slice=slice(0, C))
                k.ts("V", rs2.ap, k.pb[2].h[0:2, 0:C], 1.0, ALU.mult, s2=1e-6, op1=ALU.add,
                     reads=[k.pb[2]], writes=[rs2])
                k.act(rs2.ap, rs2.ap, AF.Sqrt, reads=[rs2], writes=[rs2])
                k.recip(rs2.ap, rs2.ap, reads=[rs2], writes=[rs2])
                k.copy("V", rs2b.ap, rs2.ap, reads=[rs2], writes=[rs2b])
                k.mmop(k.pb[3], [(sel2.h[0:2, 0:128], rs2b.h[0:2, :])],
                       reads=[sel2, rs2b], out_slice=slice(0, C))
                k.copy("V", nrm.ap, k.pb[3].h[:, 0:C], reads=[k.pb[3]], writes=[nrm])
                k.tt("V", dst.h[:, osl], dst.h[:, osl], nrm.ap, ALU.mult,
                     reads=[dst, nrm], writes=[dst])
        k.transp(k.pb[4], lgf.h[0:16, csl], k.identf.h[0:16, 0:16], slice(0, 16),
                 reads=[lgf, k.identf], nparts=128)
        k.copy("V", lgtm.ap, k.pb[4].h[:, 0:16], reads=[k.pb[4]], writes=[lgtm])
        k.transp(k.pbh[0], betaf.h[0:16, csl], k.ident.h[0:16, 0:16], slice(0, 16),
                 reads=[betaf, k.ident], nparts=128)
        k.copy("V", bttm.ap, k.pbh[0].h[:, 0:16], reads=[k.pbh[0]], writes=[bttm])
        for h in range(H):
            u = US[h % 2]
            B0 = k.pb[(h % 2) * 3]
            B1 = k.pb[(h % 2) * 3 + 1]
            B2 = k.pb[(h % 2) * 3 + 2]
            PH = k.pbh[h % 2]
            oc = h // 2
            pb_ = (h % 2) * 64
            qsl = slice(oc * C, (oc + 1) * C)
            ssl = slice(h * 64, (h + 1) * 64)
            btcol = bttm.h[:, h:h + 1]
            k.copy("V", u["qh_t"].ap, qcf.h[pb_:pb_ + 64, qsl], reads=[qcf], writes=[u["qh_t"]])
            k.copy("G", u["kh_t"].ap, kcf.h[pb_:pb_ + 64, qsl], reads=[kcf], writes=[u["kh_t"]])
            k.mmop(B0, [(sel16.h[0:16, h * 128:(h + 1) * 128], lgf.h[0:16, csl])],
                   reads=[sel16, lgf], out_slice=slice(0, C))
            k.act(u["grow"].ap, B0.h[:, 0:C], AF.Exp, reads=[B0], writes=[u["grow"]])
            k.ts("V", u["m2f"].ap, B0.h[:, 0:C], lgtm.h[:, h:h + 1], ALU.subtract,
                 reads=[B0, lgtm], writes=[u["m2f"]])
            k.ts("V", u["t1f"].ap, u["m2f"].ap, 0.0, ALU.max, reads=[u["m2f"]], writes=[u["t1f"]])
            k.act(u["Ebf"].ap, u["t1f"].ap, AF.Exp, scale=-1.0, reads=[u["t1f"]], writes=[u["Ebf"]])
            k.ts("V", u["t1f"].ap, u["m2f"].ap, 0.0, ALU.min, reads=[u["m2f"]], writes=[u["t1f"]])
            k.act(u["ETbf"].ap, u["t1f"].ap, AF.Exp, reads=[u["t1f"]], writes=[u["ETbf"]])
            k.mmop(B1, [(u["kh_t"].ap, u["kh_t"].ap)], reads=[u["kh_t"]], out_slice=slice(0, C))
            k.tt("V", u["Lp"].ap, B1.h[:, 0:C], u["Ebf"].ap, ALU.mult,
                 reads=[B1, u["Ebf"]], writes=[u["Lp"]])
            k.ts("V", u["Lp"].ap, u["Lp"].ap, btcol, ALU.mult, reads=[u["Lp"], bttm], writes=[u["Lp"]])
            k.tt("G", u["Lp"].ap, u["Lp"].ap, trilSf.ap, ALU.mult, reads=[u["Lp"], trilSf], writes=[u["Lp"]])
            k.transp(B0, u["Lp"].ap, k.identf.ap, slice(0, 128), reads=[u["Lp"], k.identf])
            k.copy("V", u["Lt"].ap, B0.h[:, 0:C], reads=[B0], writes=[u["Lt"]])
            k.mmop(B2, [(u["kh_t"].ap, u["qh_t"].ap)], reads=[u["kh_t"], u["qh_t"]],
                   out_slice=slice(0, C))
            k.tt("V", u["MT"].ap, B2.h[:, 0:C], u["ETbf"].ap, ALU.mult,
                 reads=[B2, u["ETbf"]], writes=[u["MT"]])
            k.tt("G", u["MT"].ap, u["MT"].ap, triuI.ap, ALU.mult, reads=[u["MT"], triuI], writes=[u["MT"]])
            k.tt("V", u["kb_t"].ap, u["kh_t"].ap, u["grow"].h[0:64, :], ALU.mult,
                 reads=[u["kh_t"], u["grow"]], writes=[u["kb_t"]])
            k.tt("G", u["qh_t"].ap, u["qh_t"].ap, u["grow"].h[0:64, :], ALU.mult,
                 reads=[u["qh_t"], u["grow"]], writes=[u["qh_t"]])
            k.transp(PH, vcf.h[pb_:pb_ + 64, qsl],
                     k.ident.h[pb_:pb_ + 64, pb_:pb_ + 64],
                     slice(0, 64), reads=[vcf, k.ident])
            k.copy("V", u["Vtm"].ap, PH.h[:, 0:64], reads=[PH], writes=[u["Vtm"]])
            k.transp(PH, u["kh_t"].ap, k.ident.h[0:64, 0:64], slice(64, 128),
                     reads=[u["kh_t"], k.ident])
            k.copy("V", u["ktm"].ap, PH.h[:, 64:128], reads=[PH], writes=[u["ktm"]])
            k.mmop(B2, [(u["kb_t"].ap, Sbf.h[0:64, ssl])], reads=[u["kb_t"], Sbf],
                   out_slice=slice(0, 64))
            k.tt("V", u["Rtm"].ap, u["Vtm"].ap, B2.h[:, 0:64], ALU.subtract,
                 reads=[u["Vtm"], B2], writes=[u["Rtm"]])
            k.ts("V", u["Rtm"].ap, u["Rtm"].ap, btcol, ALU.mult, reads=[u["Rtm"], bttm], writes=[u["Rtm"]])
            k.mmop(B1, [(u["Lt"].ap, u["Rtm"].ap)], reads=[u["Lt"], u["Rtm"]], out_slice=slice(0, 64))
            k.tt("V", u["Xtm"].ap, u["Rtm"].ap, B1.h[:, 0:64], ALU.subtract,
                 reads=[u["Rtm"], B1], writes=[u["Xtm"]])
            cp, ct, np_, nt = u["Lp"], u["Lt"], u["L2p"], u["L2t"]
            for lev in range(6):
                k.mmop(B1, [(ct.ap, cp.ap)], reads=[ct, cp], out_slice=slice(0, C))
                k.copy("V", np_.ap, B1.h[:, 0:C], reads=[B1], writes=[np_])
                k.transp(B0, np_.ap, k.identf.ap, slice(0, 128), reads=[np_, k.identf])
                k.copy("V", nt.ap, B0.h[:, 0:C], reads=[B0], writes=[nt])
                k.mmop(B1, [(nt.ap, u["Xtm"].ap)], reads=[nt, u["Xtm"]], out_slice=slice(0, 64))
                k.tt("V", u["Xtm"].ap, u["Xtm"].ap, B1.h[:, 0:64], ALU.add,
                     reads=[u["Xtm"], B1], writes=[u["Xtm"]])
                cp, ct, np_, nt = np_, nt, cp, ct
            k.copy("G", u["Xbf"].ap, u["Xtm"].ap, reads=[u["Xtm"]], writes=[u["Xbf"]])
            k.mmop(B2, [(u["qh_t"].ap, Sbf.h[0:64, ssl]), (u["MT"].ap, u["Xbf"].ap)],
                   reads=[u["qh_t"], Sbf, u["MT"], u["Xbf"]], out_slice=slice(0, 64))
            k.copy("V", u["obf"].ap, B2.h[:, 0:64], reads=[B2], writes=[u["obf"]])
            k.transp(PH, u["obf"].ap, k.ident.ap, slice(0, C), reads=[u["obf"], k.ident],
                     nparts=64)
            k.copy("V", o_buf.h[pb_:pb_ + 64, oc * C:(oc + 1) * C],
                   PH.h[0:64, 0:C], reads=[PH], writes=[o_buf])
            k.act(u["gcol"].ap, u["m2f"].h[:, C - 1:C], AF.Exp, reads=[u["m2f"]], writes=[u["gcol"]])
            k.ts("V", u["ktm"].ap, u["ktm"].ap, u["gcol"].h[:, 0:1], ALU.mult,
                 reads=[u["ktm"], u["gcol"]], writes=[u["ktm"]])
            k.mmop(B2, [(u["ktm"].ap, u["Xbf"].ap)], reads=[u["ktm"], u["Xbf"]],
                   out_slice=slice(0, 64), nparts=64)
            k.copy("V", u["gC2"].ap, u["grow"].h[0:64, C - 1:C], reads=[u["grow"]], writes=[u["gC2"]])
            k.stt("V", Sst.h[0:64, ssl], Sst.h[0:64, ssl], u["gC2"].h[0:64, 0:1],
                  B2.h[0:64, 0:64], ALU.mult, ALU.add,
                  reads=[Sst, u["gC2"], B2], writes=[Sst])
            k.copy("G", Sbf.h[0:64, ssl], Sst.h[0:64, ssl], reads=[Sst], writes=[Sbf])
        k.dma("S", o_s.rearrange("(c p) t -> p c t", p=128)[:, :, cc * C:(cc + 1) * C],
              o_buf.h[:].rearrange("p (c t) -> p c t", t=C), reads=[o_buf], writes=[o_sb])
    k.release(mD, sD)
    if int(os.environ.get("GD_PH", "9")) <= 3:
        k.release(mMid, sMid)
        k.release(m0, tiles)
        return
    hh = AM("hh", [128, NCH * T], BF16)

    mW = k.mark()
    sW = []
    wob = k.alloc("wob", [128, NCH * 128], BF16)
    sW.append(wob)
    ob2 = k.alloc("ob2", [128, NCH * 512], BF16)
    sW.append(ob2)
    for tc in range(4):
        k.dma("Y", ob2.h[:].rearrange("p (c t) -> p c t", t=512),
              o_s.rearrange("(c p) t -> p c t", p=128)[:, :, tc * 512:(tc + 1) * 512],
              reads=[o_sb], writes=[ob2])
        for dp in range(NCH):
            tsl = slice(tc * 512, (tc + 1) * 512)
            bank = k.pb[2 + dp % 2]
            k.wload("gd_wo", wob.h[:].rearrange("p (c i) -> p c i", i=128),
                    W["gd_wo"][:, dp * 128:(dp + 1) * 128].rearrange(
                        "(c p) i -> p c i", p=128), writes=[wob])
            k.mmop(bank, [(wob.h[:, c * 128:(c + 1) * 128],
                           ob2.h[:, c * 512:(c + 1) * 512])
                          for c in range(NCH)], reads=[wob, ob2])
            hsl2 = slice(dp * T + tc * 512, dp * T + tc * 512 + 512)
            k.tt("G", hh.h[:, hsl2], k.X[dp].h[:, tsl], mbk.h[:, tsl], ALU.subtract,
                 reads=[k.X[dp], mbk], writes=[hh])
            k.tt("V", hh.h[:, hsl2], hh.h[:, hsl2], rbk.h[:, tsl], ALU.mult,
                 reads=[hh, rbk], writes=[hh])
            k.tt("V", hh.h[:, hsl2], hh.h[:, hsl2], bank.ap, ALU.add,
                 reads=[hh, bank], writes=[hh])
            xt = k.X[dp]
            k.tt("G", xt.h[:, tsl], xt.h[:, tsl],
                 hh.h[:, dp * T + tc * 512: dp * T + tc * 512 + 512], ALU.add,
                 reads=[xt, hh], writes=[xt])
    k.release(mW, sW)

    if int(os.environ.get("GD_PH", "9")) <= 4:
        k.release(mMid, sMid)
        k.release(m0, tiles)
        return

    sW = []
    mW = k.mark()

    def AW(name, shape, dtype):
        t = k.alloc(name, shape, dtype)
        sW.append(t)
        return t
    rsb2 = AW("rsb2", [128, T], F32)
    h2b = AW("h2b", [128, 512], BF16)
    for tc in range(4):
        sl = slice(tc * 512, (tc + 1) * 512)
        for c in range(NCH):
            k.act(h2b.ap, hh.h[:, c * T + tc * 512: c * T + tc * 512 + 512],
                  AF.Square, reads=[hh], writes=[h2b])
            k.mmop(k.pb[0], [(k.ones_bf.ap, h2b.ap)], reads=[k.ones_bf, h2b],
                   accum=(c > 0), close=(c == NCH - 1))
        k.ts("V", rsb2.h[:, sl], k.pb[0].ap, 1.0 / D, ALU.mult, s2=1e-6, op1=ALU.add,
             reads=[k.pb[0]], writes=[rsb2])
        k.act(rsb2.h[:, sl], rsb2.h[:, sl], AF.Sqrt, reads=[rsb2], writes=[rsb2])
    k.recip(rsb2.ap, rsb2.ap, reads=[rsb2], writes=[rsb2])
    for c in range(NCH):
        for tc in range(4):
            sl = slice(tc * 512, (tc + 1) * 512)
            hsl = slice(c * T + tc * 512, c * T + tc * 512 + 512)
            k.tt("G", hh.h[:, hsl], hh.h[:, hsl], rsb2.h[:, sl], ALU.mult,
                 reads=[hh, rsb2], writes=[hh])
            k.ts("V", hh.h[:, hsl], hh.h[:, hsl], n12.h[:, NCH + c:NCH + c + 1],
                 ALU.mult, reads=[hh, n12], writes=[hh])
    hbuf = AW("hbuf", [128, 22 * 512], BF16)
    wgb = [AW(f"wgb{i}", [128, NCH * 128], BF16) for i in range(2)]
    wub = [AW(f"wub{i}", [128, NCH * 128], BF16) for i in range(2)]
    wdb2 = [AW(f"wdb2{i}", [128, 128], BF16) for i in range(2)]
    sgt = AW("sgt", [128, 512], F32)
    ic_off = [0]
    for w_ in ICW:
        ic_off.append(ic_off[-1] + w_)
    for tc in range(4):
        tsl = slice(tc * 512, (tc + 1) * 512)
        for ic in range(22):
            iw = ICW[ic]
            wg_ = wgb[ic % 2]
            wu_ = wub[ic % 2]
            k.wload("gd_wg", wg_.h[:, 0:NCH * iw].rearrange("p (c i) -> p c i", i=iw),
                  W["gd_wg"][:, ic_off[ic]:ic_off[ic] + iw].rearrange(
                      "(c p) i -> p c i", p=128), writes=[wg_])
            k.wload("gd_wu", wu_.h[:, 0:NCH * iw].rearrange("p (c i) -> p c i", i=iw),
                  W["gd_wu"][:, ic_off[ic]:ic_off[ic] + iw].rearrange(
                      "(c p) i -> p c i", p=128), writes=[wu_])
            k.mmop(k.pb[0], [(wg_.h[:, c * iw:(c + 1) * iw],
                              hh.h[:, c * T + tc * 512: c * T + tc * 512 + 512])
                             for c in range(NCH)], reads=[wg_, hh], nparts=iw)
            k.mmop(k.pb[1], [(wu_.h[:, c * iw:(c + 1) * iw],
                              hh.h[:, c * T + tc * 512: c * T + tc * 512 + 512])
                             for c in range(NCH)], reads=[wu_, hh], nparts=iw)
            k.act(sgt.h[0:iw, :], k.pb[0].h[0:iw, :], AF.Silu, reads=[k.pb[0]], writes=[sgt])
            k.tt("V", hbuf.h[0:iw, ic * 512:(ic + 1) * 512], sgt.h[0:iw, :],
                 k.pb[1].h[0:iw, :], ALU.mult, reads=[sgt, k.pb[1]], writes=[hbuf])
        for dp in range(NCH):
            bank = k.pb[2 + dp % 2]
            for ic in range(22):
                iw = ICW[ic]
                wd_ = wdb2[ic % 2]
                k.wload("gd_wd", wd_.h[0:iw, :],
                      W["gd_wd"][ic_off[ic]:ic_off[ic] + iw, dp * 128:(dp + 1) * 128],
                      writes=[wd_])
                k.mmop(bank, [(wd_.h[0:iw, :], hbuf.h[0:iw, ic * 512:(ic + 1) * 512])],
                       reads=[wd_, hbuf], accum=(ic > 0), close=(ic == 21))
            xt = k.X[dp]
            k.tt("V", xt.h[:, tsl], xt.h[:, tsl], bank.ap, ALU.add,
                 reads=[xt, bank], writes=[xt])
    k.release(mW, sW)
    k.release(mMid, sMid)
    k.release(m0, tiles)


STEP_FNS["l1_attn_b"] = step_l1_attn_b
STEP_WEIGHTS["l1_attn_b"] = {
    "gd_norms": (128, 2 * NCH), "gd_conv": (128, 5 * NCH),
    "gd_wq": (D, D), "gd_wk": (D, D), "gd_wv": (D, D),
    "gd_wb": (D, 16), "gd_wa": (D, 16), "gd_wo": (D, D),
    "gd_wg": (D, 2764), "gd_wu": (D, 2764), "gd_wd": (2764, D),
}
STEP_FNS["l1_attn_b"] = step_l1_attn_b


# version 12
# speedup vs baseline: 1.0397x; 1.0397x over previous
import os
import numpy as np
import concourse.bass as bass
import concourse.mybir as mybir
from concourse.bass_utils import run_bass_kernel_spmd

F32 = mybir.dt.float32
BF16 = mybir.dt.bfloat16
I32 = mybir.dt.int32
AF = mybir.ActivationFunctionType
ALU = mybir.AluOpType
AX = mybir.AxisListType

D = 1024
T = 2048
NCH = 8
PBYTES = 229000

ENGINES = ("T", "V", "S", "G", "Y")
DMA_DEPTH = 4
DMA_SEMS = {q: [f"d{q}{j}" for j in range(DMA_DEPTH)] for q in ("Y", "S", "G")}
ALL_DMA_SEMS = [s for v in DMA_SEMS.values() for s in v]


class Buf:
    __slots__ = ("name", "w", "rs", "pend", "serial")

    def __init__(self, name="", serial=False):
        self.name = name
        self.w = None
        self.rs = []
        self.pend = []
        self.serial = serial


class Tile:
    __slots__ = ("h", "buf")

    def __init__(self, h, buf):
        self.h = h
        self.buf = buf

    def __getitem__(self, idx):
        return self.h[idx]

    @property
    def ap(self):
        return self.h[:]


class Ker:
    def __init__(self, nc):
        self.nc = nc
        self.recs = {e: [] for e in ENGINES}
        self.sems = {}
        for s in list(ENGINES) + ALL_DMA_SEMS:
            self.sems[s] = nc.alloc_semaphore(name="sem_" + s)
        self.cnt = {s: 0 for s in self.sems}
        self.dmacnt = {q: 0 for q in DMA_SEMS}
        self.waited = {e: {} for e in ENGINES}
        self.nops = 0
        self.const_vals = {}
        self.cst = nc.alloc_sbuf_tensor("cst", [128, 64], F32)
        self.base = int(nc.sbuf_base) + 4 * 1024
        self.const(0.0)
        self.const(1.0)
        self.residue = []
        self.pb = [Tile(nc.alloc_psum_tensor(f"pb{i}", [128, 512], F32), Buf(f"pb{i}", serial=True))
                   for i in range(6)]
        self.pbh = [Tile(nc.alloc_psum_tensor(f"pbh{i}", [128, 1024], BF16), Buf(f"pbh{i}", serial=True))
                    for i in range(2)]
        self.init_tok = ("V", 1)
        self.cnt["V"] = 1

    def const(self, v):
        v = float(np.float32(v))
        if v in self.const_vals:
            return v
        i = len(self.const_vals)
        assert i < 64
        self.const_vals[v] = i
        self.nc.const_aps.register(F32, v, self.cst[:, i:i + 1])
        return v

    def cap(self, v):
        v = self.const(v)
        i = self.const_vals[v]
        return self.cst[:, i:i + 1]

    def alloc(self, name, shape, dtype, align=64):
        esz = 2 if dtype == BF16 else 4
        nbytes = int(np.prod(shape[1:])) * esz
        start = (self.base + align - 1) // align * align
        end = start + nbytes
        assert end <= PBYTES, f"SBUF overflow: {name} at {start}+{nbytes}"
        self.base = end
        self.nops2 = getattr(self, "nops2", 0) + 1
        h = self.nc.alloc_sbuf_tensor_at(f"{name}_{start}_{self.nops2}", shape, dtype,
                                         offset=start)
        b = Buf(name)
        for (lo, hi, toks) in self.residue:
            if lo < end and start < hi:
                b.pend.extend(toks)
        return Tile(h, b)

    def mark(self):
        return self.base

    def release(self, m, tiles):
        toks = []
        for t in tiles:
            b = t.buf
            if b.w:
                toks.append(b.w)
            toks.extend(b.rs)
        if self.base > m:
            self.residue.append((m, self.base, toks))
        self.base = m

    def op(self, eng, emit, reads=(), writes=(), waits=(), dma=False, n_inc=1):
        toks = [self.init_tok]
        for b in reads:
            b = b.buf if isinstance(b, Tile) else b
            if b.w:
                toks.append(b.w)
            if b.serial:
                toks.extend(b.rs)
            toks.extend(b.pend)
        for b in writes:
            b = b.buf if isinstance(b, Tile) else b
            if b.w:
                toks.append(b.w)
            toks.extend(b.rs)
            toks.extend(b.pend)
        toks.extend(waits)
        need = {}
        for (s, v) in toks:
            if v > need.get(s, -1):
                need[s] = v
        wl = []
        for s, v in sorted(need.items()):
            if v > self.waited[eng].get(s, -1) and v > 0:
                wl.append((s, v))
                self.waited[eng][s] = v
        if dma:
            i = self.dmacnt[eng]
            self.dmacnt[eng] += 1
            sem = DMA_SEMS[eng][i % DMA_DEPTH]
            gate = 16 * (i // DMA_DEPTH)
            if gate > self.waited[eng].get(sem, -1):
                wl.append((sem, gate))
                self.waited[eng][sem] = gate
            self.cnt[sem] += 16
            tok = (sem, self.cnt[sem])
            amt = 16
        else:
            sem = eng
            self.cnt[sem] += 1
            tok = (sem, self.cnt[sem])
            amt = 1
        self.recs[eng].append((wl, emit, self.sems[sem], amt))
        self.nops += 1
        for b in reads:
            b = b.buf if isinstance(b, Tile) else b
            b.rs.append(tok)
        for b in writes:
            b = b.buf if isinstance(b, Tile) else b
            b.w = tok
            b.rs = []
            b.pend = []
        return tok

    def dma(self, eng, dst, src, reads=(), writes=()):
        return self.op(eng, (lambda e, d=dst, s=src: e.dma_start(d, s)),
                       reads=reads, writes=writes, dma=True)

    def wload(self, nm, dst, src, writes=()):
        rd = [self.WB[nm]] if nm in getattr(self, "WB", {}) else []
        return self.dma("Y", dst, src, reads=rd, writes=writes)

    def tt(self, eng, out, a, b, alu, reads=(), writes=()):
        return self.op(eng, (lambda e, o=out, x=a, y=b, u=alu: e.tensor_tensor(o, x, y, u)),
                       reads=reads, writes=writes)

    def ts(self, eng, out, a, s1, op0, s2=None, op1=ALU.bypass, reads=(), writes=()):
        if isinstance(s1, float):
            s1 = float(np.float32(s1))
        if isinstance(s2, float):
            s2 = float(np.float32(s2))
        return self.op(eng, (lambda e, o=out, x=a, a1=s1, a2=s2, p0=op0, p1=op1:
                             e.tensor_scalar(o, x, a1, a2, p0, p1)),
                       reads=reads, writes=writes)

    def stt(self, eng, out, in0, scalar, in1, op0, op1, reads=(), writes=()):
        return self.op(eng, (lambda e, o=out, i0=in0, sc=scalar, i1=in1, p0=op0, p1=op1:
                             e.scalar_tensor_tensor(o, i0, sc, i1, p0, p1)),
                       reads=reads, writes=writes)

    def act(self, out, in_, fn, scale=1.0, bias=0.0, reads=(), writes=(), accum_out=None):
        if isinstance(scale, float):
            scale = self.const(scale)
        if isinstance(bias, float):
            bias = self.const(bias)
        return self.op("S", (lambda e, o=out, i=in_, f=fn, sc=scale, bi=bias, ac=accum_out:
                             e.activation(o, i, f, bias=bi, scale=sc, accum_out=ac)),
                       reads=reads, writes=writes)

    def copy(self, eng, out, in_, reads=(), writes=()):
        if eng == "S":
            return self.op("S", (lambda e, o=out, i=in_: e.copy(o, i)),
                           reads=reads, writes=writes)
        return self.op(eng, (lambda e, o=out, i=in_: e.tensor_copy(o, i)),
                       reads=reads, writes=writes)

    def memset(self, eng, ap, v, writes=()):
        return self.op(eng, (lambda e, a=ap, x=v: e.memset(a, x)), writes=writes)

    def red(self, eng, out, in_, alu, axis=AX.X, reads=(), writes=()):
        return self.op(eng, (lambda e, o=out, i=in_, u=alu, x=axis:
                             e.tensor_reduce(o, i, x, u)),
                       reads=reads, writes=writes)

    def recip(self, out, in_, reads=(), writes=()):
        return self.op("V", (lambda e, o=out, i=in_: e.reciprocal(o, i)),
                       reads=reads, writes=writes)

    def scan(self, eng, out, d0, d1, init, op0, op1, reads=(), writes=()):
        return self.op(eng, (lambda e, o=out, a=d0, b=d1, i=init, p0=op0, p1=op1:
                             e.tensor_tensor_scan(o, a, b, i, p0, p1)),
                       reads=reads, writes=writes)

    def mmop(self, bank, pairs, reads=(), accum=False, out_slice=None, nparts=128,
             close=True):
        ap = bank.h[0:nparts, :] if out_slice is None else bank.h[0:nparts, out_slice]

        def em(e, ap=ap, pairs=tuple(pairs), accum=accum, close=close):
            n = len(pairs)
            ins = None
            for i, (l, r) in enumerate(pairs):
                ins = e.matmul(ap, l, r, start=(i == 0 and not accum),
                               stop=(i == n - 1 and close))
            return ins
        rd = list(reads)
        if accum:
            rd.append(bank)
        return self.op("T", em, reads=rd, writes=[bank])

    def transp(self, bank, src_ap, ident_ap, out_slice, reads=(), nparts=128):
        return self.op("T", (lambda e, b=bank, s=src_ap, i=ident_ap, o=out_slice, np_=nparts:
                             e.transpose(b.h[0:np_, o], s, i)),
                       reads=list(reads), writes=[bank])

    def affsel(self, out, in_, pattern, cmp, fill, cm, base=0, reads=(), writes=()):
        return self.op("G", (lambda e, o=out, i=in_, p=pattern, c=cmp, f=fill, m=cm, b=base:
                             e.affine_select(o, i, p, c, f, base=b, channel_multiplier=m)),
                       reads=reads, writes=writes)

    def iota(self, out, pattern, cm=0, base=0, writes=()):
        return self.op("G", (lambda e, o=out, p=pattern, m=cm, b=base:
                             e.iota(o, p, base=b, channel_multiplier=m)),
                       writes=writes)

    def finalize(self):
        nc = self.nc
        recs = self.recs
        sems = self.sems
        cvals = sorted(self.const_vals.items(), key=lambda kv: kv[1])
        final = [(s, self.cnt[s]) for s in ALL_DMA_SEMS if self.cnt[s] > 0]

        with nc.Block() as block:
            @block.vector
            def _(v):
                ins = None
                for val, i in cvals:
                    ins = v.memset(self.cst[:, i:i + 1], val)
                if ins is None:
                    ins = v.memset(self.cst[:, 0:1], 0.0)
                ins.then_inc(sems["V"], 1)
                for (wl, emit, sem, amt) in recs["V"]:
                    for (s, val) in wl:
                        v.wait_ge(sems[s], val)
                    emit(v).then_inc(sem, amt)

            @block.scalar
            def _(s):
                for (wl, emit, sem, amt) in recs["S"]:
                    for (sn, val) in wl:
                        s.wait_ge(sems[sn], val)
                    emit(s).then_inc(sem, amt)
                for (sn, val) in final:
                    if sn.startswith("dS"):
                        s.wait_ge(sems[sn], val)

            @block.tensor
            def _(t):
                for (wl, emit, sem, amt) in recs["T"]:
                    for (sn, val) in wl:
                        t.wait_ge(sems[sn], val)
                    emit(t).then_inc(sem, amt)

            @block.gpsimd
            def _(g):
                for (wl, emit, sem, amt) in recs["G"]:
                    for (sn, val) in wl:
                        g.wait_ge(sems[sn], val)
                    emit(g).then_inc(sem, amt)
                for (sn, val) in final:
                    if sn.startswith("dG"):
                        g.wait_ge(sems[sn], val)

            @block.sync
            def _(y):
                for (wl, emit, sem, amt) in recs["Y"]:
                    for (sn, val) in wl:
                        y.wait_ge(sems[sn], val)
                    emit(y).then_inc(sem, amt)
                for (sn, val) in final:
                    if sn.startswith("dY"):
                        y.wait_ge(sems[sn], val)



def setup_consts(k):
    k.ident = k.alloc("ident", [128, 128], BF16)
    k.identf = k.alloc("identf", [128, 128], F32)
    k.ones_bf = k.alloc("ones_bf", [128, 128], BF16)
    k.ones_f = k.alloc("ones_f", [128, 128], F32)
    k.ones1 = k.alloc("ones1", [1, 128], F32)
    k.memset("V", k.ident.ap, 1.0, writes=[k.ident])
    k.memset("V", k.identf.ap, 1.0, writes=[k.identf])
    k.memset("V", k.ones_bf.ap, 1.0, writes=[k.ones_bf])
    k.memset("V", k.ones_f.ap, 1.0, writes=[k.ones_f])
    k.memset("V", k.ones1.ap, 1.0, writes=[k.ones1])
    k.affsel(k.ident.ap, k.ident.ap, [[1, 128]], ALU.is_equal, 0.0, -1,
             reads=[k.ident], writes=[k.ident])
    k.affsel(k.identf.ap, k.identf.ap, [[1, 128]], ALU.is_equal, 0.0, -1,
             reads=[k.identf], writes=[k.identf])
    k.e8sel = k.alloc("e8sel", [8, 8 * 128], F32)
    k.memset("V", k.e8sel.ap, 1.0, writes=[k.e8sel])
    k.affsel(k.e8sel.ap, k.e8sel.ap, [[1, 8 * 128]], ALU.is_ge, 0.0, -128,
             reads=[k.e8sel], writes=[k.e8sel])
    k.affsel(k.e8sel.ap, k.e8sel.ap, [[-1, 8 * 128]], ALU.is_ge, 0.0, 128, base=127,
             reads=[k.e8sel], writes=[k.e8sel])


def emit_layernorm(k, xtiles, out_bf, scr, eps=1e-5):
    x2s, mb, rb = scr["x2s"], scr["mb"], scr["rb"]
    t0, t1 = scr["t0"], scr["t1"]
    for tc in range(T // 512):
        sl = slice(tc * 512, (tc + 1) * 512)
        xs = x2s[tc % 2]
        for c in range(NCH):
            k.act(xs.h[:, c * 512:(c + 1) * 512], xtiles[c].h[:, sl], AF.Square,
                  reads=[xtiles[c]], writes=[xs])
        k.mmop(k.pb[0], [(k.ones_f.ap, xtiles[c].h[:, sl]) for c in range(NCH)],
               reads=[k.ones_f] + list(xtiles))
        k.ts("V", mb.h[:, sl], k.pb[0].ap, 1.0 / D, ALU.mult, reads=[k.pb[0]], writes=[mb])
        k.mmop(k.pb[1], [(k.ones_bf.ap, xs.h[:, c * 512:(c + 1) * 512])
                         for c in range(NCH)], reads=[k.ones_bf, xs])
        k.ts("V", rb.h[:, sl], k.pb[1].ap, 1.0 / D, ALU.mult, reads=[k.pb[1]], writes=[rb])
    k.tt("G", t0.ap, mb.ap, mb.ap, ALU.mult, reads=[mb], writes=[t0])
    k.tt("V", rb.ap, rb.ap, t0.ap, ALU.subtract, reads=[rb, t0], writes=[rb])
    k.act(t0.ap, rb.ap, AF.Sqrt, bias=k.const(eps), reads=[rb], writes=[t0])
    k.recip(rb.ap, t0.ap, reads=[t0], writes=[rb])
    for c in range(NCH):
        csl = slice(c * T, (c + 1) * T)
        tm = t0 if c % 2 == 0 else t1
        k.tt("G", tm.ap, xtiles[c].ap, mb.ap, ALU.subtract,
             reads=[xtiles[c], mb], writes=[tm])
        k.tt("V", out_bf.h[:, csl], tm.ap, rb.ap, ALU.mult,
             reads=[tm, rb], writes=[out_bf])
    return out_bf


def emit_ln_stats(k, xtiles, scr, eps=1e-5):
    x2s, mb, rb = scr["x2s"], scr["mb"], scr["rb"]
    t0 = scr["t0"]
    for tc in range(T // 512):
        sl = slice(tc * 512, (tc + 1) * 512)
        xs = x2s[tc % 2]
        for c in range(NCH):
            k.act(xs.h[:, c * 512:(c + 1) * 512], xtiles[c].h[:, sl], AF.Square,
                  reads=[xtiles[c]], writes=[xs])
        k.mmop(k.pb[0], [(k.ones_f.ap, xtiles[c].h[:, sl]) for c in range(NCH)],
               reads=[k.ones_f] + list(xtiles))
        k.ts("V", mb.h[:, sl], k.pb[0].ap, 1.0 / D, ALU.mult, reads=[k.pb[0]], writes=[mb])
        k.mmop(k.pb[1], [(k.ones_bf.ap, xs.h[:, c * 512:(c + 1) * 512])
                         for c in range(NCH)], reads=[k.ones_bf, xs])
        k.ts("V", rb.h[:, sl], k.pb[1].ap, 1.0 / D, ALU.mult, reads=[k.pb[1]], writes=[rb])
    k.tt("G", t0.ap, mb.ap, mb.ap, ALU.mult, reads=[mb], writes=[t0])
    k.tt("V", rb.ap, rb.ap, t0.ap, ALU.subtract, reads=[rb, t0], writes=[rb])
    k.act(t0.ap, rb.ap, AF.Sqrt, bias=k.const(eps), reads=[rb], writes=[t0])
    k.recip(rb.ap, t0.ap, reads=[t0], writes=[rb])


def ln_scratch(k, A):
    x2 = A("lnx2a", [128, NCH * 512], BF16)
    t0 = A("lnt0", [128, T], F32)
    return dict(x2s=[x2, x2],
                mb=A("lnmb", [128, T], F32), rb=A("lnrb", [128, T], F32),
                t0=t0, t1=t0)



def step_moe(k, W):
    m0 = k.mark()
    tiles = []

    def A(name, shape, dtype):
        t = k.alloc(name, shape, dtype)
        tiles.append(t)
        return t

    xn = A("xn", [128, NCH * T], BF16)
    combf = A("combf", [8, T], F32)
    m1_scr = k.mark()
    stiles = []

    def A2(name, shape, dtype):
        t = k.alloc(name, shape, dtype)
        stiles.append(t)
        return t

    scr = ln_scratch(k, A2)
    emit_layernorm(k, k.X, xn, scr)

    A_outer = A
    A = A2
    gw = A("gw", [128, NCH * 8], BF16)
    k.wload("moe_gate_wT", gw.h[:].rearrange("p (c e) -> p c e", e=8),
          W["moe_gate_wT"].rearrange("(c p) e -> p c e", p=128), writes=[gw])
    logit = A("logit", [128, 16 * 8], F32)
    comb = A("comb", [128, 16 * 8], F32)
    for tt_ in range(16):
        k.mmop(k.pb[0], [(xn.h[:, c * T + tt_ * 128: c * T + tt_ * 128 + 128],
                          gw.h[:, c * 8:(c + 1) * 8]) for c in range(NCH)],
               reads=[xn, gw], out_slice=slice(0, 8))
        k.copy("V", logit.h[:, tt_ * 8:(tt_ + 1) * 8], k.pb[0].h[:, 0:8],
               reads=[k.pb[0]], writes=[logit])
    lg3 = logit.h[:].rearrange("p (t e) -> p t e", e=8)
    mx = A("mx", [128, 16], F32)
    sm = A("sm", [128, 16], F32)
    k.red("V", mx.ap, lg3, ALU.max, reads=[logit], writes=[mx])
    bc8 = bass.AP(mx.h, 0, [[16, 128], [1, 16], [0, 8]])
    k.tt("V", lg3, lg3, bc8, ALU.subtract, reads=[logit, mx], writes=[logit])
    k.act(logit.ap, logit.ap, AF.Exp, reads=[logit], writes=[logit])
    k.red("V", sm.ap, lg3, ALU.add, reads=[logit], writes=[sm])
    k.recip(sm.ap, sm.ap, reads=[sm], writes=[sm])
    bcs = bass.AP(sm.h, 0, [[16, 128], [1, 16], [0, 8]])
    k.tt("V", lg3, lg3, bcs, ALU.mult, reads=[logit, sm], writes=[logit])
    g4 = A("g4", [128, 16 * 4], F32)
    g43 = g4.h[:].rearrange("p (t g) -> p t g", g=4)
    k.red("V", g43, logit.h[:].rearrange("p (t g e) -> p t g e", g=4, e=2),
          ALU.max, reads=[logit], writes=[g4])
    m1 = A("m1", [128, 16], F32)
    m2 = A("m2", [128, 16], F32)
    msk = A("msk", [128, 16 * 4], F32)
    tmp4 = A("tmp4", [128, 16 * 4], F32)
    k.red("V", m1.ap, g43, ALU.max, reads=[g4], writes=[m1])
    bc4 = bass.AP(m1.h, 0, [[16, 128], [1, 16], [0, 4]])
    k.tt("V", msk.h[:].rearrange("p (t g) -> p t g", g=4), g43, bc4, ALU.is_lt,
         reads=[g4, m1], writes=[msk])
    k.tt("V", tmp4.ap, g4.ap, msk.ap, ALU.mult, reads=[g4, msk], writes=[tmp4])
    k.red("V", m2.ap, tmp4.h[:].rearrange("p (t g) -> p t g", g=4), ALU.max,
          reads=[tmp4], writes=[m2])
    bcm2 = bass.AP(m2.h, 0, [[16, 128], [1, 16], [0, 4]])
    k.tt("V", msk.h[:].rearrange("p (t g) -> p t g", g=4), g43, bcm2, ALU.is_ge,
         reads=[g4, m2], writes=[msk])
    smask = bass.AP(msk.h, 0, [[64, 128], [4, 16], [1, 4], [0, 2]])
    k.tt("V", comb.h[:].rearrange("p (t g e) -> p t g e", g=4, e=2),
         logit.h[:].rearrange("p (t g e) -> p t g e", g=4, e=2), smask, ALU.mult,
         reads=[logit, msk], writes=[comb])
    c8 = comb.h[:].rearrange("p (t e) -> p t e", e=8)
    e8 = A("e8", [128, 16 * 8], F32)
    k.red("V", m1.ap, c8, ALU.max, reads=[comb], writes=[m1])
    bce = bass.AP(m1.h, 0, [[16, 128], [1, 16], [0, 8]])
    k.tt("V", e8.h[:].rearrange("p (t e) -> p t e", e=8), c8, bce, ALU.is_lt,
         reads=[comb, m1], writes=[e8])
    k.tt("V", e8.ap, e8.ap, comb.ap, ALU.mult, reads=[e8, comb], writes=[e8])
    k.red("V", m2.ap, e8.h[:].rearrange("p (t e) -> p t e", e=8), ALU.max,
          reads=[e8], writes=[m2])
    bcm2e = bass.AP(m2.h, 0, [[16, 128], [1, 16], [0, 8]])
    k.tt("V", e8.h[:].rearrange("p (t e) -> p t e", e=8), c8, bcm2e, ALU.is_ge,
         reads=[comb, m2], writes=[e8])
    k.tt("V", m1.ap, m1.ap, m2.ap, ALU.add, reads=[m1, m2], writes=[m1])
    k.recip(m1.ap, m1.ap, reads=[m1], writes=[m1])
    k.tt("V", comb.ap, comb.ap, e8.ap, ALU.mult, reads=[comb, e8], writes=[comb])
    bcn = bass.AP(m1.h, 0, [[16, 128], [1, 16], [0, 8]])
    k.tt("V", c8, c8, bcn, ALU.mult, reads=[comb, m1], writes=[comb])
    for tt_ in range(16):
        k.transp(k.pb[2], comb.h[:, tt_ * 8:(tt_ + 1) * 8], k.identf.ap,
                 slice(0, 128), reads=[comb, k.identf], nparts=8)
        k.copy("V", combf.h[0:8, tt_ * 128:(tt_ + 1) * 128], k.pb[2].h[0:8, 0:128],
               reads=[k.pb[2]], writes=[combf])
    combf_b = combf

    k.release(m1_scr, stiles)
    A = A_outer

    wg_b = [A(f"wgb{i}", [128, NCH * 128], BF16) for i in range(2)]
    wu_b = [A(f"wub{i}", [128, NCH * 128], BF16) for i in range(2)]
    w2e = A("w2e", [128, 16 * D], BF16)
    actt = A("actt", [128, 8 * T], BF16)
    cbf = A("cbf", [128, 4 * 512], BF16)
    sil2 = [A(f"sil{i}", [128, 512], F32) for i in range(4)]
    for e in range(8):
        k.wload("moe_w2T", w2e.h[:].rearrange("p (kc n) -> p kc n", n=D),
              W["moe_w2T"][e].rearrange("(kc p) n -> p kc n", p=128), writes=[w2e])
        for tc4 in range(4):
            k.mmop(k.pb[2 + tc4 % 2], [(k.e8sel.h[0:8, e * 128:(e + 1) * 128],
                                        combf.h[0:8, tc4 * 512:(tc4 + 1) * 512])],
                   reads=[k.e8sel, combf])
            k.copy("V", cbf.h[:, tc4 * 512:(tc4 + 1) * 512], k.pb[2 + tc4 % 2].ap,
                   reads=[k.pb[2 + tc4 % 2]], writes=[cbf])
        for half in range(2):
            for ib8 in range(8):
                ib = half * 8 + ib8
                wg = wg_b[ib % 2]
                wu = wu_b[ib % 2]
                k.wload("moe_w13T", wg.h[:].rearrange("p (c i) -> p c i", i=128),
                      W["moe_w13T"][e, :, ib * 128:(ib + 1) * 128].rearrange(
                          "(c p) i -> p c i", p=128), writes=[wg])
                k.wload("moe_w13T", wu.h[:].rearrange("p (c i) -> p c i", i=128),
                      W["moe_w13T"][e, :, 2048 + ib * 128: 2048 + (ib + 1) * 128].rearrange(
                          "(c p) i -> p c i", p=128), writes=[wu])
                for tcc in range(4):
                    tsl = slice(tcc * 512, (tcc + 1) * 512)
                    bg = k.pb[(tcc % 2) * 2]
                    bu = k.pb[(tcc % 2) * 2 + 1]
                    sil = sil2[tcc % 4]
                    k.mmop(bg, [(wg.h[:, c * 128:(c + 1) * 128],
                                 xn.h[:, c * T + tcc * 512: c * T + tcc * 512 + 512])
                                for c in range(NCH)], reads=[wg, xn])
                    k.mmop(bu, [(wu.h[:, c * 128:(c + 1) * 128],
                                 xn.h[:, c * T + tcc * 512: c * T + tcc * 512 + 512])
                                for c in range(NCH)], reads=[wu, xn])
                    k.act(sil.ap, bg.ap, AF.Silu, reads=[bg], writes=[sil])
                    k.tt("V", sil.ap, sil.ap, bu.ap, ALU.mult, reads=[sil, bu], writes=[sil])
                    k.tt("G", actt.h[:, ib8 * T + tcc * 512: ib8 * T + tcc * 512 + 512],
                         sil.ap, cbf.h[:, tsl], ALU.mult, reads=[sil, cbf], writes=[actt])
            for tcc in range(4):
                for dp in range(NCH):
                    bank = k.pb[4 + dp % 2]
                    k.mmop(bank, [(w2e.h[:, (half * 8 + kc) * D + dp * 128:
                                          (half * 8 + kc) * D + dp * 128 + 128],
                                   actt.h[:, kc * T + tcc * 512: kc * T + tcc * 512 + 512])
                                  for kc in range(8)], reads=[w2e, actt])
                    xt = k.X[dp]
                    k.tt("V", xt.h[:, tcc * 512:(tcc + 1) * 512],
                         xt.h[:, tcc * 512:(tcc + 1) * 512], bank.ap, ALU.add,
                         reads=[xt, bank], writes=[xt])
    k.release(m0, tiles)



STEP_FNS = {}
STEP_FNS["moe"] = step_moe

STEP_WEIGHTS = {
    "moe": {"moe_gate_wT": (D, 8), "moe_w13T": (8, D, 4096), "moe_w2T": (8, 2048, D)},
}


F32_WEIGHTS = {"l0a_lns", "l0a_cms_b", "mam_conv", "mam_db", "mam_alog",
               "gd_norms", "gd_conv", "fp4_wg", "fp4_wu", "fp4_wd"}


def build_program(steps):
    nc = bass.Bass("TRN2", target_bir_lowering=False)
    xin = nc.dram_tensor("xT", [D, T], F32, kind="ExternalInput").ap()
    yout = nc.dram_tensor("y", [D, T], F32, kind="ExternalOutput").ap()
    W = {}
    seen = set()
    order = []
    for s in steps:
        for nm, shp in STEP_WEIGHTS[s].items():
            if nm not in seen:
                seen.add(nm)
                W[nm] = nc.dram_tensor(nm, list(shp), F32, kind="ExternalInput").ap()
                order.append((nm, shp))
    k = Ker(nc)
    setup_consts(k)
    k.X = [k.alloc(f"x{c}", [128, T], F32) for c in range(NCH)]
    for c in range(NCH):
        k.dma("Y", k.X[c].ap, xin[c * 128:(c + 1) * 128, :], writes=[k.X[c]])
    W2 = dict(W)
    k.WB = {}
    for nm, shp in order:
        if nm in F32_WEIGHTS:
            continue
        flat = int(np.prod(shp))
        assert flat % 128 == 0
        cols = flat // 128
        tw = nc.dram_tensor(nm + "_bf", list(shp), BF16, kind="Internal").ap()
        twb = Buf(nm + "_bf")
        W2[nm] = tw
        k.WB[nm] = twb
        sflat = bass.AP(W[nm].tensor, 0, [[cols, 128], [1, cols]])
        dflat = bass.AP(tw.tensor, 0, [[cols, 128], [1, cols]])
        k.dma("G", dflat, sflat, writes=[twb])
    for s in steps:
        STEP_FNS[s](k, W2)
    for c in range(NCH):
        k.dma("Y", yout[c * 128:(c + 1) * 128, :], k.X[c].ap, reads=[k.X[c]])
    k.finalize()
    return nc


def host_weights(inputs, steps):
    out = {}
    for s in steps:
        if s == "l1_attn_b":
            ii = inputs
            nw = np.stack([np.asarray(ii["L1_attn_b_norm1_w"]), np.asarray(ii["L1_attn_b_norm2_w"])])
            out["gd_norms"] = np.ascontiguousarray(nw.reshape(2, 8, 128).transpose(2, 0, 1).reshape(128, 16))
            cw = np.asarray(ii["L1_attn_b_conv_w"])
            cb = np.asarray(ii["L1_attn_b_conv_b"])[:, None]
            cc_ = np.concatenate([cw, cb], axis=1)
            out["gd_conv"] = np.ascontiguousarray(cc_.reshape(8, 128, 5).transpose(2, 1, 0).reshape(5 * 128, 8).T) if False else np.ascontiguousarray(cc_.reshape(8, 128, 5).transpose(1, 2, 0).reshape(128, 40))
            for nm, inm in (("gd_wq", "L1_attn_b_wq"), ("gd_wk", "L1_attn_b_wk"),
                            ("gd_wv", "L1_attn_b_wv"), ("gd_wb", "L1_attn_b_wb"),
                            ("gd_wa", "L1_attn_b_wa"), ("gd_wo", "L1_attn_b_wo"),
                            ("gd_wg", "L1_attn_b_wg"), ("gd_wu", "L1_attn_b_wu"),
                            ("gd_wd", "L1_attn_b_wd")):
                out[nm] = np.ascontiguousarray(np.asarray(ii[inm]))
        if s == "l0_attn_b":
            ii = inputs
            out["mam_win"] = np.ascontiguousarray(np.asarray(ii["L0_attn_b_W_in"]))
            cw = np.asarray(ii["L0_attn_b_conv_w"])[:, 0, :]
            out["mam_conv"] = np.ascontiguousarray(cw.reshape(8, 128, 3).transpose(1, 0, 2).reshape(128, 24))
            out["mam_wbc"] = np.ascontiguousarray(np.concatenate(
                [np.asarray(ii["L0_attn_b_W_B"]), np.asarray(ii["L0_attn_b_W_C"])], axis=1))
            out["mam_wd"] = np.ascontiguousarray(np.asarray(ii["L0_attn_b_W_delta"]))
            out["mam_db"] = np.ascontiguousarray(
                np.asarray(ii["L0_attn_b_delta_base"]).reshape(8, 128).T)
            out["mam_wt"] = np.ascontiguousarray(np.asarray(ii["L0_attn_b_W_tau"]))
            out["mam_alog"] = np.ascontiguousarray(
                np.asarray(ii["L0_attn_b_A_log"]).reshape(8, 128, 16).transpose(1, 0, 2).reshape(128, 128))
            out["mam_wout"] = np.ascontiguousarray(np.asarray(ii["L0_attn_b_W_out"]))
        if s == "l0_attn_a":
            ii = inputs
            lns = np.stack([np.asarray(ii["L0_attn_a_ln1_g"]), np.asarray(ii["L0_attn_a_ln1_b"]),
                            np.asarray(ii["L0_attn_a_ln2_g"]), np.asarray(ii["L0_attn_a_ln2_b"])])
            out["l0a_lns"] = np.ascontiguousarray(
                lns.reshape(4, 8, 128).transpose(2, 0, 1).reshape(128, 32))
            out["l0a_wq"] = np.ascontiguousarray(np.asarray(ii["L0_attn_a_wq"]))
            out["l0a_wk"] = np.ascontiguousarray(np.asarray(ii["L0_attn_a_wk"]))
            out["l0a_wv"] = np.ascontiguousarray(np.asarray(ii["L0_attn_a_wv"]))
            out["l0a_wo"] = np.ascontiguousarray(np.asarray(ii["L0_attn_a_wo"]))
            out["l0a_cms_w1"] = np.ascontiguousarray(np.asarray(ii["L0_attn_a_cms_w1"]))
            out["l0a_cms_w2"] = np.ascontiguousarray(np.asarray(ii["L0_attn_a_cms_w2"]))
            b1 = np.asarray(ii["L0_attn_a_cms_b1"]).reshape(3, 32, 128).transpose(0, 2, 1)
            b2 = np.asarray(ii["L0_attn_a_cms_b2"]).reshape(3, 8, 128).transpose(0, 2, 1)
            out["l0a_cms_b"] = np.ascontiguousarray(np.concatenate([b1, b2], axis=2))
        if s == "l1_attn_a":
            out["att_wq"] = np.ascontiguousarray(np.asarray(inputs["L1_attn_a_wq"]).reshape(1024, 1024))
            out["att_wk"] = np.ascontiguousarray(np.asarray(inputs["L1_attn_a_wk"]).reshape(1024, 1024))
            out["att_wv"] = np.ascontiguousarray(np.asarray(inputs["L1_attn_a_wv"]).reshape(1024, 1024))
            out["att_wo"] = np.ascontiguousarray(np.asarray(inputs["L1_attn_a_wo"]).reshape(1024, 1024))
        if s == "l1_ffn":
            out["fp4_wg"] = np.ascontiguousarray(np.asarray(inputs["L1_ffn_w_gate"]))
            out["fp4_wu"] = np.ascontiguousarray(np.asarray(inputs["L1_ffn_w_up"]))
            out["fp4_wd"] = np.ascontiguousarray(np.asarray(inputs["L1_ffn_w_down"]))
        if s == "moe":
            out["moe_gate_wT"] = np.ascontiguousarray(np.asarray(inputs["L0_ffn_gate_w"]).T)
            out["moe_w13T"] = np.ascontiguousarray(np.asarray(inputs["L0_ffn_w13_weight"]).transpose(0, 2, 1))
            out["moe_w2T"] = np.ascontiguousarray(np.asarray(inputs["L0_ffn_w2_weight"]).transpose(0, 2, 1))
    return out


MODULE_STEPS = ["l0_attn_a", "moe", "l0_attn_b", "moe",
                "l1_attn_a", "l1_ffn", "l1_attn_b", "l1_ffn"]
FUSE = True


def kernel(**inputs):
    x = np.asarray(inputs["L0_attn_a_x"], np.float32)
    B = x.shape[0]
    if FUSE:
        groups = [MODULE_STEPS]
    else:
        groups = [[s] for s in MODULE_STEPS]
    xTs = [np.ascontiguousarray(x[b].T) for b in range(B)]
    for steps in groups:
        nc = build_program(steps)
        wmap = host_weights(inputs, steps)
        in_maps = [dict(xT=xTs[b], **wmap) for b in range(B)]
        res = run_bass_kernel_spmd(nc, in_maps, core_ids=list(range(B)))
        xTs = [np.asarray(res.results[b]["y"], np.float32) for b in range(B)]
    return np.ascontiguousarray(np.stack([xt.T for xt in xTs], 0))



QMAX = 1408


def emit_fp4_quant(k, out_ap_fn, in_ap_fn, scrs, ncols, in_tiles, out_tiles):
    for lo in range(0, ncols, QMAX):
        n = min(QMAX, ncols - lo)
        nblk = n // 16
        k.qpar = getattr(k, "qpar", 0) ^ 1
        scr = scrs[k.qpar]
        absx, acc, tmp, mx = scr["absx"], scr["acc"], scr["tmp"], scr["mx"]
        in_ap = in_ap_fn(lo, n)
        out_ap = out_ap_fn(lo, n)
        aab = absx.h[:, 0:n]
        acb = acc.h[:, 0:n]
        tmb = tmp.h[:, 0:n]
        mxb = mx.h[:, 0:nblk]
        k.act(aab, in_ap, AF.Abs, reads=in_tiles, writes=[absx])
        k.red("V", mxb, absx.h[:, 0:n].rearrange("p (b s) -> p b s", s=16),
              ALU.max, reads=[absx], writes=[mx])
        k.ts("V", mxb, mxb, k.const(1.0 / 6.0), ALU.mult, s2=k.const(1e-12),
             op1=ALU.max, reads=[mx], writes=[mx])
        k.recip(mxb, mxb, reads=[mx], writes=[mx])
        rbc = bass.AP(mx.h, 0, [[mx.h.shape[1], 128], [1, nblk], [0, 16]])
        k.tt("G", aab, aab, rbc, ALU.mult, reads=[absx, mx], writes=[absx])
        k.ts("V", acb, aab, 0.25, ALU.is_gt, reads=[absx], writes=[acc])
        for thr in (0.75, 1.25, 1.75):
            k.ts("V", tmb, aab, k.const(thr), ALU.is_gt, reads=[absx], writes=[tmp])
            k.tt("G", acb, acb, tmb, ALU.add, reads=[acc, tmp], writes=[acc])
        k.ts("V", tmb, aab, k.const(2.5), ALU.is_gt, reads=[absx], writes=[tmp])
        k.stt("V", acb, acb, 0.5, tmb, ALU.mult, ALU.add, reads=[acc, tmp], writes=[acc])
        k.ts("V", tmb, aab, k.const(3.5), ALU.is_gt, reads=[absx], writes=[tmp])
        k.tt("G", acb, acb, tmb, ALU.add, reads=[acc, tmp], writes=[acc])
        k.ts("V", tmb, aab, k.const(5.0), ALU.is_gt, reads=[absx], writes=[tmp])
        k.stt("V", acb, tmb, 2.0, acb, ALU.mult, ALU.add, reads=[acc, tmp], writes=[acc])
        k.recip(mxb, mxb, reads=[mx], writes=[mx])
        k.tt("V", acb, acb, rbc, ALU.mult, reads=[acc, mx], writes=[acc])
        k.op("S", (lambda e, o=tmb, i=in_ap: e.sign(o, i)), reads=in_tiles, writes=[tmp])
        k.tt("G", out_ap, acb, tmb, ALU.mult, reads=[acc, tmp], writes=out_tiles)


def quant_weight_to_scratch(k, W, qscr, nat_name, scratch_ap, scratch_buf,
                            nrows, ncols, A):
    nrc = nrows // 128
    ncc = ncols // 128
    fch = [A(f"fq{i}", [128, ncols], F32) for i in range(2)]
    qch = A("qn", [128, ncols], BF16)
    tev = A("te", [128, 8 * 128], BF16)
    for rc in range(nrc):
        f = fch[rc % 2]
        k.dma("Y", f.ap, W[nat_name][rc * 128:(rc + 1) * 128, :], writes=[f])
        emit_fp4_quant(k, lambda lo, n, q=qch: q.h[:, lo:lo + n],
                       lambda lo, n, f=f: f.h[:, lo:lo + n],
                       qscr, ncols, [f], [qch])
        for g0 in range(0, ncc, 8):
            gn = min(8, ncc - g0)
            for j in range(gn):
                k.transp(k.pbh[0], qch.h[:, (g0 + j) * 128:(g0 + j + 1) * 128],
                         k.ident.ap, slice(j * 128, (j + 1) * 128), reads=[qch, k.ident])
            k.copy("V", tev.h[:, 0:gn * 128], k.pbh[0].h[:, 0:gn * 128],
                   reads=[k.pbh[0]], writes=[tev])
            k.dma("S", scratch_ap[g0 * 128:(g0 + gn) * 128,
                                  rc * 128:(rc + 1) * 128].rearrange(
                "(cc p) r -> p cc r", p=128),
                tev.h[:, 0:gn * 128].rearrange("p (cc r) -> p cc r", r=128),
                reads=[tev], writes=[scratch_buf])


def step_l1_ffn(k, W):
    IN = 2816
    ICC = IN // 128
    m0 = k.mark()
    tiles = []

    def A(name, shape, dtype):
        t = k.alloc(name, shape, dtype)
        tiles.append(t)
        return t

    xq = A("xq", [128, NCH * T], BF16)
    qscr = [dict(absx=A(f"qabs{i}", [128, QMAX], F32),
                 acc=A(f"qacc{i}", [128, QMAX], BF16),
                 tmp=A(f"qtmp{i}", [128, QMAX], BF16),
                 mx=A(f"qmx{i}", [128, QMAX // 16], F32)) for i in range(2)]

    m1 = k.mark()
    s0 = []

    def A0(name, shape, dtype):
        t = k.alloc(name, shape, dtype)
        s0.append(t)
        return t
    xn = A0("xn", [128, NCH * T], BF16)
    scr = ln_scratch(k, A0)
    emit_layernorm(k, k.X, xn, scr)
    xtm = A0("xtm", [128, D], BF16)
    qtm = A0("qtm", [128, D], BF16)
    for it in range(16):
        for j in range(NCH):
            k.transp(k.pbh[0], xn.h[:, j * T + it * 128: j * T + it * 128 + 128],
                     k.ident.ap, slice(j * 128, (j + 1) * 128), reads=[xn, k.ident])
        k.copy("V", xtm.ap, k.pbh[0].ap, reads=[k.pbh[0]], writes=[xtm])
        emit_fp4_quant(k, lambda lo, n, q=qtm: q.h[:, lo:lo + n],
                       lambda lo, n, x=xtm: x.h[:, lo:lo + n],
                       qscr, D, [xtm], [qtm])
        for j in range(NCH):
            k.transp(k.pbh[1], qtm.h[:, j * 128:(j + 1) * 128],
                     k.ident.ap, slice(j * 128, (j + 1) * 128), reads=[qtm, k.ident])
        k.op("V", (lambda e, o=xq, it=it: e.tensor_copy(
            bass.AP(o.h, it * 128, [[NCH * T, 128], [T, NCH], [1, 128]]),
            k.pbh[1].h[:].rearrange("p (c r) -> p c r", r=128))),
            reads=[k.pbh[1]], writes=[xq])
    k.release(m1, s0)

    if not hasattr(k, "fp4_done"):
        k.fp4_done = True
        nc = k.nc
        k.qwg_s = nc.dram_tensor("qwg_s", [D, IN], BF16, kind="Internal").ap()
        k.qwu_s = nc.dram_tensor("qwu_s", [D, IN], BF16, kind="Internal").ap()
        k.qwd_s = nc.dram_tensor("qwd_s", [IN, D], BF16, kind="Internal").ap()
        k.qwg_b, k.qwu_b, k.qwd_b = Buf("qwg_s"), Buf("qwu_s"), Buf("qwd_s")
        m2 = k.mark()
        wt = []

        def A1(name, shape, dtype):
            t = k.alloc(name, shape, dtype)
            wt.append(t)
            return t
        quant_weight_to_scratch(k, W, qscr, "fp4_wg", k.qwg_s, k.qwg_b, IN, D, A1)
        k.release(m2, wt)
        wt = []
        quant_weight_to_scratch(k, W, qscr, "fp4_wu", k.qwu_s, k.qwu_b, IN, D, A1)
        k.release(m2, wt)
        wt = []
        quant_weight_to_scratch(k, W, qscr, "fp4_wd", k.qwd_s, k.qwd_b, D, IN, A1)
        k.release(m2, wt)

    wb = [A(f"fw{i}", [128, NCH * 128], BF16) for i in range(4)]
    htm = [A(f"htm{i}", [128, IN], BF16) for i in range(4)]
    qh = A("qh", [128, ICC * 512], BF16)
    hst2 = [A(f"hst{i}", [128, 512], BF16) for i in range(2)]
    hfm2 = [A(f"hfm{i}", [128, 512], F32) for i in range(2)]
    dwb = [A(f"dw{i}", [128, ICC * 128], BF16) for i in range(2)]
    for tc in range(4):
        tb = slice(tc * 512, (tc + 1) * 512)
        for ic in range(ICC):
            wg = wb[(ic % 2) * 2]
            wu = wb[(ic % 2) * 2 + 1]
            k.dma("Y", wg.h[:].rearrange("p (c i) -> p c i", i=128),
                  k.qwg_s[:, ic * 128:(ic + 1) * 128].rearrange("(c p) i -> p c i", p=128),
                  reads=[k.qwg_b], writes=[wg])
            k.dma("Y", wu.h[:].rearrange("p (c i) -> p c i", i=128),
                  k.qwu_s[:, ic * 128:(ic + 1) * 128].rearrange("(c p) i -> p c i", p=128),
                  reads=[k.qwu_b], writes=[wu])
            bg = k.pb[(ic % 2) * 2]
            bu = k.pb[(ic % 2) * 2 + 1]
            hfm = hfm2[ic % 2]
            hst = hst2[ic % 2]
            k.mmop(bg, [(wg.h[:, c * 128:(c + 1) * 128],
                         xq.h[:, c * T + tc * 512: c * T + tc * 512 + 512])
                        for c in range(NCH)], reads=[wg, xq])
            k.mmop(bu, [(wu.h[:, c * 128:(c + 1) * 128],
                         xq.h[:, c * T + tc * 512: c * T + tc * 512 + 512])
                        for c in range(NCH)], reads=[wu, xq])
            k.act(hfm.ap, bg.ap, AF.Silu, reads=[bg], writes=[hfm])
            k.tt("V", hfm.ap, hfm.ap, bu.ap, ALU.mult, reads=[hfm, bu], writes=[hfm])
            k.copy("G", hst.ap, hfm.ap, reads=[hfm], writes=[hst])
            for j in range(4):
                k.transp(k.pbh[ic % 2], hst.h[:, j * 128:(j + 1) * 128], k.ident.ap,
                         slice(j * 128, (j + 1) * 128), reads=[hst, k.ident])
            for j in range(4):
                k.copy("V", htm[j].h[:, ic * 128:(ic + 1) * 128],
                       k.pbh[ic % 2].h[:, j * 128:(j + 1) * 128],
                       reads=[k.pbh[ic % 2]], writes=[htm[j]])
        for j in range(4):
            emit_fp4_quant(k, lambda lo, n, h=htm[j]: h.h[:, lo:lo + n],
                           lambda lo, n, h=htm[j]: h.h[:, lo:lo + n],
                           qscr, IN, [htm[j]], [htm[j]])
            for g0 in range(0, ICC, 8):
                gn = min(8, ICC - g0)
                for u in range(gn):
                    k.transp(k.pbh[1], htm[j].h[:, (g0 + u) * 128:(g0 + u + 1) * 128],
                             k.ident.ap, slice(u * 128, (u + 1) * 128),
                             reads=[htm[j], k.ident])
                k.op("V", (lambda e, o=qh, g0=g0, gn=gn, j=j: e.tensor_copy(
                    bass.AP(o.h, g0 * 512 + j * 128,
                            [[ICC * 512, 128], [512, gn], [1, 128]]),
                    k.pbh[1].h[:, 0:gn * 128].rearrange("p (c r) -> p c r", r=128))),
                    reads=[k.pbh[1]], writes=[qh])
        for dp in range(NCH):
            dw = dwb[dp % 2]
            k.dma("Y", dw.h[:].rearrange("p (c i) -> p c i", i=128),
                  k.qwd_s[:, dp * 128:(dp + 1) * 128].rearrange("(c p) i -> p c i", p=128),
                  reads=[k.qwd_b], writes=[dw])
            k.mmop(k.pb[4 + dp % 2], [(dw.h[:, c * 128:(c + 1) * 128],
                                       qh.h[:, c * 512:(c + 1) * 512])
                                      for c in range(ICC)], reads=[dw, qh])
            xt = k.X[dp]
            k.tt("V", xt.h[:, tb], xt.h[:, tb], k.pb[4 + dp % 2].ap, ALU.add,
                 reads=[xt, k.pb[4 + dp % 2]], writes=[xt])
    k.release(m0, tiles)


STEP_FNS["l1_ffn"] = step_l1_ffn
STEP_WEIGHTS["l1_ffn"] = {"fp4_wg": (2816, D), "fp4_wu": (2816, D), "fp4_wd": (D, 2816)}



def step_l1_attn_a(k, W):
    H, HD = 8, 128
    m0 = k.mark()
    tiles = []

    def A(name, shape, dtype):
        t = k.alloc(name, shape, dtype)
        tiles.append(t)
        return t

    cosm = A("cosm", [64, T], BF16)
    sinm = A("sinm", [64, T], BF16)
    y = A("attny", [128, NCH * T], BF16)
    triu = A("triu", [128, 128], BF16)
    k.memset("V", triu.ap, 1.0, writes=[triu])
    k.affsel(triu.ap, triu.ap, [[1, 128]], ALU.is_ge, 0.0, -1, reads=[triu], writes=[triu])

    m1 = k.mark()
    s0 = []

    def A0(name, shape, dtype):
        t = k.alloc(name, shape, dtype)
        s0.append(t)
        return t
    xn = A0("xn", [128, NCH * T], BF16)
    mln = k.mark()
    sln = []

    def A1(name, shape, dtype):
        t = k.alloc(name, shape, dtype)
        sln.append(t)
        return t
    scr = ln_scratch(k, A1)
    emit_layernorm(k, k.X, xn, scr)
    k.release(mln, sln)

    sln = []
    posi = A1("posi", [64, T], I32)
    jci = A1("jci", [64, 1], I32)
    jcf = A1("jcf", [64, 1], F32)
    posf = A1("posf", [64, T], F32)
    rad = A1("rad", [64, T], F32)
    jcf2 = A1("jcf2", [64, T], F32)
    k.iota(posi.ap, [[1, T]], cm=0, writes=[posi])
    k.iota(jci.ap, [[0, 1]], cm=1, writes=[jci])
    k.copy("V", jcf.ap, jci.ap, reads=[jci], writes=[jcf])
    k.copy("V", posf.ap, posi.ap, reads=[posi], writes=[posf])
    LB = float(np.log(10000.0) / 64.0)
    k.act(jcf.ap, jcf.ap, AF.Exp, scale=-LB, reads=[jcf], writes=[jcf])
    k.ts("V", rad.ap, posf.ap, jcf[0:64, 0:1], ALU.mult, reads=[posf, jcf], writes=[rad])
    PI = float(np.pi)
    MULT = float((1.0 / 128.0) ** 0.5)
    k.ts("V", rad.ap, rad.ap, float(1.0 / (2 * PI)), ALU.mult, reads=[rad], writes=[rad])
    k.copy("V", posi.ap, rad.ap, reads=[rad], writes=[posi])
    k.copy("V", posf.ap, posi.ap, reads=[posi], writes=[posf])
    k.tt("V", posf.ap, rad.ap, posf.ap, ALU.subtract, reads=[rad, posf], writes=[posf])
    k.ts("V", jcf2.ap, posf.ap, 0.5, ALU.is_gt, reads=[posf], writes=[jcf2])
    k.tt("V", posf.ap, posf.ap, jcf2.ap, ALU.subtract, reads=[posf, jcf2], writes=[posf])
    k.act(sinm.ap, posf.ap, AF.Sin, scale=float(2 * PI), reads=[posf], writes=[sinm])
    k.ts("V", rad.ap, rad.ap, 0.25, ALU.add, reads=[rad], writes=[rad])
    k.copy("V", posi.ap, rad.ap, reads=[rad], writes=[posi])
    k.copy("V", posf.ap, posi.ap, reads=[posi], writes=[posf])
    k.tt("V", posf.ap, rad.ap, posf.ap, ALU.subtract, reads=[rad, posf], writes=[posf])
    k.ts("V", jcf2.ap, posf.ap, 0.5, ALU.is_gt, reads=[posf], writes=[jcf2])
    k.tt("V", posf.ap, posf.ap, jcf2.ap, ALU.subtract, reads=[posf, jcf2], writes=[posf])
    k.act(cosm.ap, posf.ap, AF.Sin, scale=float(2 * PI), reads=[posf], writes=[cosm])
    k.ts("V", sinm.ap, sinm.ap, MULT, ALU.mult, reads=[sinm], writes=[sinm])
    k.ts("V", cosm.ap, cosm.ap, MULT, ALU.mult, reads=[cosm], writes=[cosm])
    k.release(mln, sln)

    qrop = [A0(f"qro{i}", [128, T], BF16) for i in range(2)]
    krop = [A0(f"kro{i}", [128, T], BF16) for i in range(2)]
    vtmp = [A0(f"vtm{i}", [128, 16 * 129], BF16) for i in range(2)]
    for i in range(2):
        vv = vtmp[i]
        k.memset("V", vv.h[:].rearrange("p (c j) -> p c j", j=129)[:, :, 128:129],
                 1.0, writes=[vv])
    wqp = [A0(f"w_q{i}", [128, NCH * 128], BF16) for i in range(2)]
    wkp = [A0(f"w_k{i}", [128, NCH * 128], BF16) for i in range(2)]
    wvp = [A0(f"w_v{i}", [128, NCH * 128], BF16) for i in range(2)]
    t1p = [A0(f"t1_{i}", [64, 1024], BF16) for i in range(2)]
    rsbp = [A0(f"rsb{i}", [128, 512], F32) for i in range(2)]
    q2bp = [A0(f"q2b{i}", [128, 512], BF16) for i in range(2)]
    eap = [A0(f"ea{i}", [64, 512], BF16) for i in range(2)]
    obp = [A0(f"ob{i}", [64, 512], BF16) for i in range(2)]
    ebp = [A0(f"eb{i}", [64, 512], BF16) for i in range(2)]
    oap = [A0(f"oa{i}", [64, 512], BF16) for i in range(2)]
    vsgp = [A0(f"vsg{i}", [128, 512], BF16) for i in range(2)]
    ptbp = [A0(f"ptb{i}", [128, 512], BF16) for i in range(2)]
    oqp = [A0(f"oq{i}", [128, 128], BF16) for i in range(2)]
    rcp = [A0(f"rc{i}", [128, 1], F32) for i in range(2)]

    for h in range(H):
        hp = h & 1
        hsl = slice(h * 128, (h + 1) * 128)
        qro, kro, vtm = qrop[hp], krop[hp], vtmp[hp]
        for (wnm, wpair, dst) in (("att_wq", wqp, qro), ("att_wk", wkp, kro)):
            wv_ = W[wnm]
            wtile = wpair[hp]
            k.wload(wnm, wtile.h[:].rearrange("p (c i) -> p c i", i=128),
                    wv_[:, hsl].rearrange("(c p) i -> p c i", p=128), writes=[wtile])
            for tc in range(4):
                tp = tc & 1
                tsl = slice(tc * 512, (tc + 1) * 512)
                t1, rsb, q2b = t1p[tp], rsbp[tp], q2bp[tp]
                ea, ob, eb, oa = eap[tp], obp[tp], ebp[tp], oap[tp]
                k.mmop(k.pb[0], [(wtile.h[:, c * 128:(c + 1) * 128],
                                  xn.h[:, c * T + tc * 512: c * T + tc * 512 + 512])
                                 for c in range(NCH)], reads=[wtile, xn])
                k.act(q2b.ap, k.pb[0].ap, AF.Square, reads=[k.pb[0]], writes=[q2b])
                k.mmop(k.pb[1], [(k.ones_bf.ap, q2b.ap)], reads=[k.ones_bf, q2b])
                k.act(rsb.ap, k.pb[1].ap, AF.Ln, scale=float(1.0 / 128.0),
                      bias=1e-6, reads=[k.pb[1]], writes=[rsb])
                k.act(rsb.ap, rsb.ap, AF.Exp, scale=-0.5, reads=[rsb], writes=[rsb])
                k.tt("V", t1.h[0:64, 0:512], k.pb[0].h[0:64, 0:512],
                     rsb.h[0:64, 0:512], ALU.mult, reads=[k.pb[0], rsb], writes=[t1])
                k.tt("V", t1.h[0:64, 512:1024], k.pb[0].h[64:128, 0:512],
                     rsb.h[64:128, 0:512], ALU.mult, reads=[k.pb[0], rsb], writes=[t1])
                co = cosm.h[0:64, tsl]
                si = sinm.h[0:64, tsl]
                t1e = t1.h[0:64, 0:512]
                t1o = t1.h[0:64, 512:1024]
                k.tt("V", ea.ap, t1e, co, ALU.mult, reads=[t1, cosm], writes=[ea])
                k.tt("G", ob.ap, t1o, si, ALU.mult, reads=[t1, sinm], writes=[ob])
                k.tt("G", eb.ap, t1e, si, ALU.mult, reads=[t1, sinm], writes=[eb])
                k.tt("V", oa.ap, t1o, co, ALU.mult, reads=[t1, cosm], writes=[oa])
                k.tt("V", dst.h[0:64, tsl], ea.ap, ob.ap, ALU.subtract,
                     reads=[ea, ob], writes=[dst])
                k.tt("G", dst.h[64:128, tsl], eb.ap, oa.ap, ALU.add,
                     reads=[eb, oa], writes=[dst])
        wvt = wvp[hp]
        k.wload("att_wv", wvt.h[:].rearrange("p (c i) -> p c i", i=128),
                W["att_wv"][:, hsl].rearrange("(c p) i -> p c i", p=128), writes=[wvt])
        for tc in range(4):
            tp = tc & 1
            vsg = vsgp[tp]
            k.mmop(k.pb[0], [(wvt.h[:, c * 128:(c + 1) * 128],
                              xn.h[:, c * T + tc * 512: c * T + tc * 512 + 512])
                             for c in range(NCH)], reads=[wvt, xn])
            k.copy("V", vsg.ap, k.pb[0].ap, reads=[k.pb[0]], writes=[vsg])
            for j in range(4):
                k.transp(k.pbh[0], vsg.h[:, j * 128:(j + 1) * 128], k.ident.ap,
                         slice(j * 128, (j + 1) * 128), reads=[vsg, k.ident])
            for j in range(4):
                ch = tc * 4 + j
                k.copy("V", vtm.h[:, ch * 129: ch * 129 + 128],
                       k.pbh[0].h[:, j * 128:(j + 1) * 128],
                       reads=[k.pbh[0]], writes=[vtm])
        for qt in range(16):
            qp = qt & 1
            qsl = slice(qt * 128, (qt + 1) * 128)
            opb = k.pb[3 + 2 * qp]
            nkt = qt + 1
            first = True
            for g0 in range(0, nkt, 4):
                gp = (g0 // 4) & 1
                spb = k.pb[2 + 2 * gp]
                ptb = ptbp[gp]
                gn = min(4, nkt - g0)
                for j in range(gn):
                    k.mmop(spb, [(kro.h[:, (g0 + j) * 128:(g0 + j + 1) * 128],
                                  qro.h[:, qsl])], reads=[kro, qro],
                           out_slice=slice(j * 128, (j + 1) * 128))
                k.act(ptb.h[:, 0:gn * 128], spb.h[:, 0:gn * 128], AF.Exp,
                      reads=[spb], writes=[ptb])
                if g0 + gn == nkt:
                    k.tt("V", ptb.h[:, (gn - 1) * 128:gn * 128],
                         ptb.h[:, (gn - 1) * 128:gn * 128], triu.ap, ALU.mult,
                         reads=[ptb, triu], writes=[ptb])
                for j in range(gn):
                    last = (g0 + j == nkt - 1)
                    k.mmop(opb, [(ptb.h[:, j * 128:(j + 1) * 128],
                                  vtm.h[:, (g0 + j) * 129: (g0 + j) * 129 + 129])],
                           reads=[ptb, vtm], accum=not first,
                           out_slice=slice(0, 129), close=last)
                    first = False
            rc, oq = rcp[qp], oqp[qp]
            k.recip(rc.ap, opb.h[:, 128:129], reads=[opb], writes=[rc])
            k.ts("V", oq.ap, opb.h[:, 0:128], rc.h[:, 0:1], ALU.mult,
                 reads=[opb, rc], writes=[oq])
            k.transp(k.pbh[1], oq.ap, k.ident.ap, slice(qp * 128, qp * 128 + 128),
                     reads=[oq, k.ident])
            k.copy("V", y.h[:, h * T + qt * 128: h * T + qt * 128 + 128],
                   k.pbh[1].h[:, qp * 128: qp * 128 + 128],
                   reads=[k.pbh[1]], writes=[y])
    k.release(m1, s0)

    wo = A("w_att_wo", [128, NCH * 128], BF16)
    for dp in range(NCH):
        k.wload("att_wo", wo.h[:].rearrange("p (c i) -> p c i", i=128),
              W["att_wo"][:, dp * 128:(dp + 1) * 128].rearrange("(c p) i -> p c i", p=128),
              writes=[wo])
        for tc in range(4):
            tsl = slice(tc * 512, (tc + 1) * 512)
            k.mmop(k.pb[2 + tc % 2], [(wo.h[:, c * 128:(c + 1) * 128],
                                       y.h[:, c * T + tc * 512: c * T + tc * 512 + 512])
                                      for c in range(NCH)], reads=[wo, y])
            xt = k.X[dp]
            k.tt("V", xt.h[:, tsl], xt.h[:, tsl], k.pb[2 + tc % 2].ap, ALU.add,
                 reads=[xt, k.pb[2 + tc % 2]], writes=[xt])
    k.release(m0, tiles)


STEP_FNS["l1_attn_a"] = step_l1_attn_a
STEP_WEIGHTS["l1_attn_a"] = {"att_wq": (D, D), "att_wk": (D, D),
                             "att_wv": (D, D), "att_wo": (D, D)}



def emit_layernorm_bf(k, src, out_bf, scr, gb, goff, bb, boff, eps=1e-5):
    x2s, mb, rb = scr["x2s"], scr["mb"], scr["rb"]
    t0, t1 = scr["t0"], scr["t1"]
    for tc in range(T // 512):
        sl = slice(tc * 512, (tc + 1) * 512)
        xs = x2s[tc % 2]
        for c in range(NCH):
            k.act(xs.h[:, c * 512:(c + 1) * 512], src.h[:, c * T + tc * 512: c * T + tc * 512 + 512],
                  AF.Square, reads=[src], writes=[xs])
        k.mmop(k.pb[0], [(k.ones_bf.ap, src.h[:, c * T + tc * 512: c * T + tc * 512 + 512])
                         for c in range(NCH)], reads=[k.ones_bf, src])
        k.ts("V", mb.h[:, sl], k.pb[0].ap, 1.0 / D, ALU.mult, reads=[k.pb[0]], writes=[mb])
        k.mmop(k.pb[1], [(k.ones_bf.ap, xs.h[:, c * 512:(c + 1) * 512])
                         for c in range(NCH)], reads=[k.ones_bf, xs])
        k.ts("V", rb.h[:, sl], k.pb[1].ap, 1.0 / D, ALU.mult, reads=[k.pb[1]], writes=[rb])
    k.tt("G", t0.ap, mb.ap, mb.ap, ALU.mult, reads=[mb], writes=[t0])
    k.tt("V", rb.ap, rb.ap, t0.ap, ALU.subtract, reads=[rb, t0], writes=[rb])
    k.act(t0.ap, rb.ap, AF.Sqrt, bias=k.const(eps), reads=[rb], writes=[t0])
    k.recip(rb.ap, t0.ap, reads=[t0], writes=[rb])
    for c in range(NCH):
        csl = slice(c * T, (c + 1) * T)
        k.tt("G", t0.ap, src.h[:, csl], mb.ap, ALU.subtract, reads=[src, mb], writes=[t0])
        k.tt("V", t0.ap, t0.ap, rb.ap, ALU.mult, reads=[t0, rb], writes=[t0])
        k.ts("V", out_bf.h[:, csl], t0.ap, gb.h[:, goff + c:goff + c + 1], ALU.mult,
             s2=bb.h[:, boff + c:boff + c + 1], op1=ALU.add,
             reads=[t0, gb, bb], writes=[out_bf])
    return out_bf


def step_l0_attn_a(k, W):
    m0 = k.mark()
    tiles = []

    def A(name, shape, dtype):
        t = k.alloc(name, shape, dtype)
        tiles.append(t)
        return t

    triu = A("triu0", [128, 128], BF16)
    k.memset("V", triu.ap, 1.0, writes=[triu])
    k.affsel(triu.ap, triu.ap, [[1, 128]], ALU.is_ge, 0.0, -1, reads=[triu], writes=[triu])
    lng = A("lng", [128, NCH * 4], F32)
    k.dma("Y", lng.ap, W["l0a_lns"], writes=[lng])
    lnbb = A("lnbb", [128, NCH], BF16)
    k.copy("V", lnbb.ap, lng.h[:, NCH:2 * NCH], reads=[lng], writes=[lnbb])
    x1 = A("x1c", [128, NCH * T], BF16)

    mA = k.mark()
    sA = []

    def AA(name, shape, dtype):
        t = k.alloc(name, shape, dtype)
        sA.append(t)
        return t
    h0 = AA("h0", [128, NCH * T], BF16)
    y = AA("lay", [128, NCH * T], BF16)

    mL = k.mark()
    sL = []

    def AL(name, shape, dtype):
        t = k.alloc(name, shape, dtype)
        sL.append(t)
        return t
    scr = ln_scratch(k, AL)
    emit_layernorm(k, k.X, h0, scr)
    k.release(mL, sL)
    if int(os.environ.get("L0A_PHASE", "9")) <= 1:
        k.release(mA, sA)
        k.release(m0, tiles)
        return

    sL = []
    qh = AL("qh", [128, T], BF16)
    kh = AL("kh", [128, T], BF16)
    vh = AL("vh", [128, T], BF16)
    msb = AL("msb", [128, 64], BF16)
    msf = AL("msf", [64, 64], F32)
    ptb = AL("pt0", [128, 128], BF16)
    vtc = AL("vtc", [128, 128], BF16)
    ktc = AL("ktc", [128, 128], BF16)
    wqb = AL("wqb", [128, NCH * 64], BF16)
    wkb = AL("wkb", [128, NCH * 64], BF16)
    wvb = AL("wvb", [128, NCH * 64], BF16)
    bv = AL("bv", [64, 3], F32)
    k.memset("V", qh.h[64:128, :], 0.0, writes=[qh])
    k.memset("V", kh.h[64:128, :], 0.0, writes=[kh])
    k.memset("V", vh.h[64:128, :], 0.0, writes=[vh])
    k.memset("V", msb.h[64:128, :], 0.0, writes=[msb])
    for h in range(16):
        hsl = slice(h * 64, (h + 1) * 64)
        for wi, (wnm, wtile, dst) in enumerate((("l0a_wq", wqb, qh),
                                                ("l0a_wk", wkb, kh),
                                                ("l0a_wv", wvb, vh))):
            k.wload("l0a_wo", wtile.h[:].rearrange("p (c i) -> p c i", i=64),
                  W[wnm][:, hsl].rearrange("(c p) i -> p c i", p=128), writes=[wtile])
            k.mmop(k.pb[5], [(wtile.h[:, c * 64:(c + 1) * 64], lnbb.h[:, c:c + 1])
                             for c in range(NCH)], reads=[wtile, lnbb],
                   out_slice=slice(0, 1), nparts=64)
            k.copy("V", bv.h[:, wi:wi + 1], k.pb[5].h[0:64, 0:1],
                   reads=[k.pb[5]], writes=[bv])
            for c in range(NCH):
                k.ts("V", wtile.h[:, c * 64:(c + 1) * 64],
                     wtile.h[:, c * 64:(c + 1) * 64], lng.h[:, c:c + 1], ALU.mult,
                     reads=[wtile, lng], writes=[wtile])
            for tc in range(4):
                k.mmop(k.pb[0], [(wtile.h[:, c * 64:(c + 1) * 64],
                                  h0.h[:, c * T + tc * 512: c * T + tc * 512 + 512])
                                 for c in range(NCH)], reads=[wtile, h0], nparts=64)
                k.ts("V", dst.h[0:64, tc * 512:(tc + 1) * 512],
                     k.pb[0].h[0:64, :], bv.h[:, wi:wi + 1], ALU.add,
                     reads=[k.pb[0], bv], writes=[dst])
        k.memset("V", msb.h[0:64, :], 0.0, writes=[msb])
        k.memset("V", msf.ap, 0.0, writes=[msf])
        SUB = int(os.environ.get("L0A_SUB", "9"))
        for c in range(16):
            csl = slice(c * 128, (c + 1) * 128)
            if SUB == 0:
                continue
            k.mmop(k.pb[1], [(kh.h[:, csl], qh.h[:, csl])], reads=[kh, qh],
                   out_slice=slice(0, 128))
            if SUB == 1:
                continue
            k.tt("V", ptb.ap, k.pb[1].h[:, 0:128], triu.ap, ALU.mult,
                 reads=[k.pb[1], triu], writes=[ptb])
            if SUB == 2:
                continue
            k.transp(k.pbh[0], vh.h[:, csl], k.ident.ap, slice(0, 128),
                     reads=[vh, k.ident])
            k.transp(k.pbh[0], kh.h[:, csl], k.ident.ap, slice(128, 256),
                     reads=[kh, k.ident])
            k.copy("V", vtc.ap, k.pbh[0].h[:, 0:128], reads=[k.pbh[0]], writes=[vtc])
            k.copy("V", ktc.ap, k.pbh[0].h[:, 128:256], reads=[k.pbh[0]], writes=[ktc])
            if SUB == 3:
                continue
            k.mmop(k.pb[2], [(msb.ap, qh.h[:, csl]),
                             (vtc.h[:, 0:64], ptb.ap)], reads=[msb, qh, vtc, ptb],
                   out_slice=slice(0, 128), nparts=64)
            if SUB == 4:
                continue
            k.copy("V",
                   y.h[(h % 2) * 64:(h % 2) * 64 + 64,
                       (h // 2) * T + c * 128:(h // 2) * T + c * 128 + 128],
                   k.pb[2].h[0:64, 0:128], reads=[k.pb[2]], writes=[y])
            if c < 15:
                k.mmop(k.pb[3], [(ktc.h[:, 0:64], vtc.h[:, 0:64])], reads=[ktc, vtc],
                       out_slice=slice(0, 64), nparts=64)
                k.tt("V", msf.ap, msf.ap, k.pb[3].h[0:64, 0:64], ALU.add,
                     reads=[msf, k.pb[3]], writes=[msf])
                k.copy("G", msb.h[0:64, :], msf.ap, reads=[msf], writes=[msb])
    k.release(mL, sL)

    sL = []
    wob = AL("wob", [128, NCH * 128], BF16)
    for dp in range(NCH):
        k.wload("l0a_wo", wob.h[:].rearrange("p (c i) -> p c i", i=128),
                W["l0a_wo"][:, dp * 128:(dp + 1) * 128].rearrange("(c p) i -> p c i", p=128),
                writes=[wob])
        for tc in range(4):
            tsl = slice(tc * 512, (tc + 1) * 512)
            bank = k.pb[2 + tc % 2]
            k.mmop(bank, [(wob.h[:, c * 128:(c + 1) * 128],
                           y.h[:, c * T + tc * 512: c * T + tc * 512 + 512])
                          for c in range(NCH)], reads=[wob, y])
            xt = k.X[dp]
            k.tt("V", x1.h[:, dp * T + tc * 512: dp * T + tc * 512 + 512],
                 bank.ap, h0.h[:, dp * T + tc * 512: dp * T + tc * 512 + 512],
                 ALU.add, reads=[bank, h0], writes=[x1])
            k.tt("V", xt.h[:, tsl], xt.h[:, tsl], bank.ap, ALU.add,
                 reads=[xt, bank], writes=[xt])
            k.tt("G", xt.h[:, tsl], xt.h[:, tsl],
                 h0.h[:, dp * T + tc * 512: dp * T + tc * 512 + 512], ALU.add,
                 reads=[xt, h0], writes=[xt])
    k.release(mL, sL)
    k.release(mA, sA)

    sA = []
    scr = ln_scratch(k, AA)
    emit_layernorm_bf(k, x1, x1, scr, lng, 2 * NCH, lng, 3 * NCH)
    k.release(mA, sA)
    ping = x1
    if int(os.environ.get("L0A_PHASE", "9")) <= 4:
        k.release(m0, tiles)
        return

    pong = A("pong", [128, NCH * T], BF16)
    ub = A("ub", [128, 32 * 512], BF16)
    w1b = [A(f"w1b{i}", [128, NCH * 128], BF16) for i in range(2)]
    w2b = [A(f"w2b{i}", [128, 32 * 128], BF16) for i in range(2)]
    bcol = A("bcol", [128, 32 + NCH], F32)
    zt2 = [A(f"zt{i}", [128, 512], F32) for i in range(2)]
    et2 = [A(f"et{i}", [128, 512], F32) for i in range(2)]
    GC1 = 0.7978845608028654
    GC2 = 0.044715 * 0.7978845608028654
    GC1f = float(np.float32(GC1))
    GC2f = float(np.float32(GC2))
    for lvl in range(3):
        src_t = ping if lvl % 2 == 0 else pong
        dst_t = pong if lvl % 2 == 0 else ping
        k.dma("Y", bcol.ap, W["l0a_cms_b"][lvl], writes=[bcol])
        for tc in range(4):
            for ib in range(32):
                wb_ = w1b[ib % 2]
                k.wload("l0a_cms_w1", wb_.h[:].rearrange("p (c i) -> p c i", i=128),
                      W["l0a_cms_w1"][lvl][:, ib * 128:(ib + 1) * 128].rearrange(
                          "(c p) i -> p c i", p=128), writes=[wb_])
                bank = k.pb[ib % 4]
                zt = zt2[ib % 2]
                et = et2[ib % 2]
                k.mmop(bank, [(wb_.h[:, c * 128:(c + 1) * 128],
                               src_t.h[:, c * T + tc * 512: c * T + tc * 512 + 512])
                              for c in range(NCH)], reads=[wb_, src_t])
                k.ts("V", zt.ap, bank.ap, bcol.h[:, ib:ib + 1], ALU.add,
                     reads=[bank, bcol], writes=[zt])
                k.act(et.ap, zt.ap, AF.Square, reads=[zt], writes=[et])
                k.ts("V", et.ap, et.ap, GC2f, ALU.mult, s2=GC1f, op1=ALU.add,
                     reads=[et], writes=[et])
                k.tt("G", et.ap, et.ap, zt.ap, ALU.mult, reads=[et, zt], writes=[et])
                k.act(et.ap, et.ap, AF.Tanh, reads=[et], writes=[et])
                k.ts("V", et.ap, et.ap, 0.5, ALU.mult, s2=0.5, op1=ALU.add,
                     reads=[et], writes=[et])
                k.tt("G", ub.h[:, ib * 512:(ib + 1) * 512], et.ap, zt.ap, ALU.mult,
                     reads=[et, zt], writes=[ub])
            for dp in range(NCH):
                wb2 = w2b[dp % 2]
                k.wload("l0a_cms_w2", wb2.h[:].rearrange("p (c i) -> p c i", i=128),
                      W["l0a_cms_w2"][lvl][:, dp * 128:(dp + 1) * 128].rearrange(
                          "(c p) i -> p c i", p=128), writes=[wb2])
                bank = k.pb[4 + dp % 2]
                zt = zt2[dp % 2]
                k.mmop(bank, [(wb2.h[:, c * 128:(c + 1) * 128],
                               ub.h[:, c * 512:(c + 1) * 512])
                              for c in range(32)], reads=[wb2, ub])
                if lvl < 2:
                    k.ts("V", dst_t.h[:, dp * T + tc * 512: dp * T + tc * 512 + 512],
                         bank.ap, bcol.h[:, 32 + dp:33 + dp], ALU.add,
                         reads=[bank, bcol], writes=[dst_t])
                else:
                    xt = k.X[dp]
                    tsl = slice(tc * 512, (tc + 1) * 512)
                    k.ts("V", zt.ap, bank.ap, bcol.h[:, 32 + dp:33 + dp], ALU.add,
                         reads=[bank, bcol], writes=[zt])
                    k.tt("G", xt.h[:, tsl], xt.h[:, tsl], zt.ap, ALU.add,
                         reads=[xt, zt], writes=[xt])
    k.release(m0, tiles)


STEP_FNS["l0_attn_a"] = step_l0_attn_a
STEP_WEIGHTS["l0_attn_a"] = {
    "l0a_lns": (128, NCH * 4), "l0a_wq": (D, D), "l0a_wk": (D, D),
    "l0a_wv": (D, D), "l0a_wo": (D, D), "l0a_cms_w1": (3, D, 4096),
    "l0a_cms_w2": (3, 4096, D), "l0a_cms_b": (3, 128, 32 + NCH),
}



def step_l0_attn_b(k, W):
    TH = T // 2
    m0 = k.mark()
    tiles = []

    def A(name, shape, dtype):
        t = k.alloc(name, shape, dtype)
        tiles.append(t)
        return t

    sel32 = A("sel32", [32, 32 * 128], BF16)
    k.memset("V", sel32.ap, 1.0, writes=[sel32])
    k.affsel(sel32.ap, sel32.ap, [[1, 32 * 128]], ALU.is_ge, 0.0, -128,
             reads=[sel32], writes=[sel32])
    k.affsel(sel32.ap, sel32.ap, [[-1, 32 * 128]], ALU.is_ge, 0.0, 128, base=127,
             reads=[sel32], writes=[sel32])
    Atile = A("Atile", [128, NCH * 16], F32)
    k.dma("Y", Atile.ap, W["mam_alog"], writes=[Atile])
    k.act(Atile.ap, Atile.ap, AF.Exp, reads=[Atile], writes=[Atile])
    k.ts("V", Atile.ap, Atile.ap, -1.0, ALU.mult, reads=[Atile], writes=[Atile])
    convw = A("convw", [128, NCH * 3], F32)
    k.dma("Y", convw.ap, W["mam_conv"], writes=[convw])
    dbc = A("dbc", [128, NCH], F32)
    k.dma("Y", dbc.ap, W["mam_db"], writes=[dbc])
    valc = A("valc", [128, NCH * T], BF16)
    d1t = A("d1t", [128, TH], F32)
    wib = [A(f"wib{i}", [128, NCH * 128], BF16) for i in range(2)]
    mbk = A("mbk", [128, T], BF16)
    rbk = A("rbk", [128, T], BF16)
    stt_ = A("stt", [128, NCH * 16], F32)
    wbc = A("wbc", [128, NCH * 32], BF16)
    k.wload("mam_wbc", wbc.h[:].rearrange("p (c i) -> p c i", i=32),
          W["mam_wbc"].rearrange("(c p) i -> p c i", p=128), writes=[wbc])

    mV = k.mark()
    sV = []

    def AV(name, shape, dtype):
        t = k.alloc(name, shape, dtype)
        sV.append(t)
        return t
    val = AV("val", [128, NCH * (T + 2)], BF16)
    mX = k.mark()
    sX = []

    def AX(name, shape, dtype):
        t = k.alloc(name, shape, dtype)
        sX.append(t)
        return t
    scr = ln_scratch(k, AX)
    emit_ln_stats(k, k.X, scr)
    k.copy("V", mbk.ap, scr["mb"].ap, reads=[scr["mb"]], writes=[mbk])
    k.copy("V", rbk.ap, scr["rb"].ap, reads=[scr["rb"]], writes=[rbk])
    k.release(mX, sX)
    xtc = AV("xtc", [128, NCH * 512], BF16)
    for oc in range(NCH):
        k.memset("V", val.h[:, oc * (T + 2):oc * (T + 2) + 1], 0.0, writes=[val])
        k.memset("V", val.h[:, oc * (T + 2) + T + 1:(oc + 1) * (T + 2)], 0.0, writes=[val])
    for tc in range(4):
        xsl = slice(tc * 512, (tc + 1) * 512)
        for c in range(NCH):
            k.tt("G", d1t.h[:, 0:512], k.X[c].h[:, xsl], mbk.h[:, xsl],
                 ALU.subtract, reads=[k.X[c], mbk], writes=[d1t])
            k.tt("V", xtc.h[:, c * 512:(c + 1) * 512], d1t.h[:, 0:512],
                 rbk.h[:, xsl], ALU.mult, reads=[d1t, rbk], writes=[xtc])
        for oc in range(NCH):
            wb = wib[oc % 2]
            k.wload("mam_win", wb.h[:].rearrange("p (c i) -> p c i", i=128),
                  W["mam_win"][:, oc * 128:(oc + 1) * 128].rearrange(
                      "(c p) i -> p c i", p=128), writes=[wb])
            bank = k.pb[oc % 2]
            k.mmop(bank, [(wb.h[:, c * 128:(c + 1) * 128],
                           xtc.h[:, c * 512:(c + 1) * 512])
                          for c in range(NCH)], reads=[wb, xtc])
            k.copy("V", val.h[:, oc * (T + 2) + 1 + tc * 512: oc * (T + 2) + 1 + tc * 512 + 512],
                   bank.ap, reads=[bank], writes=[val])
    for oc in range(NCH):
        base = oc * (T + 2)
        for hc in range(2):
            t0 = hc * TH
            sm1 = val.h[:, base + t0: base + t0 + TH]
            s0 = val.h[:, base + 1 + t0: base + 1 + t0 + TH]
            sp1 = val.h[:, base + 2 + t0: base + 2 + t0 + TH]
            k.ts("V", d1t.ap, sm1, convw.h[:, oc * 3:oc * 3 + 1], ALU.mult,
                 reads=[val, convw], writes=[d1t])
            k.stt("V", d1t.ap, s0, convw.h[:, oc * 3 + 1:oc * 3 + 2], d1t.ap,
                  ALU.mult, ALU.add, reads=[val, convw, d1t], writes=[d1t])
            k.stt("V", d1t.ap, sp1, convw.h[:, oc * 3 + 2:oc * 3 + 3], d1t.ap,
                  ALU.mult, ALU.add, reads=[val, convw, d1t], writes=[d1t])
            k.act(valc.h[:, oc * T + t0: oc * T + t0 + TH], d1t.ap, AF.Silu,
                  reads=[d1t], writes=[valc])
    k.release(mV, sV)

    for h2 in range(2):
        t0 = h2 * TH
        mH = k.mark()
        sH = []

        def AH(name, shape, dtype):
            t = k.alloc(name, shape, dtype)
            sH.append(t)
            return t
        bcf = AH("bcf", [32, TH], BF16)
        dlt = AH("dlt", [128, NCH * TH], BF16)
        gsil = AH("gsil", [128, NCH * TH], BF16)
        mXN = k.mark()
        xnh = k.alloc("xnh", [128, NCH * TH], BF16)
        for c in range(NCH):
            for tc2 in range(2):
                xsl = slice(t0 + tc2 * 512, t0 + tc2 * 512 + 512)
                k.tt("G", d1t.h[:, 0:512], k.X[c].h[:, xsl], mbk.h[:, xsl],
                     ALU.subtract, reads=[k.X[c], mbk], writes=[d1t])
                k.tt("V", xnh.h[:, c * TH + tc2 * 512: c * TH + tc2 * 512 + 512],
                     d1t.h[:, 0:512], rbk.h[:, xsl], ALU.mult,
                     reads=[d1t, rbk], writes=[xnh])
        for tc2 in range(2):
            k.mmop(k.pb[2], [(wbc.h[:, c * 32:(c + 1) * 32],
                              valc.h[:, c * T + t0 + tc2 * 512: c * T + t0 + tc2 * 512 + 512])
                             for c in range(NCH)], reads=[wbc, valc], nparts=32)
            k.copy("V", bcf.h[:, tc2 * 512:(tc2 + 1) * 512], k.pb[2].h[0:32, :],
                   reads=[k.pb[2]], writes=[bcf])
        mC = k.mark()
        sC = []
        cdh = k.alloc("cdh", [128, NCH * TH], BF16)
        sC.append(cdh)
        wdb = [k.alloc(f"wdb{i}", [128, NCH * 128], BF16) for i in range(2)]
        sC.extend(wdb)
        for dp in range(NCH):
            wb = wdb[dp % 2]
            k.wload("mam_wd", wb.h[:].rearrange("p (c i) -> p c i", i=128),
                  W["mam_wd"][:, dp * 128:(dp + 1) * 128].rearrange("(c p) i -> p c i", p=128),
                  writes=[wb])
            for tc2 in range(2):
                bank = k.pb[tc2 % 2]
                k.mmop(bank, [(wb.h[:, c * 128:(c + 1) * 128],
                               valc.h[:, c * T + t0 + tc2 * 512: c * T + t0 + tc2 * 512 + 512])
                              for c in range(NCH)], reads=[wb, valc])
                k.ts("V", cdh.h[:, dp * TH + tc2 * 512: dp * TH + tc2 * 512 + 512],
                     bank.ap, dbc.h[:, dp:dp + 1], ALU.add, reads=[bank, dbc], writes=[cdh])
        for dp in range(NCH):
            wb = wdb[dp % 2]
            k.wload("mam_wt", wb.h[:].rearrange("p (c i) -> p c i", i=128),
                  W["mam_wt"][:, dp * 128:(dp + 1) * 128].rearrange("(c p) i -> p c i", p=128),
                  writes=[wb])
            for tc2 in range(2):
                bank = k.pb[tc2 % 2]
                k.mmop(bank, [(wb.h[:, c * 128:(c + 1) * 128],
                               cdh.h[:, c * TH + tc2 * 512: c * TH + tc2 * 512 + 512])
                              for c in range(NCH)], reads=[wb, cdh])
                dsl = slice(dp * TH + tc2 * 512, dp * TH + tc2 * 512 + 512)
                k.act(d1t.h[:, 512:1024], bank.ap, AF.Abs, reads=[bank], writes=[d1t])
                k.act(d1t.h[:, 512:1024], d1t.h[:, 512:1024], AF.Exp, scale=-1.0,
                     reads=[d1t], writes=[d1t])
                k.act(d1t.h[:, 512:1024], d1t.h[:, 512:1024], AF.Ln, bias=1.0,
                     reads=[d1t], writes=[d1t])
                k.ts("V", d1t.h[:, 0:512], bank.ap, 0.0, ALU.max, reads=[bank], writes=[d1t])
                k.tt("V", dlt.h[:, dsl], d1t.h[:, 0:512], d1t.h[:, 512:1024], ALU.add,
                     reads=[d1t], writes=[dlt])
        k.release(mC, sC)

        for gc in range(NCH):
            wb = wib[gc % 2]
            k.wload("mam_win", wb.h[:].rearrange("p (c i) -> p c i", i=128),
                  W["mam_win"][:, D + gc * 128: D + (gc + 1) * 128].rearrange(
                      "(c p) i -> p c i", p=128), writes=[wb])
            for tc2 in range(2):
                bank = k.pb[tc2 % 2]
                k.mmop(bank, [(wb.h[:, c * 128:(c + 1) * 128],
                               xnh.h[:, c * TH + tc2 * 512: c * TH + tc2 * 512 + 512])
                              for c in range(NCH)], reads=[wb, xnh])
                k.act(gsil.h[:, gc * TH + tc2 * 512: gc * TH + tc2 * 512 + 512],
                      bank.ap, AF.Silu, reads=[bank], writes=[gsil])
        for dp in range(NCH):
            for tc2 in range(2):
                xsl = slice(t0 + tc2 * 512, t0 + tc2 * 512 + 512)
                k.tt("G", k.X[dp].h[:, xsl], k.X[dp].h[:, xsl],
                     xnh.h[:, dp * TH + tc2 * 512: dp * TH + tc2 * 512 + 512],
                     ALU.add, reads=[k.X[dp], xnh], writes=[k.X[dp]])
        k.release(mXN, [xnh])
        ashp = [AH(f"ash{i}", [128, TH], BF16) for i in range(2)]
        sgtp = [AH(f"sgt{i}", [128, TH], BF16) for i in range(2)]
        rmgp = [d1t, AH("rmg1", [128, TH], F32)]
        wshp = [AH(f"wsh{i}", [128, TH], BF16) for i in range(2)]
        sshp = [AH(f"ssh{i}", [128, TH], BF16) for i in range(2)]
        prdp = [AH(f"prd{i}", [128, TH], BF16) for i in range(2)]
        bbtp = [AH(f"bbt{i}", [128, TH], BF16) for i in range(2)]
        cbtp = [AH(f"cbt{i}", [128, TH], BF16) for i in range(2)]
        dvtp = [AH(f"dvt{i}", [128, TH], BF16) for i in range(2)]
        ycb = AH("ycb", [128, TH], F32)
        units = [(cb, sg) for cb in range(NCH) for sg in range(16)]

        def emit_ash(idx):
            cb_, sg_ = units[idx]
            ash_ = ashp[sg_ & 1]
            acol_ = Atile.h[:, cb_ * 16 + sg_: cb_ * 16 + sg_ + 1]
            k.ts("V", ash_.ap, dlt.h[:, cb_ * TH:(cb_ + 1) * TH], acol_,
                 ALU.mult, reads=[dlt, Atile], writes=[ash_])

        for idx, (cb, sg) in enumerate(units):
            dsl = slice(cb * TH, (cb + 1) * TH)
            dvt = dvtp[cb & 1]
            if sg == 0:
                k.tt("V", dvt.ap, dlt.h[:, dsl],
                     valc.h[:, cb * T + t0: cb * T + t0 + TH], ALU.mult,
                     reads=[dlt, valc], writes=[dvt])
            p = sg & 1
            ash, sgt, rmg = ashp[p], sgtp[p], rmgp[p]
            wsh, ssh, prd = wshp[p], sshp[p], prdp[p]
            bbt, cbt = bbtp[p], cbtp[p]
            for (row, dst) in ((sg, bbt), (16 + sg, cbt)):
                for tc2 in range(2):
                    k.mmop(k.pb[2 + tc2], [(sel32.h[0:32, row * 128:(row + 1) * 128],
                                            bcf.h[0:32, tc2 * 512:(tc2 + 1) * 512])],
                           reads=[sel32, bcf])
                    k.act(dst.h[:, tc2 * 512:(tc2 + 1) * 512],
                          k.pb[2 + tc2].ap, AF.Copy,
                          reads=[k.pb[2 + tc2]], writes=[dst])
            if idx == 0:
                emit_ash(0)
            k.act(sgt.ap, ash.ap, AF.Sign, bias=1e-6, reads=[ash], writes=[sgt])
            k.act(rmg.ap, ash.ap, AF.Abs, bias=1e-6, reads=[ash], writes=[rmg])
            k.act(rmg.ap, rmg.ap, AF.Ln, reads=[rmg], writes=[rmg])
            k.act(rmg.ap, rmg.ap, AF.Exp, scale=-1.0, reads=[rmg], writes=[rmg])
            k.act(ash.ap, ash.ap, AF.Exp, reads=[ash], writes=[ash])
            k.tt("V", rmg.ap, rmg.ap, sgt.ap, ALU.mult,
                 reads=[rmg, sgt], writes=[rmg])
            k.stt("V", wsh.ap, ash.ap, -1.0, rmg.ap, ALU.add, ALU.mult,
                  reads=[ash, rmg], writes=[wsh])
            k.tt("G", wsh.ap, wsh.ap, bbt.ap, ALU.mult, reads=[wsh, bbt], writes=[wsh])
            k.tt("G", wsh.ap, wsh.ap, dvt.ap, ALU.mult,
                 reads=[wsh, dvt], writes=[wsh])
            if idx + 1 < len(units):
                emit_ash(idx + 1)
            init = 0.0 if h2 == 0 else stt_.h[:, cb * 16 + sg: cb * 16 + sg + 1]
            rd = [ash, wsh] + ([stt_] if h2 == 1 else [])
            k.scan("V", ssh.ap, ash.ap, wsh.ap, init, ALU.mult, ALU.add,
                   reads=rd, writes=[ssh])
            k.copy("V", stt_.h[:, cb * 16 + sg: cb * 16 + sg + 1],
                   ssh.h[:, TH - 1:TH], reads=[ssh], writes=[stt_])
            if sg == 0:
                k.tt("G", ycb.ap, ssh.ap, cbt.ap, ALU.mult,
                     reads=[ssh, cbt], writes=[ycb])
            else:
                k.tt("G", prd.ap, ssh.ap, cbt.ap, ALU.mult,
                     reads=[ssh, cbt], writes=[prd])
                k.tt("V", ycb.ap, ycb.ap, prd.ap, ALU.add,
                     reads=[ycb, prd], writes=[ycb])
            if sg == 15:
                k.tt("V", gsil.h[:, dsl], ycb.ap, gsil.h[:, dsl], ALU.mult,
                     reads=[ycb, gsil], writes=[gsil])
        for dp in range(NCH):
            wb = wib[dp % 2]
            k.wload("mam_wout", wb.h[:].rearrange("p (c i) -> p c i", i=128),
                  W["mam_wout"][:, dp * 128:(dp + 1) * 128].rearrange(
                      "(c p) i -> p c i", p=128), writes=[wb])
            for tc2 in range(2):
                bank = k.pb[2 + tc2 % 2]
                k.mmop(bank, [(wb.h[:, c * 128:(c + 1) * 128],
                               gsil.h[:, c * TH + tc2 * 512: c * TH + tc2 * 512 + 512])
                              for c in range(NCH)], reads=[wb, gsil])
                xt = k.X[dp]
                xsl = slice(t0 + tc2 * 512, t0 + tc2 * 512 + 512)
                k.tt("V", xt.h[:, xsl], xt.h[:, xsl], bank.ap, ALU.add,
                     reads=[xt, bank], writes=[xt])
        k.release(mH, sH)
    k.release(m0, tiles)


STEP_FNS["l0_attn_b"] = step_l0_attn_b
STEP_WEIGHTS["l0_attn_b"] = {
    "mam_win": (D, 2 * D), "mam_conv": (128, NCH * 3), "mam_wbc": (D, 32),
    "mam_wd": (D, D), "mam_db": (128, NCH), "mam_wt": (D, D),
    "mam_alog": (128, NCH * 16), "mam_wout": (D, D),
}



def step_l1_attn_b(k, W):
    H, HD, C = 16, 64, 128
    NCHK = T // C
    NI = 2764
    ICW = [128] * 21 + [76]
    m0 = k.mark()
    tiles = []

    def A(name, shape, dtype):
        t = k.alloc(name, shape, dtype)
        tiles.append(t)
        return t

    mbk = A("mbk", [128, T], BF16)
    rbk = A("rbk", [128, T], BF16)
    n12 = A("n12", [128, 2 * NCH], F32)
    k.dma("Y", n12.ap, W["gd_norms"], writes=[n12])
    cwb = A("cwb", [128, 5 * NCH], F32)
    k.dma("Y", cwb.ap, W["gd_conv"], writes=[cwb])
    trilS = A("trilS", [128, 128], BF16)
    k.memset("V", trilS.ap, 1.0, writes=[trilS])
    k.affsel(trilS.ap, trilS.ap, [[-1, 128]], ALU.is_ge, 0.0, 1, base=-1,
             reads=[trilS], writes=[trilS])
    triuI = A("triuI", [128, 128], BF16)
    k.memset("V", triuI.ap, 1.0, writes=[triuI])
    k.affsel(triuI.ap, triuI.ap, [[1, 128]], ALU.is_ge, 0.0, -1,
             reads=[triuI], writes=[triuI])
    sel16 = A("sel16", [16, 16 * 128], F32)
    k.memset("V", sel16.ap, 1.0, writes=[sel16])
    k.affsel(sel16.ap, sel16.ap, [[1, 16 * 128]], ALU.is_ge, 0.0, -128,
             reads=[sel16], writes=[sel16])
    k.affsel(sel16.ap, sel16.ap, [[-1, 16 * 128]], ALU.is_ge, 0.0, 128, base=127,
             reads=[sel16], writes=[sel16])
    sel2 = A("sel2", [2, 2 * 128], BF16)
    k.memset("V", sel2.ap, 1.0, writes=[sel2])
    k.affsel(sel2.ap, sel2.ap, [[1, 2 * 128]], ALU.is_ge, 0.0, -64,
             reads=[sel2], writes=[sel2])
    k.affsel(sel2.ap, sel2.ap, [[-1, 2 * 128]], ALU.is_ge, 0.0, 64, base=63,
             reads=[sel2], writes=[sel2])

    mL = k.mark()
    sL = []

    def AL(name, shape, dtype):
        t = k.alloc(name, shape, dtype)
        sL.append(t)
        return t
    scr = ln_scratch(k, AL)
    emit_ln_stats(k, k.X, scr)
    k.copy("V", mbk.ap, scr["mb"].ap, reads=[scr["mb"]], writes=[mbk])
    k.copy("V", rbk.ap, scr["rb"].ap, reads=[scr["rb"]], writes=[rbk])
    k.release(mL, sL)

    mMid = k.mark()
    sMid = []

    def AM(name, shape, dtype):
        t = k.alloc(name, shape, dtype)
        sMid.append(t)
        return t
    o_s = k.nc.dram_tensor("gd_o_scratch", [D, T], BF16, kind="Internal").ap()
    o_sb = Buf("gd_o_scratch")
    o_buf = AM("obuf", [128, NCH * C], BF16)
    xc = AM("xc", [128, NCH * T], BF16)
    mC = k.mark()
    sC = []

    def AC(name, shape, dtype):
        t = k.alloc(name, shape, dtype)
        sC.append(t)
        return t
    xn1 = AC("xn1", [128, T + 3], BF16)
    rsb = AC("rsb", [128, T], F32)
    h2t = AC("h2t", [128, 512], BF16)
    cvt = AC("cvt", [128, 512], F32)
    for tc in range(4):
        sl = slice(tc * 512, (tc + 1) * 512)
        for c in range(NCH):
            k.tt("G", cvt.ap, k.X[c].h[:, sl], mbk.h[:, sl], ALU.subtract,
                 reads=[k.X[c], mbk], writes=[cvt])
            k.tt("V", h2t.ap, cvt.ap, rbk.h[:, sl], ALU.mult,
                 reads=[cvt, rbk], writes=[h2t])
            k.act(h2t.ap, h2t.ap, AF.Square, reads=[h2t], writes=[h2t])
            k.mmop(k.pb[0], [(k.ones_bf.ap, h2t.ap)], reads=[k.ones_bf, h2t],
                   accum=(c > 0), close=(c == NCH - 1))
        k.ts("V", rsb.h[:, sl], k.pb[0].ap, 1.0 / D, ALU.mult, s2=1e-6, op1=ALU.add,
             reads=[k.pb[0]], writes=[rsb])
        k.act(rsb.h[:, sl], rsb.h[:, sl], AF.Sqrt, reads=[rsb], writes=[rsb])
    k.recip(rsb.ap, rsb.ap, reads=[rsb], writes=[rsb])
    for c in range(NCH):
        k.memset("V", xn1.h[:, 0:3], 0.0, writes=[xn1])
        for tc in range(4):
            sl = slice(tc * 512, (tc + 1) * 512)
            k.tt("G", cvt.ap, k.X[c].h[:, sl], mbk.h[:, sl], ALU.subtract,
                 reads=[k.X[c], mbk], writes=[cvt])
            k.tt("V", cvt.ap, cvt.ap, rbk.h[:, sl], ALU.mult,
                 reads=[cvt, rbk], writes=[cvt])
            k.tt("G", cvt.ap, cvt.ap, rsb.h[:, sl], ALU.mult, reads=[cvt, rsb], writes=[cvt])
            k.ts("V", xn1.h[:, 3 + tc * 512: 3 + tc * 512 + 512],
                 cvt.ap, n12.h[:, c:c + 1], ALU.mult, reads=[cvt, n12], writes=[xn1])
        for hc in range(4):
            t0 = hc * 512
            k.ts("V", cvt.ap, xn1.h[:, t0: t0 + 512],
                 cwb.h[:, c:c + 1], ALU.mult, reads=[xn1, cwb], writes=[cvt])
            for j in range(1, 4):
                k.stt("V", cvt.ap, xn1.h[:, j + t0: j + t0 + 512],
                      cwb.h[:, j * NCH + c: j * NCH + c + 1], cvt.ap,
                      ALU.mult, ALU.add, reads=[xn1, cwb, cvt], writes=[cvt])
            k.act(xc.h[:, c * T + t0: c * T + t0 + 512], cvt.ap, AF.Silu,
                  bias=cwb.h[:, 4 * NCH + c: 4 * NCH + c + 1],
                  reads=[cvt, cwb], writes=[xc])
    k.release(mC, sC)
    if int(os.environ.get("GD_PH", "9")) <= 1:
        k.release(mMid, sMid)
        k.release(m0, tiles)
        return

    mD = k.mark()
    sD = []

    def AD(name, shape, dtype):
        t = k.alloc(name, shape, dtype)
        sD.append(t)
        return t
    betaf = AD("betaf", [16, T], BF16)
    lgf = AD("lgf", [16, T], F32)
    Sst = AD("Sst", [64, H * 64], F32)
    Sbf = AD("Sbf", [64, H * 64], BF16)
    k.memset("V", Sst.ap, 0.0, writes=[Sst])
    k.memset("V", Sbf.ap, 0.0, writes=[Sbf])
    mB = k.mark()
    sB = []
    wba = k.alloc("wba", [128, NCH * 16], BF16)
    t16a = k.alloc("t16a", [16, 512], F32)
    t16b = k.alloc("t16b", [16, 512], F32)
    sB.extend([wba, t16a, t16b])
    for (wnm, coff) in (("gd_wb", 0), ("gd_wa", 1)):
        k.wload("gd_wo", wba.h[:].rearrange("p (c i) -> p c i", i=16),
              W[wnm].rearrange("(c p) i -> p c i", p=128), writes=[wba])
        for tc in range(4):
            sl = slice(tc * 512, (tc + 1) * 512)
            k.mmop(k.pb[0], [(wba.h[:, c * 16:(c + 1) * 16],
                              xc.h[:, c * T + tc * 512: c * T + tc * 512 + 512])
                             for c in range(NCH)], reads=[wba, xc], nparts=16)
            if coff == 0:
                k.act(betaf.h[:, sl], k.pb[0].h[0:16, :], AF.Sigmoid,
                      reads=[k.pb[0]], writes=[betaf])
            else:
                k.ts("V", t16a.ap, k.pb[0].h[0:16, :], 4.0, ALU.add,
                     reads=[k.pb[0]], writes=[t16a])
                k.act(t16b.ap, t16a.ap, AF.Abs, reads=[t16a], writes=[t16b])
                k.act(t16b.ap, t16b.ap, AF.Exp, scale=-1.0, reads=[t16b], writes=[t16b])
                k.act(t16b.ap, t16b.ap, AF.Ln, bias=1.0, reads=[t16b], writes=[t16b])
                k.ts("V", t16a.ap, t16a.ap, 0.0, ALU.min, reads=[t16a], writes=[t16a])
                k.tt("V", lgf.h[:, sl], t16a.ap, t16b.ap, ALU.subtract,
                     reads=[t16a, t16b], writes=[lgf])
    zz16 = t16a
    k.memset("V", zz16.ap, 0.0, writes=[zz16])
    for cc in range(NCHK):
        k.scan("V", lgf.h[0:16, cc * C:(cc + 1) * C], lgf.h[0:16, cc * C:(cc + 1) * C],
               zz16.h[0:16, 0:C], 0.0, ALU.add, ALU.add, reads=[lgf, zz16], writes=[lgf])
    k.release(mB, sB)
    if int(os.environ.get("GD_PH", "9")) <= 2:
        k.release(mMid, sMid)
        k.release(m0, tiles)
        return

    qcf = AD("qcf", [128, NCH * C], BF16)
    kcf = AD("kcf", [128, NCH * C], BF16)
    vcf = AD("vcf", [128, NCH * C], BF16)
    wpj = [AD(f"wpj{i}", [128, NCH * 1024], BF16) for i in range(2)]
    wvs = [AD(f"wvs{i}", [128, NCH * 128], BF16) for i in range(2)]
    lgtm = AD("lgtm", [128, 16], F32)
    bttm = AD("bttm", [128, 16], F32)
    sq2 = AD("sq2", [128, C], BF16)
    rs2 = AD("rs2", [2, C], F32)
    rs2b = AD("rs2b", [2, C], BF16)
    nrm = AD("nrm", [128, C], BF16)
    US = []
    for ui in range(2):
        u = {}
        for nm, sh, dt_ in (("gC2", [64, 1], F32), ("gcol", [128, 1], F32),
                            ("m2f", [128, C], F32), ("t1f", [128, C], F32),
                            ("qh_t", [64, C], BF16), ("kh_t", [64, C], BF16),
                            ("kb_t", [64, C], BF16), ("Ebf", [128, C], BF16),
                            ("ETbf", [128, C], BF16), ("grow", [128, C], BF16),
                            ("Lp", [128, C], F32), ("Lt", [128, C], F32),
                            ("L2p", [128, C], F32), ("L2t", [128, C], F32),
                            ("MT", [128, C], BF16), ("Vtm", [128, 64], BF16),
                            ("Rtm", [128, 64], F32), ("Xtm", [128, 64], F32),
                            ("Xbf", [128, 64], BF16), ("obf", [128, 64], BF16),
                            ("ktm", [128, 64], BF16)):
            u[nm] = AD(f"{nm}{ui}", sh, dt_)
        US.append(u)
    trilSf = AD("trilSf", [128, 128], F32)
    hsum2 = AD("hsum2", [128, 2], BF16)
    k.memset("V", trilSf.ap, 1.0, writes=[trilSf])
    k.affsel(trilSf.ap, trilSf.ap, [[-1, 128]], ALU.is_ge, 0.0, 1, base=-1,
             reads=[trilSf], writes=[trilSf])
    k.transp(k.pbh[0], sel2.h[0:2, 0:128], k.ident.h[0:2, 0:2], slice(0, 2),
             reads=[sel2, k.ident], nparts=128)
    k.copy("V", hsum2.ap, k.pbh[0].h[:, 0:2], reads=[k.pbh[0]], writes=[hsum2])
    for cc in range(NCHK):
        csl = slice(cc * C, (cc + 1) * C)
        for wi, (wnm, dst) in enumerate((("gd_wq", qcf), ("gd_wk", kcf))):
            wt = wpj[wi]
            if cc == 0:
                k.wload(wnm, wt.h[:].rearrange("p (c i) -> p c i", i=1024),
                        W[wnm].rearrange("(c p) i -> p c i", p=128), writes=[wt])
            for oc in range(NCH):
                bank = k.pb[oc % 2]
                k.mmop(bank, [(wt.h[:, c * 1024 + oc * 128: c * 1024 + oc * 128 + 128],
                               xc.h[:, c * T + cc * C: c * T + cc * C + C])
                              for c in range(NCH)], reads=[wt, xc],
                       out_slice=slice(0, C))
                k.copy("V", dst.h[:, oc * C:(oc + 1) * C], bank.h[:, 0:C],
                       reads=[bank], writes=[dst])
        for oc in range(NCH):
            wv_ = wvs[oc % 2]
            k.wload("gd_wv", wv_.h[:].rearrange("p (c i) -> p c i", i=128),
                    W["gd_wv"][:, oc * 128:(oc + 1) * 128].rearrange(
                        "(c p) i -> p c i", p=128), writes=[wv_])
            bank = k.pb[2 + oc % 2]
            k.mmop(bank, [(wv_.h[:, c * 128:(c + 1) * 128],
                           xc.h[:, c * T + cc * C: c * T + cc * C + C])
                          for c in range(NCH)], reads=[wv_, xc],
                   out_slice=slice(0, C))
            k.copy("V", vcf.h[:, oc * C:(oc + 1) * C], bank.h[:, 0:C],
                   reads=[bank], writes=[vcf])
        for dst in (qcf, kcf):
            for oc in range(NCH):
                osl = slice(oc * C, (oc + 1) * C)
                k.act(sq2.ap, dst.h[:, osl], AF.Square, reads=[dst], writes=[sq2])
                k.mmop(k.pb[2], [(hsum2.ap, sq2.ap)], reads=[hsum2, sq2], nparts=2,
                       out_slice=slice(0, C))
                k.ts("V", rs2.ap, k.pb[2].h[0:2, 0:C], 1.0, ALU.mult, s2=1e-6, op1=ALU.add,
                     reads=[k.pb[2]], writes=[rs2])
                k.act(rs2.ap, rs2.ap, AF.Sqrt, reads=[rs2], writes=[rs2])
                k.recip(rs2.ap, rs2.ap, reads=[rs2], writes=[rs2])
                k.copy("V", rs2b.ap, rs2.ap, reads=[rs2], writes=[rs2b])
                k.mmop(k.pb[3], [(sel2.h[0:2, 0:128], rs2b.h[0:2, :])],
                       reads=[sel2, rs2b], out_slice=slice(0, C))
                k.copy("V", nrm.ap, k.pb[3].h[:, 0:C], reads=[k.pb[3]], writes=[nrm])
                k.tt("V", dst.h[:, osl], dst.h[:, osl], nrm.ap, ALU.mult,
                     reads=[dst, nrm], writes=[dst])
        k.transp(k.pb[4], lgf.h[0:16, csl], k.identf.h[0:16, 0:16], slice(0, 16),
                 reads=[lgf, k.identf], nparts=128)
        k.copy("V", lgtm.ap, k.pb[4].h[:, 0:16], reads=[k.pb[4]], writes=[lgtm])
        k.transp(k.pbh[0], betaf.h[0:16, csl], k.ident.h[0:16, 0:16], slice(0, 16),
                 reads=[betaf, k.ident], nparts=128)
        k.copy("V", bttm.ap, k.pbh[0].h[:, 0:16], reads=[k.pbh[0]], writes=[bttm])
        for h in range(H):
            u = US[h % 2]
            B0 = k.pb[(h % 2) * 3]
            B1 = k.pb[(h % 2) * 3 + 1]
            B2 = k.pb[(h % 2) * 3 + 2]
            PH = k.pbh[h % 2]
            oc = h // 2
            pb_ = (h % 2) * 64
            qsl = slice(oc * C, (oc + 1) * C)
            ssl = slice(h * 64, (h + 1) * 64)
            btcol = bttm.h[:, h:h + 1]
            k.copy("V", u["qh_t"].ap, qcf.h[pb_:pb_ + 64, qsl], reads=[qcf], writes=[u["qh_t"]])
            k.copy("G", u["kh_t"].ap, kcf.h[pb_:pb_ + 64, qsl], reads=[kcf], writes=[u["kh_t"]])
            k.mmop(B0, [(sel16.h[0:16, h * 128:(h + 1) * 128], lgf.h[0:16, csl])],
                   reads=[sel16, lgf], out_slice=slice(0, C))
            k.act(u["grow"].ap, B0.h[:, 0:C], AF.Exp, reads=[B0], writes=[u["grow"]])
            k.ts("V", u["m2f"].ap, B0.h[:, 0:C], lgtm.h[:, h:h + 1], ALU.subtract,
                 reads=[B0, lgtm], writes=[u["m2f"]])
            k.ts("V", u["t1f"].ap, u["m2f"].ap, 0.0, ALU.max, reads=[u["m2f"]], writes=[u["t1f"]])
            k.act(u["Ebf"].ap, u["t1f"].ap, AF.Exp, scale=-1.0, reads=[u["t1f"]], writes=[u["Ebf"]])
            k.ts("V", u["t1f"].ap, u["m2f"].ap, 0.0, ALU.min, reads=[u["m2f"]], writes=[u["t1f"]])
            k.act(u["ETbf"].ap, u["t1f"].ap, AF.Exp, reads=[u["t1f"]], writes=[u["ETbf"]])
            k.mmop(B1, [(u["kh_t"].ap, u["kh_t"].ap)], reads=[u["kh_t"]], out_slice=slice(0, C))
            k.tt("V", u["Lp"].ap, B1.h[:, 0:C], u["Ebf"].ap, ALU.mult,
                 reads=[B1, u["Ebf"]], writes=[u["Lp"]])
            k.ts("V", u["Lp"].ap, u["Lp"].ap, btcol, ALU.mult, reads=[u["Lp"], bttm], writes=[u["Lp"]])
            k.tt("G", u["Lp"].ap, u["Lp"].ap, trilSf.ap, ALU.mult, reads=[u["Lp"], trilSf], writes=[u["Lp"]])
            k.transp(B0, u["Lp"].ap, k.identf.ap, slice(0, 128), reads=[u["Lp"], k.identf])
            k.copy("V", u["Lt"].ap, B0.h[:, 0:C], reads=[B0], writes=[u["Lt"]])
            k.mmop(B2, [(u["kh_t"].ap, u["qh_t"].ap)], reads=[u["kh_t"], u["qh_t"]],
                   out_slice=slice(0, C))
            k.tt("V", u["MT"].ap, B2.h[:, 0:C], u["ETbf"].ap, ALU.mult,
                 reads=[B2, u["ETbf"]], writes=[u["MT"]])
            k.tt("G", u["MT"].ap, u["MT"].ap, triuI.ap, ALU.mult, reads=[u["MT"], triuI], writes=[u["MT"]])
            k.tt("V", u["kb_t"].ap, u["kh_t"].ap, u["grow"].h[0:64, :], ALU.mult,
                 reads=[u["kh_t"], u["grow"]], writes=[u["kb_t"]])
            k.tt("G", u["qh_t"].ap, u["qh_t"].ap, u["grow"].h[0:64, :], ALU.mult,
                 reads=[u["qh_t"], u["grow"]], writes=[u["qh_t"]])
            k.transp(PH, vcf.h[pb_:pb_ + 64, qsl],
                     k.ident.h[pb_:pb_ + 64, pb_:pb_ + 64],
                     slice(0, 64), reads=[vcf, k.ident])
            k.copy("V", u["Vtm"].ap, PH.h[:, 0:64], reads=[PH], writes=[u["Vtm"]])
            k.transp(PH, u["kh_t"].ap, k.ident.h[0:64, 0:64], slice(64, 128),
                     reads=[u["kh_t"], k.ident])
            k.copy("V", u["ktm"].ap, PH.h[:, 64:128], reads=[PH], writes=[u["ktm"]])
            k.mmop(B2, [(u["kb_t"].ap, Sbf.h[0:64, ssl])], reads=[u["kb_t"], Sbf],
                   out_slice=slice(0, 64))
            k.tt("V", u["Rtm"].ap, u["Vtm"].ap, B2.h[:, 0:64], ALU.subtract,
                 reads=[u["Vtm"], B2], writes=[u["Rtm"]])
            k.ts("V", u["Rtm"].ap, u["Rtm"].ap, btcol, ALU.mult, reads=[u["Rtm"], bttm], writes=[u["Rtm"]])
            k.mmop(B1, [(u["Lt"].ap, u["Rtm"].ap)], reads=[u["Lt"], u["Rtm"]], out_slice=slice(0, 64))
            k.tt("V", u["Xtm"].ap, u["Rtm"].ap, B1.h[:, 0:64], ALU.subtract,
                 reads=[u["Rtm"], B1], writes=[u["Xtm"]])
            cp, ct, np_, nt = u["Lp"], u["Lt"], u["L2p"], u["L2t"]
            for lev in range(6):
                k.mmop(B1, [(ct.ap, cp.ap)], reads=[ct, cp], out_slice=slice(0, C))
                k.copy("V", np_.ap, B1.h[:, 0:C], reads=[B1], writes=[np_])
                k.transp(B0, np_.ap, k.identf.ap, slice(0, 128), reads=[np_, k.identf])
                k.copy("V", nt.ap, B0.h[:, 0:C], reads=[B0], writes=[nt])
                k.mmop(B1, [(nt.ap, u["Xtm"].ap)], reads=[nt, u["Xtm"]], out_slice=slice(0, 64))
                k.tt("V", u["Xtm"].ap, u["Xtm"].ap, B1.h[:, 0:64], ALU.add,
                     reads=[u["Xtm"], B1], writes=[u["Xtm"]])
                cp, ct, np_, nt = np_, nt, cp, ct
            k.copy("G", u["Xbf"].ap, u["Xtm"].ap, reads=[u["Xtm"]], writes=[u["Xbf"]])
            k.mmop(B2, [(u["qh_t"].ap, Sbf.h[0:64, ssl]), (u["MT"].ap, u["Xbf"].ap)],
                   reads=[u["qh_t"], Sbf, u["MT"], u["Xbf"]], out_slice=slice(0, 64))
            k.copy("V", u["obf"].ap, B2.h[:, 0:64], reads=[B2], writes=[u["obf"]])
            k.transp(PH, u["obf"].ap, k.ident.ap, slice(0, C), reads=[u["obf"], k.ident],
                     nparts=64)
            k.copy("V", o_buf.h[pb_:pb_ + 64, oc * C:(oc + 1) * C],
                   PH.h[0:64, 0:C], reads=[PH], writes=[o_buf])
            k.act(u["gcol"].ap, u["m2f"].h[:, C - 1:C], AF.Exp, reads=[u["m2f"]], writes=[u["gcol"]])
            k.ts("V", u["ktm"].ap, u["ktm"].ap, u["gcol"].h[:, 0:1], ALU.mult,
                 reads=[u["ktm"], u["gcol"]], writes=[u["ktm"]])
            k.mmop(B2, [(u["ktm"].ap, u["Xbf"].ap)], reads=[u["ktm"], u["Xbf"]],
                   out_slice=slice(0, 64), nparts=64)
            k.copy("V", u["gC2"].ap, u["grow"].h[0:64, C - 1:C], reads=[u["grow"]], writes=[u["gC2"]])
            k.stt("V", Sst.h[0:64, ssl], Sst.h[0:64, ssl], u["gC2"].h[0:64, 0:1],
                  B2.h[0:64, 0:64], ALU.mult, ALU.add,
                  reads=[Sst, u["gC2"], B2], writes=[Sst])
            k.copy("G", Sbf.h[0:64, ssl], Sst.h[0:64, ssl], reads=[Sst], writes=[Sbf])
        k.dma("S", o_s.rearrange("(c p) t -> p c t", p=128)[:, :, cc * C:(cc + 1) * C],
              o_buf.h[:].rearrange("p (c t) -> p c t", t=C), reads=[o_buf], writes=[o_sb])
    k.release(mD, sD)
    if int(os.environ.get("GD_PH", "9")) <= 3:
        k.release(mMid, sMid)
        k.release(m0, tiles)
        return
    hh = AM("hh", [128, NCH * T], BF16)

    mW = k.mark()
    sW = []
    wob = k.alloc("wob", [128, NCH * 128], BF16)
    sW.append(wob)
    ob2 = k.alloc("ob2", [128, NCH * 512], BF16)
    sW.append(ob2)
    for tc in range(4):
        k.dma("Y", ob2.h[:].rearrange("p (c t) -> p c t", t=512),
              o_s.rearrange("(c p) t -> p c t", p=128)[:, :, tc * 512:(tc + 1) * 512],
              reads=[o_sb], writes=[ob2])
        for dp in range(NCH):
            tsl = slice(tc * 512, (tc + 1) * 512)
            bank = k.pb[2 + dp % 2]
            k.wload("gd_wo", wob.h[:].rearrange("p (c i) -> p c i", i=128),
                    W["gd_wo"][:, dp * 128:(dp + 1) * 128].rearrange(
                        "(c p) i -> p c i", p=128), writes=[wob])
            k.mmop(bank, [(wob.h[:, c * 128:(c + 1) * 128],
                           ob2.h[:, c * 512:(c + 1) * 512])
                          for c in range(NCH)], reads=[wob, ob2])
            hsl2 = slice(dp * T + tc * 512, dp * T + tc * 512 + 512)
            k.tt("G", hh.h[:, hsl2], k.X[dp].h[:, tsl], mbk.h[:, tsl], ALU.subtract,
                 reads=[k.X[dp], mbk], writes=[hh])
            k.tt("V", hh.h[:, hsl2], hh.h[:, hsl2], rbk.h[:, tsl], ALU.mult,
                 reads=[hh, rbk], writes=[hh])
            k.tt("V", hh.h[:, hsl2], hh.h[:, hsl2], bank.ap, ALU.add,
                 reads=[hh, bank], writes=[hh])
            xt = k.X[dp]
            k.tt("G", xt.h[:, tsl], xt.h[:, tsl],
                 hh.h[:, dp * T + tc * 512: dp * T + tc * 512 + 512], ALU.add,
                 reads=[xt, hh], writes=[xt])
    k.release(mW, sW)

    if int(os.environ.get("GD_PH", "9")) <= 4:
        k.release(mMid, sMid)
        k.release(m0, tiles)
        return

    sW = []
    mW = k.mark()

    def AW(name, shape, dtype):
        t = k.alloc(name, shape, dtype)
        sW.append(t)
        return t
    rsb2 = AW("rsb2", [128, T], F32)
    h2b = AW("h2b", [128, 512], BF16)
    for tc in range(4):
        sl = slice(tc * 512, (tc + 1) * 512)
        for c in range(NCH):
            k.act(h2b.ap, hh.h[:, c * T + tc * 512: c * T + tc * 512 + 512],
                  AF.Square, reads=[hh], writes=[h2b])
            k.mmop(k.pb[0], [(k.ones_bf.ap, h2b.ap)], reads=[k.ones_bf, h2b],
                   accum=(c > 0), close=(c == NCH - 1))
        k.ts("V", rsb2.h[:, sl], k.pb[0].ap, 1.0 / D, ALU.mult, s2=1e-6, op1=ALU.add,
             reads=[k.pb[0]], writes=[rsb2])
        k.act(rsb2.h[:, sl], rsb2.h[:, sl], AF.Sqrt, reads=[rsb2], writes=[rsb2])
    k.recip(rsb2.ap, rsb2.ap, reads=[rsb2], writes=[rsb2])
    for c in range(NCH):
        for tc in range(4):
            sl = slice(tc * 512, (tc + 1) * 512)
            hsl = slice(c * T + tc * 512, c * T + tc * 512 + 512)
            k.tt("G", hh.h[:, hsl], hh.h[:, hsl], rsb2.h[:, sl], ALU.mult,
                 reads=[hh, rsb2], writes=[hh])
            k.ts("V", hh.h[:, hsl], hh.h[:, hsl], n12.h[:, NCH + c:NCH + c + 1],
                 ALU.mult, reads=[hh, n12], writes=[hh])
    hbuf = AW("hbuf", [128, 22 * 512], BF16)
    wgb = [AW(f"wgb{i}", [128, NCH * 128], BF16) for i in range(2)]
    wub = [AW(f"wub{i}", [128, NCH * 128], BF16) for i in range(2)]
    wdb2 = [AW(f"wdb2{i}", [128, 128], BF16) for i in range(2)]
    sgt = AW("sgt", [128, 512], F32)
    ic_off = [0]
    for w_ in ICW:
        ic_off.append(ic_off[-1] + w_)
    for tc in range(4):
        tsl = slice(tc * 512, (tc + 1) * 512)
        for ic in range(22):
            iw = ICW[ic]
            wg_ = wgb[ic % 2]
            wu_ = wub[ic % 2]
            k.wload("gd_wg", wg_.h[:, 0:NCH * iw].rearrange("p (c i) -> p c i", i=iw),
                  W["gd_wg"][:, ic_off[ic]:ic_off[ic] + iw].rearrange(
                      "(c p) i -> p c i", p=128), writes=[wg_])
            k.wload("gd_wu", wu_.h[:, 0:NCH * iw].rearrange("p (c i) -> p c i", i=iw),
                  W["gd_wu"][:, ic_off[ic]:ic_off[ic] + iw].rearrange(
                      "(c p) i -> p c i", p=128), writes=[wu_])
            k.mmop(k.pb[0], [(wg_.h[:, c * iw:(c + 1) * iw],
                              hh.h[:, c * T + tc * 512: c * T + tc * 512 + 512])
                             for c in range(NCH)], reads=[wg_, hh], nparts=iw)
            k.mmop(k.pb[1], [(wu_.h[:, c * iw:(c + 1) * iw],
                              hh.h[:, c * T + tc * 512: c * T + tc * 512 + 512])
                             for c in range(NCH)], reads=[wu_, hh], nparts=iw)
            k.act(sgt.h[0:iw, :], k.pb[0].h[0:iw, :], AF.Silu, reads=[k.pb[0]], writes=[sgt])
            k.tt("V", hbuf.h[0:iw, ic * 512:(ic + 1) * 512], sgt.h[0:iw, :],
                 k.pb[1].h[0:iw, :], ALU.mult, reads=[sgt, k.pb[1]], writes=[hbuf])
        for dp in range(NCH):
            bank = k.pb[2 + dp % 2]
            for ic in range(22):
                iw = ICW[ic]
                wd_ = wdb2[ic % 2]
                k.wload("gd_wd", wd_.h[0:iw, :],
                      W["gd_wd"][ic_off[ic]:ic_off[ic] + iw, dp * 128:(dp + 1) * 128],
                      writes=[wd_])
                k.mmop(bank, [(wd_.h[0:iw, :], hbuf.h[0:iw, ic * 512:(ic + 1) * 512])],
                       reads=[wd_, hbuf], accum=(ic > 0), close=(ic == 21))
            xt = k.X[dp]
            k.tt("V", xt.h[:, tsl], xt.h[:, tsl], bank.ap, ALU.add,
                 reads=[xt, bank], writes=[xt])
    k.release(mW, sW)
    k.release(mMid, sMid)
    k.release(m0, tiles)


STEP_FNS["l1_attn_b"] = step_l1_attn_b
STEP_WEIGHTS["l1_attn_b"] = {
    "gd_norms": (128, 2 * NCH), "gd_conv": (128, 5 * NCH),
    "gd_wq": (D, D), "gd_wk": (D, D), "gd_wv": (D, D),
    "gd_wb": (D, 16), "gd_wa": (D, 16), "gd_wo": (D, D),
    "gd_wg": (D, 2764), "gd_wu": (D, 2764), "gd_wd": (2764, D),
}
STEP_FNS["l1_attn_b"] = step_l1_attn_b


# version 13
# speedup vs baseline: 1.2065x; 1.1604x over previous
import os
import numpy as np
import concourse.bass as bass
import concourse.mybir as mybir
from concourse.bass_utils import run_bass_kernel_spmd

F32 = mybir.dt.float32
BF16 = mybir.dt.bfloat16
I32 = mybir.dt.int32
AF = mybir.ActivationFunctionType
ALU = mybir.AluOpType
AX = mybir.AxisListType

D = 1024
T = 2048
NCH = 8
PBYTES = 229000

ENGINES = ("T", "V", "S", "G", "Y")
DMA_DEPTH = 4
DMA_SEMS = {q: [f"d{q}{j}" for j in range(DMA_DEPTH)] for q in ("Y", "S", "G")}
ALL_DMA_SEMS = [s for v in DMA_SEMS.values() for s in v]


class Buf:
    __slots__ = ("name", "w", "rs", "pend", "serial")

    def __init__(self, name="", serial=False):
        self.name = name
        self.w = None
        self.rs = []
        self.pend = []
        self.serial = serial


class Tile:
    __slots__ = ("h", "buf")

    def __init__(self, h, buf):
        self.h = h
        self.buf = buf

    def __getitem__(self, idx):
        return self.h[idx]

    @property
    def ap(self):
        return self.h[:]


class Ker:
    def __init__(self, nc):
        self.nc = nc
        self.recs = {e: [] for e in ENGINES}
        self.sems = {}
        for s in list(ENGINES) + ALL_DMA_SEMS:
            self.sems[s] = nc.alloc_semaphore(name="sem_" + s)
        self.cnt = {s: 0 for s in self.sems}
        self.dmacnt = {q: 0 for q in DMA_SEMS}
        self.waited = {e: {} for e in ENGINES}
        self.nops = 0
        self.const_vals = {}
        self.cst = nc.alloc_sbuf_tensor("cst", [128, 64], F32)
        self.base = int(nc.sbuf_base) + 4 * 1024
        self.const(0.0)
        self.const(1.0)
        self.residue = []
        self.pb = [Tile(nc.alloc_psum_tensor(f"pb{i}", [128, 512], F32), Buf(f"pb{i}", serial=True))
                   for i in range(6)]
        self.pbh = [Tile(nc.alloc_psum_tensor(f"pbh{i}", [128, 1024], BF16), Buf(f"pbh{i}", serial=True))
                    for i in range(2)]
        self.init_tok = ("V", 1)
        self.cnt["V"] = 1

    def const(self, v):
        v = float(np.float32(v))
        if v in self.const_vals:
            return v
        i = len(self.const_vals)
        assert i < 64
        self.const_vals[v] = i
        self.nc.const_aps.register(F32, v, self.cst[:, i:i + 1])
        return v

    def cap(self, v):
        v = self.const(v)
        i = self.const_vals[v]
        return self.cst[:, i:i + 1]

    def alloc(self, name, shape, dtype, align=64):
        esz = 2 if dtype == BF16 else 4
        nbytes = int(np.prod(shape[1:])) * esz
        start = (self.base + align - 1) // align * align
        end = start + nbytes
        assert end <= PBYTES, f"SBUF overflow: {name} at {start}+{nbytes}"
        self.base = end
        self.nops2 = getattr(self, "nops2", 0) + 1
        h = self.nc.alloc_sbuf_tensor_at(f"{name}_{start}_{self.nops2}", shape, dtype,
                                         offset=start)
        b = Buf(name)
        for (lo, hi, toks) in self.residue:
            if lo < end and start < hi:
                b.pend.extend(toks)
        return Tile(h, b)

    def mark(self):
        return self.base

    def release(self, m, tiles):
        toks = []
        for t in tiles:
            b = t.buf
            if b.w:
                toks.append(b.w)
            toks.extend(b.rs)
        if self.base > m:
            self.residue.append((m, self.base, toks))
        self.base = m

    def op(self, eng, emit, reads=(), writes=(), waits=(), dma=False, n_inc=1):
        toks = [self.init_tok]
        for b in reads:
            b = b.buf if isinstance(b, Tile) else b
            if b.w:
                toks.append(b.w)
            if b.serial:
                toks.extend(b.rs)
            toks.extend(b.pend)
        for b in writes:
            b = b.buf if isinstance(b, Tile) else b
            if b.w:
                toks.append(b.w)
            toks.extend(b.rs)
            toks.extend(b.pend)
        toks.extend(waits)
        need = {}
        for (s, v) in toks:
            if v > need.get(s, -1):
                need[s] = v
        wl = []
        for s, v in sorted(need.items()):
            if v > self.waited[eng].get(s, -1) and v > 0:
                wl.append((s, v))
                self.waited[eng][s] = v
        if dma:
            i = self.dmacnt[eng]
            self.dmacnt[eng] += 1
            sem = DMA_SEMS[eng][i % DMA_DEPTH]
            gate = 16 * (i // DMA_DEPTH)
            if gate > self.waited[eng].get(sem, -1):
                wl.append((sem, gate))
                self.waited[eng][sem] = gate
            self.cnt[sem] += 16
            tok = (sem, self.cnt[sem])
            amt = 16
        else:
            sem = eng
            self.cnt[sem] += 1
            tok = (sem, self.cnt[sem])
            amt = 1
        self.recs[eng].append((wl, emit, self.sems[sem], amt))
        self.nops += 1
        for b in reads:
            b = b.buf if isinstance(b, Tile) else b
            b.rs.append(tok)
        for b in writes:
            b = b.buf if isinstance(b, Tile) else b
            b.w = tok
            b.rs = []
            b.pend = []
        return tok

    def dma(self, eng, dst, src, reads=(), writes=()):
        return self.op(eng, (lambda e, d=dst, s=src: e.dma_start(d, s)),
                       reads=reads, writes=writes, dma=True)

    def wload(self, nm, dst, src, writes=()):
        rd = [self.WB[nm]] if nm in getattr(self, "WB", {}) else []
        return self.dma("Y", dst, src, reads=rd, writes=writes)

    def tt(self, eng, out, a, b, alu, reads=(), writes=()):
        return self.op(eng, (lambda e, o=out, x=a, y=b, u=alu: e.tensor_tensor(o, x, y, u)),
                       reads=reads, writes=writes)

    def ts(self, eng, out, a, s1, op0, s2=None, op1=ALU.bypass, reads=(), writes=()):
        if isinstance(s1, float):
            s1 = float(np.float32(s1))
        if isinstance(s2, float):
            s2 = float(np.float32(s2))
        return self.op(eng, (lambda e, o=out, x=a, a1=s1, a2=s2, p0=op0, p1=op1:
                             e.tensor_scalar(o, x, a1, a2, p0, p1)),
                       reads=reads, writes=writes)

    def stt(self, eng, out, in0, scalar, in1, op0, op1, reads=(), writes=()):
        return self.op(eng, (lambda e, o=out, i0=in0, sc=scalar, i1=in1, p0=op0, p1=op1:
                             e.scalar_tensor_tensor(o, i0, sc, i1, p0, p1)),
                       reads=reads, writes=writes)

    def act(self, out, in_, fn, scale=1.0, bias=0.0, reads=(), writes=(), accum_out=None):
        if isinstance(scale, float):
            scale = self.const(scale)
        if isinstance(bias, float):
            bias = self.const(bias)
        return self.op("S", (lambda e, o=out, i=in_, f=fn, sc=scale, bi=bias, ac=accum_out:
                             e.activation(o, i, f, bias=bi, scale=sc, accum_out=ac)),
                       reads=reads, writes=writes)

    def copy(self, eng, out, in_, reads=(), writes=()):
        if eng == "S":
            return self.op("S", (lambda e, o=out, i=in_: e.copy(o, i)),
                           reads=reads, writes=writes)
        return self.op(eng, (lambda e, o=out, i=in_: e.tensor_copy(o, i)),
                       reads=reads, writes=writes)

    def memset(self, eng, ap, v, writes=()):
        return self.op(eng, (lambda e, a=ap, x=v: e.memset(a, x)), writes=writes)

    def red(self, eng, out, in_, alu, axis=AX.X, reads=(), writes=()):
        return self.op(eng, (lambda e, o=out, i=in_, u=alu, x=axis:
                             e.tensor_reduce(o, i, x, u)),
                       reads=reads, writes=writes)

    def recip(self, out, in_, reads=(), writes=()):
        return self.op("V", (lambda e, o=out, i=in_: e.reciprocal(o, i)),
                       reads=reads, writes=writes)

    def scan(self, eng, out, d0, d1, init, op0, op1, reads=(), writes=()):
        return self.op(eng, (lambda e, o=out, a=d0, b=d1, i=init, p0=op0, p1=op1:
                             e.tensor_tensor_scan(o, a, b, i, p0, p1)),
                       reads=reads, writes=writes)

    def mmop(self, bank, pairs, reads=(), accum=False, out_slice=None, nparts=128,
             close=True):
        ap = bank.h[0:nparts, :] if out_slice is None else bank.h[0:nparts, out_slice]

        def em(e, ap=ap, pairs=tuple(pairs), accum=accum, close=close):
            n = len(pairs)
            ins = None
            for i, (l, r) in enumerate(pairs):
                ins = e.matmul(ap, l, r, start=(i == 0 and not accum),
                               stop=(i == n - 1 and close))
            return ins
        rd = list(reads)
        if accum:
            rd.append(bank)
        return self.op("T", em, reads=rd, writes=[bank])

    def transp(self, bank, src_ap, ident_ap, out_slice, reads=(), nparts=128):
        return self.op("T", (lambda e, b=bank, s=src_ap, i=ident_ap, o=out_slice, np_=nparts:
                             e.transpose(b.h[0:np_, o], s, i)),
                       reads=list(reads), writes=[bank])

    def affsel(self, out, in_, pattern, cmp, fill, cm, base=0, reads=(), writes=()):
        return self.op("G", (lambda e, o=out, i=in_, p=pattern, c=cmp, f=fill, m=cm, b=base:
                             e.affine_select(o, i, p, c, f, base=b, channel_multiplier=m)),
                       reads=reads, writes=writes)

    def iota(self, out, pattern, cm=0, base=0, writes=()):
        return self.op("G", (lambda e, o=out, p=pattern, m=cm, b=base:
                             e.iota(o, p, base=b, channel_multiplier=m)),
                       writes=writes)

    def finalize(self):
        nc = self.nc
        recs = self.recs
        sems = self.sems
        cvals = sorted(self.const_vals.items(), key=lambda kv: kv[1])
        final = [(s, self.cnt[s]) for s in ALL_DMA_SEMS if self.cnt[s] > 0]

        with nc.Block() as block:
            @block.vector
            def _(v):
                ins = None
                for val, i in cvals:
                    ins = v.memset(self.cst[:, i:i + 1], val)
                if ins is None:
                    ins = v.memset(self.cst[:, 0:1], 0.0)
                ins.then_inc(sems["V"], 1)
                for (wl, emit, sem, amt) in recs["V"]:
                    for (s, val) in wl:
                        v.wait_ge(sems[s], val)
                    emit(v).then_inc(sem, amt)

            @block.scalar
            def _(s):
                for (wl, emit, sem, amt) in recs["S"]:
                    for (sn, val) in wl:
                        s.wait_ge(sems[sn], val)
                    emit(s).then_inc(sem, amt)
                for (sn, val) in final:
                    if sn.startswith("dS"):
                        s.wait_ge(sems[sn], val)

            @block.tensor
            def _(t):
                for (wl, emit, sem, amt) in recs["T"]:
                    for (sn, val) in wl:
                        t.wait_ge(sems[sn], val)
                    emit(t).then_inc(sem, amt)

            @block.gpsimd
            def _(g):
                for (wl, emit, sem, amt) in recs["G"]:
                    for (sn, val) in wl:
                        g.wait_ge(sems[sn], val)
                    emit(g).then_inc(sem, amt)
                for (sn, val) in final:
                    if sn.startswith("dG"):
                        g.wait_ge(sems[sn], val)

            @block.sync
            def _(y):
                for (wl, emit, sem, amt) in recs["Y"]:
                    for (sn, val) in wl:
                        y.wait_ge(sems[sn], val)
                    emit(y).then_inc(sem, amt)
                for (sn, val) in final:
                    if sn.startswith("dY"):
                        y.wait_ge(sems[sn], val)



def setup_consts(k):
    k.ident = k.alloc("ident", [128, 128], BF16)
    k.identf = k.alloc("identf", [128, 128], F32)
    k.ones_bf = k.alloc("ones_bf", [128, 128], BF16)
    k.ones_f = k.alloc("ones_f", [128, 128], F32)
    k.ones1 = k.alloc("ones1", [1, 128], F32)
    k.memset("V", k.ident.ap, 1.0, writes=[k.ident])
    k.memset("V", k.identf.ap, 1.0, writes=[k.identf])
    k.memset("V", k.ones_bf.ap, 1.0, writes=[k.ones_bf])
    k.memset("V", k.ones_f.ap, 1.0, writes=[k.ones_f])
    k.memset("V", k.ones1.ap, 1.0, writes=[k.ones1])
    k.affsel(k.ident.ap, k.ident.ap, [[1, 128]], ALU.is_equal, 0.0, -1,
             reads=[k.ident], writes=[k.ident])
    k.affsel(k.identf.ap, k.identf.ap, [[1, 128]], ALU.is_equal, 0.0, -1,
             reads=[k.identf], writes=[k.identf])
    k.e8sel = k.alloc("e8sel", [8, 8 * 128], F32)
    k.memset("V", k.e8sel.ap, 1.0, writes=[k.e8sel])
    k.affsel(k.e8sel.ap, k.e8sel.ap, [[1, 8 * 128]], ALU.is_ge, 0.0, -128,
             reads=[k.e8sel], writes=[k.e8sel])
    k.affsel(k.e8sel.ap, k.e8sel.ap, [[-1, 8 * 128]], ALU.is_ge, 0.0, 128, base=127,
             reads=[k.e8sel], writes=[k.e8sel])


def emit_layernorm(k, xtiles, out_bf, scr, eps=1e-5):
    x2s, mb, rb = scr["x2s"], scr["mb"], scr["rb"]
    t0, t1 = scr["t0"], scr["t1"]
    for tc in range(T // 512):
        sl = slice(tc * 512, (tc + 1) * 512)
        xs = x2s[tc % 2]
        for c in range(NCH):
            k.act(xs.h[:, c * 512:(c + 1) * 512], xtiles[c].h[:, sl], AF.Square,
                  reads=[xtiles[c]], writes=[xs])
        k.mmop(k.pb[0], [(k.ones_f.ap, xtiles[c].h[:, sl]) for c in range(NCH)],
               reads=[k.ones_f] + list(xtiles))
        k.ts("V", mb.h[:, sl], k.pb[0].ap, 1.0 / D, ALU.mult, reads=[k.pb[0]], writes=[mb])
        k.mmop(k.pb[1], [(k.ones_bf.ap, xs.h[:, c * 512:(c + 1) * 512])
                         for c in range(NCH)], reads=[k.ones_bf, xs])
        k.ts("V", rb.h[:, sl], k.pb[1].ap, 1.0 / D, ALU.mult, reads=[k.pb[1]], writes=[rb])
    k.tt("G", t0.ap, mb.ap, mb.ap, ALU.mult, reads=[mb], writes=[t0])
    k.tt("V", rb.ap, rb.ap, t0.ap, ALU.subtract, reads=[rb, t0], writes=[rb])
    k.act(t0.ap, rb.ap, AF.Sqrt, bias=k.const(eps), reads=[rb], writes=[t0])
    k.recip(rb.ap, t0.ap, reads=[t0], writes=[rb])
    for c in range(NCH):
        csl = slice(c * T, (c + 1) * T)
        tm = t0 if c % 2 == 0 else t1
        k.tt("G", tm.ap, xtiles[c].ap, mb.ap, ALU.subtract,
             reads=[xtiles[c], mb], writes=[tm])
        k.tt("V", out_bf.h[:, csl], tm.ap, rb.ap, ALU.mult,
             reads=[tm, rb], writes=[out_bf])
    return out_bf


def emit_ln_stats(k, xtiles, scr, eps=1e-5):
    x2s, mb, rb = scr["x2s"], scr["mb"], scr["rb"]
    t0 = scr["t0"]
    for tc in range(T // 512):
        sl = slice(tc * 512, (tc + 1) * 512)
        xs = x2s[tc % 2]
        for c in range(NCH):
            k.act(xs.h[:, c * 512:(c + 1) * 512], xtiles[c].h[:, sl], AF.Square,
                  reads=[xtiles[c]], writes=[xs])
        k.mmop(k.pb[0], [(k.ones_f.ap, xtiles[c].h[:, sl]) for c in range(NCH)],
               reads=[k.ones_f] + list(xtiles))
        k.ts("V", mb.h[:, sl], k.pb[0].ap, 1.0 / D, ALU.mult, reads=[k.pb[0]], writes=[mb])
        k.mmop(k.pb[1], [(k.ones_bf.ap, xs.h[:, c * 512:(c + 1) * 512])
                         for c in range(NCH)], reads=[k.ones_bf, xs])
        k.ts("V", rb.h[:, sl], k.pb[1].ap, 1.0 / D, ALU.mult, reads=[k.pb[1]], writes=[rb])
    k.tt("G", t0.ap, mb.ap, mb.ap, ALU.mult, reads=[mb], writes=[t0])
    k.tt("V", rb.ap, rb.ap, t0.ap, ALU.subtract, reads=[rb, t0], writes=[rb])
    k.act(t0.ap, rb.ap, AF.Sqrt, bias=k.const(eps), reads=[rb], writes=[t0])
    k.recip(rb.ap, t0.ap, reads=[t0], writes=[rb])


def ln_scratch(k, A):
    x2 = A("lnx2a", [128, NCH * 512], BF16)
    t0 = A("lnt0", [128, T], F32)
    return dict(x2s=[x2, x2],
                mb=A("lnmb", [128, T], F32), rb=A("lnrb", [128, T], F32),
                t0=t0, t1=t0)



def step_moe(k, W):
    m0 = k.mark()
    tiles = []

    def A(name, shape, dtype):
        t = k.alloc(name, shape, dtype)
        tiles.append(t)
        return t

    xn = A("xn", [128, NCH * T], BF16)
    combf = A("combf", [8, T], F32)
    m1_scr = k.mark()
    stiles = []

    def A2(name, shape, dtype):
        t = k.alloc(name, shape, dtype)
        stiles.append(t)
        return t

    scr = ln_scratch(k, A2)
    emit_layernorm(k, k.X, xn, scr)

    A_outer = A
    A = A2
    gw = A("gw", [128, NCH * 8], BF16)
    k.wload("moe_gate_wT", gw.h[:].rearrange("p (c e) -> p c e", e=8),
          W["moe_gate_wT"].rearrange("(c p) e -> p c e", p=128), writes=[gw])
    logit = A("logit", [128, 16 * 8], F32)
    comb = A("comb", [128, 16 * 8], F32)
    for tt_ in range(16):
        k.mmop(k.pb[0], [(xn.h[:, c * T + tt_ * 128: c * T + tt_ * 128 + 128],
                          gw.h[:, c * 8:(c + 1) * 8]) for c in range(NCH)],
               reads=[xn, gw], out_slice=slice(0, 8))
        k.copy("V", logit.h[:, tt_ * 8:(tt_ + 1) * 8], k.pb[0].h[:, 0:8],
               reads=[k.pb[0]], writes=[logit])
    lg3 = logit.h[:].rearrange("p (t e) -> p t e", e=8)
    mx = A("mx", [128, 16], F32)
    sm = A("sm", [128, 16], F32)
    k.red("V", mx.ap, lg3, ALU.max, reads=[logit], writes=[mx])
    bc8 = bass.AP(mx.h, 0, [[16, 128], [1, 16], [0, 8]])
    k.tt("V", lg3, lg3, bc8, ALU.subtract, reads=[logit, mx], writes=[logit])
    k.act(logit.ap, logit.ap, AF.Exp, reads=[logit], writes=[logit])
    k.red("V", sm.ap, lg3, ALU.add, reads=[logit], writes=[sm])
    k.recip(sm.ap, sm.ap, reads=[sm], writes=[sm])
    bcs = bass.AP(sm.h, 0, [[16, 128], [1, 16], [0, 8]])
    k.tt("V", lg3, lg3, bcs, ALU.mult, reads=[logit, sm], writes=[logit])
    g4 = A("g4", [128, 16 * 4], F32)
    g43 = g4.h[:].rearrange("p (t g) -> p t g", g=4)
    k.red("V", g43, logit.h[:].rearrange("p (t g e) -> p t g e", g=4, e=2),
          ALU.max, reads=[logit], writes=[g4])
    m1 = A("m1", [128, 16], F32)
    m2 = A("m2", [128, 16], F32)
    msk = A("msk", [128, 16 * 4], F32)
    tmp4 = A("tmp4", [128, 16 * 4], F32)
    k.red("V", m1.ap, g43, ALU.max, reads=[g4], writes=[m1])
    bc4 = bass.AP(m1.h, 0, [[16, 128], [1, 16], [0, 4]])
    k.tt("V", msk.h[:].rearrange("p (t g) -> p t g", g=4), g43, bc4, ALU.is_lt,
         reads=[g4, m1], writes=[msk])
    k.tt("V", tmp4.ap, g4.ap, msk.ap, ALU.mult, reads=[g4, msk], writes=[tmp4])
    k.red("V", m2.ap, tmp4.h[:].rearrange("p (t g) -> p t g", g=4), ALU.max,
          reads=[tmp4], writes=[m2])
    bcm2 = bass.AP(m2.h, 0, [[16, 128], [1, 16], [0, 4]])
    k.tt("V", msk.h[:].rearrange("p (t g) -> p t g", g=4), g43, bcm2, ALU.is_ge,
         reads=[g4, m2], writes=[msk])
    smask = bass.AP(msk.h, 0, [[64, 128], [4, 16], [1, 4], [0, 2]])
    k.tt("V", comb.h[:].rearrange("p (t g e) -> p t g e", g=4, e=2),
         logit.h[:].rearrange("p (t g e) -> p t g e", g=4, e=2), smask, ALU.mult,
         reads=[logit, msk], writes=[comb])
    c8 = comb.h[:].rearrange("p (t e) -> p t e", e=8)
    e8 = A("e8", [128, 16 * 8], F32)
    k.red("V", m1.ap, c8, ALU.max, reads=[comb], writes=[m1])
    bce = bass.AP(m1.h, 0, [[16, 128], [1, 16], [0, 8]])
    k.tt("V", e8.h[:].rearrange("p (t e) -> p t e", e=8), c8, bce, ALU.is_lt,
         reads=[comb, m1], writes=[e8])
    k.tt("V", e8.ap, e8.ap, comb.ap, ALU.mult, reads=[e8, comb], writes=[e8])
    k.red("V", m2.ap, e8.h[:].rearrange("p (t e) -> p t e", e=8), ALU.max,
          reads=[e8], writes=[m2])
    bcm2e = bass.AP(m2.h, 0, [[16, 128], [1, 16], [0, 8]])
    k.tt("V", e8.h[:].rearrange("p (t e) -> p t e", e=8), c8, bcm2e, ALU.is_ge,
         reads=[comb, m2], writes=[e8])
    k.tt("V", m1.ap, m1.ap, m2.ap, ALU.add, reads=[m1, m2], writes=[m1])
    k.recip(m1.ap, m1.ap, reads=[m1], writes=[m1])
    k.tt("V", comb.ap, comb.ap, e8.ap, ALU.mult, reads=[comb, e8], writes=[comb])
    bcn = bass.AP(m1.h, 0, [[16, 128], [1, 16], [0, 8]])
    k.tt("V", c8, c8, bcn, ALU.mult, reads=[comb, m1], writes=[comb])
    for tt_ in range(16):
        k.transp(k.pb[2], comb.h[:, tt_ * 8:(tt_ + 1) * 8], k.identf.ap,
                 slice(0, 128), reads=[comb, k.identf], nparts=8)
        k.copy("V", combf.h[0:8, tt_ * 128:(tt_ + 1) * 128], k.pb[2].h[0:8, 0:128],
               reads=[k.pb[2]], writes=[combf])
    combf_b = combf

    k.release(m1_scr, stiles)
    A = A_outer

    wg_b = [A(f"wgb{i}", [128, NCH * 128], BF16) for i in range(2)]
    wu_b = [A(f"wub{i}", [128, NCH * 128], BF16) for i in range(2)]
    w2e = A("w2e", [128, 16 * D], BF16)
    actt = A("actt", [128, 8 * T], BF16)
    cbf = A("cbf", [128, 4 * 512], BF16)
    sil2 = [A(f"sil{i}", [128, 512], F32) for i in range(4)]
    for e in range(8):
        k.wload("moe_w2T", w2e.h[:].rearrange("p (kc n) -> p kc n", n=D),
              W["moe_w2T"][e].rearrange("(kc p) n -> p kc n", p=128), writes=[w2e])
        for tc4 in range(4):
            k.mmop(k.pb[2 + tc4 % 2], [(k.e8sel.h[0:8, e * 128:(e + 1) * 128],
                                        combf.h[0:8, tc4 * 512:(tc4 + 1) * 512])],
                   reads=[k.e8sel, combf])
            k.copy("V", cbf.h[:, tc4 * 512:(tc4 + 1) * 512], k.pb[2 + tc4 % 2].ap,
                   reads=[k.pb[2 + tc4 % 2]], writes=[cbf])
        for half in range(2):
            for ib8 in range(8):
                ib = half * 8 + ib8
                wg = wg_b[ib % 2]
                wu = wu_b[ib % 2]
                k.wload("moe_w13T", wg.h[:].rearrange("p (c i) -> p c i", i=128),
                      W["moe_w13T"][e, :, ib * 128:(ib + 1) * 128].rearrange(
                          "(c p) i -> p c i", p=128), writes=[wg])
                k.wload("moe_w13T", wu.h[:].rearrange("p (c i) -> p c i", i=128),
                      W["moe_w13T"][e, :, 2048 + ib * 128: 2048 + (ib + 1) * 128].rearrange(
                          "(c p) i -> p c i", p=128), writes=[wu])
                for tcc in range(4):
                    tsl = slice(tcc * 512, (tcc + 1) * 512)
                    bg = k.pb[(tcc % 2) * 2]
                    bu = k.pb[(tcc % 2) * 2 + 1]
                    sil = sil2[tcc % 4]
                    k.mmop(bg, [(wg.h[:, c * 128:(c + 1) * 128],
                                 xn.h[:, c * T + tcc * 512: c * T + tcc * 512 + 512])
                                for c in range(NCH)], reads=[wg, xn])
                    k.mmop(bu, [(wu.h[:, c * 128:(c + 1) * 128],
                                 xn.h[:, c * T + tcc * 512: c * T + tcc * 512 + 512])
                                for c in range(NCH)], reads=[wu, xn])
                    k.act(sil.ap, bg.ap, AF.Silu, reads=[bg], writes=[sil])
                    k.tt("V", sil.ap, sil.ap, bu.ap, ALU.mult, reads=[sil, bu], writes=[sil])
                    k.tt("G", actt.h[:, ib8 * T + tcc * 512: ib8 * T + tcc * 512 + 512],
                         sil.ap, cbf.h[:, tsl], ALU.mult, reads=[sil, cbf], writes=[actt])
            for tcc in range(4):
                for dp in range(NCH):
                    bank = k.pb[4 + dp % 2]
                    k.mmop(bank, [(w2e.h[:, (half * 8 + kc) * D + dp * 128:
                                          (half * 8 + kc) * D + dp * 128 + 128],
                                   actt.h[:, kc * T + tcc * 512: kc * T + tcc * 512 + 512])
                                  for kc in range(8)], reads=[w2e, actt])
                    xt = k.X[dp]
                    k.tt("V", xt.h[:, tcc * 512:(tcc + 1) * 512],
                         xt.h[:, tcc * 512:(tcc + 1) * 512], bank.ap, ALU.add,
                         reads=[xt, bank], writes=[xt])
    k.release(m0, tiles)



STEP_FNS = {}
STEP_FNS["moe"] = step_moe

STEP_WEIGHTS = {
    "moe": {"moe_gate_wT": (D, 8), "moe_w13T": (8, D, 4096), "moe_w2T": (8, 2048, D)},
}


F32_WEIGHTS = {"l0a_lns", "l0a_cms_b", "mam_conv", "mam_db", "mam_alog",
               "gd_norms", "gd_conv", "fp4_wg", "fp4_wu", "fp4_wd"}


def build_program(steps):
    nc = bass.Bass("TRN2", target_bir_lowering=False)
    xin = nc.dram_tensor("xT", [D, T], F32, kind="ExternalInput").ap()
    yout = nc.dram_tensor("y", [D, T], F32, kind="ExternalOutput").ap()
    W = {}
    seen = set()
    order = []
    for s in steps:
        for nm, shp in STEP_WEIGHTS[s].items():
            if nm not in seen:
                seen.add(nm)
                W[nm] = nc.dram_tensor(nm, list(shp), F32, kind="ExternalInput").ap()
                order.append((nm, shp))
    k = Ker(nc)
    setup_consts(k)
    k.X = [k.alloc(f"x{c}", [128, T], F32) for c in range(NCH)]
    for c in range(NCH):
        k.dma("Y", k.X[c].ap, xin[c * 128:(c + 1) * 128, :], writes=[k.X[c]])
    W2 = dict(W)
    k.WB = {}
    for nm, shp in order:
        if nm in F32_WEIGHTS:
            continue
        flat = int(np.prod(shp))
        assert flat % 128 == 0
        cols = flat // 128
        tw = nc.dram_tensor(nm + "_bf", list(shp), BF16, kind="Internal").ap()
        twb = Buf(nm + "_bf")
        W2[nm] = tw
        k.WB[nm] = twb
        sflat = bass.AP(W[nm].tensor, 0, [[cols, 128], [1, cols]])
        dflat = bass.AP(tw.tensor, 0, [[cols, 128], [1, cols]])
        k.dma("G", dflat, sflat, writes=[twb])
    for s in steps:
        STEP_FNS[s](k, W2)
    for c in range(NCH):
        k.dma("Y", yout[c * 128:(c + 1) * 128, :], k.X[c].ap, reads=[k.X[c]])
    k.finalize()
    return nc


def host_weights(inputs, steps):
    out = {}
    for s in steps:
        if s == "l1_attn_b":
            ii = inputs
            nw = np.stack([np.asarray(ii["L1_attn_b_norm1_w"]), np.asarray(ii["L1_attn_b_norm2_w"])])
            out["gd_norms"] = np.ascontiguousarray(nw.reshape(2, 8, 128).transpose(2, 0, 1).reshape(128, 16))
            cw = np.asarray(ii["L1_attn_b_conv_w"])
            cb = np.asarray(ii["L1_attn_b_conv_b"])[:, None]
            cc_ = np.concatenate([cw, cb], axis=1)
            out["gd_conv"] = np.ascontiguousarray(cc_.reshape(8, 128, 5).transpose(2, 1, 0).reshape(5 * 128, 8).T) if False else np.ascontiguousarray(cc_.reshape(8, 128, 5).transpose(1, 2, 0).reshape(128, 40))
            for nm, inm in (("gd_wq", "L1_attn_b_wq"), ("gd_wk", "L1_attn_b_wk"),
                            ("gd_wv", "L1_attn_b_wv"), ("gd_wb", "L1_attn_b_wb"),
                            ("gd_wa", "L1_attn_b_wa"), ("gd_wo", "L1_attn_b_wo"),
                            ("gd_wg", "L1_attn_b_wg"), ("gd_wu", "L1_attn_b_wu"),
                            ("gd_wd", "L1_attn_b_wd")):
                out[nm] = np.ascontiguousarray(np.asarray(ii[inm]))
        if s == "l0_attn_b":
            ii = inputs
            out["mam_win"] = np.ascontiguousarray(np.asarray(ii["L0_attn_b_W_in"]))
            cw = np.asarray(ii["L0_attn_b_conv_w"])[:, 0, :]
            out["mam_conv"] = np.ascontiguousarray(cw.reshape(8, 128, 3).transpose(1, 0, 2).reshape(128, 24))
            out["mam_wbc"] = np.ascontiguousarray(np.concatenate(
                [np.asarray(ii["L0_attn_b_W_B"]), np.asarray(ii["L0_attn_b_W_C"])], axis=1))
            out["mam_wd"] = np.ascontiguousarray(np.asarray(ii["L0_attn_b_W_delta"]))
            out["mam_db"] = np.ascontiguousarray(
                np.asarray(ii["L0_attn_b_delta_base"]).reshape(8, 128).T)
            out["mam_wt"] = np.ascontiguousarray(np.asarray(ii["L0_attn_b_W_tau"]))
            out["mam_alog"] = np.ascontiguousarray(
                np.asarray(ii["L0_attn_b_A_log"]).reshape(8, 128, 16).transpose(1, 0, 2).reshape(128, 128))
            out["mam_wout"] = np.ascontiguousarray(np.asarray(ii["L0_attn_b_W_out"]))
        if s == "l0_attn_a":
            ii = inputs
            lns = np.stack([np.asarray(ii["L0_attn_a_ln1_g"]), np.asarray(ii["L0_attn_a_ln1_b"]),
                            np.asarray(ii["L0_attn_a_ln2_g"]), np.asarray(ii["L0_attn_a_ln2_b"])])
            out["l0a_lns"] = np.ascontiguousarray(
                lns.reshape(4, 8, 128).transpose(2, 0, 1).reshape(128, 32))
            out["l0a_wq"] = np.ascontiguousarray(np.asarray(ii["L0_attn_a_wq"]))
            out["l0a_wk"] = np.ascontiguousarray(np.asarray(ii["L0_attn_a_wk"]))
            out["l0a_wv"] = np.ascontiguousarray(np.asarray(ii["L0_attn_a_wv"]))
            out["l0a_wo"] = np.ascontiguousarray(np.asarray(ii["L0_attn_a_wo"]))
            out["l0a_cms_w1"] = np.ascontiguousarray(np.asarray(ii["L0_attn_a_cms_w1"]))
            out["l0a_cms_w2"] = np.ascontiguousarray(np.asarray(ii["L0_attn_a_cms_w2"]))
            b1 = np.asarray(ii["L0_attn_a_cms_b1"]).reshape(3, 32, 128).transpose(0, 2, 1)
            b2 = np.asarray(ii["L0_attn_a_cms_b2"]).reshape(3, 8, 128).transpose(0, 2, 1)
            out["l0a_cms_b"] = np.ascontiguousarray(np.concatenate([b1, b2], axis=2))
        if s == "l1_attn_a":
            out["att_wq"] = np.ascontiguousarray(np.asarray(inputs["L1_attn_a_wq"]).reshape(1024, 1024))
            out["att_wk"] = np.ascontiguousarray(np.asarray(inputs["L1_attn_a_wk"]).reshape(1024, 1024))
            out["att_wv"] = np.ascontiguousarray(np.asarray(inputs["L1_attn_a_wv"]).reshape(1024, 1024))
            out["att_wo"] = np.ascontiguousarray(np.asarray(inputs["L1_attn_a_wo"]).reshape(1024, 1024))
        if s == "l1_ffn":
            out["fp4_wg"] = np.ascontiguousarray(np.asarray(inputs["L1_ffn_w_gate"]))
            out["fp4_wu"] = np.ascontiguousarray(np.asarray(inputs["L1_ffn_w_up"]))
            out["fp4_wd"] = np.ascontiguousarray(np.asarray(inputs["L1_ffn_w_down"]))
        if s == "moe":
            out["moe_gate_wT"] = np.ascontiguousarray(np.asarray(inputs["L0_ffn_gate_w"]).T)
            out["moe_w13T"] = np.ascontiguousarray(np.asarray(inputs["L0_ffn_w13_weight"]).transpose(0, 2, 1))
            out["moe_w2T"] = np.ascontiguousarray(np.asarray(inputs["L0_ffn_w2_weight"]).transpose(0, 2, 1))
    return out


MODULE_STEPS = ["l0_attn_a", "moe", "l0_attn_b", "moe",
                "l1_attn_a", "l1_ffn", "l1_attn_b", "l1_ffn"]
FUSE = True


def kernel(**inputs):
    x = np.asarray(inputs["L0_attn_a_x"], np.float32)
    B = x.shape[0]
    if FUSE:
        groups = [MODULE_STEPS]
    else:
        groups = [[s] for s in MODULE_STEPS]
    xTs = [np.ascontiguousarray(x[b].T) for b in range(B)]
    for steps in groups:
        nc = build_program(steps)
        wmap = host_weights(inputs, steps)
        in_maps = [dict(xT=xTs[b], **wmap) for b in range(B)]
        res = run_bass_kernel_spmd(nc, in_maps, core_ids=list(range(B)))
        xTs = [np.asarray(res.results[b]["y"], np.float32) for b in range(B)]
    return np.ascontiguousarray(np.stack([xt.T for xt in xTs], 0))



QMAX = 1408


def emit_fp4_quant(k, out_ap_fn, in_ap_fn, scrs, ncols, in_tiles, out_tiles):
    for lo in range(0, ncols, QMAX):
        n = min(QMAX, ncols - lo)
        nblk = n // 16
        k.qpar = getattr(k, "qpar", 0) ^ 1
        scr = scrs[k.qpar]
        absx, acc, tmp, mx = scr["absx"], scr["acc"], scr["tmp"], scr["mx"]
        in_ap = in_ap_fn(lo, n)
        out_ap = out_ap_fn(lo, n)
        aab = absx.h[:, 0:n]
        acb = acc.h[:, 0:n]
        tmb = tmp.h[:, 0:n]
        mxb = mx.h[:, 0:nblk]
        k.act(aab, in_ap, AF.Abs, reads=in_tiles, writes=[absx])
        k.red("V", mxb, absx.h[:, 0:n].rearrange("p (b s) -> p b s", s=16),
              ALU.max, reads=[absx], writes=[mx])
        k.ts("V", mxb, mxb, k.const(1.0 / 6.0), ALU.mult, s2=k.const(1e-12),
             op1=ALU.max, reads=[mx], writes=[mx])
        k.recip(mxb, mxb, reads=[mx], writes=[mx])
        rbc = bass.AP(mx.h, 0, [[mx.h.shape[1], 128], [1, nblk], [0, 16]])
        k.tt("G", aab, aab, rbc, ALU.mult, reads=[absx, mx], writes=[absx])
        k.ts("V", acb, aab, 0.25, ALU.is_gt, reads=[absx], writes=[acc])
        for thr in (0.75, 1.25, 1.75):
            k.ts("V", tmb, aab, k.const(thr), ALU.is_gt, reads=[absx], writes=[tmp])
            k.tt("G", acb, acb, tmb, ALU.add, reads=[acc, tmp], writes=[acc])
        k.ts("V", tmb, aab, k.const(2.5), ALU.is_gt, reads=[absx], writes=[tmp])
        k.stt("V", acb, acb, 0.5, tmb, ALU.mult, ALU.add, reads=[acc, tmp], writes=[acc])
        k.ts("V", tmb, aab, k.const(3.5), ALU.is_gt, reads=[absx], writes=[tmp])
        k.tt("G", acb, acb, tmb, ALU.add, reads=[acc, tmp], writes=[acc])
        k.ts("V", tmb, aab, k.const(5.0), ALU.is_gt, reads=[absx], writes=[tmp])
        k.stt("V", acb, tmb, 2.0, acb, ALU.mult, ALU.add, reads=[acc, tmp], writes=[acc])
        k.recip(mxb, mxb, reads=[mx], writes=[mx])
        k.tt("V", acb, acb, rbc, ALU.mult, reads=[acc, mx], writes=[acc])
        k.op("S", (lambda e, o=tmb, i=in_ap: e.sign(o, i)), reads=in_tiles, writes=[tmp])
        k.tt("G", out_ap, acb, tmb, ALU.mult, reads=[acc, tmp], writes=out_tiles)


def quant_weight_to_scratch(k, W, qscr, nat_name, scratch_ap, scratch_buf,
                            nrows, ncols, A):
    nrc = nrows // 128
    ncc = ncols // 128
    fch = [A(f"fq{i}", [128, ncols], F32) for i in range(2)]
    qch = A("qn", [128, ncols], BF16)
    tev = A("te", [128, 8 * 128], BF16)
    for rc in range(nrc):
        f = fch[rc % 2]
        k.dma("Y", f.ap, W[nat_name][rc * 128:(rc + 1) * 128, :], writes=[f])
        emit_fp4_quant(k, lambda lo, n, q=qch: q.h[:, lo:lo + n],
                       lambda lo, n, f=f: f.h[:, lo:lo + n],
                       qscr, ncols, [f], [qch])
        for g0 in range(0, ncc, 8):
            gn = min(8, ncc - g0)
            for j in range(gn):
                k.transp(k.pbh[0], qch.h[:, (g0 + j) * 128:(g0 + j + 1) * 128],
                         k.ident.ap, slice(j * 128, (j + 1) * 128), reads=[qch, k.ident])
            k.copy("V", tev.h[:, 0:gn * 128], k.pbh[0].h[:, 0:gn * 128],
                   reads=[k.pbh[0]], writes=[tev])
            k.dma("S", scratch_ap[g0 * 128:(g0 + gn) * 128,
                                  rc * 128:(rc + 1) * 128].rearrange(
                "(cc p) r -> p cc r", p=128),
                tev.h[:, 0:gn * 128].rearrange("p (cc r) -> p cc r", r=128),
                reads=[tev], writes=[scratch_buf])


def step_l1_ffn(k, W):
    IN = 2816
    ICC = IN // 128
    m0 = k.mark()
    tiles = []

    def A(name, shape, dtype):
        t = k.alloc(name, shape, dtype)
        tiles.append(t)
        return t

    xq = A("xq", [128, NCH * T], BF16)
    qscr = [dict(absx=A(f"qabs{i}", [128, QMAX], F32),
                 acc=A(f"qacc{i}", [128, QMAX], BF16),
                 tmp=A(f"qtmp{i}", [128, QMAX], BF16),
                 mx=A(f"qmx{i}", [128, QMAX // 16], F32)) for i in range(2)]

    m1 = k.mark()
    s0 = []

    def A0(name, shape, dtype):
        t = k.alloc(name, shape, dtype)
        s0.append(t)
        return t
    xn = A0("xn", [128, NCH * T], BF16)
    scr = ln_scratch(k, A0)
    emit_layernorm(k, k.X, xn, scr)
    xtm = A0("xtm", [128, D], BF16)
    qtm = A0("qtm", [128, D], BF16)
    for it in range(16):
        for j in range(NCH):
            k.transp(k.pbh[0], xn.h[:, j * T + it * 128: j * T + it * 128 + 128],
                     k.ident.ap, slice(j * 128, (j + 1) * 128), reads=[xn, k.ident])
        k.copy("V", xtm.ap, k.pbh[0].ap, reads=[k.pbh[0]], writes=[xtm])
        emit_fp4_quant(k, lambda lo, n, q=qtm: q.h[:, lo:lo + n],
                       lambda lo, n, x=xtm: x.h[:, lo:lo + n],
                       qscr, D, [xtm], [qtm])
        for j in range(NCH):
            k.transp(k.pbh[1], qtm.h[:, j * 128:(j + 1) * 128],
                     k.ident.ap, slice(j * 128, (j + 1) * 128), reads=[qtm, k.ident])
        k.op("V", (lambda e, o=xq, it=it: e.tensor_copy(
            bass.AP(o.h, it * 128, [[NCH * T, 128], [T, NCH], [1, 128]]),
            k.pbh[1].h[:].rearrange("p (c r) -> p c r", r=128))),
            reads=[k.pbh[1]], writes=[xq])
    k.release(m1, s0)

    if not hasattr(k, "fp4_done"):
        k.fp4_done = True
        nc = k.nc
        k.qwg_s = nc.dram_tensor("qwg_s", [D, IN], BF16, kind="Internal").ap()
        k.qwu_s = nc.dram_tensor("qwu_s", [D, IN], BF16, kind="Internal").ap()
        k.qwd_s = nc.dram_tensor("qwd_s", [IN, D], BF16, kind="Internal").ap()
        k.qwg_b, k.qwu_b, k.qwd_b = Buf("qwg_s"), Buf("qwu_s"), Buf("qwd_s")
        m2 = k.mark()
        wt = []

        def A1(name, shape, dtype):
            t = k.alloc(name, shape, dtype)
            wt.append(t)
            return t
        quant_weight_to_scratch(k, W, qscr, "fp4_wg", k.qwg_s, k.qwg_b, IN, D, A1)
        k.release(m2, wt)
        wt = []
        quant_weight_to_scratch(k, W, qscr, "fp4_wu", k.qwu_s, k.qwu_b, IN, D, A1)
        k.release(m2, wt)
        wt = []
        quant_weight_to_scratch(k, W, qscr, "fp4_wd", k.qwd_s, k.qwd_b, D, IN, A1)
        k.release(m2, wt)

    wb = [A(f"fw{i}", [128, NCH * 128], BF16) for i in range(4)]
    htm = [A(f"htm{i}", [128, IN], BF16) for i in range(4)]
    qh = A("qh", [128, ICC * 512], BF16)
    hst2 = [A(f"hst{i}", [128, 512], BF16) for i in range(2)]
    hfm2 = [A(f"hfm{i}", [128, 512], F32) for i in range(2)]
    dwb = [A(f"dw{i}", [128, ICC * 128], BF16) for i in range(2)]
    for tc in range(4):
        tb = slice(tc * 512, (tc + 1) * 512)
        for ic in range(ICC):
            wg = wb[(ic % 2) * 2]
            wu = wb[(ic % 2) * 2 + 1]
            k.dma("Y", wg.h[:].rearrange("p (c i) -> p c i", i=128),
                  k.qwg_s[:, ic * 128:(ic + 1) * 128].rearrange("(c p) i -> p c i", p=128),
                  reads=[k.qwg_b], writes=[wg])
            k.dma("Y", wu.h[:].rearrange("p (c i) -> p c i", i=128),
                  k.qwu_s[:, ic * 128:(ic + 1) * 128].rearrange("(c p) i -> p c i", p=128),
                  reads=[k.qwu_b], writes=[wu])
            bg = k.pb[(ic % 2) * 2]
            bu = k.pb[(ic % 2) * 2 + 1]
            hfm = hfm2[ic % 2]
            hst = hst2[ic % 2]
            k.mmop(bg, [(wg.h[:, c * 128:(c + 1) * 128],
                         xq.h[:, c * T + tc * 512: c * T + tc * 512 + 512])
                        for c in range(NCH)], reads=[wg, xq])
            k.mmop(bu, [(wu.h[:, c * 128:(c + 1) * 128],
                         xq.h[:, c * T + tc * 512: c * T + tc * 512 + 512])
                        for c in range(NCH)], reads=[wu, xq])
            k.act(hfm.ap, bg.ap, AF.Silu, reads=[bg], writes=[hfm])
            k.tt("V", hfm.ap, hfm.ap, bu.ap, ALU.mult, reads=[hfm, bu], writes=[hfm])
            k.copy("G", hst.ap, hfm.ap, reads=[hfm], writes=[hst])
            for j in range(4):
                k.transp(k.pbh[ic % 2], hst.h[:, j * 128:(j + 1) * 128], k.ident.ap,
                         slice(j * 128, (j + 1) * 128), reads=[hst, k.ident])
            for j in range(4):
                k.copy("V", htm[j].h[:, ic * 128:(ic + 1) * 128],
                       k.pbh[ic % 2].h[:, j * 128:(j + 1) * 128],
                       reads=[k.pbh[ic % 2]], writes=[htm[j]])
        for j in range(4):
            emit_fp4_quant(k, lambda lo, n, h=htm[j]: h.h[:, lo:lo + n],
                           lambda lo, n, h=htm[j]: h.h[:, lo:lo + n],
                           qscr, IN, [htm[j]], [htm[j]])
            for g0 in range(0, ICC, 8):
                gn = min(8, ICC - g0)
                for u in range(gn):
                    k.transp(k.pbh[1], htm[j].h[:, (g0 + u) * 128:(g0 + u + 1) * 128],
                             k.ident.ap, slice(u * 128, (u + 1) * 128),
                             reads=[htm[j], k.ident])
                k.op("V", (lambda e, o=qh, g0=g0, gn=gn, j=j: e.tensor_copy(
                    bass.AP(o.h, g0 * 512 + j * 128,
                            [[ICC * 512, 128], [512, gn], [1, 128]]),
                    k.pbh[1].h[:, 0:gn * 128].rearrange("p (c r) -> p c r", r=128))),
                    reads=[k.pbh[1]], writes=[qh])
        for dp in range(NCH):
            dw = dwb[dp % 2]
            k.dma("Y", dw.h[:].rearrange("p (c i) -> p c i", i=128),
                  k.qwd_s[:, dp * 128:(dp + 1) * 128].rearrange("(c p) i -> p c i", p=128),
                  reads=[k.qwd_b], writes=[dw])
            k.mmop(k.pb[4 + dp % 2], [(dw.h[:, c * 128:(c + 1) * 128],
                                       qh.h[:, c * 512:(c + 1) * 512])
                                      for c in range(ICC)], reads=[dw, qh])
            xt = k.X[dp]
            k.tt("V", xt.h[:, tb], xt.h[:, tb], k.pb[4 + dp % 2].ap, ALU.add,
                 reads=[xt, k.pb[4 + dp % 2]], writes=[xt])
    k.release(m0, tiles)


STEP_FNS["l1_ffn"] = step_l1_ffn
STEP_WEIGHTS["l1_ffn"] = {"fp4_wg": (2816, D), "fp4_wu": (2816, D), "fp4_wd": (D, 2816)}



def step_l1_attn_a(k, W):
    H, HD = 8, 128
    m0 = k.mark()
    tiles = []

    def A(name, shape, dtype):
        t = k.alloc(name, shape, dtype)
        tiles.append(t)
        return t

    cosm = A("cosm", [64, T], BF16)
    sinm = A("sinm", [64, T], BF16)
    y = A("attny", [128, NCH * T], BF16)
    triu = A("triu", [128, 128], BF16)
    k.memset("V", triu.ap, 1.0, writes=[triu])
    k.affsel(triu.ap, triu.ap, [[1, 128]], ALU.is_ge, 0.0, -1, reads=[triu], writes=[triu])

    m1 = k.mark()
    s0 = []

    def A0(name, shape, dtype):
        t = k.alloc(name, shape, dtype)
        s0.append(t)
        return t
    xn = A0("xn", [128, NCH * T], BF16)
    mln = k.mark()
    sln = []

    def A1(name, shape, dtype):
        t = k.alloc(name, shape, dtype)
        sln.append(t)
        return t
    scr = ln_scratch(k, A1)
    emit_layernorm(k, k.X, xn, scr)
    k.release(mln, sln)

    sln = []
    posi = A1("posi", [64, T], I32)
    jci = A1("jci", [64, 1], I32)
    jcf = A1("jcf", [64, 1], F32)
    posf = A1("posf", [64, T], F32)
    rad = A1("rad", [64, T], F32)
    jcf2 = A1("jcf2", [64, T], F32)
    k.iota(posi.ap, [[1, T]], cm=0, writes=[posi])
    k.iota(jci.ap, [[0, 1]], cm=1, writes=[jci])
    k.copy("V", jcf.ap, jci.ap, reads=[jci], writes=[jcf])
    k.copy("V", posf.ap, posi.ap, reads=[posi], writes=[posf])
    LB = float(np.log(10000.0) / 64.0)
    k.act(jcf.ap, jcf.ap, AF.Exp, scale=-LB, reads=[jcf], writes=[jcf])
    k.ts("V", rad.ap, posf.ap, jcf[0:64, 0:1], ALU.mult, reads=[posf, jcf], writes=[rad])
    PI = float(np.pi)
    MULT = float((1.0 / 128.0) ** 0.5)
    k.ts("V", rad.ap, rad.ap, float(1.0 / (2 * PI)), ALU.mult, reads=[rad], writes=[rad])
    k.copy("V", posi.ap, rad.ap, reads=[rad], writes=[posi])
    k.copy("V", posf.ap, posi.ap, reads=[posi], writes=[posf])
    k.tt("V", posf.ap, rad.ap, posf.ap, ALU.subtract, reads=[rad, posf], writes=[posf])
    k.ts("V", jcf2.ap, posf.ap, 0.5, ALU.is_gt, reads=[posf], writes=[jcf2])
    k.tt("V", posf.ap, posf.ap, jcf2.ap, ALU.subtract, reads=[posf, jcf2], writes=[posf])
    k.act(sinm.ap, posf.ap, AF.Sin, scale=float(2 * PI), reads=[posf], writes=[sinm])
    k.ts("V", rad.ap, rad.ap, 0.25, ALU.add, reads=[rad], writes=[rad])
    k.copy("V", posi.ap, rad.ap, reads=[rad], writes=[posi])
    k.copy("V", posf.ap, posi.ap, reads=[posi], writes=[posf])
    k.tt("V", posf.ap, rad.ap, posf.ap, ALU.subtract, reads=[rad, posf], writes=[posf])
    k.ts("V", jcf2.ap, posf.ap, 0.5, ALU.is_gt, reads=[posf], writes=[jcf2])
    k.tt("V", posf.ap, posf.ap, jcf2.ap, ALU.subtract, reads=[posf, jcf2], writes=[posf])
    k.act(cosm.ap, posf.ap, AF.Sin, scale=float(2 * PI), reads=[posf], writes=[cosm])
    k.ts("V", sinm.ap, sinm.ap, MULT, ALU.mult, reads=[sinm], writes=[sinm])
    k.ts("V", cosm.ap, cosm.ap, MULT, ALU.mult, reads=[cosm], writes=[cosm])
    k.release(mln, sln)

    qrop = [A0(f"qro{i}", [128, T], BF16) for i in range(2)]
    krop = [A0(f"kro{i}", [128, T], BF16) for i in range(2)]
    vtmp = [A0(f"vtm{i}", [128, 16 * 129], BF16) for i in range(2)]
    for i in range(2):
        vv = vtmp[i]
        k.memset("V", vv.h[:].rearrange("p (c j) -> p c j", j=129)[:, :, 128:129],
                 1.0, writes=[vv])
    wqp = [A0(f"w_q{i}", [128, NCH * 128], BF16) for i in range(2)]
    wkp = [A0(f"w_k{i}", [128, NCH * 128], BF16) for i in range(2)]
    wvp = [A0(f"w_v{i}", [128, NCH * 128], BF16) for i in range(2)]
    t1p = [A0(f"t1_{i}", [64, 1024], BF16) for i in range(2)]
    rsbp = [A0(f"rsb{i}", [128, 512], F32) for i in range(2)]
    q2bp = [A0(f"q2b{i}", [128, 512], BF16) for i in range(2)]
    eap = [A0(f"ea{i}", [64, 512], BF16) for i in range(2)]
    obp = [A0(f"ob{i}", [64, 512], BF16) for i in range(2)]
    ebp = [A0(f"eb{i}", [64, 512], BF16) for i in range(2)]
    oap = [A0(f"oa{i}", [64, 512], BF16) for i in range(2)]
    vsgp = [A0(f"vsg{i}", [128, 512], BF16) for i in range(2)]
    ptbp = [A0(f"ptb{i}", [128, 512], BF16) for i in range(2)]
    oqp = [A0(f"oq{i}", [128, 128], BF16) for i in range(2)]
    rcp = [A0(f"rc{i}", [128, 1], F32) for i in range(2)]

    for h in range(H):
        hp = h & 1
        hsl = slice(h * 128, (h + 1) * 128)
        qro, kro, vtm = qrop[hp], krop[hp], vtmp[hp]
        for (wnm, wpair, dst) in (("att_wq", wqp, qro), ("att_wk", wkp, kro)):
            wv_ = W[wnm]
            wtile = wpair[hp]
            k.wload(wnm, wtile.h[:].rearrange("p (c i) -> p c i", i=128),
                    wv_[:, hsl].rearrange("(c p) i -> p c i", p=128), writes=[wtile])
            for tc in range(4):
                tp = tc & 1
                tsl = slice(tc * 512, (tc + 1) * 512)
                t1, rsb, q2b = t1p[tp], rsbp[tp], q2bp[tp]
                ea, ob, eb, oa = eap[tp], obp[tp], ebp[tp], oap[tp]
                k.mmop(k.pb[0], [(wtile.h[:, c * 128:(c + 1) * 128],
                                  xn.h[:, c * T + tc * 512: c * T + tc * 512 + 512])
                                 for c in range(NCH)], reads=[wtile, xn])
                k.act(q2b.ap, k.pb[0].ap, AF.Square, reads=[k.pb[0]], writes=[q2b])
                k.mmop(k.pb[1], [(k.ones_bf.ap, q2b.ap)], reads=[k.ones_bf, q2b])
                k.act(rsb.ap, k.pb[1].ap, AF.Ln, scale=float(1.0 / 128.0),
                      bias=1e-6, reads=[k.pb[1]], writes=[rsb])
                k.act(rsb.ap, rsb.ap, AF.Exp, scale=-0.5, reads=[rsb], writes=[rsb])
                k.tt("V", t1.h[0:64, 0:512], k.pb[0].h[0:64, 0:512],
                     rsb.h[0:64, 0:512], ALU.mult, reads=[k.pb[0], rsb], writes=[t1])
                k.tt("V", t1.h[0:64, 512:1024], k.pb[0].h[64:128, 0:512],
                     rsb.h[64:128, 0:512], ALU.mult, reads=[k.pb[0], rsb], writes=[t1])
                co = cosm.h[0:64, tsl]
                si = sinm.h[0:64, tsl]
                t1e = t1.h[0:64, 0:512]
                t1o = t1.h[0:64, 512:1024]
                k.tt("V", ea.ap, t1e, co, ALU.mult, reads=[t1, cosm], writes=[ea])
                k.tt("G", ob.ap, t1o, si, ALU.mult, reads=[t1, sinm], writes=[ob])
                k.tt("G", eb.ap, t1e, si, ALU.mult, reads=[t1, sinm], writes=[eb])
                k.tt("V", oa.ap, t1o, co, ALU.mult, reads=[t1, cosm], writes=[oa])
                k.tt("V", dst.h[0:64, tsl], ea.ap, ob.ap, ALU.subtract,
                     reads=[ea, ob], writes=[dst])
                k.tt("G", dst.h[64:128, tsl], eb.ap, oa.ap, ALU.add,
                     reads=[eb, oa], writes=[dst])
        wvt = wvp[hp]
        k.wload("att_wv", wvt.h[:].rearrange("p (c i) -> p c i", i=128),
                W["att_wv"][:, hsl].rearrange("(c p) i -> p c i", p=128), writes=[wvt])
        for tc in range(4):
            tp = tc & 1
            vsg = vsgp[tp]
            k.mmop(k.pb[0], [(wvt.h[:, c * 128:(c + 1) * 128],
                              xn.h[:, c * T + tc * 512: c * T + tc * 512 + 512])
                             for c in range(NCH)], reads=[wvt, xn])
            k.copy("V", vsg.ap, k.pb[0].ap, reads=[k.pb[0]], writes=[vsg])
            for j in range(4):
                k.transp(k.pbh[0], vsg.h[:, j * 128:(j + 1) * 128], k.ident.ap,
                         slice(j * 128, (j + 1) * 128), reads=[vsg, k.ident])
            for j in range(4):
                ch = tc * 4 + j
                k.copy("V", vtm.h[:, ch * 129: ch * 129 + 128],
                       k.pbh[0].h[:, j * 128:(j + 1) * 128],
                       reads=[k.pbh[0]], writes=[vtm])
        for qt in range(16):
            qp = qt & 1
            qsl = slice(qt * 128, (qt + 1) * 128)
            opb = k.pb[3 + 2 * qp]
            nkt = qt + 1
            first = True
            for g0 in range(0, nkt, 4):
                gp = (g0 // 4) & 1
                spb = k.pb[2 + 2 * gp]
                ptb = ptbp[gp]
                gn = min(4, nkt - g0)
                for j in range(gn):
                    k.mmop(spb, [(kro.h[:, (g0 + j) * 128:(g0 + j + 1) * 128],
                                  qro.h[:, qsl])], reads=[kro, qro],
                           out_slice=slice(j * 128, (j + 1) * 128))
                k.act(ptb.h[:, 0:gn * 128], spb.h[:, 0:gn * 128], AF.Exp,
                      reads=[spb], writes=[ptb])
                if g0 + gn == nkt:
                    k.tt("V", ptb.h[:, (gn - 1) * 128:gn * 128],
                         ptb.h[:, (gn - 1) * 128:gn * 128], triu.ap, ALU.mult,
                         reads=[ptb, triu], writes=[ptb])
                for j in range(gn):
                    last = (g0 + j == nkt - 1)
                    k.mmop(opb, [(ptb.h[:, j * 128:(j + 1) * 128],
                                  vtm.h[:, (g0 + j) * 129: (g0 + j) * 129 + 129])],
                           reads=[ptb, vtm], accum=not first,
                           out_slice=slice(0, 129), close=last)
                    first = False
            rc, oq = rcp[qp], oqp[qp]
            k.recip(rc.ap, opb.h[:, 128:129], reads=[opb], writes=[rc])
            k.ts("V", oq.ap, opb.h[:, 0:128], rc.h[:, 0:1], ALU.mult,
                 reads=[opb, rc], writes=[oq])
            k.transp(k.pbh[1], oq.ap, k.ident.ap, slice(qp * 128, qp * 128 + 128),
                     reads=[oq, k.ident])
            k.copy("V", y.h[:, h * T + qt * 128: h * T + qt * 128 + 128],
                   k.pbh[1].h[:, qp * 128: qp * 128 + 128],
                   reads=[k.pbh[1]], writes=[y])
    k.release(m1, s0)

    wo = A("w_att_wo", [128, NCH * 128], BF16)
    for dp in range(NCH):
        k.wload("att_wo", wo.h[:].rearrange("p (c i) -> p c i", i=128),
              W["att_wo"][:, dp * 128:(dp + 1) * 128].rearrange("(c p) i -> p c i", p=128),
              writes=[wo])
        for tc in range(4):
            tsl = slice(tc * 512, (tc + 1) * 512)
            k.mmop(k.pb[2 + tc % 2], [(wo.h[:, c * 128:(c + 1) * 128],
                                       y.h[:, c * T + tc * 512: c * T + tc * 512 + 512])
                                      for c in range(NCH)], reads=[wo, y])
            xt = k.X[dp]
            k.tt("V", xt.h[:, tsl], xt.h[:, tsl], k.pb[2 + tc % 2].ap, ALU.add,
                 reads=[xt, k.pb[2 + tc % 2]], writes=[xt])
    k.release(m0, tiles)


STEP_FNS["l1_attn_a"] = step_l1_attn_a
STEP_WEIGHTS["l1_attn_a"] = {"att_wq": (D, D), "att_wk": (D, D),
                             "att_wv": (D, D), "att_wo": (D, D)}



def emit_layernorm_bf(k, src, out_bf, scr, gb, goff, bb, boff, eps=1e-5):
    x2s, mb, rb = scr["x2s"], scr["mb"], scr["rb"]
    t0, t1 = scr["t0"], scr["t1"]
    for tc in range(T // 512):
        sl = slice(tc * 512, (tc + 1) * 512)
        xs = x2s[tc % 2]
        for c in range(NCH):
            k.act(xs.h[:, c * 512:(c + 1) * 512], src.h[:, c * T + tc * 512: c * T + tc * 512 + 512],
                  AF.Square, reads=[src], writes=[xs])
        k.mmop(k.pb[0], [(k.ones_bf.ap, src.h[:, c * T + tc * 512: c * T + tc * 512 + 512])
                         for c in range(NCH)], reads=[k.ones_bf, src])
        k.ts("V", mb.h[:, sl], k.pb[0].ap, 1.0 / D, ALU.mult, reads=[k.pb[0]], writes=[mb])
        k.mmop(k.pb[1], [(k.ones_bf.ap, xs.h[:, c * 512:(c + 1) * 512])
                         for c in range(NCH)], reads=[k.ones_bf, xs])
        k.ts("V", rb.h[:, sl], k.pb[1].ap, 1.0 / D, ALU.mult, reads=[k.pb[1]], writes=[rb])
    k.tt("G", t0.ap, mb.ap, mb.ap, ALU.mult, reads=[mb], writes=[t0])
    k.tt("V", rb.ap, rb.ap, t0.ap, ALU.subtract, reads=[rb, t0], writes=[rb])
    k.act(t0.ap, rb.ap, AF.Sqrt, bias=k.const(eps), reads=[rb], writes=[t0])
    k.recip(rb.ap, t0.ap, reads=[t0], writes=[rb])
    for c in range(NCH):
        csl = slice(c * T, (c + 1) * T)
        k.tt("G", t0.ap, src.h[:, csl], mb.ap, ALU.subtract, reads=[src, mb], writes=[t0])
        k.tt("V", t0.ap, t0.ap, rb.ap, ALU.mult, reads=[t0, rb], writes=[t0])
        k.ts("V", out_bf.h[:, csl], t0.ap, gb.h[:, goff + c:goff + c + 1], ALU.mult,
             s2=bb.h[:, boff + c:boff + c + 1], op1=ALU.add,
             reads=[t0, gb, bb], writes=[out_bf])
    return out_bf


def step_l0_attn_a(k, W):
    m0 = k.mark()
    tiles = []

    def A(name, shape, dtype):
        t = k.alloc(name, shape, dtype)
        tiles.append(t)
        return t

    triu = A("triu0", [128, 128], BF16)
    k.memset("V", triu.ap, 1.0, writes=[triu])
    k.affsel(triu.ap, triu.ap, [[1, 128]], ALU.is_ge, 0.0, -1, reads=[triu], writes=[triu])
    lng = A("lng", [128, NCH * 4], F32)
    k.dma("Y", lng.ap, W["l0a_lns"], writes=[lng])
    lnbb = A("lnbb", [128, NCH], BF16)
    k.copy("V", lnbb.ap, lng.h[:, NCH:2 * NCH], reads=[lng], writes=[lnbb])
    x1 = A("x1c", [128, NCH * T], BF16)

    mA = k.mark()
    sA = []

    def AA(name, shape, dtype):
        t = k.alloc(name, shape, dtype)
        sA.append(t)
        return t
    h0 = AA("h0", [128, NCH * T], BF16)
    y = AA("lay", [128, NCH * T], BF16)

    mL = k.mark()
    sL = []

    def AL(name, shape, dtype):
        t = k.alloc(name, shape, dtype)
        sL.append(t)
        return t
    scr = ln_scratch(k, AL)
    emit_layernorm(k, k.X, h0, scr)
    k.release(mL, sL)
    if int(os.environ.get("L0A_PHASE", "9")) <= 1:
        k.release(mA, sA)
        k.release(m0, tiles)
        return

    sL = []
    qh = AL("qh", [128, T], BF16)
    kh = AL("kh", [128, T], BF16)
    vh = AL("vh", [128, T], BF16)
    msb = AL("msb", [128, 64], BF16)
    msf = AL("msf", [64, 64], F32)
    ptb = AL("pt0", [128, 128], BF16)
    vtc = AL("vtc", [128, 128], BF16)
    ktc = AL("ktc", [128, 128], BF16)
    wqb = AL("wqb", [128, NCH * 64], BF16)
    wkb = AL("wkb", [128, NCH * 64], BF16)
    wvb = AL("wvb", [128, NCH * 64], BF16)
    bv = AL("bv", [64, 3], F32)
    k.memset("V", qh.h[64:128, :], 0.0, writes=[qh])
    k.memset("V", kh.h[64:128, :], 0.0, writes=[kh])
    k.memset("V", vh.h[64:128, :], 0.0, writes=[vh])
    k.memset("V", msb.h[64:128, :], 0.0, writes=[msb])
    for h in range(16):
        hsl = slice(h * 64, (h + 1) * 64)
        for wi, (wnm, wtile, dst) in enumerate((("l0a_wq", wqb, qh),
                                                ("l0a_wk", wkb, kh),
                                                ("l0a_wv", wvb, vh))):
            k.wload("l0a_wo", wtile.h[:].rearrange("p (c i) -> p c i", i=64),
                  W[wnm][:, hsl].rearrange("(c p) i -> p c i", p=128), writes=[wtile])
            k.mmop(k.pb[5], [(wtile.h[:, c * 64:(c + 1) * 64], lnbb.h[:, c:c + 1])
                             for c in range(NCH)], reads=[wtile, lnbb],
                   out_slice=slice(0, 1), nparts=64)
            k.copy("V", bv.h[:, wi:wi + 1], k.pb[5].h[0:64, 0:1],
                   reads=[k.pb[5]], writes=[bv])
            for c in range(NCH):
                k.ts("V", wtile.h[:, c * 64:(c + 1) * 64],
                     wtile.h[:, c * 64:(c + 1) * 64], lng.h[:, c:c + 1], ALU.mult,
                     reads=[wtile, lng], writes=[wtile])
            for tc in range(4):
                k.mmop(k.pb[0], [(wtile.h[:, c * 64:(c + 1) * 64],
                                  h0.h[:, c * T + tc * 512: c * T + tc * 512 + 512])
                                 for c in range(NCH)], reads=[wtile, h0], nparts=64)
                k.ts("V", dst.h[0:64, tc * 512:(tc + 1) * 512],
                     k.pb[0].h[0:64, :], bv.h[:, wi:wi + 1], ALU.add,
                     reads=[k.pb[0], bv], writes=[dst])
        k.memset("V", msb.h[0:64, :], 0.0, writes=[msb])
        k.memset("V", msf.ap, 0.0, writes=[msf])
        SUB = int(os.environ.get("L0A_SUB", "9"))
        for c in range(16):
            csl = slice(c * 128, (c + 1) * 128)
            if SUB == 0:
                continue
            k.mmop(k.pb[1], [(kh.h[:, csl], qh.h[:, csl])], reads=[kh, qh],
                   out_slice=slice(0, 128))
            if SUB == 1:
                continue
            k.tt("V", ptb.ap, k.pb[1].h[:, 0:128], triu.ap, ALU.mult,
                 reads=[k.pb[1], triu], writes=[ptb])
            if SUB == 2:
                continue
            k.transp(k.pbh[0], vh.h[:, csl], k.ident.ap, slice(0, 128),
                     reads=[vh, k.ident])
            k.transp(k.pbh[0], kh.h[:, csl], k.ident.ap, slice(128, 256),
                     reads=[kh, k.ident])
            k.copy("V", vtc.ap, k.pbh[0].h[:, 0:128], reads=[k.pbh[0]], writes=[vtc])
            k.copy("V", ktc.ap, k.pbh[0].h[:, 128:256], reads=[k.pbh[0]], writes=[ktc])
            if SUB == 3:
                continue
            k.mmop(k.pb[2], [(msb.ap, qh.h[:, csl]),
                             (vtc.h[:, 0:64], ptb.ap)], reads=[msb, qh, vtc, ptb],
                   out_slice=slice(0, 128), nparts=64)
            if SUB == 4:
                continue
            k.copy("V",
                   y.h[(h % 2) * 64:(h % 2) * 64 + 64,
                       (h // 2) * T + c * 128:(h // 2) * T + c * 128 + 128],
                   k.pb[2].h[0:64, 0:128], reads=[k.pb[2]], writes=[y])
            if c < 15:
                k.mmop(k.pb[3], [(ktc.h[:, 0:64], vtc.h[:, 0:64])], reads=[ktc, vtc],
                       out_slice=slice(0, 64), nparts=64)
                k.tt("V", msf.ap, msf.ap, k.pb[3].h[0:64, 0:64], ALU.add,
                     reads=[msf, k.pb[3]], writes=[msf])
                k.copy("G", msb.h[0:64, :], msf.ap, reads=[msf], writes=[msb])
    k.release(mL, sL)

    sL = []
    wob = AL("wob", [128, NCH * 128], BF16)
    for dp in range(NCH):
        k.wload("l0a_wo", wob.h[:].rearrange("p (c i) -> p c i", i=128),
                W["l0a_wo"][:, dp * 128:(dp + 1) * 128].rearrange("(c p) i -> p c i", p=128),
                writes=[wob])
        for tc in range(4):
            tsl = slice(tc * 512, (tc + 1) * 512)
            bank = k.pb[2 + tc % 2]
            k.mmop(bank, [(wob.h[:, c * 128:(c + 1) * 128],
                           y.h[:, c * T + tc * 512: c * T + tc * 512 + 512])
                          for c in range(NCH)], reads=[wob, y])
            xt = k.X[dp]
            k.tt("V", x1.h[:, dp * T + tc * 512: dp * T + tc * 512 + 512],
                 bank.ap, h0.h[:, dp * T + tc * 512: dp * T + tc * 512 + 512],
                 ALU.add, reads=[bank, h0], writes=[x1])
            k.tt("V", xt.h[:, tsl], xt.h[:, tsl], bank.ap, ALU.add,
                 reads=[xt, bank], writes=[xt])
            k.tt("G", xt.h[:, tsl], xt.h[:, tsl],
                 h0.h[:, dp * T + tc * 512: dp * T + tc * 512 + 512], ALU.add,
                 reads=[xt, h0], writes=[xt])
    k.release(mL, sL)
    k.release(mA, sA)

    sA = []
    scr = ln_scratch(k, AA)
    emit_layernorm_bf(k, x1, x1, scr, lng, 2 * NCH, lng, 3 * NCH)
    k.release(mA, sA)
    ping = x1
    if int(os.environ.get("L0A_PHASE", "9")) <= 4:
        k.release(m0, tiles)
        return

    pong = A("pong", [128, NCH * T], BF16)
    ub = A("ub", [128, 32 * 512], BF16)
    w1b = [A(f"w1b{i}", [128, NCH * 128], BF16) for i in range(2)]
    w2b = [A(f"w2b{i}", [128, 32 * 128], BF16) for i in range(2)]
    bcol = A("bcol", [128, 32 + NCH], F32)
    zt2 = [A(f"zt{i}", [128, 512], F32) for i in range(2)]
    et2 = [A(f"et{i}", [128, 512], F32) for i in range(2)]
    GC1 = 0.7978845608028654
    GC2 = 0.044715 * 0.7978845608028654
    GC1f = float(np.float32(GC1))
    GC2f = float(np.float32(GC2))
    for lvl in range(3):
        src_t = ping if lvl % 2 == 0 else pong
        dst_t = pong if lvl % 2 == 0 else ping
        k.dma("Y", bcol.ap, W["l0a_cms_b"][lvl], writes=[bcol])
        for tc in range(4):
            for ib in range(32):
                wb_ = w1b[ib % 2]
                k.wload("l0a_cms_w1", wb_.h[:].rearrange("p (c i) -> p c i", i=128),
                      W["l0a_cms_w1"][lvl][:, ib * 128:(ib + 1) * 128].rearrange(
                          "(c p) i -> p c i", p=128), writes=[wb_])
                bank = k.pb[ib % 4]
                zt = zt2[ib % 2]
                et = et2[ib % 2]
                k.mmop(bank, [(wb_.h[:, c * 128:(c + 1) * 128],
                               src_t.h[:, c * T + tc * 512: c * T + tc * 512 + 512])
                              for c in range(NCH)], reads=[wb_, src_t])
                k.ts("V", zt.ap, bank.ap, bcol.h[:, ib:ib + 1], ALU.add,
                     reads=[bank, bcol], writes=[zt])
                k.act(et.ap, zt.ap, AF.Square, reads=[zt], writes=[et])
                k.ts("V", et.ap, et.ap, GC2f, ALU.mult, s2=GC1f, op1=ALU.add,
                     reads=[et], writes=[et])
                k.tt("G", et.ap, et.ap, zt.ap, ALU.mult, reads=[et, zt], writes=[et])
                k.act(et.ap, et.ap, AF.Tanh, reads=[et], writes=[et])
                k.ts("V", et.ap, et.ap, 0.5, ALU.mult, s2=0.5, op1=ALU.add,
                     reads=[et], writes=[et])
                k.tt("G", ub.h[:, ib * 512:(ib + 1) * 512], et.ap, zt.ap, ALU.mult,
                     reads=[et, zt], writes=[ub])
            for dp in range(NCH):
                wb2 = w2b[dp % 2]
                k.wload("l0a_cms_w2", wb2.h[:].rearrange("p (c i) -> p c i", i=128),
                      W["l0a_cms_w2"][lvl][:, dp * 128:(dp + 1) * 128].rearrange(
                          "(c p) i -> p c i", p=128), writes=[wb2])
                bank = k.pb[4 + dp % 2]
                zt = zt2[dp % 2]
                k.mmop(bank, [(wb2.h[:, c * 128:(c + 1) * 128],
                               ub.h[:, c * 512:(c + 1) * 512])
                              for c in range(32)], reads=[wb2, ub])
                if lvl < 2:
                    k.ts("V", dst_t.h[:, dp * T + tc * 512: dp * T + tc * 512 + 512],
                         bank.ap, bcol.h[:, 32 + dp:33 + dp], ALU.add,
                         reads=[bank, bcol], writes=[dst_t])
                else:
                    xt = k.X[dp]
                    tsl = slice(tc * 512, (tc + 1) * 512)
                    k.ts("V", zt.ap, bank.ap, bcol.h[:, 32 + dp:33 + dp], ALU.add,
                         reads=[bank, bcol], writes=[zt])
                    k.tt("G", xt.h[:, tsl], xt.h[:, tsl], zt.ap, ALU.add,
                         reads=[xt, zt], writes=[xt])
    k.release(m0, tiles)


STEP_FNS["l0_attn_a"] = step_l0_attn_a
STEP_WEIGHTS["l0_attn_a"] = {
    "l0a_lns": (128, NCH * 4), "l0a_wq": (D, D), "l0a_wk": (D, D),
    "l0a_wv": (D, D), "l0a_wo": (D, D), "l0a_cms_w1": (3, D, 4096),
    "l0a_cms_w2": (3, 4096, D), "l0a_cms_b": (3, 128, 32 + NCH),
}



def step_l0_attn_b(k, W):
    TH = T // 2
    m0 = k.mark()
    tiles = []

    def A(name, shape, dtype):
        t = k.alloc(name, shape, dtype)
        tiles.append(t)
        return t

    sel32 = A("sel32", [32, 32 * 128], BF16)
    k.memset("V", sel32.ap, 1.0, writes=[sel32])
    k.affsel(sel32.ap, sel32.ap, [[1, 32 * 128]], ALU.is_ge, 0.0, -128,
             reads=[sel32], writes=[sel32])
    k.affsel(sel32.ap, sel32.ap, [[-1, 32 * 128]], ALU.is_ge, 0.0, 128, base=127,
             reads=[sel32], writes=[sel32])
    Atile = A("Atile", [128, NCH * 16], F32)
    k.dma("Y", Atile.ap, W["mam_alog"], writes=[Atile])
    k.act(Atile.ap, Atile.ap, AF.Exp, reads=[Atile], writes=[Atile])
    k.ts("V", Atile.ap, Atile.ap, -1.0, ALU.mult, reads=[Atile], writes=[Atile])
    convw = A("convw", [128, NCH * 3], F32)
    k.dma("Y", convw.ap, W["mam_conv"], writes=[convw])
    dbc = A("dbc", [128, NCH], F32)
    k.dma("Y", dbc.ap, W["mam_db"], writes=[dbc])
    valc = A("valc", [128, NCH * T], BF16)
    d1t = A("d1t", [128, TH], F32)
    wib = [A(f"wib{i}", [128, NCH * 128], BF16) for i in range(2)]
    mbk = A("mbk", [128, T], BF16)
    rbk = A("rbk", [128, T], BF16)
    stt_ = A("stt", [128, NCH * 16], F32)
    wbc = A("wbc", [128, NCH * 32], BF16)
    k.wload("mam_wbc", wbc.h[:].rearrange("p (c i) -> p c i", i=32),
          W["mam_wbc"].rearrange("(c p) i -> p c i", p=128), writes=[wbc])

    mV = k.mark()
    sV = []

    def AV(name, shape, dtype):
        t = k.alloc(name, shape, dtype)
        sV.append(t)
        return t
    val = AV("val", [128, NCH * (T + 2)], BF16)
    mX = k.mark()
    sX = []

    def AX(name, shape, dtype):
        t = k.alloc(name, shape, dtype)
        sX.append(t)
        return t
    scr = ln_scratch(k, AX)
    emit_ln_stats(k, k.X, scr)
    k.copy("V", mbk.ap, scr["mb"].ap, reads=[scr["mb"]], writes=[mbk])
    k.copy("V", rbk.ap, scr["rb"].ap, reads=[scr["rb"]], writes=[rbk])
    k.release(mX, sX)
    xtc = AV("xtc", [128, NCH * 512], BF16)
    for oc in range(NCH):
        k.memset("V", val.h[:, oc * (T + 2):oc * (T + 2) + 1], 0.0, writes=[val])
        k.memset("V", val.h[:, oc * (T + 2) + T + 1:(oc + 1) * (T + 2)], 0.0, writes=[val])
    for tc in range(4):
        xsl = slice(tc * 512, (tc + 1) * 512)
        for c in range(NCH):
            k.tt("G", d1t.h[:, 0:512], k.X[c].h[:, xsl], mbk.h[:, xsl],
                 ALU.subtract, reads=[k.X[c], mbk], writes=[d1t])
            k.tt("V", xtc.h[:, c * 512:(c + 1) * 512], d1t.h[:, 0:512],
                 rbk.h[:, xsl], ALU.mult, reads=[d1t, rbk], writes=[xtc])
        for oc in range(NCH):
            wb = wib[oc % 2]
            k.wload("mam_win", wb.h[:].rearrange("p (c i) -> p c i", i=128),
                  W["mam_win"][:, oc * 128:(oc + 1) * 128].rearrange(
                      "(c p) i -> p c i", p=128), writes=[wb])
            bank = k.pb[oc % 2]
            k.mmop(bank, [(wb.h[:, c * 128:(c + 1) * 128],
                           xtc.h[:, c * 512:(c + 1) * 512])
                          for c in range(NCH)], reads=[wb, xtc])
            k.copy("V", val.h[:, oc * (T + 2) + 1 + tc * 512: oc * (T + 2) + 1 + tc * 512 + 512],
                   bank.ap, reads=[bank], writes=[val])
    for oc in range(NCH):
        base = oc * (T + 2)
        for hc in range(2):
            t0 = hc * TH
            sm1 = val.h[:, base + t0: base + t0 + TH]
            s0 = val.h[:, base + 1 + t0: base + 1 + t0 + TH]
            sp1 = val.h[:, base + 2 + t0: base + 2 + t0 + TH]
            k.ts("V", d1t.ap, sm1, convw.h[:, oc * 3:oc * 3 + 1], ALU.mult,
                 reads=[val, convw], writes=[d1t])
            k.stt("V", d1t.ap, s0, convw.h[:, oc * 3 + 1:oc * 3 + 2], d1t.ap,
                  ALU.mult, ALU.add, reads=[val, convw, d1t], writes=[d1t])
            k.stt("V", d1t.ap, sp1, convw.h[:, oc * 3 + 2:oc * 3 + 3], d1t.ap,
                  ALU.mult, ALU.add, reads=[val, convw, d1t], writes=[d1t])
            k.act(valc.h[:, oc * T + t0: oc * T + t0 + TH], d1t.ap, AF.Silu,
                  reads=[d1t], writes=[valc])
    k.release(mV, sV)

    for h2 in range(2):
        t0 = h2 * TH
        mH = k.mark()
        sH = []

        def AH(name, shape, dtype):
            t = k.alloc(name, shape, dtype)
            sH.append(t)
            return t
        bcf = AH("bcf", [32, TH], BF16)
        dlt = AH("dlt", [128, NCH * TH], BF16)
        gsil = AH("gsil", [128, NCH * TH], BF16)
        mXN = k.mark()
        xnh = k.alloc("xnh", [128, NCH * TH], BF16)
        for c in range(NCH):
            for tc2 in range(2):
                xsl = slice(t0 + tc2 * 512, t0 + tc2 * 512 + 512)
                k.tt("G", d1t.h[:, 0:512], k.X[c].h[:, xsl], mbk.h[:, xsl],
                     ALU.subtract, reads=[k.X[c], mbk], writes=[d1t])
                k.tt("V", xnh.h[:, c * TH + tc2 * 512: c * TH + tc2 * 512 + 512],
                     d1t.h[:, 0:512], rbk.h[:, xsl], ALU.mult,
                     reads=[d1t, rbk], writes=[xnh])
        for tc2 in range(2):
            k.mmop(k.pb[2], [(wbc.h[:, c * 32:(c + 1) * 32],
                              valc.h[:, c * T + t0 + tc2 * 512: c * T + t0 + tc2 * 512 + 512])
                             for c in range(NCH)], reads=[wbc, valc], nparts=32)
            k.copy("V", bcf.h[:, tc2 * 512:(tc2 + 1) * 512], k.pb[2].h[0:32, :],
                   reads=[k.pb[2]], writes=[bcf])
        mC = k.mark()
        sC = []
        cdh = k.alloc("cdh", [128, NCH * TH], BF16)
        sC.append(cdh)
        wdb = [k.alloc(f"wdb{i}", [128, NCH * 128], BF16) for i in range(2)]
        sC.extend(wdb)
        for dp in range(NCH):
            wb = wdb[dp % 2]
            k.wload("mam_wd", wb.h[:].rearrange("p (c i) -> p c i", i=128),
                  W["mam_wd"][:, dp * 128:(dp + 1) * 128].rearrange("(c p) i -> p c i", p=128),
                  writes=[wb])
            for tc2 in range(2):
                bank = k.pb[tc2 % 2]
                k.mmop(bank, [(wb.h[:, c * 128:(c + 1) * 128],
                               valc.h[:, c * T + t0 + tc2 * 512: c * T + t0 + tc2 * 512 + 512])
                              for c in range(NCH)], reads=[wb, valc])
                k.ts("V", cdh.h[:, dp * TH + tc2 * 512: dp * TH + tc2 * 512 + 512],
                     bank.ap, dbc.h[:, dp:dp + 1], ALU.add, reads=[bank, dbc], writes=[cdh])
        for dp in range(NCH):
            wb = wdb[dp % 2]
            k.wload("mam_wt", wb.h[:].rearrange("p (c i) -> p c i", i=128),
                  W["mam_wt"][:, dp * 128:(dp + 1) * 128].rearrange("(c p) i -> p c i", p=128),
                  writes=[wb])
            for tc2 in range(2):
                bank = k.pb[tc2 % 2]
                k.mmop(bank, [(wb.h[:, c * 128:(c + 1) * 128],
                               cdh.h[:, c * TH + tc2 * 512: c * TH + tc2 * 512 + 512])
                              for c in range(NCH)], reads=[wb, cdh])
                dsl = slice(dp * TH + tc2 * 512, dp * TH + tc2 * 512 + 512)
                k.act(d1t.h[:, 512:1024], bank.ap, AF.Abs, reads=[bank], writes=[d1t])
                k.act(d1t.h[:, 512:1024], d1t.h[:, 512:1024], AF.Exp, scale=-1.0,
                     reads=[d1t], writes=[d1t])
                k.act(d1t.h[:, 512:1024], d1t.h[:, 512:1024], AF.Ln, bias=1.0,
                     reads=[d1t], writes=[d1t])
                k.ts("V", d1t.h[:, 0:512], bank.ap, 0.0, ALU.max, reads=[bank], writes=[d1t])
                k.tt("V", dlt.h[:, dsl], d1t.h[:, 0:512], d1t.h[:, 512:1024], ALU.add,
                     reads=[d1t], writes=[dlt])
        k.release(mC, sC)

        for gc in range(NCH):
            wb = wib[gc % 2]
            k.wload("mam_win", wb.h[:].rearrange("p (c i) -> p c i", i=128),
                  W["mam_win"][:, D + gc * 128: D + (gc + 1) * 128].rearrange(
                      "(c p) i -> p c i", p=128), writes=[wb])
            for tc2 in range(2):
                bank = k.pb[tc2 % 2]
                k.mmop(bank, [(wb.h[:, c * 128:(c + 1) * 128],
                               xnh.h[:, c * TH + tc2 * 512: c * TH + tc2 * 512 + 512])
                              for c in range(NCH)], reads=[wb, xnh])
                k.act(gsil.h[:, gc * TH + tc2 * 512: gc * TH + tc2 * 512 + 512],
                      bank.ap, AF.Silu, reads=[bank], writes=[gsil])
        for dp in range(NCH):
            for tc2 in range(2):
                xsl = slice(t0 + tc2 * 512, t0 + tc2 * 512 + 512)
                k.tt("G", k.X[dp].h[:, xsl], k.X[dp].h[:, xsl],
                     xnh.h[:, dp * TH + tc2 * 512: dp * TH + tc2 * 512 + 512],
                     ALU.add, reads=[k.X[dp], xnh], writes=[k.X[dp]])
        k.release(mXN, [xnh])
        ashp = [AH(f"ash{i}", [128, TH], BF16) for i in range(2)]
        sgtp = [AH(f"sgt{i}", [128, TH], BF16) for i in range(2)]
        rmgp = [d1t, AH("rmg1", [128, TH], F32)]
        wshp = [AH(f"wsh{i}", [128, TH], BF16) for i in range(2)]
        sshp = [AH(f"ssh{i}", [128, TH], BF16) for i in range(2)]
        prdp = [AH(f"prd{i}", [128, TH], BF16) for i in range(2)]
        bbtp = [AH(f"bbt{i}", [128, TH], BF16) for i in range(2)]
        cbtp = [AH(f"cbt{i}", [128, TH], BF16) for i in range(2)]
        dvtp = [AH(f"dvt{i}", [128, TH], BF16) for i in range(2)]
        ycb = AH("ycb", [128, TH], F32)
        units = [(cb, sg) for cb in range(NCH) for sg in range(16)]

        def emit_ash(idx):
            cb_, sg_ = units[idx]
            ash_ = ashp[sg_ & 1]
            acol_ = Atile.h[:, cb_ * 16 + sg_: cb_ * 16 + sg_ + 1]
            k.ts("V", ash_.ap, dlt.h[:, cb_ * TH:(cb_ + 1) * TH], acol_,
                 ALU.mult, reads=[dlt, Atile], writes=[ash_])

        for idx, (cb, sg) in enumerate(units):
            dsl = slice(cb * TH, (cb + 1) * TH)
            dvt = dvtp[cb & 1]
            if sg == 0:
                k.tt("V", dvt.ap, dlt.h[:, dsl],
                     valc.h[:, cb * T + t0: cb * T + t0 + TH], ALU.mult,
                     reads=[dlt, valc], writes=[dvt])
            p = sg & 1
            ash, sgt, rmg = ashp[p], sgtp[p], rmgp[p]
            wsh, ssh, prd = wshp[p], sshp[p], prdp[p]
            bbt, cbt = bbtp[p], cbtp[p]
            for (row, dst) in ((sg, bbt), (16 + sg, cbt)):
                for tc2 in range(2):
                    k.mmop(k.pb[2 + tc2], [(sel32.h[0:32, row * 128:(row + 1) * 128],
                                            bcf.h[0:32, tc2 * 512:(tc2 + 1) * 512])],
                           reads=[sel32, bcf])
                    k.act(dst.h[:, tc2 * 512:(tc2 + 1) * 512],
                          k.pb[2 + tc2].ap, AF.Copy,
                          reads=[k.pb[2 + tc2]], writes=[dst])
            if idx == 0:
                emit_ash(0)
            k.act(sgt.ap, ash.ap, AF.Sign, bias=1e-6, reads=[ash], writes=[sgt])
            k.act(rmg.ap, ash.ap, AF.Abs, bias=1e-6, reads=[ash], writes=[rmg])
            k.act(rmg.ap, rmg.ap, AF.Ln, reads=[rmg], writes=[rmg])
            k.act(rmg.ap, rmg.ap, AF.Exp, scale=-1.0, reads=[rmg], writes=[rmg])
            k.act(ash.ap, ash.ap, AF.Exp, reads=[ash], writes=[ash])
            k.tt("V", rmg.ap, rmg.ap, sgt.ap, ALU.mult,
                 reads=[rmg, sgt], writes=[rmg])
            k.stt("V", wsh.ap, ash.ap, -1.0, rmg.ap, ALU.add, ALU.mult,
                  reads=[ash, rmg], writes=[wsh])
            k.tt("G", wsh.ap, wsh.ap, bbt.ap, ALU.mult, reads=[wsh, bbt], writes=[wsh])
            k.tt("G", wsh.ap, wsh.ap, dvt.ap, ALU.mult,
                 reads=[wsh, dvt], writes=[wsh])
            if idx + 1 < len(units):
                emit_ash(idx + 1)
            init = 0.0 if h2 == 0 else stt_.h[:, cb * 16 + sg: cb * 16 + sg + 1]
            rd = [ash, wsh] + ([stt_] if h2 == 1 else [])
            k.scan("V", ssh.ap, ash.ap, wsh.ap, init, ALU.mult, ALU.add,
                   reads=rd, writes=[ssh])
            k.copy("V", stt_.h[:, cb * 16 + sg: cb * 16 + sg + 1],
                   ssh.h[:, TH - 1:TH], reads=[ssh], writes=[stt_])
            if sg == 0:
                k.tt("G", ycb.ap, ssh.ap, cbt.ap, ALU.mult,
                     reads=[ssh, cbt], writes=[ycb])
            else:
                k.tt("G", prd.ap, ssh.ap, cbt.ap, ALU.mult,
                     reads=[ssh, cbt], writes=[prd])
                k.tt("V", ycb.ap, ycb.ap, prd.ap, ALU.add,
                     reads=[ycb, prd], writes=[ycb])
            if sg == 15:
                k.tt("V", gsil.h[:, dsl], ycb.ap, gsil.h[:, dsl], ALU.mult,
                     reads=[ycb, gsil], writes=[gsil])
        for dp in range(NCH):
            wb = wib[dp % 2]
            k.wload("mam_wout", wb.h[:].rearrange("p (c i) -> p c i", i=128),
                  W["mam_wout"][:, dp * 128:(dp + 1) * 128].rearrange(
                      "(c p) i -> p c i", p=128), writes=[wb])
            for tc2 in range(2):
                bank = k.pb[2 + tc2 % 2]
                k.mmop(bank, [(wb.h[:, c * 128:(c + 1) * 128],
                               gsil.h[:, c * TH + tc2 * 512: c * TH + tc2 * 512 + 512])
                              for c in range(NCH)], reads=[wb, gsil])
                xt = k.X[dp]
                xsl = slice(t0 + tc2 * 512, t0 + tc2 * 512 + 512)
                k.tt("V", xt.h[:, xsl], xt.h[:, xsl], bank.ap, ALU.add,
                     reads=[xt, bank], writes=[xt])
        k.release(mH, sH)
    k.release(m0, tiles)


STEP_FNS["l0_attn_b"] = step_l0_attn_b
STEP_WEIGHTS["l0_attn_b"] = {
    "mam_win": (D, 2 * D), "mam_conv": (128, NCH * 3), "mam_wbc": (D, 32),
    "mam_wd": (D, D), "mam_db": (128, NCH), "mam_wt": (D, D),
    "mam_alog": (128, NCH * 16), "mam_wout": (D, D),
}



def step_l1_attn_b(k, W):
    H, HD, C = 16, 64, 128
    NCHK = T // C
    NI = 2764
    ICW = [128] * 21 + [76]
    m0 = k.mark()
    tiles = []

    def A(name, shape, dtype):
        t = k.alloc(name, shape, dtype)
        tiles.append(t)
        return t

    mbk = A("mbk", [128, T], BF16)
    rbk = A("rbk", [128, T], BF16)
    n12 = A("n12", [128, 2 * NCH], F32)
    k.dma("Y", n12.ap, W["gd_norms"], writes=[n12])
    cwb = A("cwb", [128, 5 * NCH], F32)
    k.dma("Y", cwb.ap, W["gd_conv"], writes=[cwb])
    trilS = A("trilS", [128, 128], BF16)
    k.memset("V", trilS.ap, 1.0, writes=[trilS])
    k.affsel(trilS.ap, trilS.ap, [[-1, 128]], ALU.is_ge, 0.0, 1, base=-1,
             reads=[trilS], writes=[trilS])
    triuI = A("triuI", [128, 128], BF16)
    k.memset("V", triuI.ap, 1.0, writes=[triuI])
    k.affsel(triuI.ap, triuI.ap, [[1, 128]], ALU.is_ge, 0.0, -1,
             reads=[triuI], writes=[triuI])
    sel16 = A("sel16", [16, 16 * 128], F32)
    k.memset("V", sel16.ap, 1.0, writes=[sel16])
    k.affsel(sel16.ap, sel16.ap, [[1, 16 * 128]], ALU.is_ge, 0.0, -128,
             reads=[sel16], writes=[sel16])
    k.affsel(sel16.ap, sel16.ap, [[-1, 16 * 128]], ALU.is_ge, 0.0, 128, base=127,
             reads=[sel16], writes=[sel16])
    sel2 = A("sel2", [2, 2 * 128], BF16)
    k.memset("V", sel2.ap, 1.0, writes=[sel2])
    k.affsel(sel2.ap, sel2.ap, [[1, 2 * 128]], ALU.is_ge, 0.0, -64,
             reads=[sel2], writes=[sel2])
    k.affsel(sel2.ap, sel2.ap, [[-1, 2 * 128]], ALU.is_ge, 0.0, 64, base=63,
             reads=[sel2], writes=[sel2])

    mL = k.mark()
    sL = []

    def AL(name, shape, dtype):
        t = k.alloc(name, shape, dtype)
        sL.append(t)
        return t
    scr = ln_scratch(k, AL)
    emit_ln_stats(k, k.X, scr)
    k.copy("V", mbk.ap, scr["mb"].ap, reads=[scr["mb"]], writes=[mbk])
    k.copy("V", rbk.ap, scr["rb"].ap, reads=[scr["rb"]], writes=[rbk])
    k.release(mL, sL)

    mMid = k.mark()
    sMid = []

    def AM(name, shape, dtype):
        t = k.alloc(name, shape, dtype)
        sMid.append(t)
        return t
    o_s = k.nc.dram_tensor("gd_o_scratch", [D, T], BF16, kind="Internal").ap()
    o_sb = Buf("gd_o_scratch")
    o_buf = AM("obuf", [128, NCH * C], BF16)
    xc = AM("xc", [128, NCH * T], BF16)
    mC = k.mark()
    sC = []

    def AC(name, shape, dtype):
        t = k.alloc(name, shape, dtype)
        sC.append(t)
        return t
    xn1 = AC("xn1", [128, T + 3], BF16)
    rsb = AC("rsb", [128, T], F32)
    h2t = AC("h2t", [128, 512], BF16)
    cvt = AC("cvt", [128, 512], F32)
    for tc in range(4):
        sl = slice(tc * 512, (tc + 1) * 512)
        for c in range(NCH):
            k.tt("G", cvt.ap, k.X[c].h[:, sl], mbk.h[:, sl], ALU.subtract,
                 reads=[k.X[c], mbk], writes=[cvt])
            k.tt("V", h2t.ap, cvt.ap, rbk.h[:, sl], ALU.mult,
                 reads=[cvt, rbk], writes=[h2t])
            k.act(h2t.ap, h2t.ap, AF.Square, reads=[h2t], writes=[h2t])
            k.mmop(k.pb[0], [(k.ones_bf.ap, h2t.ap)], reads=[k.ones_bf, h2t],
                   accum=(c > 0), close=(c == NCH - 1))
        k.ts("V", rsb.h[:, sl], k.pb[0].ap, 1.0 / D, ALU.mult, s2=1e-6, op1=ALU.add,
             reads=[k.pb[0]], writes=[rsb])
        k.act(rsb.h[:, sl], rsb.h[:, sl], AF.Sqrt, reads=[rsb], writes=[rsb])
    k.recip(rsb.ap, rsb.ap, reads=[rsb], writes=[rsb])
    for c in range(NCH):
        k.memset("V", xn1.h[:, 0:3], 0.0, writes=[xn1])
        for tc in range(4):
            sl = slice(tc * 512, (tc + 1) * 512)
            k.tt("G", cvt.ap, k.X[c].h[:, sl], mbk.h[:, sl], ALU.subtract,
                 reads=[k.X[c], mbk], writes=[cvt])
            k.tt("V", cvt.ap, cvt.ap, rbk.h[:, sl], ALU.mult,
                 reads=[cvt, rbk], writes=[cvt])
            k.tt("G", cvt.ap, cvt.ap, rsb.h[:, sl], ALU.mult, reads=[cvt, rsb], writes=[cvt])
            k.ts("V", xn1.h[:, 3 + tc * 512: 3 + tc * 512 + 512],
                 cvt.ap, n12.h[:, c:c + 1], ALU.mult, reads=[cvt, n12], writes=[xn1])
        for hc in range(4):
            t0 = hc * 512
            k.ts("V", cvt.ap, xn1.h[:, t0: t0 + 512],
                 cwb.h[:, c:c + 1], ALU.mult, reads=[xn1, cwb], writes=[cvt])
            for j in range(1, 4):
                k.stt("V", cvt.ap, xn1.h[:, j + t0: j + t0 + 512],
                      cwb.h[:, j * NCH + c: j * NCH + c + 1], cvt.ap,
                      ALU.mult, ALU.add, reads=[xn1, cwb, cvt], writes=[cvt])
            k.act(xc.h[:, c * T + t0: c * T + t0 + 512], cvt.ap, AF.Silu,
                  bias=cwb.h[:, 4 * NCH + c: 4 * NCH + c + 1],
                  reads=[cvt, cwb], writes=[xc])
    k.release(mC, sC)
    if int(os.environ.get("GD_PH", "9")) <= 1:
        k.release(mMid, sMid)
        k.release(m0, tiles)
        return

    mD = k.mark()
    sD = []

    def AD(name, shape, dtype):
        t = k.alloc(name, shape, dtype)
        sD.append(t)
        return t
    betaf = AD("betaf", [16, T], BF16)
    lgf = AD("lgf", [16, T], F32)
    Sst = AD("Sst", [64, H * 64], F32)
    Sbf = AD("Sbf", [64, H * 64], BF16)
    k.memset("V", Sst.ap, 0.0, writes=[Sst])
    k.memset("V", Sbf.ap, 0.0, writes=[Sbf])
    mB = k.mark()
    sB = []
    wba = k.alloc("wba", [128, NCH * 16], BF16)
    t16a = k.alloc("t16a", [16, 512], F32)
    t16b = k.alloc("t16b", [16, 512], F32)
    sB.extend([wba, t16a, t16b])
    for (wnm, coff) in (("gd_wb", 0), ("gd_wa", 1)):
        k.wload("gd_wo", wba.h[:].rearrange("p (c i) -> p c i", i=16),
              W[wnm].rearrange("(c p) i -> p c i", p=128), writes=[wba])
        for tc in range(4):
            sl = slice(tc * 512, (tc + 1) * 512)
            k.mmop(k.pb[0], [(wba.h[:, c * 16:(c + 1) * 16],
                              xc.h[:, c * T + tc * 512: c * T + tc * 512 + 512])
                             for c in range(NCH)], reads=[wba, xc], nparts=16)
            if coff == 0:
                k.act(betaf.h[:, sl], k.pb[0].h[0:16, :], AF.Sigmoid,
                      reads=[k.pb[0]], writes=[betaf])
            else:
                k.ts("V", t16a.ap, k.pb[0].h[0:16, :], 4.0, ALU.add,
                     reads=[k.pb[0]], writes=[t16a])
                k.act(t16b.ap, t16a.ap, AF.Abs, reads=[t16a], writes=[t16b])
                k.act(t16b.ap, t16b.ap, AF.Exp, scale=-1.0, reads=[t16b], writes=[t16b])
                k.act(t16b.ap, t16b.ap, AF.Ln, bias=1.0, reads=[t16b], writes=[t16b])
                k.ts("V", t16a.ap, t16a.ap, 0.0, ALU.min, reads=[t16a], writes=[t16a])
                k.tt("V", lgf.h[:, sl], t16a.ap, t16b.ap, ALU.subtract,
                     reads=[t16a, t16b], writes=[lgf])
    zz16 = t16a
    k.memset("V", zz16.ap, 0.0, writes=[zz16])
    for cc in range(NCHK):
        k.scan("V", lgf.h[0:16, cc * C:(cc + 1) * C], lgf.h[0:16, cc * C:(cc + 1) * C],
               zz16.h[0:16, 0:C], 0.0, ALU.add, ALU.add, reads=[lgf, zz16], writes=[lgf])
    k.release(mB, sB)
    if int(os.environ.get("GD_PH", "9")) <= 2:
        k.release(mMid, sMid)
        k.release(m0, tiles)
        return

    qcf = AD("qcf", [128, NCH * C], BF16)
    kcf = AD("kcf", [128, NCH * C], BF16)
    vcf = AD("vcf", [128, NCH * C], BF16)
    wpj = [AD(f"wpj{i}", [128, NCH * 1024], BF16) for i in range(2)]
    wvs = [AD(f"wvs{i}", [128, NCH * 128], BF16) for i in range(2)]
    lgtm = AD("lgtm", [128, 16], F32)
    bttm = AD("bttm", [128, 16], F32)
    sq2 = AD("sq2", [128, C], BF16)
    rs2 = AD("rs2", [2, C], F32)
    rs2b = AD("rs2b", [2, C], BF16)
    nrm = AD("nrm", [128, C], BF16)
    US = []
    for ui in range(2):
        u = {}
        for nm, sh, dt_ in (("gC2", [64, 1], F32), ("gcol", [128, 1], F32),
                            ("m2f", [128, C], F32), ("t1f", [128, C], F32),
                            ("qh_t", [64, C], BF16), ("kh_t", [64, C], BF16),
                            ("kb_t", [64, C], BF16), ("Ebf", [128, C], BF16),
                            ("ETbf", [128, C], BF16), ("grow", [128, C], BF16),
                            ("Lp", [128, C], F32), ("Lt", [128, C], F32),
                            ("L2p", [128, C], F32), ("L2t", [128, C], F32),
                            ("MT", [128, C], BF16), ("Vtm", [128, 64], BF16),
                            ("Rtm", [128, 64], F32), ("Xtm", [128, 64], F32),
                            ("Xbf", [128, 64], BF16), ("obf", [128, 64], BF16),
                            ("ktm", [128, 64], BF16)):
            u[nm] = AD(f"{nm}{ui}", sh, dt_)
        US.append(u)
    trilSf = AD("trilSf", [128, 128], F32)
    hsum2 = AD("hsum2", [128, 2], BF16)
    k.memset("V", trilSf.ap, 1.0, writes=[trilSf])
    k.affsel(trilSf.ap, trilSf.ap, [[-1, 128]], ALU.is_ge, 0.0, 1, base=-1,
             reads=[trilSf], writes=[trilSf])
    k.transp(k.pbh[0], sel2.h[0:2, 0:128], k.ident.h[0:2, 0:2], slice(0, 2),
             reads=[sel2, k.ident], nparts=128)
    k.copy("V", hsum2.ap, k.pbh[0].h[:, 0:2], reads=[k.pbh[0]], writes=[hsum2])
    for cc in range(NCHK):
        csl = slice(cc * C, (cc + 1) * C)
        for wi, (wnm, dst) in enumerate((("gd_wq", qcf), ("gd_wk", kcf))):
            wt = wpj[wi]
            if cc == 0:
                k.wload(wnm, wt.h[:].rearrange("p (c i) -> p c i", i=1024),
                        W[wnm].rearrange("(c p) i -> p c i", p=128), writes=[wt])
            for oc in range(NCH):
                bank = k.pb[oc % 2]
                k.mmop(bank, [(wt.h[:, c * 1024 + oc * 128: c * 1024 + oc * 128 + 128],
                               xc.h[:, c * T + cc * C: c * T + cc * C + C])
                              for c in range(NCH)], reads=[wt, xc],
                       out_slice=slice(0, C))
                k.copy("S" if oc % 2 else "V", dst.h[:, oc * C:(oc + 1) * C],
                       bank.h[:, 0:C], reads=[bank], writes=[dst])
        for oc in range(NCH):
            wv_ = wvs[oc % 2]
            k.wload("gd_wv", wv_.h[:].rearrange("p (c i) -> p c i", i=128),
                    W["gd_wv"][:, oc * 128:(oc + 1) * 128].rearrange(
                        "(c p) i -> p c i", p=128), writes=[wv_])
            bank = k.pb[2 + oc % 2]
            k.mmop(bank, [(wv_.h[:, c * 128:(c + 1) * 128],
                           xc.h[:, c * T + cc * C: c * T + cc * C + C])
                          for c in range(NCH)], reads=[wv_, xc],
                   out_slice=slice(0, C))
            k.copy("S" if oc % 2 else "V", vcf.h[:, oc * C:(oc + 1) * C],
                   bank.h[:, 0:C], reads=[bank], writes=[vcf])
        for dst in (qcf, kcf):
            for oc in range(NCH):
                osl = slice(oc * C, (oc + 1) * C)
                k.act(sq2.ap, dst.h[:, osl], AF.Square, reads=[dst], writes=[sq2])
                k.mmop(k.pb[2], [(hsum2.ap, sq2.ap)], reads=[hsum2, sq2], nparts=2,
                       out_slice=slice(0, C))
                k.ts("V", rs2.ap, k.pb[2].h[0:2, 0:C], 1.0, ALU.mult, s2=1e-6, op1=ALU.add,
                     reads=[k.pb[2]], writes=[rs2])
                k.act(rs2.ap, rs2.ap, AF.Sqrt, reads=[rs2], writes=[rs2])
                k.recip(rs2.ap, rs2.ap, reads=[rs2], writes=[rs2])
                k.copy("V", rs2b.ap, rs2.ap, reads=[rs2], writes=[rs2b])
                k.mmop(k.pb[3], [(sel2.h[0:2, 0:128], rs2b.h[0:2, :])],
                       reads=[sel2, rs2b], out_slice=slice(0, C))
                k.copy("V", nrm.ap, k.pb[3].h[:, 0:C], reads=[k.pb[3]], writes=[nrm])
                k.tt("V", dst.h[:, osl], dst.h[:, osl], nrm.ap, ALU.mult,
                     reads=[dst, nrm], writes=[dst])
        k.transp(k.pb[4], lgf.h[0:16, csl], k.identf.h[0:16, 0:16], slice(0, 16),
                 reads=[lgf, k.identf], nparts=128)
        k.copy("V", lgtm.ap, k.pb[4].h[:, 0:16], reads=[k.pb[4]], writes=[lgtm])
        k.transp(k.pbh[0], betaf.h[0:16, csl], k.ident.h[0:16, 0:16], slice(0, 16),
                 reads=[betaf, k.ident], nparts=128)
        k.copy("V", bttm.ap, k.pbh[0].h[:, 0:16], reads=[k.pbh[0]], writes=[bttm])
        for h in range(H):
            u = US[h % 2]
            B0 = k.pb[(h % 2) * 3]
            B1 = k.pb[(h % 2) * 3 + 1]
            B2 = k.pb[(h % 2) * 3 + 2]
            PH = k.pbh[h % 2]
            oc = h // 2
            pb_ = (h % 2) * 64
            qsl = slice(oc * C, (oc + 1) * C)
            ssl = slice(h * 64, (h + 1) * 64)
            btcol = bttm.h[:, h:h + 1]
            k.copy("V", u["qh_t"].ap, qcf.h[pb_:pb_ + 64, qsl], reads=[qcf], writes=[u["qh_t"]])
            k.copy("G", u["kh_t"].ap, kcf.h[pb_:pb_ + 64, qsl], reads=[kcf], writes=[u["kh_t"]])
            k.mmop(B0, [(sel16.h[0:16, h * 128:(h + 1) * 128], lgf.h[0:16, csl])],
                   reads=[sel16, lgf], out_slice=slice(0, C))
            k.act(u["grow"].ap, B0.h[:, 0:C], AF.Exp, reads=[B0], writes=[u["grow"]])
            k.ts("V", u["m2f"].ap, B0.h[:, 0:C], lgtm.h[:, h:h + 1], ALU.subtract,
                 reads=[B0, lgtm], writes=[u["m2f"]])
            k.ts("V", u["t1f"].ap, u["m2f"].ap, 0.0, ALU.max, reads=[u["m2f"]], writes=[u["t1f"]])
            k.act(u["Ebf"].ap, u["t1f"].ap, AF.Exp, scale=-1.0, reads=[u["t1f"]], writes=[u["Ebf"]])
            k.ts("V", u["t1f"].ap, u["m2f"].ap, 0.0, ALU.min, reads=[u["m2f"]], writes=[u["t1f"]])
            k.act(u["ETbf"].ap, u["t1f"].ap, AF.Exp, reads=[u["t1f"]], writes=[u["ETbf"]])
            k.mmop(B1, [(u["kh_t"].ap, u["kh_t"].ap)], reads=[u["kh_t"]], out_slice=slice(0, C))
            k.tt("V", u["Lp"].ap, B1.h[:, 0:C], u["Ebf"].ap, ALU.mult,
                 reads=[B1, u["Ebf"]], writes=[u["Lp"]])
            k.ts("V", u["Lp"].ap, u["Lp"].ap, btcol, ALU.mult, reads=[u["Lp"], bttm], writes=[u["Lp"]])
            k.tt("G", u["Lp"].ap, u["Lp"].ap, trilSf.ap, ALU.mult, reads=[u["Lp"], trilSf], writes=[u["Lp"]])
            k.transp(B0, u["Lp"].ap, k.identf.ap, slice(0, 128), reads=[u["Lp"], k.identf])
            k.copy("S", u["Lt"].ap, B0.h[:, 0:C], reads=[B0], writes=[u["Lt"]])
            k.mmop(B2, [(u["kh_t"].ap, u["qh_t"].ap)], reads=[u["kh_t"], u["qh_t"]],
                   out_slice=slice(0, C))
            k.tt("V", u["MT"].ap, B2.h[:, 0:C], u["ETbf"].ap, ALU.mult,
                 reads=[B2, u["ETbf"]], writes=[u["MT"]])
            k.tt("G", u["MT"].ap, u["MT"].ap, triuI.ap, ALU.mult, reads=[u["MT"], triuI], writes=[u["MT"]])
            k.tt("V", u["kb_t"].ap, u["kh_t"].ap, u["grow"].h[0:64, :], ALU.mult,
                 reads=[u["kh_t"], u["grow"]], writes=[u["kb_t"]])
            k.tt("G", u["qh_t"].ap, u["qh_t"].ap, u["grow"].h[0:64, :], ALU.mult,
                 reads=[u["qh_t"], u["grow"]], writes=[u["qh_t"]])
            k.transp(PH, vcf.h[pb_:pb_ + 64, qsl],
                     k.ident.h[pb_:pb_ + 64, pb_:pb_ + 64],
                     slice(0, 64), reads=[vcf, k.ident])
            k.copy("V", u["Vtm"].ap, PH.h[:, 0:64], reads=[PH], writes=[u["Vtm"]])
            k.transp(PH, u["kh_t"].ap, k.ident.h[0:64, 0:64], slice(64, 128),
                     reads=[u["kh_t"], k.ident])
            k.copy("V", u["ktm"].ap, PH.h[:, 64:128], reads=[PH], writes=[u["ktm"]])
            k.mmop(B2, [(u["kb_t"].ap, Sbf.h[0:64, ssl])], reads=[u["kb_t"], Sbf],
                   out_slice=slice(0, 64))
            k.tt("V", u["Rtm"].ap, u["Vtm"].ap, B2.h[:, 0:64], ALU.subtract,
                 reads=[u["Vtm"], B2], writes=[u["Rtm"]])
            k.ts("V", u["Rtm"].ap, u["Rtm"].ap, btcol, ALU.mult, reads=[u["Rtm"], bttm], writes=[u["Rtm"]])
            k.mmop(B1, [(u["Lt"].ap, u["Rtm"].ap)], reads=[u["Lt"], u["Rtm"]], out_slice=slice(0, 64))
            k.tt("V", u["Xtm"].ap, u["Rtm"].ap, B1.h[:, 0:64], ALU.subtract,
                 reads=[u["Rtm"], B1], writes=[u["Xtm"]])
            cp, ct, np_, nt = u["Lp"], u["Lt"], u["L2p"], u["L2t"]
            for lev in range(6):
                k.mmop(B1, [(ct.ap, cp.ap)], reads=[ct, cp], out_slice=slice(0, C))
                k.copy("S", np_.ap, B1.h[:, 0:C], reads=[B1], writes=[np_])
                k.transp(B0, np_.ap, k.identf.ap, slice(0, 128), reads=[np_, k.identf])
                k.copy("S", nt.ap, B0.h[:, 0:C], reads=[B0], writes=[nt])
                k.mmop(B1, [(nt.ap, u["Xtm"].ap)], reads=[nt, u["Xtm"]], out_slice=slice(0, 64))
                k.tt("V", u["Xtm"].ap, u["Xtm"].ap, B1.h[:, 0:64], ALU.add,
                     reads=[u["Xtm"], B1], writes=[u["Xtm"]])
                cp, ct, np_, nt = np_, nt, cp, ct
            k.copy("G", u["Xbf"].ap, u["Xtm"].ap, reads=[u["Xtm"]], writes=[u["Xbf"]])
            k.mmop(B2, [(u["qh_t"].ap, Sbf.h[0:64, ssl]), (u["MT"].ap, u["Xbf"].ap)],
                   reads=[u["qh_t"], Sbf, u["MT"], u["Xbf"]], out_slice=slice(0, 64))
            k.copy("S", u["obf"].ap, B2.h[:, 0:64], reads=[B2], writes=[u["obf"]])
            k.transp(PH, u["obf"].ap, k.ident.ap, slice(0, C), reads=[u["obf"], k.ident],
                     nparts=64)
            k.copy("V", o_buf.h[pb_:pb_ + 64, oc * C:(oc + 1) * C],
                   PH.h[0:64, 0:C], reads=[PH], writes=[o_buf])
            k.act(u["gcol"].ap, u["m2f"].h[:, C - 1:C], AF.Exp, reads=[u["m2f"]], writes=[u["gcol"]])
            k.ts("V", u["ktm"].ap, u["ktm"].ap, u["gcol"].h[:, 0:1], ALU.mult,
                 reads=[u["ktm"], u["gcol"]], writes=[u["ktm"]])
            k.mmop(B2, [(u["ktm"].ap, u["Xbf"].ap)], reads=[u["ktm"], u["Xbf"]],
                   out_slice=slice(0, 64), nparts=64)
            k.copy("V", u["gC2"].ap, u["grow"].h[0:64, C - 1:C], reads=[u["grow"]], writes=[u["gC2"]])
            k.stt("V", Sst.h[0:64, ssl], Sst.h[0:64, ssl], u["gC2"].h[0:64, 0:1],
                  B2.h[0:64, 0:64], ALU.mult, ALU.add,
                  reads=[Sst, u["gC2"], B2], writes=[Sst])
            k.copy("G", Sbf.h[0:64, ssl], Sst.h[0:64, ssl], reads=[Sst], writes=[Sbf])
        k.dma("S", o_s.rearrange("(c p) t -> p c t", p=128)[:, :, cc * C:(cc + 1) * C],
              o_buf.h[:].rearrange("p (c t) -> p c t", t=C), reads=[o_buf], writes=[o_sb])
    k.release(mD, sD)
    if int(os.environ.get("GD_PH", "9")) <= 3:
        k.release(mMid, sMid)
        k.release(m0, tiles)
        return
    hh = AM("hh", [128, NCH * T], BF16)

    mW = k.mark()
    sW = []
    wob = k.alloc("wob", [128, NCH * 128], BF16)
    sW.append(wob)
    ob2 = k.alloc("ob2", [128, NCH * 512], BF16)
    sW.append(ob2)
    for tc in range(4):
        k.dma("Y", ob2.h[:].rearrange("p (c t) -> p c t", t=512),
              o_s.rearrange("(c p) t -> p c t", p=128)[:, :, tc * 512:(tc + 1) * 512],
              reads=[o_sb], writes=[ob2])
        for dp in range(NCH):
            tsl = slice(tc * 512, (tc + 1) * 512)
            bank = k.pb[2 + dp % 2]
            k.wload("gd_wo", wob.h[:].rearrange("p (c i) -> p c i", i=128),
                    W["gd_wo"][:, dp * 128:(dp + 1) * 128].rearrange(
                        "(c p) i -> p c i", p=128), writes=[wob])
            k.mmop(bank, [(wob.h[:, c * 128:(c + 1) * 128],
                           ob2.h[:, c * 512:(c + 1) * 512])
                          for c in range(NCH)], reads=[wob, ob2])
            hsl2 = slice(dp * T + tc * 512, dp * T + tc * 512 + 512)
            k.tt("G", hh.h[:, hsl2], k.X[dp].h[:, tsl], mbk.h[:, tsl], ALU.subtract,
                 reads=[k.X[dp], mbk], writes=[hh])
            k.tt("V", hh.h[:, hsl2], hh.h[:, hsl2], rbk.h[:, tsl], ALU.mult,
                 reads=[hh, rbk], writes=[hh])
            k.tt("V", hh.h[:, hsl2], hh.h[:, hsl2], bank.ap, ALU.add,
                 reads=[hh, bank], writes=[hh])
            xt = k.X[dp]
            k.tt("G", xt.h[:, tsl], xt.h[:, tsl],
                 hh.h[:, dp * T + tc * 512: dp * T + tc * 512 + 512], ALU.add,
                 reads=[xt, hh], writes=[xt])
    k.release(mW, sW)

    if int(os.environ.get("GD_PH", "9")) <= 4:
        k.release(mMid, sMid)
        k.release(m0, tiles)
        return

    sW = []
    mW = k.mark()

    def AW(name, shape, dtype):
        t = k.alloc(name, shape, dtype)
        sW.append(t)
        return t
    rsb2 = AW("rsb2", [128, T], F32)
    h2b = AW("h2b", [128, 512], BF16)
    for tc in range(4):
        sl = slice(tc * 512, (tc + 1) * 512)
        for c in range(NCH):
            k.act(h2b.ap, hh.h[:, c * T + tc * 512: c * T + tc * 512 + 512],
                  AF.Square, reads=[hh], writes=[h2b])
            k.mmop(k.pb[0], [(k.ones_bf.ap, h2b.ap)], reads=[k.ones_bf, h2b],
                   accum=(c > 0), close=(c == NCH - 1))
        k.ts("V", rsb2.h[:, sl], k.pb[0].ap, 1.0 / D, ALU.mult, s2=1e-6, op1=ALU.add,
             reads=[k.pb[0]], writes=[rsb2])
        k.act(rsb2.h[:, sl], rsb2.h[:, sl], AF.Sqrt, reads=[rsb2], writes=[rsb2])
    k.recip(rsb2.ap, rsb2.ap, reads=[rsb2], writes=[rsb2])
    for c in range(NCH):
        for tc in range(4):
            sl = slice(tc * 512, (tc + 1) * 512)
            hsl = slice(c * T + tc * 512, c * T + tc * 512 + 512)
            k.tt("G", hh.h[:, hsl], hh.h[:, hsl], rsb2.h[:, sl], ALU.mult,
                 reads=[hh, rsb2], writes=[hh])
            k.ts("V", hh.h[:, hsl], hh.h[:, hsl], n12.h[:, NCH + c:NCH + c + 1],
                 ALU.mult, reads=[hh, n12], writes=[hh])
    hbuf = AW("hbuf", [128, 22 * 512], BF16)
    wgb = [AW(f"wgb{i}", [128, NCH * 128], BF16) for i in range(2)]
    wub = [AW(f"wub{i}", [128, NCH * 128], BF16) for i in range(2)]
    wdb2 = [AW(f"wdb2{i}", [128, 128], BF16) for i in range(2)]
    sgtp2 = [AW(f"sgt{i}", [128, 512], F32) for i in range(2)]
    ic_off = [0]
    for w_ in ICW:
        ic_off.append(ic_off[-1] + w_)
    for tc in range(4):
        tsl = slice(tc * 512, (tc + 1) * 512)
        for ic in range(22):
            iw = ICW[ic]
            wg_ = wgb[ic % 2]
            wu_ = wub[ic % 2]
            k.wload("gd_wg", wg_.h[:, 0:NCH * iw].rearrange("p (c i) -> p c i", i=iw),
                  W["gd_wg"][:, ic_off[ic]:ic_off[ic] + iw].rearrange(
                      "(c p) i -> p c i", p=128), writes=[wg_])
            k.wload("gd_wu", wu_.h[:, 0:NCH * iw].rearrange("p (c i) -> p c i", i=iw),
                  W["gd_wu"][:, ic_off[ic]:ic_off[ic] + iw].rearrange(
                      "(c p) i -> p c i", p=128), writes=[wu_])
            bg_ = k.pb[(ic % 2) * 2]
            bu_ = k.pb[(ic % 2) * 2 + 1]
            k.mmop(bg_, [(wg_.h[:, c * iw:(c + 1) * iw],
                          hh.h[:, c * T + tc * 512: c * T + tc * 512 + 512])
                         for c in range(NCH)], reads=[wg_, hh], nparts=iw)
            k.mmop(bu_, [(wu_.h[:, c * iw:(c + 1) * iw],
                          hh.h[:, c * T + tc * 512: c * T + tc * 512 + 512])
                         for c in range(NCH)], reads=[wu_, hh], nparts=iw)
            sgt_ = sgtp2[ic % 2]
            k.act(sgt_.h[0:iw, :], bg_.h[0:iw, :], AF.Silu, reads=[bg_], writes=[sgt_])
            k.tt("V", hbuf.h[0:iw, ic * 512:(ic + 1) * 512], sgt_.h[0:iw, :],
                 bu_.h[0:iw, :], ALU.mult, reads=[sgt_, bu_], writes=[hbuf])
        for dp in range(NCH):
            bank = k.pb[2 + dp % 2]
            for ic in range(22):
                iw = ICW[ic]
                wd_ = wdb2[ic % 2]
                k.wload("gd_wd", wd_.h[0:iw, :],
                      W["gd_wd"][ic_off[ic]:ic_off[ic] + iw, dp * 128:(dp + 1) * 128],
                      writes=[wd_])
                k.mmop(bank, [(wd_.h[0:iw, :], hbuf.h[0:iw, ic * 512:(ic + 1) * 512])],
                       reads=[wd_, hbuf], accum=(ic > 0), close=(ic == 21))
            xt = k.X[dp]
            k.tt("V", xt.h[:, tsl], xt.h[:, tsl], bank.ap, ALU.add,
                 reads=[xt, bank], writes=[xt])
    k.release(mW, sW)
    k.release(mMid, sMid)
    k.release(m0, tiles)


STEP_FNS["l1_attn_b"] = step_l1_attn_b
STEP_WEIGHTS["l1_attn_b"] = {
    "gd_norms": (128, 2 * NCH), "gd_conv": (128, 5 * NCH),
    "gd_wq": (D, D), "gd_wk": (D, D), "gd_wv": (D, D),
    "gd_wb": (D, 16), "gd_wa": (D, 16), "gd_wo": (D, D),
    "gd_wg": (D, 2764), "gd_wu": (D, 2764), "gd_wd": (2764, D),
}
STEP_FNS["l1_attn_b"] = step_l1_attn_b


# version 14 (final)
# speedup vs baseline: 1.2158x; 1.0078x over previous
import os
import numpy as np
import concourse.bass as bass
import concourse.mybir as mybir
from concourse.bass_utils import run_bass_kernel_spmd

F32 = mybir.dt.float32
BF16 = mybir.dt.bfloat16
I32 = mybir.dt.int32
AF = mybir.ActivationFunctionType
ALU = mybir.AluOpType
AX = mybir.AxisListType

D = 1024
T = 2048
NCH = 8
PBYTES = 229000

ENGINES = ("T", "V", "S", "G", "Y")
DMA_DEPTH = 4
DMA_SEMS = {q: [f"d{q}{j}" for j in range(DMA_DEPTH)] for q in ("Y", "S", "G")}
ALL_DMA_SEMS = [s for v in DMA_SEMS.values() for s in v]


class Buf:
    __slots__ = ("name", "w", "rs", "pend", "serial")

    def __init__(self, name="", serial=False):
        self.name = name
        self.w = None
        self.rs = []
        self.pend = []
        self.serial = serial


class Tile:
    __slots__ = ("h", "buf")

    def __init__(self, h, buf):
        self.h = h
        self.buf = buf

    def __getitem__(self, idx):
        return self.h[idx]

    @property
    def ap(self):
        return self.h[:]


class Ker:
    def __init__(self, nc):
        self.nc = nc
        self.recs = {e: [] for e in ENGINES}
        self.sems = {}
        for s in list(ENGINES) + ALL_DMA_SEMS:
            self.sems[s] = nc.alloc_semaphore(name="sem_" + s)
        self.cnt = {s: 0 for s in self.sems}
        self.dmacnt = {q: 0 for q in DMA_SEMS}
        self.waited = {e: {} for e in ENGINES}
        self.nops = 0
        self.const_vals = {}
        self.cst = nc.alloc_sbuf_tensor("cst", [128, 64], F32)
        self.base = int(nc.sbuf_base) + 4 * 1024
        self.const(0.0)
        self.const(1.0)
        self.residue = []
        self.pb = [Tile(nc.alloc_psum_tensor(f"pb{i}", [128, 512], F32), Buf(f"pb{i}", serial=True))
                   for i in range(6)]
        self.pbh = [Tile(nc.alloc_psum_tensor(f"pbh{i}", [128, 1024], BF16), Buf(f"pbh{i}", serial=True))
                    for i in range(2)]
        self.init_tok = ("V", 1)
        self.cnt["V"] = 1

    def const(self, v):
        v = float(np.float32(v))
        if v in self.const_vals:
            return v
        i = len(self.const_vals)
        assert i < 64
        self.const_vals[v] = i
        self.nc.const_aps.register(F32, v, self.cst[:, i:i + 1])
        return v

    def cap(self, v):
        v = self.const(v)
        i = self.const_vals[v]
        return self.cst[:, i:i + 1]

    def alloc(self, name, shape, dtype, align=64):
        esz = 2 if dtype == BF16 else 4
        nbytes = int(np.prod(shape[1:])) * esz
        start = (self.base + align - 1) // align * align
        end = start + nbytes
        assert end <= PBYTES, f"SBUF overflow: {name} at {start}+{nbytes}"
        self.base = end
        self.nops2 = getattr(self, "nops2", 0) + 1
        h = self.nc.alloc_sbuf_tensor_at(f"{name}_{start}_{self.nops2}", shape, dtype,
                                         offset=start)
        b = Buf(name)
        for (lo, hi, toks) in self.residue:
            if lo < end and start < hi:
                b.pend.extend(toks)
        return Tile(h, b)

    def mark(self):
        return self.base

    def release(self, m, tiles):
        toks = []
        for t in tiles:
            b = t.buf
            if b.w:
                toks.append(b.w)
            toks.extend(b.rs)
        if self.base > m:
            self.residue.append((m, self.base, toks))
        self.base = m

    def op(self, eng, emit, reads=(), writes=(), waits=(), dma=False, n_inc=1):
        toks = [self.init_tok]
        for b in reads:
            b = b.buf if isinstance(b, Tile) else b
            if b.w:
                toks.append(b.w)
            if b.serial:
                toks.extend(b.rs)
            toks.extend(b.pend)
        for b in writes:
            b = b.buf if isinstance(b, Tile) else b
            if b.w:
                toks.append(b.w)
            toks.extend(b.rs)
            toks.extend(b.pend)
        toks.extend(waits)
        need = {}
        for (s, v) in toks:
            if v > need.get(s, -1):
                need[s] = v
        wl = []
        for s, v in sorted(need.items()):
            if v > self.waited[eng].get(s, -1) and v > 0:
                wl.append((s, v))
                self.waited[eng][s] = v
        if dma:
            i = self.dmacnt[eng]
            self.dmacnt[eng] += 1
            sem = DMA_SEMS[eng][i % DMA_DEPTH]
            gate = 16 * (i // DMA_DEPTH)
            if gate > self.waited[eng].get(sem, -1):
                wl.append((sem, gate))
                self.waited[eng][sem] = gate
            self.cnt[sem] += 16
            tok = (sem, self.cnt[sem])
            amt = 16
        else:
            sem = eng
            self.cnt[sem] += 1
            tok = (sem, self.cnt[sem])
            amt = 1
        self.recs[eng].append((wl, emit, self.sems[sem], amt))
        self.nops += 1
        for b in reads:
            b = b.buf if isinstance(b, Tile) else b
            b.rs.append(tok)
        for b in writes:
            b = b.buf if isinstance(b, Tile) else b
            b.w = tok
            b.rs = []
            b.pend = []
        return tok

    def dma(self, eng, dst, src, reads=(), writes=()):
        return self.op(eng, (lambda e, d=dst, s=src: e.dma_start(d, s)),
                       reads=reads, writes=writes, dma=True)

    def wload(self, nm, dst, src, writes=()):
        rd = [self.WB[nm]] if nm in getattr(self, "WB", {}) else []
        return self.dma("Y", dst, src, reads=rd, writes=writes)

    def tt(self, eng, out, a, b, alu, reads=(), writes=()):
        return self.op(eng, (lambda e, o=out, x=a, y=b, u=alu: e.tensor_tensor(o, x, y, u)),
                       reads=reads, writes=writes)

    def ts(self, eng, out, a, s1, op0, s2=None, op1=ALU.bypass, reads=(), writes=()):
        if isinstance(s1, float):
            s1 = float(np.float32(s1))
        if isinstance(s2, float):
            s2 = float(np.float32(s2))
        return self.op(eng, (lambda e, o=out, x=a, a1=s1, a2=s2, p0=op0, p1=op1:
                             e.tensor_scalar(o, x, a1, a2, p0, p1)),
                       reads=reads, writes=writes)

    def stt(self, eng, out, in0, scalar, in1, op0, op1, reads=(), writes=()):
        return self.op(eng, (lambda e, o=out, i0=in0, sc=scalar, i1=in1, p0=op0, p1=op1:
                             e.scalar_tensor_tensor(o, i0, sc, i1, p0, p1)),
                       reads=reads, writes=writes)

    def act(self, out, in_, fn, scale=1.0, bias=0.0, reads=(), writes=(), accum_out=None):
        if isinstance(scale, float):
            scale = self.const(scale)
        if isinstance(bias, float):
            bias = self.const(bias)
        return self.op("S", (lambda e, o=out, i=in_, f=fn, sc=scale, bi=bias, ac=accum_out:
                             e.activation(o, i, f, bias=bi, scale=sc, accum_out=ac)),
                       reads=reads, writes=writes)

    def copy(self, eng, out, in_, reads=(), writes=()):
        if eng == "S":
            return self.op("S", (lambda e, o=out, i=in_: e.copy(o, i)),
                           reads=reads, writes=writes)
        return self.op(eng, (lambda e, o=out, i=in_: e.tensor_copy(o, i)),
                       reads=reads, writes=writes)

    def memset(self, eng, ap, v, writes=()):
        return self.op(eng, (lambda e, a=ap, x=v: e.memset(a, x)), writes=writes)

    def red(self, eng, out, in_, alu, axis=AX.X, reads=(), writes=()):
        return self.op(eng, (lambda e, o=out, i=in_, u=alu, x=axis:
                             e.tensor_reduce(o, i, x, u)),
                       reads=reads, writes=writes)

    def recip(self, out, in_, reads=(), writes=()):
        return self.op("V", (lambda e, o=out, i=in_: e.reciprocal(o, i)),
                       reads=reads, writes=writes)

    def scan(self, eng, out, d0, d1, init, op0, op1, reads=(), writes=()):
        return self.op(eng, (lambda e, o=out, a=d0, b=d1, i=init, p0=op0, p1=op1:
                             e.tensor_tensor_scan(o, a, b, i, p0, p1)),
                       reads=reads, writes=writes)

    def mmop(self, bank, pairs, reads=(), accum=False, out_slice=None, nparts=128,
             close=True):
        ap = bank.h[0:nparts, :] if out_slice is None else bank.h[0:nparts, out_slice]

        def em(e, ap=ap, pairs=tuple(pairs), accum=accum, close=close):
            n = len(pairs)
            ins = None
            for i, (l, r) in enumerate(pairs):
                ins = e.matmul(ap, l, r, start=(i == 0 and not accum),
                               stop=(i == n - 1 and close))
            return ins
        rd = list(reads)
        if accum:
            rd.append(bank)
        return self.op("T", em, reads=rd, writes=[bank])

    def transp(self, bank, src_ap, ident_ap, out_slice, reads=(), nparts=128):
        return self.op("T", (lambda e, b=bank, s=src_ap, i=ident_ap, o=out_slice, np_=nparts:
                             e.transpose(b.h[0:np_, o], s, i)),
                       reads=list(reads), writes=[bank])

    def affsel(self, out, in_, pattern, cmp, fill, cm, base=0, reads=(), writes=()):
        return self.op("G", (lambda e, o=out, i=in_, p=pattern, c=cmp, f=fill, m=cm, b=base:
                             e.affine_select(o, i, p, c, f, base=b, channel_multiplier=m)),
                       reads=reads, writes=writes)

    def iota(self, out, pattern, cm=0, base=0, writes=()):
        return self.op("G", (lambda e, o=out, p=pattern, m=cm, b=base:
                             e.iota(o, p, base=b, channel_multiplier=m)),
                       writes=writes)

    def finalize(self):
        nc = self.nc
        recs = self.recs
        sems = self.sems
        cvals = sorted(self.const_vals.items(), key=lambda kv: kv[1])
        final = [(s, self.cnt[s]) for s in ALL_DMA_SEMS if self.cnt[s] > 0]

        with nc.Block() as block:
            @block.vector
            def _(v):
                ins = None
                for val, i in cvals:
                    ins = v.memset(self.cst[:, i:i + 1], val)
                if ins is None:
                    ins = v.memset(self.cst[:, 0:1], 0.0)
                ins.then_inc(sems["V"], 1)
                for (wl, emit, sem, amt) in recs["V"]:
                    for (s, val) in wl:
                        v.wait_ge(sems[s], val)
                    emit(v).then_inc(sem, amt)

            @block.scalar
            def _(s):
                for (wl, emit, sem, amt) in recs["S"]:
                    for (sn, val) in wl:
                        s.wait_ge(sems[sn], val)
                    emit(s).then_inc(sem, amt)
                for (sn, val) in final:
                    if sn.startswith("dS"):
                        s.wait_ge(sems[sn], val)

            @block.tensor
            def _(t):
                for (wl, emit, sem, amt) in recs["T"]:
                    for (sn, val) in wl:
                        t.wait_ge(sems[sn], val)
                    emit(t).then_inc(sem, amt)

            @block.gpsimd
            def _(g):
                for (wl, emit, sem, amt) in recs["G"]:
                    for (sn, val) in wl:
                        g.wait_ge(sems[sn], val)
                    emit(g).then_inc(sem, amt)
                for (sn, val) in final:
                    if sn.startswith("dG"):
                        g.wait_ge(sems[sn], val)

            @block.sync
            def _(y):
                for (wl, emit, sem, amt) in recs["Y"]:
                    for (sn, val) in wl:
                        y.wait_ge(sems[sn], val)
                    emit(y).then_inc(sem, amt)
                for (sn, val) in final:
                    if sn.startswith("dY"):
                        y.wait_ge(sems[sn], val)



def setup_consts(k):
    k.ident = k.alloc("ident", [128, 128], BF16)
    k.identf = k.alloc("identf", [128, 128], F32)
    k.ones_bf = k.alloc("ones_bf", [128, 128], BF16)
    k.ones_f = k.alloc("ones_f", [128, 128], F32)
    k.ones1 = k.alloc("ones1", [1, 128], F32)
    k.memset("V", k.ident.ap, 1.0, writes=[k.ident])
    k.memset("V", k.identf.ap, 1.0, writes=[k.identf])
    k.memset("V", k.ones_bf.ap, 1.0, writes=[k.ones_bf])
    k.memset("V", k.ones_f.ap, 1.0, writes=[k.ones_f])
    k.memset("V", k.ones1.ap, 1.0, writes=[k.ones1])
    k.affsel(k.ident.ap, k.ident.ap, [[1, 128]], ALU.is_equal, 0.0, -1,
             reads=[k.ident], writes=[k.ident])
    k.affsel(k.identf.ap, k.identf.ap, [[1, 128]], ALU.is_equal, 0.0, -1,
             reads=[k.identf], writes=[k.identf])
    k.e8sel = k.alloc("e8sel", [8, 8 * 128], F32)
    k.memset("V", k.e8sel.ap, 1.0, writes=[k.e8sel])
    k.affsel(k.e8sel.ap, k.e8sel.ap, [[1, 8 * 128]], ALU.is_ge, 0.0, -128,
             reads=[k.e8sel], writes=[k.e8sel])
    k.affsel(k.e8sel.ap, k.e8sel.ap, [[-1, 8 * 128]], ALU.is_ge, 0.0, 128, base=127,
             reads=[k.e8sel], writes=[k.e8sel])


def emit_layernorm(k, xtiles, out_bf, scr, eps=1e-5):
    x2s, mb, rb = scr["x2s"], scr["mb"], scr["rb"]
    t0, t1 = scr["t0"], scr["t1"]
    for tc in range(T // 512):
        sl = slice(tc * 512, (tc + 1) * 512)
        xs = x2s[tc % 2]
        for c in range(NCH):
            k.act(xs.h[:, c * 512:(c + 1) * 512], xtiles[c].h[:, sl], AF.Square,
                  reads=[xtiles[c]], writes=[xs])
        k.mmop(k.pb[0], [(k.ones_f.ap, xtiles[c].h[:, sl]) for c in range(NCH)],
               reads=[k.ones_f] + list(xtiles))
        k.ts("V", mb.h[:, sl], k.pb[0].ap, 1.0 / D, ALU.mult, reads=[k.pb[0]], writes=[mb])
        k.mmop(k.pb[1], [(k.ones_bf.ap, xs.h[:, c * 512:(c + 1) * 512])
                         for c in range(NCH)], reads=[k.ones_bf, xs])
        k.ts("V", rb.h[:, sl], k.pb[1].ap, 1.0 / D, ALU.mult, reads=[k.pb[1]], writes=[rb])
    k.tt("G", t0.ap, mb.ap, mb.ap, ALU.mult, reads=[mb], writes=[t0])
    k.tt("V", rb.ap, rb.ap, t0.ap, ALU.subtract, reads=[rb, t0], writes=[rb])
    k.act(t0.ap, rb.ap, AF.Sqrt, bias=k.const(eps), reads=[rb], writes=[t0])
    k.recip(rb.ap, t0.ap, reads=[t0], writes=[rb])
    for c in range(NCH):
        csl = slice(c * T, (c + 1) * T)
        tm = t0 if c % 2 == 0 else t1
        k.tt("G", tm.ap, xtiles[c].ap, mb.ap, ALU.subtract,
             reads=[xtiles[c], mb], writes=[tm])
        k.tt("V", out_bf.h[:, csl], tm.ap, rb.ap, ALU.mult,
             reads=[tm, rb], writes=[out_bf])
    return out_bf


def emit_ln_stats(k, xtiles, scr, eps=1e-5):
    x2s, mb, rb = scr["x2s"], scr["mb"], scr["rb"]
    t0 = scr["t0"]
    for tc in range(T // 512):
        sl = slice(tc * 512, (tc + 1) * 512)
        xs = x2s[tc % 2]
        for c in range(NCH):
            k.act(xs.h[:, c * 512:(c + 1) * 512], xtiles[c].h[:, sl], AF.Square,
                  reads=[xtiles[c]], writes=[xs])
        k.mmop(k.pb[0], [(k.ones_f.ap, xtiles[c].h[:, sl]) for c in range(NCH)],
               reads=[k.ones_f] + list(xtiles))
        k.ts("V", mb.h[:, sl], k.pb[0].ap, 1.0 / D, ALU.mult, reads=[k.pb[0]], writes=[mb])
        k.mmop(k.pb[1], [(k.ones_bf.ap, xs.h[:, c * 512:(c + 1) * 512])
                         for c in range(NCH)], reads=[k.ones_bf, xs])
        k.ts("V", rb.h[:, sl], k.pb[1].ap, 1.0 / D, ALU.mult, reads=[k.pb[1]], writes=[rb])
    k.tt("G", t0.ap, mb.ap, mb.ap, ALU.mult, reads=[mb], writes=[t0])
    k.tt("V", rb.ap, rb.ap, t0.ap, ALU.subtract, reads=[rb, t0], writes=[rb])
    k.act(t0.ap, rb.ap, AF.Sqrt, bias=k.const(eps), reads=[rb], writes=[t0])
    k.recip(rb.ap, t0.ap, reads=[t0], writes=[rb])


def ln_scratch(k, A):
    x2 = A("lnx2a", [128, NCH * 512], BF16)
    t0 = A("lnt0", [128, T], F32)
    return dict(x2s=[x2, x2],
                mb=A("lnmb", [128, T], F32), rb=A("lnrb", [128, T], F32),
                t0=t0, t1=t0)



def step_moe(k, W):
    m0 = k.mark()
    tiles = []

    def A(name, shape, dtype):
        t = k.alloc(name, shape, dtype)
        tiles.append(t)
        return t

    xn = A("xn", [128, NCH * T], BF16)
    combf = A("combf", [8, T], F32)
    m1_scr = k.mark()
    stiles = []

    def A2(name, shape, dtype):
        t = k.alloc(name, shape, dtype)
        stiles.append(t)
        return t

    scr = ln_scratch(k, A2)
    emit_layernorm(k, k.X, xn, scr)

    A_outer = A
    A = A2
    gw = A("gw", [128, NCH * 8], BF16)
    k.wload("moe_gate_wT", gw.h[:].rearrange("p (c e) -> p c e", e=8),
          W["moe_gate_wT"].rearrange("(c p) e -> p c e", p=128), writes=[gw])
    logit = A("logit", [128, 16 * 8], F32)
    comb = A("comb", [128, 16 * 8], F32)
    for tt_ in range(16):
        k.mmop(k.pb[0], [(xn.h[:, c * T + tt_ * 128: c * T + tt_ * 128 + 128],
                          gw.h[:, c * 8:(c + 1) * 8]) for c in range(NCH)],
               reads=[xn, gw], out_slice=slice(0, 8))
        k.copy("V", logit.h[:, tt_ * 8:(tt_ + 1) * 8], k.pb[0].h[:, 0:8],
               reads=[k.pb[0]], writes=[logit])
    lg3 = logit.h[:].rearrange("p (t e) -> p t e", e=8)
    mx = A("mx", [128, 16], F32)
    sm = A("sm", [128, 16], F32)
    k.red("V", mx.ap, lg3, ALU.max, reads=[logit], writes=[mx])
    bc8 = bass.AP(mx.h, 0, [[16, 128], [1, 16], [0, 8]])
    k.tt("V", lg3, lg3, bc8, ALU.subtract, reads=[logit, mx], writes=[logit])
    k.act(logit.ap, logit.ap, AF.Exp, reads=[logit], writes=[logit])
    k.red("V", sm.ap, lg3, ALU.add, reads=[logit], writes=[sm])
    k.recip(sm.ap, sm.ap, reads=[sm], writes=[sm])
    bcs = bass.AP(sm.h, 0, [[16, 128], [1, 16], [0, 8]])
    k.tt("V", lg3, lg3, bcs, ALU.mult, reads=[logit, sm], writes=[logit])
    g4 = A("g4", [128, 16 * 4], F32)
    g43 = g4.h[:].rearrange("p (t g) -> p t g", g=4)
    k.red("V", g43, logit.h[:].rearrange("p (t g e) -> p t g e", g=4, e=2),
          ALU.max, reads=[logit], writes=[g4])
    m1 = A("m1", [128, 16], F32)
    m2 = A("m2", [128, 16], F32)
    msk = A("msk", [128, 16 * 4], F32)
    tmp4 = A("tmp4", [128, 16 * 4], F32)
    k.red("V", m1.ap, g43, ALU.max, reads=[g4], writes=[m1])
    bc4 = bass.AP(m1.h, 0, [[16, 128], [1, 16], [0, 4]])
    k.tt("V", msk.h[:].rearrange("p (t g) -> p t g", g=4), g43, bc4, ALU.is_lt,
         reads=[g4, m1], writes=[msk])
    k.tt("V", tmp4.ap, g4.ap, msk.ap, ALU.mult, reads=[g4, msk], writes=[tmp4])
    k.red("V", m2.ap, tmp4.h[:].rearrange("p (t g) -> p t g", g=4), ALU.max,
          reads=[tmp4], writes=[m2])
    bcm2 = bass.AP(m2.h, 0, [[16, 128], [1, 16], [0, 4]])
    k.tt("V", msk.h[:].rearrange("p (t g) -> p t g", g=4), g43, bcm2, ALU.is_ge,
         reads=[g4, m2], writes=[msk])
    smask = bass.AP(msk.h, 0, [[64, 128], [4, 16], [1, 4], [0, 2]])
    k.tt("V", comb.h[:].rearrange("p (t g e) -> p t g e", g=4, e=2),
         logit.h[:].rearrange("p (t g e) -> p t g e", g=4, e=2), smask, ALU.mult,
         reads=[logit, msk], writes=[comb])
    c8 = comb.h[:].rearrange("p (t e) -> p t e", e=8)
    e8 = A("e8", [128, 16 * 8], F32)
    k.red("V", m1.ap, c8, ALU.max, reads=[comb], writes=[m1])
    bce = bass.AP(m1.h, 0, [[16, 128], [1, 16], [0, 8]])
    k.tt("V", e8.h[:].rearrange("p (t e) -> p t e", e=8), c8, bce, ALU.is_lt,
         reads=[comb, m1], writes=[e8])
    k.tt("V", e8.ap, e8.ap, comb.ap, ALU.mult, reads=[e8, comb], writes=[e8])
    k.red("V", m2.ap, e8.h[:].rearrange("p (t e) -> p t e", e=8), ALU.max,
          reads=[e8], writes=[m2])
    bcm2e = bass.AP(m2.h, 0, [[16, 128], [1, 16], [0, 8]])
    k.tt("V", e8.h[:].rearrange("p (t e) -> p t e", e=8), c8, bcm2e, ALU.is_ge,
         reads=[comb, m2], writes=[e8])
    k.tt("V", m1.ap, m1.ap, m2.ap, ALU.add, reads=[m1, m2], writes=[m1])
    k.recip(m1.ap, m1.ap, reads=[m1], writes=[m1])
    k.tt("V", comb.ap, comb.ap, e8.ap, ALU.mult, reads=[comb, e8], writes=[comb])
    bcn = bass.AP(m1.h, 0, [[16, 128], [1, 16], [0, 8]])
    k.tt("V", c8, c8, bcn, ALU.mult, reads=[comb, m1], writes=[comb])
    for tt_ in range(16):
        k.transp(k.pb[2], comb.h[:, tt_ * 8:(tt_ + 1) * 8], k.identf.ap,
                 slice(0, 128), reads=[comb, k.identf], nparts=8)
        k.copy("V", combf.h[0:8, tt_ * 128:(tt_ + 1) * 128], k.pb[2].h[0:8, 0:128],
               reads=[k.pb[2]], writes=[combf])
    combf_b = combf

    k.release(m1_scr, stiles)
    A = A_outer

    wg_b = [A(f"wgb{i}", [128, NCH * 128], BF16) for i in range(2)]
    wu_b = [A(f"wub{i}", [128, NCH * 128], BF16) for i in range(2)]
    w2e = A("w2e", [128, 16 * D], BF16)
    actt = A("actt", [128, 8 * T], BF16)
    cbf = A("cbf", [128, 4 * 512], BF16)
    sil2 = [A(f"sil{i}", [128, 512], F32) for i in range(4)]
    for e in range(8):
        k.wload("moe_w2T", w2e.h[:].rearrange("p (kc n) -> p kc n", n=D),
              W["moe_w2T"][e].rearrange("(kc p) n -> p kc n", p=128), writes=[w2e])
        for tc4 in range(4):
            k.mmop(k.pb[2 + tc4 % 2], [(k.e8sel.h[0:8, e * 128:(e + 1) * 128],
                                        combf.h[0:8, tc4 * 512:(tc4 + 1) * 512])],
                   reads=[k.e8sel, combf])
            k.copy("V", cbf.h[:, tc4 * 512:(tc4 + 1) * 512], k.pb[2 + tc4 % 2].ap,
                   reads=[k.pb[2 + tc4 % 2]], writes=[cbf])
        for half in range(2):
            for ib8 in range(8):
                ib = half * 8 + ib8
                wg = wg_b[ib % 2]
                wu = wu_b[ib % 2]
                k.wload("moe_w13T", wg.h[:].rearrange("p (c i) -> p c i", i=128),
                      W["moe_w13T"][e, :, ib * 128:(ib + 1) * 128].rearrange(
                          "(c p) i -> p c i", p=128), writes=[wg])
                k.wload("moe_w13T", wu.h[:].rearrange("p (c i) -> p c i", i=128),
                      W["moe_w13T"][e, :, 2048 + ib * 128: 2048 + (ib + 1) * 128].rearrange(
                          "(c p) i -> p c i", p=128), writes=[wu])
                for tcc in range(4):
                    tsl = slice(tcc * 512, (tcc + 1) * 512)
                    bg = k.pb[(tcc % 2) * 2]
                    bu = k.pb[(tcc % 2) * 2 + 1]
                    sil = sil2[tcc % 4]
                    k.mmop(bg, [(wg.h[:, c * 128:(c + 1) * 128],
                                 xn.h[:, c * T + tcc * 512: c * T + tcc * 512 + 512])
                                for c in range(NCH)], reads=[wg, xn])
                    k.mmop(bu, [(wu.h[:, c * 128:(c + 1) * 128],
                                 xn.h[:, c * T + tcc * 512: c * T + tcc * 512 + 512])
                                for c in range(NCH)], reads=[wu, xn])
                    k.act(sil.ap, bg.ap, AF.Silu, reads=[bg], writes=[sil])
                    k.tt("V", sil.ap, sil.ap, bu.ap, ALU.mult, reads=[sil, bu], writes=[sil])
                    k.tt("G", actt.h[:, ib8 * T + tcc * 512: ib8 * T + tcc * 512 + 512],
                         sil.ap, cbf.h[:, tsl], ALU.mult, reads=[sil, cbf], writes=[actt])
            for tcc in range(4):
                for dp in range(NCH):
                    bank = k.pb[4 + dp % 2]
                    k.mmop(bank, [(w2e.h[:, (half * 8 + kc) * D + dp * 128:
                                          (half * 8 + kc) * D + dp * 128 + 128],
                                   actt.h[:, kc * T + tcc * 512: kc * T + tcc * 512 + 512])
                                  for kc in range(8)], reads=[w2e, actt])
                    xt = k.X[dp]
                    k.tt("V", xt.h[:, tcc * 512:(tcc + 1) * 512],
                         xt.h[:, tcc * 512:(tcc + 1) * 512], bank.ap, ALU.add,
                         reads=[xt, bank], writes=[xt])
    k.release(m0, tiles)



STEP_FNS = {}
STEP_FNS["moe"] = step_moe

STEP_WEIGHTS = {
    "moe": {"moe_gate_wT": (D, 8), "moe_w13T": (8, D, 4096), "moe_w2T": (8, 2048, D)},
}


F32_WEIGHTS = {"l0a_lns", "l0a_cms_b", "mam_conv", "mam_db", "mam_alog",
               "gd_norms", "gd_conv", "fp4_wg", "fp4_wu", "fp4_wd"}


def build_program(steps):
    nc = bass.Bass("TRN2", target_bir_lowering=False)
    xin = nc.dram_tensor("xT", [D, T], F32, kind="ExternalInput").ap()
    yout = nc.dram_tensor("y", [D, T], F32, kind="ExternalOutput").ap()
    W = {}
    seen = set()
    order = []
    for s in steps:
        for nm, shp in STEP_WEIGHTS[s].items():
            if nm not in seen:
                seen.add(nm)
                W[nm] = nc.dram_tensor(nm, list(shp), F32, kind="ExternalInput").ap()
                order.append((nm, shp))
    k = Ker(nc)
    setup_consts(k)
    k.X = [k.alloc(f"x{c}", [128, T], F32) for c in range(NCH)]
    for c in range(NCH):
        k.dma("Y", k.X[c].ap, xin[c * 128:(c + 1) * 128, :], writes=[k.X[c]])
    W2 = dict(W)
    k.WB = {}
    for nm, shp in order:
        if nm in F32_WEIGHTS:
            continue
        flat = int(np.prod(shp))
        assert flat % 128 == 0
        cols = flat // 128
        tw = nc.dram_tensor(nm + "_bf", list(shp), BF16, kind="Internal").ap()
        twb = Buf(nm + "_bf")
        W2[nm] = tw
        k.WB[nm] = twb
        sflat = bass.AP(W[nm].tensor, 0, [[cols, 128], [1, cols]])
        dflat = bass.AP(tw.tensor, 0, [[cols, 128], [1, cols]])
        k.dma("G", dflat, sflat, writes=[twb])
    for s in steps:
        STEP_FNS[s](k, W2)
    for c in range(NCH):
        k.dma("Y", yout[c * 128:(c + 1) * 128, :], k.X[c].ap, reads=[k.X[c]])
    k.finalize()
    return nc


def host_weights(inputs, steps):
    out = {}
    for s in steps:
        if s == "l1_attn_b":
            ii = inputs
            nw = np.stack([np.asarray(ii["L1_attn_b_norm1_w"]), np.asarray(ii["L1_attn_b_norm2_w"])])
            out["gd_norms"] = np.ascontiguousarray(nw.reshape(2, 8, 128).transpose(2, 0, 1).reshape(128, 16))
            cw = np.asarray(ii["L1_attn_b_conv_w"])
            cb = np.asarray(ii["L1_attn_b_conv_b"])[:, None]
            cc_ = np.concatenate([cw, cb], axis=1)
            out["gd_conv"] = np.ascontiguousarray(cc_.reshape(8, 128, 5).transpose(2, 1, 0).reshape(5 * 128, 8).T) if False else np.ascontiguousarray(cc_.reshape(8, 128, 5).transpose(1, 2, 0).reshape(128, 40))
            for nm, inm in (("gd_wq", "L1_attn_b_wq"), ("gd_wk", "L1_attn_b_wk"),
                            ("gd_wv", "L1_attn_b_wv"), ("gd_wb", "L1_attn_b_wb"),
                            ("gd_wa", "L1_attn_b_wa"), ("gd_wo", "L1_attn_b_wo"),
                            ("gd_wg", "L1_attn_b_wg"), ("gd_wu", "L1_attn_b_wu"),
                            ("gd_wd", "L1_attn_b_wd")):
                out[nm] = np.ascontiguousarray(np.asarray(ii[inm]))
        if s == "l0_attn_b":
            ii = inputs
            out["mam_win"] = np.ascontiguousarray(np.asarray(ii["L0_attn_b_W_in"]))
            cw = np.asarray(ii["L0_attn_b_conv_w"])[:, 0, :]
            out["mam_conv"] = np.ascontiguousarray(cw.reshape(8, 128, 3).transpose(1, 0, 2).reshape(128, 24))
            out["mam_wbc"] = np.ascontiguousarray(np.concatenate(
                [np.asarray(ii["L0_attn_b_W_B"]), np.asarray(ii["L0_attn_b_W_C"])], axis=1))
            out["mam_wd"] = np.ascontiguousarray(np.asarray(ii["L0_attn_b_W_delta"]))
            out["mam_db"] = np.ascontiguousarray(
                np.asarray(ii["L0_attn_b_delta_base"]).reshape(8, 128).T)
            out["mam_wt"] = np.ascontiguousarray(np.asarray(ii["L0_attn_b_W_tau"]))
            out["mam_alog"] = np.ascontiguousarray(
                np.asarray(ii["L0_attn_b_A_log"]).reshape(8, 128, 16).transpose(1, 0, 2).reshape(128, 128))
            out["mam_wout"] = np.ascontiguousarray(np.asarray(ii["L0_attn_b_W_out"]))
        if s == "l0_attn_a":
            ii = inputs
            lns = np.stack([np.asarray(ii["L0_attn_a_ln1_g"]), np.asarray(ii["L0_attn_a_ln1_b"]),
                            np.asarray(ii["L0_attn_a_ln2_g"]), np.asarray(ii["L0_attn_a_ln2_b"])])
            out["l0a_lns"] = np.ascontiguousarray(
                lns.reshape(4, 8, 128).transpose(2, 0, 1).reshape(128, 32))
            out["l0a_wq"] = np.ascontiguousarray(np.asarray(ii["L0_attn_a_wq"]))
            out["l0a_wk"] = np.ascontiguousarray(np.asarray(ii["L0_attn_a_wk"]))
            out["l0a_wv"] = np.ascontiguousarray(np.asarray(ii["L0_attn_a_wv"]))
            out["l0a_wo"] = np.ascontiguousarray(np.asarray(ii["L0_attn_a_wo"]))
            out["l0a_cms_w1"] = np.ascontiguousarray(np.asarray(ii["L0_attn_a_cms_w1"]))
            out["l0a_cms_w2"] = np.ascontiguousarray(np.asarray(ii["L0_attn_a_cms_w2"]))
            b1 = np.asarray(ii["L0_attn_a_cms_b1"]).reshape(3, 32, 128).transpose(0, 2, 1)
            b2 = np.asarray(ii["L0_attn_a_cms_b2"]).reshape(3, 8, 128).transpose(0, 2, 1)
            out["l0a_cms_b"] = np.ascontiguousarray(np.concatenate([b1, b2], axis=2))
        if s == "l1_attn_a":
            out["att_wq"] = np.ascontiguousarray(np.asarray(inputs["L1_attn_a_wq"]).reshape(1024, 1024))
            out["att_wk"] = np.ascontiguousarray(np.asarray(inputs["L1_attn_a_wk"]).reshape(1024, 1024))
            out["att_wv"] = np.ascontiguousarray(np.asarray(inputs["L1_attn_a_wv"]).reshape(1024, 1024))
            out["att_wo"] = np.ascontiguousarray(np.asarray(inputs["L1_attn_a_wo"]).reshape(1024, 1024))
        if s == "l1_ffn":
            out["fp4_wg"] = np.ascontiguousarray(np.asarray(inputs["L1_ffn_w_gate"]))
            out["fp4_wu"] = np.ascontiguousarray(np.asarray(inputs["L1_ffn_w_up"]))
            out["fp4_wd"] = np.ascontiguousarray(np.asarray(inputs["L1_ffn_w_down"]))
        if s == "moe":
            out["moe_gate_wT"] = np.ascontiguousarray(np.asarray(inputs["L0_ffn_gate_w"]).T)
            out["moe_w13T"] = np.ascontiguousarray(np.asarray(inputs["L0_ffn_w13_weight"]).transpose(0, 2, 1))
            out["moe_w2T"] = np.ascontiguousarray(np.asarray(inputs["L0_ffn_w2_weight"]).transpose(0, 2, 1))
    return out


MODULE_STEPS = ["l0_attn_a", "moe", "l0_attn_b", "moe",
                "l1_attn_a", "l1_ffn", "l1_attn_b", "l1_ffn"]
FUSE = True


def kernel(**inputs):
    x = np.asarray(inputs["L0_attn_a_x"], np.float32)
    B = x.shape[0]
    if FUSE:
        groups = [MODULE_STEPS]
    else:
        groups = [[s] for s in MODULE_STEPS]
    xTs = [np.ascontiguousarray(x[b].T) for b in range(B)]
    for steps in groups:
        nc = build_program(steps)
        wmap = host_weights(inputs, steps)
        in_maps = [dict(xT=xTs[b], **wmap) for b in range(B)]
        res = run_bass_kernel_spmd(nc, in_maps, core_ids=list(range(B)))
        xTs = [np.asarray(res.results[b]["y"], np.float32) for b in range(B)]
    return np.ascontiguousarray(np.stack([xt.T for xt in xTs], 0))



QMAX = 1408


def emit_fp4_quant(k, out_ap_fn, in_ap_fn, scrs, ncols, in_tiles, out_tiles):
    for lo in range(0, ncols, QMAX):
        n = min(QMAX, ncols - lo)
        nblk = n // 16
        k.qpar = getattr(k, "qpar", 0) ^ 1
        scr = scrs[k.qpar]
        absx, acc, tmp, mx = scr["absx"], scr["acc"], scr["tmp"], scr["mx"]
        in_ap = in_ap_fn(lo, n)
        out_ap = out_ap_fn(lo, n)
        aab = absx.h[:, 0:n]
        acb = acc.h[:, 0:n]
        tmb = tmp.h[:, 0:n]
        mxb = mx.h[:, 0:nblk]
        k.act(aab, in_ap, AF.Abs, reads=in_tiles, writes=[absx])
        k.red("V", mxb, absx.h[:, 0:n].rearrange("p (b s) -> p b s", s=16),
              ALU.max, reads=[absx], writes=[mx])
        k.ts("V", mxb, mxb, k.const(1.0 / 6.0), ALU.mult, s2=k.const(1e-12),
             op1=ALU.max, reads=[mx], writes=[mx])
        k.recip(mxb, mxb, reads=[mx], writes=[mx])
        rbc = bass.AP(mx.h, 0, [[mx.h.shape[1], 128], [1, nblk], [0, 16]])
        k.tt("G", aab, aab, rbc, ALU.mult, reads=[absx, mx], writes=[absx])
        k.ts("V", acb, aab, 0.25, ALU.is_gt, reads=[absx], writes=[acc])
        for thr in (0.75, 1.25, 1.75):
            k.ts("V", tmb, aab, k.const(thr), ALU.is_gt, reads=[absx], writes=[tmp])
            k.tt("G", acb, acb, tmb, ALU.add, reads=[acc, tmp], writes=[acc])
        k.ts("V", tmb, aab, k.const(2.5), ALU.is_gt, reads=[absx], writes=[tmp])
        k.stt("V", acb, acb, 0.5, tmb, ALU.mult, ALU.add, reads=[acc, tmp], writes=[acc])
        k.ts("V", tmb, aab, k.const(3.5), ALU.is_gt, reads=[absx], writes=[tmp])
        k.tt("G", acb, acb, tmb, ALU.add, reads=[acc, tmp], writes=[acc])
        k.ts("V", tmb, aab, k.const(5.0), ALU.is_gt, reads=[absx], writes=[tmp])
        k.stt("V", acb, tmb, 2.0, acb, ALU.mult, ALU.add, reads=[acc, tmp], writes=[acc])
        k.recip(mxb, mxb, reads=[mx], writes=[mx])
        k.tt("V", acb, acb, rbc, ALU.mult, reads=[acc, mx], writes=[acc])
        k.op("S", (lambda e, o=tmb, i=in_ap: e.sign(o, i)), reads=in_tiles, writes=[tmp])
        k.tt("G", out_ap, acb, tmb, ALU.mult, reads=[acc, tmp], writes=out_tiles)


def quant_weight_to_scratch(k, W, qscr, nat_name, scratch_ap, scratch_buf,
                            nrows, ncols, A):
    nrc = nrows // 128
    ncc = ncols // 128
    fch = [A(f"fq{i}", [128, ncols], F32) for i in range(2)]
    qch = A("qn", [128, ncols], BF16)
    tev = A("te", [128, 8 * 128], BF16)
    for rc in range(nrc):
        f = fch[rc % 2]
        k.dma("Y", f.ap, W[nat_name][rc * 128:(rc + 1) * 128, :], writes=[f])
        emit_fp4_quant(k, lambda lo, n, q=qch: q.h[:, lo:lo + n],
                       lambda lo, n, f=f: f.h[:, lo:lo + n],
                       qscr, ncols, [f], [qch])
        for g0 in range(0, ncc, 8):
            gn = min(8, ncc - g0)
            for j in range(gn):
                k.transp(k.pbh[0], qch.h[:, (g0 + j) * 128:(g0 + j + 1) * 128],
                         k.ident.ap, slice(j * 128, (j + 1) * 128), reads=[qch, k.ident])
            k.copy("V", tev.h[:, 0:gn * 128], k.pbh[0].h[:, 0:gn * 128],
                   reads=[k.pbh[0]], writes=[tev])
            k.dma("S", scratch_ap[g0 * 128:(g0 + gn) * 128,
                                  rc * 128:(rc + 1) * 128].rearrange(
                "(cc p) r -> p cc r", p=128),
                tev.h[:, 0:gn * 128].rearrange("p (cc r) -> p cc r", r=128),
                reads=[tev], writes=[scratch_buf])


def step_l1_ffn(k, W):
    IN = 2816
    ICC = IN // 128
    m0 = k.mark()
    tiles = []

    def A(name, shape, dtype):
        t = k.alloc(name, shape, dtype)
        tiles.append(t)
        return t

    xq = A("xq", [128, NCH * T], BF16)
    qscr = [dict(absx=A(f"qabs{i}", [128, QMAX], F32),
                 acc=A(f"qacc{i}", [128, QMAX], BF16),
                 tmp=A(f"qtmp{i}", [128, QMAX], BF16),
                 mx=A(f"qmx{i}", [128, QMAX // 16], F32)) for i in range(2)]

    m1 = k.mark()
    s0 = []

    def A0(name, shape, dtype):
        t = k.alloc(name, shape, dtype)
        s0.append(t)
        return t
    xn = A0("xn", [128, NCH * T], BF16)
    scr = ln_scratch(k, A0)
    emit_layernorm(k, k.X, xn, scr)
    xtm = A0("xtm", [128, D], BF16)
    qtm = A0("qtm", [128, D], BF16)
    for it in range(16):
        for j in range(NCH):
            k.transp(k.pbh[0], xn.h[:, j * T + it * 128: j * T + it * 128 + 128],
                     k.ident.ap, slice(j * 128, (j + 1) * 128), reads=[xn, k.ident])
        k.copy("V", xtm.ap, k.pbh[0].ap, reads=[k.pbh[0]], writes=[xtm])
        emit_fp4_quant(k, lambda lo, n, q=qtm: q.h[:, lo:lo + n],
                       lambda lo, n, x=xtm: x.h[:, lo:lo + n],
                       qscr, D, [xtm], [qtm])
        for j in range(NCH):
            k.transp(k.pbh[1], qtm.h[:, j * 128:(j + 1) * 128],
                     k.ident.ap, slice(j * 128, (j + 1) * 128), reads=[qtm, k.ident])
        k.op("V", (lambda e, o=xq, it=it: e.tensor_copy(
            bass.AP(o.h, it * 128, [[NCH * T, 128], [T, NCH], [1, 128]]),
            k.pbh[1].h[:].rearrange("p (c r) -> p c r", r=128))),
            reads=[k.pbh[1]], writes=[xq])
    k.release(m1, s0)

    if not hasattr(k, "fp4_done"):
        k.fp4_done = True
        nc = k.nc
        k.qwg_s = nc.dram_tensor("qwg_s", [D, IN], BF16, kind="Internal").ap()
        k.qwu_s = nc.dram_tensor("qwu_s", [D, IN], BF16, kind="Internal").ap()
        k.qwd_s = nc.dram_tensor("qwd_s", [IN, D], BF16, kind="Internal").ap()
        k.qwg_b, k.qwu_b, k.qwd_b = Buf("qwg_s"), Buf("qwu_s"), Buf("qwd_s")
        m2 = k.mark()
        wt = []

        def A1(name, shape, dtype):
            t = k.alloc(name, shape, dtype)
            wt.append(t)
            return t
        quant_weight_to_scratch(k, W, qscr, "fp4_wg", k.qwg_s, k.qwg_b, IN, D, A1)
        k.release(m2, wt)
        wt = []
        quant_weight_to_scratch(k, W, qscr, "fp4_wu", k.qwu_s, k.qwu_b, IN, D, A1)
        k.release(m2, wt)
        wt = []
        quant_weight_to_scratch(k, W, qscr, "fp4_wd", k.qwd_s, k.qwd_b, D, IN, A1)
        k.release(m2, wt)

    wb = [A(f"fw{i}", [128, NCH * 128], BF16) for i in range(4)]
    htm = [A(f"htm{i}", [128, IN], BF16) for i in range(4)]
    qh = A("qh", [128, ICC * 512], BF16)
    hst2 = [A(f"hst{i}", [128, 512], BF16) for i in range(2)]
    hfm2 = [A(f"hfm{i}", [128, 512], F32) for i in range(2)]
    dwb = [A(f"dw{i}", [128, ICC * 128], BF16) for i in range(2)]
    for tc in range(4):
        tb = slice(tc * 512, (tc + 1) * 512)
        for ic in range(ICC):
            wg = wb[(ic % 2) * 2]
            wu = wb[(ic % 2) * 2 + 1]
            k.dma("Y", wg.h[:].rearrange("p (c i) -> p c i", i=128),
                  k.qwg_s[:, ic * 128:(ic + 1) * 128].rearrange("(c p) i -> p c i", p=128),
                  reads=[k.qwg_b], writes=[wg])
            k.dma("Y", wu.h[:].rearrange("p (c i) -> p c i", i=128),
                  k.qwu_s[:, ic * 128:(ic + 1) * 128].rearrange("(c p) i -> p c i", p=128),
                  reads=[k.qwu_b], writes=[wu])
            bg = k.pb[(ic % 2) * 2]
            bu = k.pb[(ic % 2) * 2 + 1]
            hfm = hfm2[ic % 2]
            hst = hst2[ic % 2]
            k.mmop(bg, [(wg.h[:, c * 128:(c + 1) * 128],
                         xq.h[:, c * T + tc * 512: c * T + tc * 512 + 512])
                        for c in range(NCH)], reads=[wg, xq])
            k.mmop(bu, [(wu.h[:, c * 128:(c + 1) * 128],
                         xq.h[:, c * T + tc * 512: c * T + tc * 512 + 512])
                        for c in range(NCH)], reads=[wu, xq])
            k.act(hfm.ap, bg.ap, AF.Silu, reads=[bg], writes=[hfm])
            k.tt("V", hfm.ap, hfm.ap, bu.ap, ALU.mult, reads=[hfm, bu], writes=[hfm])
            k.copy("G", hst.ap, hfm.ap, reads=[hfm], writes=[hst])
            for j in range(4):
                k.transp(k.pbh[ic % 2], hst.h[:, j * 128:(j + 1) * 128], k.ident.ap,
                         slice(j * 128, (j + 1) * 128), reads=[hst, k.ident])
            for j in range(4):
                k.copy("V", htm[j].h[:, ic * 128:(ic + 1) * 128],
                       k.pbh[ic % 2].h[:, j * 128:(j + 1) * 128],
                       reads=[k.pbh[ic % 2]], writes=[htm[j]])
        for j in range(4):
            emit_fp4_quant(k, lambda lo, n, h=htm[j]: h.h[:, lo:lo + n],
                           lambda lo, n, h=htm[j]: h.h[:, lo:lo + n],
                           qscr, IN, [htm[j]], [htm[j]])
            for g0 in range(0, ICC, 8):
                gn = min(8, ICC - g0)
                for u in range(gn):
                    k.transp(k.pbh[1], htm[j].h[:, (g0 + u) * 128:(g0 + u + 1) * 128],
                             k.ident.ap, slice(u * 128, (u + 1) * 128),
                             reads=[htm[j], k.ident])
                k.op("V", (lambda e, o=qh, g0=g0, gn=gn, j=j: e.tensor_copy(
                    bass.AP(o.h, g0 * 512 + j * 128,
                            [[ICC * 512, 128], [512, gn], [1, 128]]),
                    k.pbh[1].h[:, 0:gn * 128].rearrange("p (c r) -> p c r", r=128))),
                    reads=[k.pbh[1]], writes=[qh])
        for dp in range(NCH):
            dw = dwb[dp % 2]
            k.dma("Y", dw.h[:].rearrange("p (c i) -> p c i", i=128),
                  k.qwd_s[:, dp * 128:(dp + 1) * 128].rearrange("(c p) i -> p c i", p=128),
                  reads=[k.qwd_b], writes=[dw])
            k.mmop(k.pb[4 + dp % 2], [(dw.h[:, c * 128:(c + 1) * 128],
                                       qh.h[:, c * 512:(c + 1) * 512])
                                      for c in range(ICC)], reads=[dw, qh])
            xt = k.X[dp]
            k.tt("V", xt.h[:, tb], xt.h[:, tb], k.pb[4 + dp % 2].ap, ALU.add,
                 reads=[xt, k.pb[4 + dp % 2]], writes=[xt])
    k.release(m0, tiles)


STEP_FNS["l1_ffn"] = step_l1_ffn
STEP_WEIGHTS["l1_ffn"] = {"fp4_wg": (2816, D), "fp4_wu": (2816, D), "fp4_wd": (D, 2816)}



def step_l1_attn_a(k, W):
    H, HD = 8, 128
    m0 = k.mark()
    tiles = []

    def A(name, shape, dtype):
        t = k.alloc(name, shape, dtype)
        tiles.append(t)
        return t

    cosm = A("cosm", [64, T], BF16)
    sinm = A("sinm", [64, T], BF16)
    y = A("attny", [128, NCH * T], BF16)
    triu = A("triu", [128, 128], BF16)
    k.memset("V", triu.ap, 1.0, writes=[triu])
    k.affsel(triu.ap, triu.ap, [[1, 128]], ALU.is_ge, 0.0, -1, reads=[triu], writes=[triu])

    m1 = k.mark()
    s0 = []

    def A0(name, shape, dtype):
        t = k.alloc(name, shape, dtype)
        s0.append(t)
        return t
    xn = A0("xn", [128, NCH * T], BF16)
    mln = k.mark()
    sln = []

    def A1(name, shape, dtype):
        t = k.alloc(name, shape, dtype)
        sln.append(t)
        return t
    scr = ln_scratch(k, A1)
    emit_layernorm(k, k.X, xn, scr)
    k.release(mln, sln)

    sln = []
    posi = A1("posi", [64, T], I32)
    jci = A1("jci", [64, 1], I32)
    jcf = A1("jcf", [64, 1], F32)
    posf = A1("posf", [64, T], F32)
    rad = A1("rad", [64, T], F32)
    jcf2 = A1("jcf2", [64, T], F32)
    k.iota(posi.ap, [[1, T]], cm=0, writes=[posi])
    k.iota(jci.ap, [[0, 1]], cm=1, writes=[jci])
    k.copy("V", jcf.ap, jci.ap, reads=[jci], writes=[jcf])
    k.copy("V", posf.ap, posi.ap, reads=[posi], writes=[posf])
    LB = float(np.log(10000.0) / 64.0)
    k.act(jcf.ap, jcf.ap, AF.Exp, scale=-LB, reads=[jcf], writes=[jcf])
    k.ts("V", rad.ap, posf.ap, jcf[0:64, 0:1], ALU.mult, reads=[posf, jcf], writes=[rad])
    PI = float(np.pi)
    MULT = float((1.0 / 128.0) ** 0.5)
    k.ts("V", rad.ap, rad.ap, float(1.0 / (2 * PI)), ALU.mult, reads=[rad], writes=[rad])
    k.copy("V", posi.ap, rad.ap, reads=[rad], writes=[posi])
    k.copy("V", posf.ap, posi.ap, reads=[posi], writes=[posf])
    k.tt("V", posf.ap, rad.ap, posf.ap, ALU.subtract, reads=[rad, posf], writes=[posf])
    k.ts("V", jcf2.ap, posf.ap, 0.5, ALU.is_gt, reads=[posf], writes=[jcf2])
    k.tt("V", posf.ap, posf.ap, jcf2.ap, ALU.subtract, reads=[posf, jcf2], writes=[posf])
    k.act(sinm.ap, posf.ap, AF.Sin, scale=float(2 * PI), reads=[posf], writes=[sinm])
    k.ts("V", rad.ap, rad.ap, 0.25, ALU.add, reads=[rad], writes=[rad])
    k.copy("V", posi.ap, rad.ap, reads=[rad], writes=[posi])
    k.copy("V", posf.ap, posi.ap, reads=[posi], writes=[posf])
    k.tt("V", posf.ap, rad.ap, posf.ap, ALU.subtract, reads=[rad, posf], writes=[posf])
    k.ts("V", jcf2.ap, posf.ap, 0.5, ALU.is_gt, reads=[posf], writes=[jcf2])
    k.tt("V", posf.ap, posf.ap, jcf2.ap, ALU.subtract, reads=[posf, jcf2], writes=[posf])
    k.act(cosm.ap, posf.ap, AF.Sin, scale=float(2 * PI), reads=[posf], writes=[cosm])
    k.ts("V", sinm.ap, sinm.ap, MULT, ALU.mult, reads=[sinm], writes=[sinm])
    k.ts("V", cosm.ap, cosm.ap, MULT, ALU.mult, reads=[cosm], writes=[cosm])
    k.release(mln, sln)

    qrop = [A0(f"qro{i}", [128, T], BF16) for i in range(2)]
    krop = [A0(f"kro{i}", [128, T], BF16) for i in range(2)]
    vtmp = [A0(f"vtm{i}", [128, 16 * 129], BF16) for i in range(2)]
    for i in range(2):
        vv = vtmp[i]
        k.memset("V", vv.h[:].rearrange("p (c j) -> p c j", j=129)[:, :, 128:129],
                 1.0, writes=[vv])
    wqp = [A0(f"w_q{i}", [128, NCH * 128], BF16) for i in range(2)]
    wkp = [A0(f"w_k{i}", [128, NCH * 128], BF16) for i in range(2)]
    wvp = [A0(f"w_v{i}", [128, NCH * 128], BF16) for i in range(2)]
    t1p = [A0(f"t1_{i}", [64, 1024], BF16) for i in range(2)]
    rsbp = [A0(f"rsb{i}", [128, 512], F32) for i in range(2)]
    q2bp = [A0(f"q2b{i}", [128, 512], BF16) for i in range(2)]
    eap = [A0(f"ea{i}", [64, 512], BF16) for i in range(2)]
    obp = [A0(f"ob{i}", [64, 512], BF16) for i in range(2)]
    ebp = [A0(f"eb{i}", [64, 512], BF16) for i in range(2)]
    oap = [A0(f"oa{i}", [64, 512], BF16) for i in range(2)]
    vsgp = [A0(f"vsg{i}", [128, 512], BF16) for i in range(2)]
    ptbp = [A0(f"ptb{i}", [128, 512], BF16) for i in range(2)]
    oqp = [A0(f"oq{i}", [128, 128], BF16) for i in range(2)]
    rcp = [A0(f"rc{i}", [128, 1], F32) for i in range(2)]

    for h in range(H):
        hp = h & 1
        hsl = slice(h * 128, (h + 1) * 128)
        qro, kro, vtm = qrop[hp], krop[hp], vtmp[hp]
        for (wnm, wpair, dst) in (("att_wq", wqp, qro), ("att_wk", wkp, kro)):
            wv_ = W[wnm]
            wtile = wpair[hp]
            k.wload(wnm, wtile.h[:].rearrange("p (c i) -> p c i", i=128),
                    wv_[:, hsl].rearrange("(c p) i -> p c i", p=128), writes=[wtile])
            for tc in range(4):
                tp = tc & 1
                tsl = slice(tc * 512, (tc + 1) * 512)
                t1, rsb, q2b = t1p[tp], rsbp[tp], q2bp[tp]
                ea, ob, eb, oa = eap[tp], obp[tp], ebp[tp], oap[tp]
                k.mmop(k.pb[0], [(wtile.h[:, c * 128:(c + 1) * 128],
                                  xn.h[:, c * T + tc * 512: c * T + tc * 512 + 512])
                                 for c in range(NCH)], reads=[wtile, xn])
                k.act(q2b.ap, k.pb[0].ap, AF.Square, reads=[k.pb[0]], writes=[q2b])
                k.mmop(k.pb[1], [(k.ones_bf.ap, q2b.ap)], reads=[k.ones_bf, q2b])
                k.act(rsb.ap, k.pb[1].ap, AF.Ln, scale=float(1.0 / 128.0),
                      bias=1e-6, reads=[k.pb[1]], writes=[rsb])
                k.act(rsb.ap, rsb.ap, AF.Exp, scale=-0.5, reads=[rsb], writes=[rsb])
                k.tt("V", t1.h[0:64, 0:512], k.pb[0].h[0:64, 0:512],
                     rsb.h[0:64, 0:512], ALU.mult, reads=[k.pb[0], rsb], writes=[t1])
                k.tt("V", t1.h[0:64, 512:1024], k.pb[0].h[64:128, 0:512],
                     rsb.h[64:128, 0:512], ALU.mult, reads=[k.pb[0], rsb], writes=[t1])
                co = cosm.h[0:64, tsl]
                si = sinm.h[0:64, tsl]
                t1e = t1.h[0:64, 0:512]
                t1o = t1.h[0:64, 512:1024]
                k.tt("V", ea.ap, t1e, co, ALU.mult, reads=[t1, cosm], writes=[ea])
                k.tt("G", ob.ap, t1o, si, ALU.mult, reads=[t1, sinm], writes=[ob])
                k.tt("G", eb.ap, t1e, si, ALU.mult, reads=[t1, sinm], writes=[eb])
                k.tt("V", oa.ap, t1o, co, ALU.mult, reads=[t1, cosm], writes=[oa])
                k.tt("V", dst.h[0:64, tsl], ea.ap, ob.ap, ALU.subtract,
                     reads=[ea, ob], writes=[dst])
                k.tt("G", dst.h[64:128, tsl], eb.ap, oa.ap, ALU.add,
                     reads=[eb, oa], writes=[dst])
        wvt = wvp[hp]
        k.wload("att_wv", wvt.h[:].rearrange("p (c i) -> p c i", i=128),
                W["att_wv"][:, hsl].rearrange("(c p) i -> p c i", p=128), writes=[wvt])
        for tc in range(4):
            tp = tc & 1
            vsg = vsgp[tp]
            k.mmop(k.pb[0], [(wvt.h[:, c * 128:(c + 1) * 128],
                              xn.h[:, c * T + tc * 512: c * T + tc * 512 + 512])
                             for c in range(NCH)], reads=[wvt, xn])
            k.copy("V", vsg.ap, k.pb[0].ap, reads=[k.pb[0]], writes=[vsg])
            for j in range(4):
                k.transp(k.pbh[0], vsg.h[:, j * 128:(j + 1) * 128], k.ident.ap,
                         slice(j * 128, (j + 1) * 128), reads=[vsg, k.ident])
            for j in range(4):
                ch = tc * 4 + j
                k.copy("V", vtm.h[:, ch * 129: ch * 129 + 128],
                       k.pbh[0].h[:, j * 128:(j + 1) * 128],
                       reads=[k.pbh[0]], writes=[vtm])
        for qt in range(16):
            qp = qt & 1
            qsl = slice(qt * 128, (qt + 1) * 128)
            opb = k.pb[3 + 2 * qp]
            nkt = qt + 1
            first = True
            for g0 in range(0, nkt, 4):
                gp = (g0 // 4) & 1
                spb = k.pb[2 + 2 * gp]
                ptb = ptbp[gp]
                gn = min(4, nkt - g0)
                for j in range(gn):
                    k.mmop(spb, [(kro.h[:, (g0 + j) * 128:(g0 + j + 1) * 128],
                                  qro.h[:, qsl])], reads=[kro, qro],
                           out_slice=slice(j * 128, (j + 1) * 128))
                k.act(ptb.h[:, 0:gn * 128], spb.h[:, 0:gn * 128], AF.Exp,
                      reads=[spb], writes=[ptb])
                if g0 + gn == nkt:
                    k.tt("V", ptb.h[:, (gn - 1) * 128:gn * 128],
                         ptb.h[:, (gn - 1) * 128:gn * 128], triu.ap, ALU.mult,
                         reads=[ptb, triu], writes=[ptb])
                for j in range(gn):
                    last = (g0 + j == nkt - 1)
                    k.mmop(opb, [(ptb.h[:, j * 128:(j + 1) * 128],
                                  vtm.h[:, (g0 + j) * 129: (g0 + j) * 129 + 129])],
                           reads=[ptb, vtm], accum=not first,
                           out_slice=slice(0, 129), close=last)
                    first = False
            rc, oq = rcp[qp], oqp[qp]
            k.recip(rc.ap, opb.h[:, 128:129], reads=[opb], writes=[rc])
            k.ts("V", oq.ap, opb.h[:, 0:128], rc.h[:, 0:1], ALU.mult,
                 reads=[opb, rc], writes=[oq])
            k.transp(k.pbh[1], oq.ap, k.ident.ap, slice(qp * 128, qp * 128 + 128),
                     reads=[oq, k.ident])
            k.copy("V", y.h[:, h * T + qt * 128: h * T + qt * 128 + 128],
                   k.pbh[1].h[:, qp * 128: qp * 128 + 128],
                   reads=[k.pbh[1]], writes=[y])
    k.release(m1, s0)

    wo = A("w_att_wo", [128, NCH * 128], BF16)
    for dp in range(NCH):
        k.wload("att_wo", wo.h[:].rearrange("p (c i) -> p c i", i=128),
              W["att_wo"][:, dp * 128:(dp + 1) * 128].rearrange("(c p) i -> p c i", p=128),
              writes=[wo])
        for tc in range(4):
            tsl = slice(tc * 512, (tc + 1) * 512)
            k.mmop(k.pb[2 + tc % 2], [(wo.h[:, c * 128:(c + 1) * 128],
                                       y.h[:, c * T + tc * 512: c * T + tc * 512 + 512])
                                      for c in range(NCH)], reads=[wo, y])
            xt = k.X[dp]
            k.tt("V", xt.h[:, tsl], xt.h[:, tsl], k.pb[2 + tc % 2].ap, ALU.add,
                 reads=[xt, k.pb[2 + tc % 2]], writes=[xt])
    k.release(m0, tiles)


STEP_FNS["l1_attn_a"] = step_l1_attn_a
STEP_WEIGHTS["l1_attn_a"] = {"att_wq": (D, D), "att_wk": (D, D),
                             "att_wv": (D, D), "att_wo": (D, D)}



def emit_layernorm_bf(k, src, out_bf, scr, gb, goff, bb, boff, eps=1e-5):
    x2s, mb, rb = scr["x2s"], scr["mb"], scr["rb"]
    t0, t1 = scr["t0"], scr["t1"]
    for tc in range(T // 512):
        sl = slice(tc * 512, (tc + 1) * 512)
        xs = x2s[tc % 2]
        for c in range(NCH):
            k.act(xs.h[:, c * 512:(c + 1) * 512], src.h[:, c * T + tc * 512: c * T + tc * 512 + 512],
                  AF.Square, reads=[src], writes=[xs])
        k.mmop(k.pb[0], [(k.ones_bf.ap, src.h[:, c * T + tc * 512: c * T + tc * 512 + 512])
                         for c in range(NCH)], reads=[k.ones_bf, src])
        k.ts("V", mb.h[:, sl], k.pb[0].ap, 1.0 / D, ALU.mult, reads=[k.pb[0]], writes=[mb])
        k.mmop(k.pb[1], [(k.ones_bf.ap, xs.h[:, c * 512:(c + 1) * 512])
                         for c in range(NCH)], reads=[k.ones_bf, xs])
        k.ts("V", rb.h[:, sl], k.pb[1].ap, 1.0 / D, ALU.mult, reads=[k.pb[1]], writes=[rb])
    k.tt("G", t0.ap, mb.ap, mb.ap, ALU.mult, reads=[mb], writes=[t0])
    k.tt("V", rb.ap, rb.ap, t0.ap, ALU.subtract, reads=[rb, t0], writes=[rb])
    k.act(t0.ap, rb.ap, AF.Sqrt, bias=k.const(eps), reads=[rb], writes=[t0])
    k.recip(rb.ap, t0.ap, reads=[t0], writes=[rb])
    for c in range(NCH):
        csl = slice(c * T, (c + 1) * T)
        k.tt("G", t0.ap, src.h[:, csl], mb.ap, ALU.subtract, reads=[src, mb], writes=[t0])
        k.tt("V", t0.ap, t0.ap, rb.ap, ALU.mult, reads=[t0, rb], writes=[t0])
        k.ts("V", out_bf.h[:, csl], t0.ap, gb.h[:, goff + c:goff + c + 1], ALU.mult,
             s2=bb.h[:, boff + c:boff + c + 1], op1=ALU.add,
             reads=[t0, gb, bb], writes=[out_bf])
    return out_bf


def step_l0_attn_a(k, W):
    m0 = k.mark()
    tiles = []

    def A(name, shape, dtype):
        t = k.alloc(name, shape, dtype)
        tiles.append(t)
        return t

    triu = A("triu0", [128, 128], BF16)
    k.memset("V", triu.ap, 1.0, writes=[triu])
    k.affsel(triu.ap, triu.ap, [[1, 128]], ALU.is_ge, 0.0, -1, reads=[triu], writes=[triu])
    lng = A("lng", [128, NCH * 4], F32)
    k.dma("Y", lng.ap, W["l0a_lns"], writes=[lng])
    lnbb = A("lnbb", [128, NCH], BF16)
    k.copy("V", lnbb.ap, lng.h[:, NCH:2 * NCH], reads=[lng], writes=[lnbb])
    x1 = A("x1c", [128, NCH * T], BF16)

    mA = k.mark()
    sA = []

    def AA(name, shape, dtype):
        t = k.alloc(name, shape, dtype)
        sA.append(t)
        return t
    h0 = AA("h0", [128, NCH * T], BF16)
    y = AA("lay", [128, NCH * T], BF16)

    mL = k.mark()
    sL = []

    def AL(name, shape, dtype):
        t = k.alloc(name, shape, dtype)
        sL.append(t)
        return t
    scr = ln_scratch(k, AL)
    emit_layernorm(k, k.X, h0, scr)
    k.release(mL, sL)
    if int(os.environ.get("L0A_PHASE", "9")) <= 1:
        k.release(mA, sA)
        k.release(m0, tiles)
        return

    sL = []
    qh = AL("qh", [128, T], BF16)
    kh = AL("kh", [128, T], BF16)
    vh = AL("vh", [128, T], BF16)
    msb = AL("msb", [128, 64], BF16)
    msf = AL("msf", [64, 64], F32)
    ptb = AL("pt0", [128, 128], BF16)
    vtc = AL("vtc", [128, 128], BF16)
    ktc = AL("ktc", [128, 128], BF16)
    wqb = AL("wqb", [128, NCH * 64], BF16)
    wkb = AL("wkb", [128, NCH * 64], BF16)
    wvb = AL("wvb", [128, NCH * 64], BF16)
    bv = AL("bv", [64, 3], F32)
    k.memset("V", qh.h[64:128, :], 0.0, writes=[qh])
    k.memset("V", kh.h[64:128, :], 0.0, writes=[kh])
    k.memset("V", vh.h[64:128, :], 0.0, writes=[vh])
    k.memset("V", msb.h[64:128, :], 0.0, writes=[msb])
    for h in range(16):
        hsl = slice(h * 64, (h + 1) * 64)
        for wi, (wnm, wtile, dst) in enumerate((("l0a_wq", wqb, qh),
                                                ("l0a_wk", wkb, kh),
                                                ("l0a_wv", wvb, vh))):
            k.wload("l0a_wo", wtile.h[:].rearrange("p (c i) -> p c i", i=64),
                  W[wnm][:, hsl].rearrange("(c p) i -> p c i", p=128), writes=[wtile])
            k.mmop(k.pb[5], [(wtile.h[:, c * 64:(c + 1) * 64], lnbb.h[:, c:c + 1])
                             for c in range(NCH)], reads=[wtile, lnbb],
                   out_slice=slice(0, 1), nparts=64)
            k.copy("V", bv.h[:, wi:wi + 1], k.pb[5].h[0:64, 0:1],
                   reads=[k.pb[5]], writes=[bv])
            for c in range(NCH):
                k.ts("V", wtile.h[:, c * 64:(c + 1) * 64],
                     wtile.h[:, c * 64:(c + 1) * 64], lng.h[:, c:c + 1], ALU.mult,
                     reads=[wtile, lng], writes=[wtile])
            for tc in range(4):
                k.mmop(k.pb[0], [(wtile.h[:, c * 64:(c + 1) * 64],
                                  h0.h[:, c * T + tc * 512: c * T + tc * 512 + 512])
                                 for c in range(NCH)], reads=[wtile, h0], nparts=64)
                k.ts("V", dst.h[0:64, tc * 512:(tc + 1) * 512],
                     k.pb[0].h[0:64, :], bv.h[:, wi:wi + 1], ALU.add,
                     reads=[k.pb[0], bv], writes=[dst])
        k.memset("V", msb.h[0:64, :], 0.0, writes=[msb])
        k.memset("V", msf.ap, 0.0, writes=[msf])
        SUB = int(os.environ.get("L0A_SUB", "9"))
        for c in range(16):
            csl = slice(c * 128, (c + 1) * 128)
            if SUB == 0:
                continue
            k.mmop(k.pb[1], [(kh.h[:, csl], qh.h[:, csl])], reads=[kh, qh],
                   out_slice=slice(0, 128))
            if SUB == 1:
                continue
            k.tt("V", ptb.ap, k.pb[1].h[:, 0:128], triu.ap, ALU.mult,
                 reads=[k.pb[1], triu], writes=[ptb])
            if SUB == 2:
                continue
            k.transp(k.pbh[0], vh.h[:, csl], k.ident.ap, slice(0, 128),
                     reads=[vh, k.ident])
            k.transp(k.pbh[0], kh.h[:, csl], k.ident.ap, slice(128, 256),
                     reads=[kh, k.ident])
            k.copy("V", vtc.ap, k.pbh[0].h[:, 0:128], reads=[k.pbh[0]], writes=[vtc])
            k.copy("V", ktc.ap, k.pbh[0].h[:, 128:256], reads=[k.pbh[0]], writes=[ktc])
            if SUB == 3:
                continue
            k.mmop(k.pb[2], [(msb.ap, qh.h[:, csl]),
                             (vtc.h[:, 0:64], ptb.ap)], reads=[msb, qh, vtc, ptb],
                   out_slice=slice(0, 128), nparts=64)
            if SUB == 4:
                continue
            k.copy("V",
                   y.h[(h % 2) * 64:(h % 2) * 64 + 64,
                       (h // 2) * T + c * 128:(h // 2) * T + c * 128 + 128],
                   k.pb[2].h[0:64, 0:128], reads=[k.pb[2]], writes=[y])
            if c < 15:
                k.mmop(k.pb[3], [(ktc.h[:, 0:64], vtc.h[:, 0:64])], reads=[ktc, vtc],
                       out_slice=slice(0, 64), nparts=64)
                k.tt("V", msf.ap, msf.ap, k.pb[3].h[0:64, 0:64], ALU.add,
                     reads=[msf, k.pb[3]], writes=[msf])
                k.copy("G", msb.h[0:64, :], msf.ap, reads=[msf], writes=[msb])
    k.release(mL, sL)

    sL = []
    wob = AL("wob", [128, NCH * 128], BF16)
    for dp in range(NCH):
        k.wload("l0a_wo", wob.h[:].rearrange("p (c i) -> p c i", i=128),
                W["l0a_wo"][:, dp * 128:(dp + 1) * 128].rearrange("(c p) i -> p c i", p=128),
                writes=[wob])
        for tc in range(4):
            tsl = slice(tc * 512, (tc + 1) * 512)
            bank = k.pb[2 + tc % 2]
            k.mmop(bank, [(wob.h[:, c * 128:(c + 1) * 128],
                           y.h[:, c * T + tc * 512: c * T + tc * 512 + 512])
                          for c in range(NCH)], reads=[wob, y])
            xt = k.X[dp]
            k.tt("V", x1.h[:, dp * T + tc * 512: dp * T + tc * 512 + 512],
                 bank.ap, h0.h[:, dp * T + tc * 512: dp * T + tc * 512 + 512],
                 ALU.add, reads=[bank, h0], writes=[x1])
            k.tt("V", xt.h[:, tsl], xt.h[:, tsl], bank.ap, ALU.add,
                 reads=[xt, bank], writes=[xt])
            k.tt("G", xt.h[:, tsl], xt.h[:, tsl],
                 h0.h[:, dp * T + tc * 512: dp * T + tc * 512 + 512], ALU.add,
                 reads=[xt, h0], writes=[xt])
    k.release(mL, sL)
    k.release(mA, sA)

    sA = []
    scr = ln_scratch(k, AA)
    emit_layernorm_bf(k, x1, x1, scr, lng, 2 * NCH, lng, 3 * NCH)
    k.release(mA, sA)
    ping = x1
    if int(os.environ.get("L0A_PHASE", "9")) <= 4:
        k.release(m0, tiles)
        return

    pong = A("pong", [128, NCH * T], BF16)
    ub = A("ub", [128, 32 * 512], BF16)
    w1b = [A(f"w1b{i}", [128, NCH * 128], BF16) for i in range(2)]
    w2b = [A(f"w2b{i}", [128, 32 * 128], BF16) for i in range(2)]
    bcol = A("bcol", [128, 32 + NCH], F32)
    zt2 = [A(f"zt{i}", [128, 512], F32) for i in range(2)]
    et2 = [A(f"et{i}", [128, 512], F32) for i in range(2)]
    GC1 = 0.7978845608028654
    GC2 = 0.044715 * 0.7978845608028654
    GC1f = float(np.float32(GC1))
    GC2f = float(np.float32(GC2))
    for lvl in range(3):
        src_t = ping if lvl % 2 == 0 else pong
        dst_t = pong if lvl % 2 == 0 else ping
        k.dma("Y", bcol.ap, W["l0a_cms_b"][lvl], writes=[bcol])
        for tc in range(4):
            for ib in range(32):
                wb_ = w1b[ib % 2]
                k.wload("l0a_cms_w1", wb_.h[:].rearrange("p (c i) -> p c i", i=128),
                      W["l0a_cms_w1"][lvl][:, ib * 128:(ib + 1) * 128].rearrange(
                          "(c p) i -> p c i", p=128), writes=[wb_])
                bank = k.pb[ib % 4]
                zt = zt2[ib % 2]
                et = et2[ib % 2]
                k.mmop(bank, [(wb_.h[:, c * 128:(c + 1) * 128],
                               src_t.h[:, c * T + tc * 512: c * T + tc * 512 + 512])
                              for c in range(NCH)], reads=[wb_, src_t])
                k.ts("V", zt.ap, bank.ap, bcol.h[:, ib:ib + 1], ALU.add,
                     reads=[bank, bcol], writes=[zt])
                k.act(et.ap, zt.ap, AF.Square, reads=[zt], writes=[et])
                k.ts("V", et.ap, et.ap, GC2f, ALU.mult, s2=GC1f, op1=ALU.add,
                     reads=[et], writes=[et])
                k.tt("G", et.ap, et.ap, zt.ap, ALU.mult, reads=[et, zt], writes=[et])
                k.act(et.ap, et.ap, AF.Tanh, reads=[et], writes=[et])
                k.ts("V", et.ap, et.ap, 0.5, ALU.mult, s2=0.5, op1=ALU.add,
                     reads=[et], writes=[et])
                k.tt("G", ub.h[:, ib * 512:(ib + 1) * 512], et.ap, zt.ap, ALU.mult,
                     reads=[et, zt], writes=[ub])
            for dp in range(NCH):
                wb2 = w2b[dp % 2]
                k.wload("l0a_cms_w2", wb2.h[:].rearrange("p (c i) -> p c i", i=128),
                      W["l0a_cms_w2"][lvl][:, dp * 128:(dp + 1) * 128].rearrange(
                          "(c p) i -> p c i", p=128), writes=[wb2])
                bank = k.pb[4 + dp % 2]
                zt = zt2[dp % 2]
                k.mmop(bank, [(wb2.h[:, c * 128:(c + 1) * 128],
                               ub.h[:, c * 512:(c + 1) * 512])
                              for c in range(32)], reads=[wb2, ub])
                if lvl < 2:
                    k.ts("V", dst_t.h[:, dp * T + tc * 512: dp * T + tc * 512 + 512],
                         bank.ap, bcol.h[:, 32 + dp:33 + dp], ALU.add,
                         reads=[bank, bcol], writes=[dst_t])
                else:
                    xt = k.X[dp]
                    tsl = slice(tc * 512, (tc + 1) * 512)
                    k.ts("V", zt.ap, bank.ap, bcol.h[:, 32 + dp:33 + dp], ALU.add,
                         reads=[bank, bcol], writes=[zt])
                    k.tt("G", xt.h[:, tsl], xt.h[:, tsl], zt.ap, ALU.add,
                         reads=[xt, zt], writes=[xt])
    k.release(m0, tiles)


STEP_FNS["l0_attn_a"] = step_l0_attn_a
STEP_WEIGHTS["l0_attn_a"] = {
    "l0a_lns": (128, NCH * 4), "l0a_wq": (D, D), "l0a_wk": (D, D),
    "l0a_wv": (D, D), "l0a_wo": (D, D), "l0a_cms_w1": (3, D, 4096),
    "l0a_cms_w2": (3, 4096, D), "l0a_cms_b": (3, 128, 32 + NCH),
}



def step_l0_attn_b(k, W):
    TH = T // 2
    m0 = k.mark()
    tiles = []

    def A(name, shape, dtype):
        t = k.alloc(name, shape, dtype)
        tiles.append(t)
        return t

    sel32 = A("sel32", [32, 32 * 128], BF16)
    k.memset("V", sel32.ap, 1.0, writes=[sel32])
    k.affsel(sel32.ap, sel32.ap, [[1, 32 * 128]], ALU.is_ge, 0.0, -128,
             reads=[sel32], writes=[sel32])
    k.affsel(sel32.ap, sel32.ap, [[-1, 32 * 128]], ALU.is_ge, 0.0, 128, base=127,
             reads=[sel32], writes=[sel32])
    Atile = A("Atile", [128, NCH * 16], F32)
    k.dma("Y", Atile.ap, W["mam_alog"], writes=[Atile])
    k.act(Atile.ap, Atile.ap, AF.Exp, reads=[Atile], writes=[Atile])
    k.ts("V", Atile.ap, Atile.ap, -1.0, ALU.mult, reads=[Atile], writes=[Atile])
    convw = A("convw", [128, NCH * 3], F32)
    k.dma("Y", convw.ap, W["mam_conv"], writes=[convw])
    dbc = A("dbc", [128, NCH], F32)
    k.dma("Y", dbc.ap, W["mam_db"], writes=[dbc])
    valc = A("valc", [128, NCH * T], BF16)
    d1t = A("d1t", [128, TH], F32)
    wib = [A(f"wib{i}", [128, NCH * 128], BF16) for i in range(2)]
    mbk = A("mbk", [128, T], BF16)
    rbk = A("rbk", [128, T], BF16)
    stt_ = A("stt", [128, NCH * 16], F32)
    wbc = A("wbc", [128, NCH * 32], BF16)
    k.wload("mam_wbc", wbc.h[:].rearrange("p (c i) -> p c i", i=32),
          W["mam_wbc"].rearrange("(c p) i -> p c i", p=128), writes=[wbc])

    mV = k.mark()
    sV = []

    def AV(name, shape, dtype):
        t = k.alloc(name, shape, dtype)
        sV.append(t)
        return t
    val = AV("val", [128, NCH * (T + 2)], BF16)
    mX = k.mark()
    sX = []

    def AX(name, shape, dtype):
        t = k.alloc(name, shape, dtype)
        sX.append(t)
        return t
    scr = ln_scratch(k, AX)
    emit_ln_stats(k, k.X, scr)
    k.copy("V", mbk.ap, scr["mb"].ap, reads=[scr["mb"]], writes=[mbk])
    k.copy("V", rbk.ap, scr["rb"].ap, reads=[scr["rb"]], writes=[rbk])
    k.release(mX, sX)
    xtc = AV("xtc", [128, NCH * 512], BF16)
    for oc in range(NCH):
        k.memset("V", val.h[:, oc * (T + 2):oc * (T + 2) + 1], 0.0, writes=[val])
        k.memset("V", val.h[:, oc * (T + 2) + T + 1:(oc + 1) * (T + 2)], 0.0, writes=[val])
    for tc in range(4):
        xsl = slice(tc * 512, (tc + 1) * 512)
        for c in range(NCH):
            k.tt("G", d1t.h[:, 0:512], k.X[c].h[:, xsl], mbk.h[:, xsl],
                 ALU.subtract, reads=[k.X[c], mbk], writes=[d1t])
            k.tt("V", xtc.h[:, c * 512:(c + 1) * 512], d1t.h[:, 0:512],
                 rbk.h[:, xsl], ALU.mult, reads=[d1t, rbk], writes=[xtc])
        for oc in range(NCH):
            wb = wib[oc % 2]
            k.wload("mam_win", wb.h[:].rearrange("p (c i) -> p c i", i=128),
                  W["mam_win"][:, oc * 128:(oc + 1) * 128].rearrange(
                      "(c p) i -> p c i", p=128), writes=[wb])
            bank = k.pb[oc % 2]
            k.mmop(bank, [(wb.h[:, c * 128:(c + 1) * 128],
                           xtc.h[:, c * 512:(c + 1) * 512])
                          for c in range(NCH)], reads=[wb, xtc])
            k.copy("V", val.h[:, oc * (T + 2) + 1 + tc * 512: oc * (T + 2) + 1 + tc * 512 + 512],
                   bank.ap, reads=[bank], writes=[val])
    for oc in range(NCH):
        base = oc * (T + 2)
        for hc in range(2):
            t0 = hc * TH
            sm1 = val.h[:, base + t0: base + t0 + TH]
            s0 = val.h[:, base + 1 + t0: base + 1 + t0 + TH]
            sp1 = val.h[:, base + 2 + t0: base + 2 + t0 + TH]
            k.ts("V", d1t.ap, sm1, convw.h[:, oc * 3:oc * 3 + 1], ALU.mult,
                 reads=[val, convw], writes=[d1t])
            k.stt("V", d1t.ap, s0, convw.h[:, oc * 3 + 1:oc * 3 + 2], d1t.ap,
                  ALU.mult, ALU.add, reads=[val, convw, d1t], writes=[d1t])
            k.stt("V", d1t.ap, sp1, convw.h[:, oc * 3 + 2:oc * 3 + 3], d1t.ap,
                  ALU.mult, ALU.add, reads=[val, convw, d1t], writes=[d1t])
            k.act(valc.h[:, oc * T + t0: oc * T + t0 + TH], d1t.ap, AF.Silu,
                  reads=[d1t], writes=[valc])
    k.release(mV, sV)

    for h2 in range(2):
        t0 = h2 * TH
        mH = k.mark()
        sH = []

        def AH(name, shape, dtype):
            t = k.alloc(name, shape, dtype)
            sH.append(t)
            return t
        bcf = AH("bcf", [32, TH], BF16)
        dlt = AH("dlt", [128, NCH * TH], BF16)
        gsil = AH("gsil", [128, NCH * TH], BF16)
        mXN = k.mark()
        xnh = k.alloc("xnh", [128, NCH * TH], BF16)
        for c in range(NCH):
            for tc2 in range(2):
                xsl = slice(t0 + tc2 * 512, t0 + tc2 * 512 + 512)
                k.tt("G", d1t.h[:, 0:512], k.X[c].h[:, xsl], mbk.h[:, xsl],
                     ALU.subtract, reads=[k.X[c], mbk], writes=[d1t])
                k.tt("V", xnh.h[:, c * TH + tc2 * 512: c * TH + tc2 * 512 + 512],
                     d1t.h[:, 0:512], rbk.h[:, xsl], ALU.mult,
                     reads=[d1t, rbk], writes=[xnh])
        for tc2 in range(2):
            k.mmop(k.pb[2], [(wbc.h[:, c * 32:(c + 1) * 32],
                              valc.h[:, c * T + t0 + tc2 * 512: c * T + t0 + tc2 * 512 + 512])
                             for c in range(NCH)], reads=[wbc, valc], nparts=32)
            k.copy("V", bcf.h[:, tc2 * 512:(tc2 + 1) * 512], k.pb[2].h[0:32, :],
                   reads=[k.pb[2]], writes=[bcf])
        mC = k.mark()
        sC = []
        cdh = k.alloc("cdh", [128, NCH * TH], BF16)
        sC.append(cdh)
        wdb = [k.alloc(f"wdb{i}", [128, NCH * 128], BF16) for i in range(2)]
        sC.extend(wdb)
        for dp in range(NCH):
            wb = wdb[dp % 2]
            k.wload("mam_wd", wb.h[:].rearrange("p (c i) -> p c i", i=128),
                  W["mam_wd"][:, dp * 128:(dp + 1) * 128].rearrange("(c p) i -> p c i", p=128),
                  writes=[wb])
            for tc2 in range(2):
                bank = k.pb[tc2 % 2]
                k.mmop(bank, [(wb.h[:, c * 128:(c + 1) * 128],
                               valc.h[:, c * T + t0 + tc2 * 512: c * T + t0 + tc2 * 512 + 512])
                              for c in range(NCH)], reads=[wb, valc])
                k.ts("V", cdh.h[:, dp * TH + tc2 * 512: dp * TH + tc2 * 512 + 512],
                     bank.ap, dbc.h[:, dp:dp + 1], ALU.add, reads=[bank, dbc], writes=[cdh])
        for dp in range(NCH):
            wb = wdb[dp % 2]
            k.wload("mam_wt", wb.h[:].rearrange("p (c i) -> p c i", i=128),
                  W["mam_wt"][:, dp * 128:(dp + 1) * 128].rearrange("(c p) i -> p c i", p=128),
                  writes=[wb])
            for tc2 in range(2):
                bank = k.pb[tc2 % 2]
                k.mmop(bank, [(wb.h[:, c * 128:(c + 1) * 128],
                               cdh.h[:, c * TH + tc2 * 512: c * TH + tc2 * 512 + 512])
                              for c in range(NCH)], reads=[wb, cdh])
                dsl = slice(dp * TH + tc2 * 512, dp * TH + tc2 * 512 + 512)
                k.act(d1t.h[:, 512:1024], bank.ap, AF.Abs, reads=[bank], writes=[d1t])
                k.act(d1t.h[:, 512:1024], d1t.h[:, 512:1024], AF.Exp, scale=-1.0,
                     reads=[d1t], writes=[d1t])
                k.act(d1t.h[:, 512:1024], d1t.h[:, 512:1024], AF.Ln, bias=1.0,
                     reads=[d1t], writes=[d1t])
                k.ts("V", d1t.h[:, 0:512], bank.ap, 0.0, ALU.max, reads=[bank], writes=[d1t])
                k.tt("V", dlt.h[:, dsl], d1t.h[:, 0:512], d1t.h[:, 512:1024], ALU.add,
                     reads=[d1t], writes=[dlt])
        k.release(mC, sC)

        for gc in range(NCH):
            wb = wib[gc % 2]
            k.wload("mam_win", wb.h[:].rearrange("p (c i) -> p c i", i=128),
                  W["mam_win"][:, D + gc * 128: D + (gc + 1) * 128].rearrange(
                      "(c p) i -> p c i", p=128), writes=[wb])
            for tc2 in range(2):
                bank = k.pb[tc2 % 2]
                k.mmop(bank, [(wb.h[:, c * 128:(c + 1) * 128],
                               xnh.h[:, c * TH + tc2 * 512: c * TH + tc2 * 512 + 512])
                              for c in range(NCH)], reads=[wb, xnh])
                k.act(gsil.h[:, gc * TH + tc2 * 512: gc * TH + tc2 * 512 + 512],
                      bank.ap, AF.Silu, reads=[bank], writes=[gsil])
        for dp in range(NCH):
            for tc2 in range(2):
                xsl = slice(t0 + tc2 * 512, t0 + tc2 * 512 + 512)
                k.tt("G", k.X[dp].h[:, xsl], k.X[dp].h[:, xsl],
                     xnh.h[:, dp * TH + tc2 * 512: dp * TH + tc2 * 512 + 512],
                     ALU.add, reads=[k.X[dp], xnh], writes=[k.X[dp]])
        k.release(mXN, [xnh])
        ashp = [AH(f"ash{i}", [128, TH], BF16) for i in range(2)]
        sgtp = [AH(f"sgt{i}", [128, TH], BF16) for i in range(2)]
        rmgp = [d1t, AH("rmg1", [128, TH], F32)]
        wshp = [AH(f"wsh{i}", [128, TH], BF16) for i in range(2)]
        sshp = [AH(f"ssh{i}", [128, TH], BF16) for i in range(2)]
        prdp = [AH(f"prd{i}", [128, TH], BF16) for i in range(2)]
        bbtp = [AH(f"bbt{i}", [128, TH], BF16) for i in range(2)]
        cbtp = [AH(f"cbt{i}", [128, TH], BF16) for i in range(2)]
        dvtp = [AH(f"dvt{i}", [128, TH], BF16) for i in range(2)]
        ycb = AH("ycb", [128, TH], F32)
        units = [(cb, sg) for cb in range(NCH) for sg in range(16)]

        def emit_ash(idx):
            cb_, sg_ = units[idx]
            ash_ = ashp[sg_ & 1]
            acol_ = Atile.h[:, cb_ * 16 + sg_: cb_ * 16 + sg_ + 1]
            k.ts("V", ash_.ap, dlt.h[:, cb_ * TH:(cb_ + 1) * TH], acol_,
                 ALU.mult, reads=[dlt, Atile], writes=[ash_])

        for idx, (cb, sg) in enumerate(units):
            dsl = slice(cb * TH, (cb + 1) * TH)
            dvt = dvtp[cb & 1]
            if sg == 0:
                k.tt("V", dvt.ap, dlt.h[:, dsl],
                     valc.h[:, cb * T + t0: cb * T + t0 + TH], ALU.mult,
                     reads=[dlt, valc], writes=[dvt])
            p = sg & 1
            ash, sgt, rmg = ashp[p], sgtp[p], rmgp[p]
            wsh, ssh, prd = wshp[p], sshp[p], prdp[p]
            bbt, cbt = bbtp[p], cbtp[p]
            for (row, dst) in ((sg, bbt), (16 + sg, cbt)):
                for tc2 in range(2):
                    k.mmop(k.pb[2 + tc2], [(sel32.h[0:32, row * 128:(row + 1) * 128],
                                            bcf.h[0:32, tc2 * 512:(tc2 + 1) * 512])],
                           reads=[sel32, bcf])
                    k.act(dst.h[:, tc2 * 512:(tc2 + 1) * 512],
                          k.pb[2 + tc2].ap, AF.Copy,
                          reads=[k.pb[2 + tc2]], writes=[dst])
            if idx == 0:
                emit_ash(0)
            k.act(sgt.ap, ash.ap, AF.Sign, bias=1e-6, reads=[ash], writes=[sgt])
            k.act(rmg.ap, ash.ap, AF.Abs, bias=1e-6, reads=[ash], writes=[rmg])
            k.act(rmg.ap, rmg.ap, AF.Ln, reads=[rmg], writes=[rmg])
            k.act(rmg.ap, rmg.ap, AF.Exp, scale=-1.0, reads=[rmg], writes=[rmg])
            k.act(ash.ap, ash.ap, AF.Exp, reads=[ash], writes=[ash])
            k.tt("V", rmg.ap, rmg.ap, sgt.ap, ALU.mult,
                 reads=[rmg, sgt], writes=[rmg])
            k.ts("V", wsh.ap, ash.ap, -1.0, ALU.add, reads=[ash], writes=[wsh])
            k.tt("V", wsh.ap, wsh.ap, rmg.ap, ALU.mult,
                 reads=[wsh, rmg], writes=[wsh])
            k.tt("G", wsh.ap, wsh.ap, bbt.ap, ALU.mult, reads=[wsh, bbt], writes=[wsh])
            k.tt("G", wsh.ap, wsh.ap, dvt.ap, ALU.mult,
                 reads=[wsh, dvt], writes=[wsh])
            if idx + 1 < len(units):
                emit_ash(idx + 1)
            init = 0.0 if h2 == 0 else stt_.h[:, cb * 16 + sg: cb * 16 + sg + 1]
            rd = [ash, wsh] + ([stt_] if h2 == 1 else [])
            k.scan("V", ssh.ap, ash.ap, wsh.ap, init, ALU.mult, ALU.add,
                   reads=rd, writes=[ssh])
            k.copy("V", stt_.h[:, cb * 16 + sg: cb * 16 + sg + 1],
                   ssh.h[:, TH - 1:TH], reads=[ssh], writes=[stt_])
            if sg == 0:
                k.tt("G", ycb.ap, ssh.ap, cbt.ap, ALU.mult,
                     reads=[ssh, cbt], writes=[ycb])
            else:
                k.tt("G", prd.ap, ssh.ap, cbt.ap, ALU.mult,
                     reads=[ssh, cbt], writes=[prd])
                k.tt("V", ycb.ap, ycb.ap, prd.ap, ALU.add,
                     reads=[ycb, prd], writes=[ycb])
            if sg == 15:
                k.tt("V", gsil.h[:, dsl], ycb.ap, gsil.h[:, dsl], ALU.mult,
                     reads=[ycb, gsil], writes=[gsil])
        for dp in range(NCH):
            wb = wib[dp % 2]
            k.wload("mam_wout", wb.h[:].rearrange("p (c i) -> p c i", i=128),
                  W["mam_wout"][:, dp * 128:(dp + 1) * 128].rearrange(
                      "(c p) i -> p c i", p=128), writes=[wb])
            for tc2 in range(2):
                bank = k.pb[2 + tc2 % 2]
                k.mmop(bank, [(wb.h[:, c * 128:(c + 1) * 128],
                               gsil.h[:, c * TH + tc2 * 512: c * TH + tc2 * 512 + 512])
                              for c in range(NCH)], reads=[wb, gsil])
                xt = k.X[dp]
                xsl = slice(t0 + tc2 * 512, t0 + tc2 * 512 + 512)
                k.tt("V", xt.h[:, xsl], xt.h[:, xsl], bank.ap, ALU.add,
                     reads=[xt, bank], writes=[xt])
        k.release(mH, sH)
    k.release(m0, tiles)


STEP_FNS["l0_attn_b"] = step_l0_attn_b
STEP_WEIGHTS["l0_attn_b"] = {
    "mam_win": (D, 2 * D), "mam_conv": (128, NCH * 3), "mam_wbc": (D, 32),
    "mam_wd": (D, D), "mam_db": (128, NCH), "mam_wt": (D, D),
    "mam_alog": (128, NCH * 16), "mam_wout": (D, D),
}



def step_l1_attn_b(k, W):
    H, HD, C = 16, 64, 128
    NCHK = T // C
    NI = 2764
    ICW = [128] * 21 + [76]
    m0 = k.mark()
    tiles = []

    def A(name, shape, dtype):
        t = k.alloc(name, shape, dtype)
        tiles.append(t)
        return t

    mbk = A("mbk", [128, T], BF16)
    rbk = A("rbk", [128, T], BF16)
    n12 = A("n12", [128, 2 * NCH], F32)
    k.dma("Y", n12.ap, W["gd_norms"], writes=[n12])
    cwb = A("cwb", [128, 5 * NCH], F32)
    k.dma("Y", cwb.ap, W["gd_conv"], writes=[cwb])
    trilS = A("trilS", [128, 128], BF16)
    k.memset("V", trilS.ap, 1.0, writes=[trilS])
    k.affsel(trilS.ap, trilS.ap, [[-1, 128]], ALU.is_ge, 0.0, 1, base=-1,
             reads=[trilS], writes=[trilS])
    triuI = A("triuI", [128, 128], BF16)
    k.memset("V", triuI.ap, 1.0, writes=[triuI])
    k.affsel(triuI.ap, triuI.ap, [[1, 128]], ALU.is_ge, 0.0, -1,
             reads=[triuI], writes=[triuI])
    sel16 = A("sel16", [16, 16 * 128], F32)
    k.memset("V", sel16.ap, 1.0, writes=[sel16])
    k.affsel(sel16.ap, sel16.ap, [[1, 16 * 128]], ALU.is_ge, 0.0, -128,
             reads=[sel16], writes=[sel16])
    k.affsel(sel16.ap, sel16.ap, [[-1, 16 * 128]], ALU.is_ge, 0.0, 128, base=127,
             reads=[sel16], writes=[sel16])
    sel2 = A("sel2", [2, 2 * 128], BF16)
    k.memset("V", sel2.ap, 1.0, writes=[sel2])
    k.affsel(sel2.ap, sel2.ap, [[1, 2 * 128]], ALU.is_ge, 0.0, -64,
             reads=[sel2], writes=[sel2])
    k.affsel(sel2.ap, sel2.ap, [[-1, 2 * 128]], ALU.is_ge, 0.0, 64, base=63,
             reads=[sel2], writes=[sel2])

    mL = k.mark()
    sL = []

    def AL(name, shape, dtype):
        t = k.alloc(name, shape, dtype)
        sL.append(t)
        return t
    scr = ln_scratch(k, AL)
    emit_ln_stats(k, k.X, scr)
    k.copy("V", mbk.ap, scr["mb"].ap, reads=[scr["mb"]], writes=[mbk])
    k.copy("V", rbk.ap, scr["rb"].ap, reads=[scr["rb"]], writes=[rbk])
    k.release(mL, sL)

    mMid = k.mark()
    sMid = []

    def AM(name, shape, dtype):
        t = k.alloc(name, shape, dtype)
        sMid.append(t)
        return t
    o_s = k.nc.dram_tensor("gd_o_scratch", [D, T], BF16, kind="Internal").ap()
    o_sb = Buf("gd_o_scratch")
    o_buf = AM("obuf", [128, NCH * C], BF16)
    xc = AM("xc", [128, NCH * T], BF16)
    mC = k.mark()
    sC = []

    def AC(name, shape, dtype):
        t = k.alloc(name, shape, dtype)
        sC.append(t)
        return t
    xn1 = AC("xn1", [128, T + 3], BF16)
    rsb = AC("rsb", [128, T], F32)
    h2t = AC("h2t", [128, 512], BF16)
    cvt = AC("cvt", [128, 512], F32)
    for tc in range(4):
        sl = slice(tc * 512, (tc + 1) * 512)
        for c in range(NCH):
            k.tt("G", cvt.ap, k.X[c].h[:, sl], mbk.h[:, sl], ALU.subtract,
                 reads=[k.X[c], mbk], writes=[cvt])
            k.tt("V", h2t.ap, cvt.ap, rbk.h[:, sl], ALU.mult,
                 reads=[cvt, rbk], writes=[h2t])
            k.act(h2t.ap, h2t.ap, AF.Square, reads=[h2t], writes=[h2t])
            k.mmop(k.pb[0], [(k.ones_bf.ap, h2t.ap)], reads=[k.ones_bf, h2t],
                   accum=(c > 0), close=(c == NCH - 1))
        k.ts("V", rsb.h[:, sl], k.pb[0].ap, 1.0 / D, ALU.mult, s2=1e-6, op1=ALU.add,
             reads=[k.pb[0]], writes=[rsb])
        k.act(rsb.h[:, sl], rsb.h[:, sl], AF.Sqrt, reads=[rsb], writes=[rsb])
    k.recip(rsb.ap, rsb.ap, reads=[rsb], writes=[rsb])
    for c in range(NCH):
        k.memset("V", xn1.h[:, 0:3], 0.0, writes=[xn1])
        for tc in range(4):
            sl = slice(tc * 512, (tc + 1) * 512)
            k.tt("G", cvt.ap, k.X[c].h[:, sl], mbk.h[:, sl], ALU.subtract,
                 reads=[k.X[c], mbk], writes=[cvt])
            k.tt("V", cvt.ap, cvt.ap, rbk.h[:, sl], ALU.mult,
                 reads=[cvt, rbk], writes=[cvt])
            k.tt("G", cvt.ap, cvt.ap, rsb.h[:, sl], ALU.mult, reads=[cvt, rsb], writes=[cvt])
            k.ts("V", xn1.h[:, 3 + tc * 512: 3 + tc * 512 + 512],
                 cvt.ap, n12.h[:, c:c + 1], ALU.mult, reads=[cvt, n12], writes=[xn1])
        for hc in range(4):
            t0 = hc * 512
            k.ts("V", cvt.ap, xn1.h[:, t0: t0 + 512],
                 cwb.h[:, c:c + 1], ALU.mult, reads=[xn1, cwb], writes=[cvt])
            for j in range(1, 4):
                k.stt("V", cvt.ap, xn1.h[:, j + t0: j + t0 + 512],
                      cwb.h[:, j * NCH + c: j * NCH + c + 1], cvt.ap,
                      ALU.mult, ALU.add, reads=[xn1, cwb, cvt], writes=[cvt])
            k.act(xc.h[:, c * T + t0: c * T + t0 + 512], cvt.ap, AF.Silu,
                  bias=cwb.h[:, 4 * NCH + c: 4 * NCH + c + 1],
                  reads=[cvt, cwb], writes=[xc])
    k.release(mC, sC)
    if int(os.environ.get("GD_PH", "9")) <= 1:
        k.release(mMid, sMid)
        k.release(m0, tiles)
        return

    mD = k.mark()
    sD = []

    def AD(name, shape, dtype):
        t = k.alloc(name, shape, dtype)
        sD.append(t)
        return t
    betaf = AD("betaf", [16, T], BF16)
    lgf = AD("lgf", [16, T], F32)
    Sst = AD("Sst", [64, H * 64], F32)
    Sbf = AD("Sbf", [64, H * 64], BF16)
    k.memset("V", Sst.ap, 0.0, writes=[Sst])
    k.memset("V", Sbf.ap, 0.0, writes=[Sbf])
    mB = k.mark()
    sB = []
    wba = k.alloc("wba", [128, NCH * 16], BF16)
    t16a = k.alloc("t16a", [16, 512], F32)
    t16b = k.alloc("t16b", [16, 512], F32)
    sB.extend([wba, t16a, t16b])
    for (wnm, coff) in (("gd_wb", 0), ("gd_wa", 1)):
        k.wload("gd_wo", wba.h[:].rearrange("p (c i) -> p c i", i=16),
              W[wnm].rearrange("(c p) i -> p c i", p=128), writes=[wba])
        for tc in range(4):
            sl = slice(tc * 512, (tc + 1) * 512)
            k.mmop(k.pb[0], [(wba.h[:, c * 16:(c + 1) * 16],
                              xc.h[:, c * T + tc * 512: c * T + tc * 512 + 512])
                             for c in range(NCH)], reads=[wba, xc], nparts=16)
            if coff == 0:
                k.act(betaf.h[:, sl], k.pb[0].h[0:16, :], AF.Sigmoid,
                      reads=[k.pb[0]], writes=[betaf])
            else:
                k.ts("V", t16a.ap, k.pb[0].h[0:16, :], 4.0, ALU.add,
                     reads=[k.pb[0]], writes=[t16a])
                k.act(t16b.ap, t16a.ap, AF.Abs, reads=[t16a], writes=[t16b])
                k.act(t16b.ap, t16b.ap, AF.Exp, scale=-1.0, reads=[t16b], writes=[t16b])
                k.act(t16b.ap, t16b.ap, AF.Ln, bias=1.0, reads=[t16b], writes=[t16b])
                k.ts("V", t16a.ap, t16a.ap, 0.0, ALU.min, reads=[t16a], writes=[t16a])
                k.tt("V", lgf.h[:, sl], t16a.ap, t16b.ap, ALU.subtract,
                     reads=[t16a, t16b], writes=[lgf])
    zz16 = t16a
    k.memset("V", zz16.ap, 0.0, writes=[zz16])
    for cc in range(NCHK):
        k.scan("V", lgf.h[0:16, cc * C:(cc + 1) * C], lgf.h[0:16, cc * C:(cc + 1) * C],
               zz16.h[0:16, 0:C], 0.0, ALU.add, ALU.add, reads=[lgf, zz16], writes=[lgf])
    k.release(mB, sB)
    if int(os.environ.get("GD_PH", "9")) <= 2:
        k.release(mMid, sMid)
        k.release(m0, tiles)
        return

    qcf = AD("qcf", [128, NCH * C], BF16)
    kcf = AD("kcf", [128, NCH * C], BF16)
    vcf = AD("vcf", [128, NCH * C], BF16)
    wpj = [AD(f"wpj{i}", [128, NCH * 1024], BF16) for i in range(2)]
    wvs = [AD(f"wvs{i}", [128, NCH * 128], BF16) for i in range(2)]
    lgtm = AD("lgtm", [128, 16], F32)
    bttm = AD("bttm", [128, 16], F32)
    sq2 = AD("sq2", [128, C], BF16)
    rs2 = AD("rs2", [2, C], F32)
    rs2b = AD("rs2b", [2, C], BF16)
    nrm = AD("nrm", [128, C], BF16)
    US = []
    for ui in range(2):
        u = {}
        for nm, sh, dt_ in (("gC2", [64, 1], F32), ("gcol", [128, 1], F32),
                            ("m2f", [128, C], F32), ("t1f", [128, C], F32),
                            ("qh_t", [64, C], BF16), ("kh_t", [64, C], BF16),
                            ("kb_t", [64, C], BF16), ("Ebf", [128, C], BF16),
                            ("ETbf", [128, C], BF16), ("grow", [128, C], BF16),
                            ("Lp", [128, C], F32), ("Lt", [128, C], F32),
                            ("L2p", [128, C], F32), ("L2t", [128, C], F32),
                            ("MT", [128, C], BF16), ("Vtm", [128, 64], BF16),
                            ("Rtm", [128, 64], F32), ("Xtm", [128, 64], F32),
                            ("Xbf", [128, 64], BF16), ("obf", [128, 64], BF16),
                            ("ktm", [128, 64], BF16)):
            u[nm] = AD(f"{nm}{ui}", sh, dt_)
        US.append(u)
    trilSf = AD("trilSf", [128, 128], F32)
    hsum2 = AD("hsum2", [128, 2], BF16)
    k.memset("V", trilSf.ap, 1.0, writes=[trilSf])
    k.affsel(trilSf.ap, trilSf.ap, [[-1, 128]], ALU.is_ge, 0.0, 1, base=-1,
             reads=[trilSf], writes=[trilSf])
    k.transp(k.pbh[0], sel2.h[0:2, 0:128], k.ident.h[0:2, 0:2], slice(0, 2),
             reads=[sel2, k.ident], nparts=128)
    k.copy("V", hsum2.ap, k.pbh[0].h[:, 0:2], reads=[k.pbh[0]], writes=[hsum2])
    for cc in range(NCHK):
        csl = slice(cc * C, (cc + 1) * C)
        for wi, (wnm, dst) in enumerate((("gd_wq", qcf), ("gd_wk", kcf))):
            wt = wpj[wi]
            if cc == 0:
                k.wload(wnm, wt.h[:].rearrange("p (c i) -> p c i", i=1024),
                        W[wnm].rearrange("(c p) i -> p c i", p=128), writes=[wt])
            for oc in range(NCH):
                bank = k.pb[oc % 2]
                k.mmop(bank, [(wt.h[:, c * 1024 + oc * 128: c * 1024 + oc * 128 + 128],
                               xc.h[:, c * T + cc * C: c * T + cc * C + C])
                              for c in range(NCH)], reads=[wt, xc],
                       out_slice=slice(0, C))
                k.copy("S" if oc % 2 else "V", dst.h[:, oc * C:(oc + 1) * C],
                       bank.h[:, 0:C], reads=[bank], writes=[dst])
        for oc in range(NCH):
            wv_ = wvs[oc % 2]
            k.wload("gd_wv", wv_.h[:].rearrange("p (c i) -> p c i", i=128),
                    W["gd_wv"][:, oc * 128:(oc + 1) * 128].rearrange(
                        "(c p) i -> p c i", p=128), writes=[wv_])
            bank = k.pb[2 + oc % 2]
            k.mmop(bank, [(wv_.h[:, c * 128:(c + 1) * 128],
                           xc.h[:, c * T + cc * C: c * T + cc * C + C])
                          for c in range(NCH)], reads=[wv_, xc],
                   out_slice=slice(0, C))
            k.copy("S" if oc % 2 else "V", vcf.h[:, oc * C:(oc + 1) * C],
                   bank.h[:, 0:C], reads=[bank], writes=[vcf])
        for dst in (qcf, kcf):
            for oc in range(NCH):
                osl = slice(oc * C, (oc + 1) * C)
                k.act(sq2.ap, dst.h[:, osl], AF.Square, reads=[dst], writes=[sq2])
                k.mmop(k.pb[2], [(hsum2.ap, sq2.ap)], reads=[hsum2, sq2], nparts=2,
                       out_slice=slice(0, C))
                k.ts("V", rs2.ap, k.pb[2].h[0:2, 0:C], 1.0, ALU.mult, s2=1e-6, op1=ALU.add,
                     reads=[k.pb[2]], writes=[rs2])
                k.act(rs2.ap, rs2.ap, AF.Sqrt, reads=[rs2], writes=[rs2])
                k.recip(rs2.ap, rs2.ap, reads=[rs2], writes=[rs2])
                k.copy("V", rs2b.ap, rs2.ap, reads=[rs2], writes=[rs2b])
                k.mmop(k.pb[3], [(sel2.h[0:2, 0:128], rs2b.h[0:2, :])],
                       reads=[sel2, rs2b], out_slice=slice(0, C))
                k.copy("V", nrm.ap, k.pb[3].h[:, 0:C], reads=[k.pb[3]], writes=[nrm])
                k.tt("V", dst.h[:, osl], dst.h[:, osl], nrm.ap, ALU.mult,
                     reads=[dst, nrm], writes=[dst])
        k.transp(k.pb[4], lgf.h[0:16, csl], k.identf.h[0:16, 0:16], slice(0, 16),
                 reads=[lgf, k.identf], nparts=128)
        k.copy("V", lgtm.ap, k.pb[4].h[:, 0:16], reads=[k.pb[4]], writes=[lgtm])
        k.transp(k.pbh[0], betaf.h[0:16, csl], k.ident.h[0:16, 0:16], slice(0, 16),
                 reads=[betaf, k.ident], nparts=128)
        k.copy("V", bttm.ap, k.pbh[0].h[:, 0:16], reads=[k.pbh[0]], writes=[bttm])
        for h in range(H):
            u = US[h % 2]
            B0 = k.pb[(h % 2) * 3]
            B1 = k.pb[(h % 2) * 3 + 1]
            B2 = k.pb[(h % 2) * 3 + 2]
            PH = k.pbh[h % 2]
            oc = h // 2
            pb_ = (h % 2) * 64
            qsl = slice(oc * C, (oc + 1) * C)
            ssl = slice(h * 64, (h + 1) * 64)
            btcol = bttm.h[:, h:h + 1]
            k.copy("V", u["qh_t"].ap, qcf.h[pb_:pb_ + 64, qsl], reads=[qcf], writes=[u["qh_t"]])
            k.copy("G", u["kh_t"].ap, kcf.h[pb_:pb_ + 64, qsl], reads=[kcf], writes=[u["kh_t"]])
            k.mmop(B0, [(sel16.h[0:16, h * 128:(h + 1) * 128], lgf.h[0:16, csl])],
                   reads=[sel16, lgf], out_slice=slice(0, C))
            k.act(u["grow"].ap, B0.h[:, 0:C], AF.Exp, reads=[B0], writes=[u["grow"]])
            k.ts("V", u["m2f"].ap, B0.h[:, 0:C], lgtm.h[:, h:h + 1], ALU.subtract,
                 reads=[B0, lgtm], writes=[u["m2f"]])
            k.ts("V", u["t1f"].ap, u["m2f"].ap, 0.0, ALU.max, reads=[u["m2f"]], writes=[u["t1f"]])
            k.act(u["Ebf"].ap, u["t1f"].ap, AF.Exp, scale=-1.0, reads=[u["t1f"]], writes=[u["Ebf"]])
            k.ts("V", u["t1f"].ap, u["m2f"].ap, 0.0, ALU.min, reads=[u["m2f"]], writes=[u["t1f"]])
            k.act(u["ETbf"].ap, u["t1f"].ap, AF.Exp, reads=[u["t1f"]], writes=[u["ETbf"]])
            k.mmop(B1, [(u["kh_t"].ap, u["kh_t"].ap)], reads=[u["kh_t"]], out_slice=slice(0, C))
            k.tt("V", u["Lp"].ap, B1.h[:, 0:C], u["Ebf"].ap, ALU.mult,
                 reads=[B1, u["Ebf"]], writes=[u["Lp"]])
            k.ts("V", u["Lp"].ap, u["Lp"].ap, btcol, ALU.mult, reads=[u["Lp"], bttm], writes=[u["Lp"]])
            k.tt("G", u["Lp"].ap, u["Lp"].ap, trilSf.ap, ALU.mult, reads=[u["Lp"], trilSf], writes=[u["Lp"]])
            k.transp(B0, u["Lp"].ap, k.identf.ap, slice(0, 128), reads=[u["Lp"], k.identf])
            k.copy("S", u["Lt"].ap, B0.h[:, 0:C], reads=[B0], writes=[u["Lt"]])
            k.mmop(B2, [(u["kh_t"].ap, u["qh_t"].ap)], reads=[u["kh_t"], u["qh_t"]],
                   out_slice=slice(0, C))
            k.tt("V", u["MT"].ap, B2.h[:, 0:C], u["ETbf"].ap, ALU.mult,
                 reads=[B2, u["ETbf"]], writes=[u["MT"]])
            k.tt("G", u["MT"].ap, u["MT"].ap, triuI.ap, ALU.mult, reads=[u["MT"], triuI], writes=[u["MT"]])
            k.tt("V", u["kb_t"].ap, u["kh_t"].ap, u["grow"].h[0:64, :], ALU.mult,
                 reads=[u["kh_t"], u["grow"]], writes=[u["kb_t"]])
            k.tt("G", u["qh_t"].ap, u["qh_t"].ap, u["grow"].h[0:64, :], ALU.mult,
                 reads=[u["qh_t"], u["grow"]], writes=[u["qh_t"]])
            k.transp(PH, vcf.h[pb_:pb_ + 64, qsl],
                     k.ident.h[pb_:pb_ + 64, pb_:pb_ + 64],
                     slice(0, 64), reads=[vcf, k.ident])
            k.copy("V", u["Vtm"].ap, PH.h[:, 0:64], reads=[PH], writes=[u["Vtm"]])
            k.transp(PH, u["kh_t"].ap, k.ident.h[0:64, 0:64], slice(64, 128),
                     reads=[u["kh_t"], k.ident])
            k.copy("V", u["ktm"].ap, PH.h[:, 64:128], reads=[PH], writes=[u["ktm"]])
            k.mmop(B2, [(u["kb_t"].ap, Sbf.h[0:64, ssl])], reads=[u["kb_t"], Sbf],
                   out_slice=slice(0, 64))
            k.tt("V", u["Rtm"].ap, u["Vtm"].ap, B2.h[:, 0:64], ALU.subtract,
                 reads=[u["Vtm"], B2], writes=[u["Rtm"]])
            k.ts("V", u["Rtm"].ap, u["Rtm"].ap, btcol, ALU.mult, reads=[u["Rtm"], bttm], writes=[u["Rtm"]])
            k.mmop(B1, [(u["Lt"].ap, u["Rtm"].ap)], reads=[u["Lt"], u["Rtm"]], out_slice=slice(0, 64))
            k.tt("V", u["Xtm"].ap, u["Rtm"].ap, B1.h[:, 0:64], ALU.subtract,
                 reads=[u["Rtm"], B1], writes=[u["Xtm"]])
            cp, ct, np_, nt = u["Lp"], u["Lt"], u["L2p"], u["L2t"]
            for lev in range(6):
                k.mmop(B1, [(ct.ap, cp.ap)], reads=[ct, cp], out_slice=slice(0, C))
                k.copy("S", np_.ap, B1.h[:, 0:C], reads=[B1], writes=[np_])
                k.transp(B0, np_.ap, k.identf.ap, slice(0, 128), reads=[np_, k.identf])
                k.copy("S", nt.ap, B0.h[:, 0:C], reads=[B0], writes=[nt])
                k.mmop(B1, [(nt.ap, u["Xtm"].ap)], reads=[nt, u["Xtm"]], out_slice=slice(0, 64))
                k.tt("V", u["Xtm"].ap, u["Xtm"].ap, B1.h[:, 0:64], ALU.add,
                     reads=[u["Xtm"], B1], writes=[u["Xtm"]])
                cp, ct, np_, nt = np_, nt, cp, ct
            k.copy("G", u["Xbf"].ap, u["Xtm"].ap, reads=[u["Xtm"]], writes=[u["Xbf"]])
            k.mmop(B2, [(u["qh_t"].ap, Sbf.h[0:64, ssl]), (u["MT"].ap, u["Xbf"].ap)],
                   reads=[u["qh_t"], Sbf, u["MT"], u["Xbf"]], out_slice=slice(0, 64))
            k.copy("S", u["obf"].ap, B2.h[:, 0:64], reads=[B2], writes=[u["obf"]])
            k.transp(PH, u["obf"].ap, k.ident.ap, slice(0, C), reads=[u["obf"], k.ident],
                     nparts=64)
            k.copy("V", o_buf.h[pb_:pb_ + 64, oc * C:(oc + 1) * C],
                   PH.h[0:64, 0:C], reads=[PH], writes=[o_buf])
            k.act(u["gcol"].ap, u["m2f"].h[:, C - 1:C], AF.Exp, reads=[u["m2f"]], writes=[u["gcol"]])
            k.ts("V", u["ktm"].ap, u["ktm"].ap, u["gcol"].h[:, 0:1], ALU.mult,
                 reads=[u["ktm"], u["gcol"]], writes=[u["ktm"]])
            k.mmop(B2, [(u["ktm"].ap, u["Xbf"].ap)], reads=[u["ktm"], u["Xbf"]],
                   out_slice=slice(0, 64), nparts=64)
            k.copy("V", u["gC2"].ap, u["grow"].h[0:64, C - 1:C], reads=[u["grow"]], writes=[u["gC2"]])
            k.stt("V", Sst.h[0:64, ssl], Sst.h[0:64, ssl], u["gC2"].h[0:64, 0:1],
                  B2.h[0:64, 0:64], ALU.mult, ALU.add,
                  reads=[Sst, u["gC2"], B2], writes=[Sst])
            k.copy("G", Sbf.h[0:64, ssl], Sst.h[0:64, ssl], reads=[Sst], writes=[Sbf])
        k.dma("S", o_s.rearrange("(c p) t -> p c t", p=128)[:, :, cc * C:(cc + 1) * C],
              o_buf.h[:].rearrange("p (c t) -> p c t", t=C), reads=[o_buf], writes=[o_sb])
    k.release(mD, sD)
    if int(os.environ.get("GD_PH", "9")) <= 3:
        k.release(mMid, sMid)
        k.release(m0, tiles)
        return
    hh = AM("hh", [128, NCH * T], BF16)

    mW = k.mark()
    sW = []
    wob = k.alloc("wob", [128, NCH * 128], BF16)
    sW.append(wob)
    ob2 = k.alloc("ob2", [128, NCH * 512], BF16)
    sW.append(ob2)
    for tc in range(4):
        k.dma("Y", ob2.h[:].rearrange("p (c t) -> p c t", t=512),
              o_s.rearrange("(c p) t -> p c t", p=128)[:, :, tc * 512:(tc + 1) * 512],
              reads=[o_sb], writes=[ob2])
        for dp in range(NCH):
            tsl = slice(tc * 512, (tc + 1) * 512)
            bank = k.pb[2 + dp % 2]
            k.wload("gd_wo", wob.h[:].rearrange("p (c i) -> p c i", i=128),
                    W["gd_wo"][:, dp * 128:(dp + 1) * 128].rearrange(
                        "(c p) i -> p c i", p=128), writes=[wob])
            k.mmop(bank, [(wob.h[:, c * 128:(c + 1) * 128],
                           ob2.h[:, c * 512:(c + 1) * 512])
                          for c in range(NCH)], reads=[wob, ob2])
            hsl2 = slice(dp * T + tc * 512, dp * T + tc * 512 + 512)
            k.tt("G", hh.h[:, hsl2], k.X[dp].h[:, tsl], mbk.h[:, tsl], ALU.subtract,
                 reads=[k.X[dp], mbk], writes=[hh])
            k.tt("V", hh.h[:, hsl2], hh.h[:, hsl2], rbk.h[:, tsl], ALU.mult,
                 reads=[hh, rbk], writes=[hh])
            k.tt("V", hh.h[:, hsl2], hh.h[:, hsl2], bank.ap, ALU.add,
                 reads=[hh, bank], writes=[hh])
            xt = k.X[dp]
            k.tt("G", xt.h[:, tsl], xt.h[:, tsl],
                 hh.h[:, dp * T + tc * 512: dp * T + tc * 512 + 512], ALU.add,
                 reads=[xt, hh], writes=[xt])
    k.release(mW, sW)

    if int(os.environ.get("GD_PH", "9")) <= 4:
        k.release(mMid, sMid)
        k.release(m0, tiles)
        return

    sW = []
    mW = k.mark()

    def AW(name, shape, dtype):
        t = k.alloc(name, shape, dtype)
        sW.append(t)
        return t
    rsb2 = AW("rsb2", [128, T], F32)
    h2b = AW("h2b", [128, 512], BF16)
    for tc in range(4):
        sl = slice(tc * 512, (tc + 1) * 512)
        for c in range(NCH):
            k.act(h2b.ap, hh.h[:, c * T + tc * 512: c * T + tc * 512 + 512],
                  AF.Square, reads=[hh], writes=[h2b])
            k.mmop(k.pb[0], [(k.ones_bf.ap, h2b.ap)], reads=[k.ones_bf, h2b],
                   accum=(c > 0), close=(c == NCH - 1))
        k.ts("V", rsb2.h[:, sl], k.pb[0].ap, 1.0 / D, ALU.mult, s2=1e-6, op1=ALU.add,
             reads=[k.pb[0]], writes=[rsb2])
        k.act(rsb2.h[:, sl], rsb2.h[:, sl], AF.Sqrt, reads=[rsb2], writes=[rsb2])
    k.recip(rsb2.ap, rsb2.ap, reads=[rsb2], writes=[rsb2])
    for c in range(NCH):
        for tc in range(4):
            sl = slice(tc * 512, (tc + 1) * 512)
            hsl = slice(c * T + tc * 512, c * T + tc * 512 + 512)
            k.tt("G", hh.h[:, hsl], hh.h[:, hsl], rsb2.h[:, sl], ALU.mult,
                 reads=[hh, rsb2], writes=[hh])
            k.ts("V", hh.h[:, hsl], hh.h[:, hsl], n12.h[:, NCH + c:NCH + c + 1],
                 ALU.mult, reads=[hh, n12], writes=[hh])
    hbuf = AW("hbuf", [128, 22 * 512], BF16)
    wgb = [AW(f"wgb{i}", [128, NCH * 128], BF16) for i in range(2)]
    wub = [AW(f"wub{i}", [128, NCH * 128], BF16) for i in range(2)]
    wdb2 = [AW(f"wdb2{i}", [128, 128], BF16) for i in range(2)]
    sgtp2 = [AW(f"sgt{i}", [128, 512], F32) for i in range(2)]
    ic_off = [0]
    for w_ in ICW:
        ic_off.append(ic_off[-1] + w_)
    for tc in range(4):
        tsl = slice(tc * 512, (tc + 1) * 512)
        for ic in range(22):
            iw = ICW[ic]
            wg_ = wgb[ic % 2]
            wu_ = wub[ic % 2]
            k.wload("gd_wg", wg_.h[:, 0:NCH * iw].rearrange("p (c i) -> p c i", i=iw),
                  W["gd_wg"][:, ic_off[ic]:ic_off[ic] + iw].rearrange(
                      "(c p) i -> p c i", p=128), writes=[wg_])
            k.wload("gd_wu", wu_.h[:, 0:NCH * iw].rearrange("p (c i) -> p c i", i=iw),
                  W["gd_wu"][:, ic_off[ic]:ic_off[ic] + iw].rearrange(
                      "(c p) i -> p c i", p=128), writes=[wu_])
            bg_ = k.pb[(ic % 2) * 2]
            bu_ = k.pb[(ic % 2) * 2 + 1]
            k.mmop(bg_, [(wg_.h[:, c * iw:(c + 1) * iw],
                          hh.h[:, c * T + tc * 512: c * T + tc * 512 + 512])
                         for c in range(NCH)], reads=[wg_, hh], nparts=iw)
            k.mmop(bu_, [(wu_.h[:, c * iw:(c + 1) * iw],
                          hh.h[:, c * T + tc * 512: c * T + tc * 512 + 512])
                         for c in range(NCH)], reads=[wu_, hh], nparts=iw)
            sgt_ = sgtp2[ic % 2]
            k.act(sgt_.h[0:iw, :], bg_.h[0:iw, :], AF.Silu, reads=[bg_], writes=[sgt_])
            k.tt("V", hbuf.h[0:iw, ic * 512:(ic + 1) * 512], sgt_.h[0:iw, :],
                 bu_.h[0:iw, :], ALU.mult, reads=[sgt_, bu_], writes=[hbuf])
        for dp in range(NCH):
            bank = k.pb[2 + dp % 2]
            for ic in range(22):
                iw = ICW[ic]
                wd_ = wdb2[ic % 2]
                k.wload("gd_wd", wd_.h[0:iw, :],
                      W["gd_wd"][ic_off[ic]:ic_off[ic] + iw, dp * 128:(dp + 1) * 128],
                      writes=[wd_])
                k.mmop(bank, [(wd_.h[0:iw, :], hbuf.h[0:iw, ic * 512:(ic + 1) * 512])],
                       reads=[wd_, hbuf], accum=(ic > 0), close=(ic == 21))
            xt = k.X[dp]
            k.tt("V", xt.h[:, tsl], xt.h[:, tsl], bank.ap, ALU.add,
                 reads=[xt, bank], writes=[xt])
    k.release(mW, sW)
    k.release(mMid, sMid)
    k.release(m0, tiles)


STEP_FNS["l1_attn_b"] = step_l1_attn_b
STEP_WEIGHTS["l1_attn_b"] = {
    "gd_norms": (128, 2 * NCH), "gd_conv": (128, 5 * NCH),
    "gd_wq": (D, D), "gd_wk": (D, D), "gd_wv": (D, D),
    "gd_wb": (D, 16), "gd_wa": (D, 16), "gd_wo": (D, D),
    "gd_wg": (D, 2764), "gd_wu": (D, 2764), "gd_wd": (2764, D),
}
STEP_FNS["l1_attn_b"] = step_l1_attn_b
